# version 02
# baseline (speedup 1.0000x reference)
import numpy as np
import ml_dtypes
import concourse.bass as bass
import concourse.mybir as mybir
from concourse.bass_utils import run_bass_kernel_spmd

F32 = mybir.dt.float32
BF16 = mybir.dt.bfloat16
U8 = mybir.dt.uint8
AF = mybir.ActivationFunctionType
ALU = mybir.AluOpType
AX = mybir.AxisListType.X

B, S, D = 2, 8192, 1024
NCORE = 8
TPC = 2048
NCH = 8
NTT = 4
TT = 512
MAGIC = 12582912.0
GROUPS_BATCH = [[0, 1, 2, 3], [4, 5, 6, 7]]
GROUP_ALL = [list(range(8))]
BF = ml_dtypes.bfloat16

class Ctx:

    def __init__(self, nc):
        self.nc = nc
        self.dsem = nc.alloc_semaphore("dsem")
        self.dcnt = 0
        self.gsem = nc.alloc_semaphore("gsem")
        self.gcnt = 0
        self.csem = nc.alloc_semaphore("csem")
        self.ccnt = 0
        self.pe_sem = nc.alloc_semaphore("pe_sem")
        self.pe_cnt = 0
        self.act_sem = nc.alloc_semaphore("act_sem")
        self.act_cnt = 0
        self.esem = {}
        self.ecnt = {}
        for e in ("tensor", "vector", "scalar", "sync", "gpsimd"):
            self.esem[e] = nc.alloc_semaphore(f"esem_{e}")
            self.ecnt[e] = 0
        self.PS = nc.alloc_psum_tensor("PS", [128, 4096], F32)
        self.consts = {}
        self.input_specs = {}
        self._uid = 0

    def uid(self):
        self._uid += 1
        return self._uid

    def dma(self, out, in_, **kw):
        self.nc.sync.dma_start(out=out, in_=in_, **kw).then_inc(self.dsem, 16)
        self.dcnt += 16

    def dmag(self, out, in_, **kw):
        self.nc.gpsimd.dma_start(out=out, in_=in_, **kw).then_inc(self.gsem, 16)
        self.gcnt += 16

    def dma_tr(self, out, in_):
        self.nc.sync.dma_start_transpose(out, in_).then_inc(self.dsem, 16)
        self.dcnt += 16

    def dwait(self):
        self.nc.sync.wait_ge(self.dsem, self.dcnt)
        self.nc.gpsimd.wait_ge(self.gsem, self.gcnt)
        self.nc.all_engine_barrier()

    def bar(self):
        self.nc.all_engine_barrier()

    def edge(self, prod, *cons):
        getattr(self.nc, prod).drain().then_inc(self.esem[prod], 1)
        self.ecnt[prod] += 1
        for cn in cons:
            getattr(self.nc, cn).wait_ge(self.esem[prod], self.ecnt[prod])

    def coll(self, kind, op, groups, in_ap, out_ap):
        self.nc.gpsimd.collective_compute(
            kind, op, replica_groups=groups, ins=[in_ap], outs=[out_ap]
        ).then_inc(self.csem)
        self.ccnt += 1

    def cwait(self):
        self.nc.gpsimd.wait_ge(self.csem, self.ccnt)
        self.nc.all_engine_barrier()

    def inp(self, name, shape, dtype=F32):
        h = self.nc.dram_tensor(name, list(shape), dtype, kind="ExternalInput")
        self.input_specs[name] = (tuple(shape), dtype)
        return h.ap()

    def const(self, name, arr):
        if name in self.consts:
            return self.consts[name]
        h = self.nc.inline_tensor(np.ascontiguousarray(arr), name=name)
        self.consts[name] = h.ap()
        return self.consts[name]



class Sbuf:
    def __init__(self, nc):
        self.nc = nc
        self.x = [nc.alloc_sbuf_tensor(f"x{c}", [128, TPC], F32) for c in range(NCH)]
        self.A = nc.alloc_sbuf_tensor("arenaA", [128, 30208], F32)
        self.R = nc.alloc_sbuf_tensor("rows", [128, 2048], F32)
        self._amark = []
        self._aoff = 0

    def a(self, off, n):
        return self.A[:, off:off + n]

    def ab(self, off_f32, n_bf16):
        return self.A.ap().bitcast(BF16)[:, 2 * off_f32: 2 * off_f32 + n_bf16]

    def mark(self):
        self._amark.append(self._aoff)

    def release(self):
        self._aoff = self._amark.pop()

    def falloc(self, n_f32):
        off = self._aoff
        n = (n_f32 + 7) & ~7
        assert off + n <= 29952, f"arena A overflow: {off}+{n}"
        self._aoff = off + n
        return off

    def af(self, n_f32):
        return self.a(self.falloc(n_f32), n_f32)

    def abf(self, n_bf16):
        nf = (n_bf16 + 1) // 2
        return self.ab(self.falloc(nf), n_bf16)



def emit_load_x(ctx, sb, x_in):
    nc = ctx.nc
    sb.mark()
    idf = sb.af(128)
    ctx.dma(idf, ctx.const("ident128f", np.eye(128, dtype=np.float32)))
    stage = sb.af(1024)
    ctx.dwait()
    for t in range(TPC // 128):
        ctx.dma(stage, x_in[t * 128:(t + 1) * 128, :])
        ctx.dwait()
        for c in range(NCH):
            nc.tensor.transpose(ctx.PS[:, 0:128], stage[:, c * 128:(c + 1) * 128], idf)
            ctx.bar()
            nc.vector.tensor_copy(sb.x[c][:, t * 128:(t + 1) * 128], ctx.PS[:, 0:128])
            ctx.bar()
    sb.release()


def emit_store_x(ctx, sb, x_out):
    nc = ctx.nc
    sb.mark()
    idf = sb.af(128)
    ctx.dma(idf, ctx.const("ident128f", np.eye(128, dtype=np.float32)))
    stage = sb.af(1024)
    ctx.dwait()
    for t in range(TPC // 128):
        for c in range(NCH):
            nc.tensor.transpose(ctx.PS[:, 0:128], sb.x[c][:, t * 128:(t + 1) * 128], idf)
            ctx.bar()
            nc.vector.tensor_copy(stage[:, c * 128:(c + 1) * 128], ctx.PS[:, 0:128])
            ctx.bar()
        ctx.dma(x_out[t * 128:(t + 1) * 128, :], stage)
        ctx.dwait()
    sb.release()


def emit_layernorm(ctx, sb, xn_bf_dram, stats_dram=None, track_absmax=False):
    nc = ctx.nc
    PS = ctx.PS
    sb.mark()
    onesv = sb.af(8)[:, 0:1]
    ctx.dma(onesv, ctx.const("ones128f", np.ones((128, 1), dtype=np.float32)))
    sq = sb.af(TT)
    xnf = sb.af(TT)
    am = sb.af(8)[:, 0:1]
    amr = sb.af(128)
    onesb = sb.abf(128)[0:1, :]
    rstdb = sb.abf(TT)[0:1, :]
    browb = sb.abf(TT)[0:1, :]
    xnb = sb.abf(TT)
    ctx.dwait()
    nc.vector.memset(onesb, 1.0)
    if track_absmax:
        nc.vector.memset(am, 0.0)
    for t in range(NTT):
        tsl = slice(t * TT, (t + 1) * TT)
        for c in range(NCH):
            nc.tensor.matmul(PS[0:1, 0:TT], onesv, sb.x[c][:, tsl],
                             start=(c == 0), stop=(c == NCH - 1))
        for c in range(NCH):
            nc.scalar.activation(sq, sb.x[c][:, tsl], AF.Square)
            ctx.bar()
            nc.tensor.matmul(PS[0:1, TT:2 * TT], onesv, sq,
                             start=(c == 0), stop=(c == NCH - 1))
            ctx.bar()
        mean = sb.R[0:1, 0:TT]
        var = sb.R[0:1, TT:2 * TT]
        rstd = sb.R[0:1, 2 * TT:3 * TT]
        brow = sb.R[0:1, 3 * TT:4 * TT]
        nc.vector.tensor_scalar(mean, PS[0:1, 0:TT], 1.0 / D, None, ALU.mult)
        nc.vector.tensor_scalar(var, PS[0:1, TT:2 * TT], 1.0 / D, None, ALU.mult)
        nc.vector.tensor_tensor(rstd, mean, mean, ALU.mult)
        nc.vector.tensor_sub(var, var, rstd)
        nc.vector.tensor_scalar(var, var, 1e-5, None, ALU.add)
        ctx.bar()
        nc.scalar.activation(rstd, var, AF.Sqrt)
        ctx.bar()
        nc.vector.reciprocal(rstd, rstd)
        nc.vector.tensor_tensor(brow, mean, rstd, ALU.mult)
        nc.vector.tensor_scalar(brow, brow, -1.0, None, ALU.mult)
        ctx.bar()
        nc.vector.tensor_copy(rstdb, rstd)
        nc.vector.tensor_copy(browb, brow)
        ctx.bar()
        nc.tensor.matmul(PS[:, 2 * TT:3 * TT], onesb, rstdb, start=True, stop=True)
        nc.tensor.matmul(PS[:, 3 * TT:4 * TT], onesb, browb, start=True, stop=True)
        ctx.bar()
        if stats_dram is not None:
            ctx.dma(stats_dram[0:1, tsl], rstd)
            ctx.dma(stats_dram[1:2, tsl], brow)
        for c in range(NCH):
            nc.vector.tensor_tensor(xnf, sb.x[c][:, tsl], PS[:, 2 * TT:3 * TT], ALU.mult)
            nc.vector.tensor_add(xnf, xnf, PS[:, 3 * TT:4 * TT])
            nc.vector.tensor_copy(xnb, xnf)
            if track_absmax:
                nc.vector.reduce_max(sq[:, 0:1], xnf, axis=AX, apply_absolute_value=True)
                nc.vector.tensor_max(am, am, sq[:, 0:1])
            ctx.bar()
            ctx.dmag(xn_bf_dram[c * 128:(c + 1) * 128, tsl], xnb)
            nc.gpsimd.wait_ge(ctx.gsem, ctx.gcnt)
            ctx.bar()
    if track_absmax:
        identb = ctx.get_identb(sb)
        amb = sb.abf(128)
        nc.vector.tensor_copy(amb[:, 0:1], am)
        ctx.bar()
        nc.tensor.transpose(ctx.PS.ap().bitcast(BF16)[0:1, 0:128], amb[:, 0:1], identb)
        ctx.bar()
        nc.vector.tensor_copy(amr[0:1, 0:128], ctx.PS.ap().bitcast(BF16)[0:1, 0:128])
        ctx.bar()
        nc.vector.reduce_max(sb.R[0:1, 2047:2048], amr[0:1, 0:128], axis=AX)
        ctx.bar()
    sb.release()


def _ctx_get_identb(ctx, sb):
    if not hasattr(ctx, "_identb"):
        ctx._identb = sb.ab(30144, 128)
        ctx.dmag(ctx._identb, ctx.const("ident128b", np.eye(128, dtype=BF)))
        ctx._identf_sb = sb.a(29952, 128)
        ctx.dma(ctx._identf_sb, ctx.const("ident128f", np.eye(128, dtype=np.float32)))
        ctx.dwait()
    return ctx._identb

Ctx.get_identb = _ctx_get_identb




DEBUG_ATTN = False
L2F_PHASES = 4

ATT_NKT = S // 128
ATT_NQT = S // TT


def tr_copy(ctx, items, bf16=True, slots=4):
    nc = ctx.nc
    PSb = ctx.PS.ap().bitcast(BF16)
    PSf = ctx.PS
    ident = ctx._identb if bf16 else ctx._identf_sb
    n = len(items)
    i = 0
    while i < n:
        grp = items[i:i + slots]
        for j, (src, dst) in enumerate(grp):
            pp, ff = src.shape[0], src.shape[1]
            if bf16:
                out = PSb[0:ff, 7 * 1024 + j * 128: 7 * 1024 + j * 128 + pp]
                nc.tensor.transpose(out, src, ident[0:pp, 0:pp])
            else:
                out = PSf[0:ff, 7 * 512 + j * 128: 7 * 512 + j * 128 + pp]
                nc.tensor.transpose(out, src, ident[0:pp, 0:pp])
        ctx.edge("tensor", "vector")
        for j, (src, dst) in enumerate(grp):
            pp, ff = src.shape[0], src.shape[1]
            if bf16:
                nc.vector.tensor_copy(dst, PSb[0:ff, 7 * 1024 + j * 128: 7 * 1024 + j * 128 + pp])
            else:
                nc.vector.tensor_copy(dst, PSf[0:ff, 7 * 512 + j * 128: 7 * 512 + j * 128 + pp])
        ctx.edge("vector", "tensor")
        i += slots
    ctx.edge("vector", "scalar", "sync", "gpsimd")


def attn_evac_tile(ctx, sb, cfg, bufs, p, t):
    nc = ctx.nc
    PS = ctx.PS
    PSb = ctx.PS.ap().bitcast(BF16)
    kbuf, qbuf, vbuf = bufs["kbuf"], bufs["qbuf"], bufs["vbuf"]
    sqt, sq2, ltmp, rmin, nh64 = (bufs["sqt"], bufs["sq2"], bufs["ltmp"],
                                  bufs["rmin"], bufs["nh64"])
    identb = bufs["identb"]
    ss = cfg["sqrt_scale"]
    tsl = slice(t * TT, (t + 1) * TT)
    bq, bk, bv = cfg["bias"](0, p), cfg["bias"](1, p), cfg["bias"](2, p)
    if bk is not None:
        nc.vector.tensor_scalar(kbuf[0:64, tsl], PS[0:64, 0:TT], bk, ss,
                                ALU.add, ALU.mult)
    else:
        nc.vector.tensor_scalar(kbuf[0:64, tsl], PS[0:64, 0:TT], ss, None, ALU.mult)
    if bv is not None:
        nc.vector.tensor_scalar(sqt[0:64, :], PS[0:64, TT:2 * TT], bv, None, ALU.add)
    else:
        nc.vector.tensor_copy(sqt[0:64, :], PS[0:64, TT:2 * TT])
    if bq is not None:
        nc.vector.tensor_scalar(qbuf[0:64, tsl], PS[0:64, 2 * TT:3 * TT], bq, ss,
                                ALU.add, ALU.mult)
    else:
        nc.vector.tensor_scalar(qbuf[0:64, tsl], PS[0:64, 2 * TT:3 * TT], ss, None,
                                ALU.mult)
    ctx.bar()
    nc.scalar.activation(sq2[0:64, 0:TT], kbuf[0:64, tsl], AF.Square)
    nc.scalar.activation(sq2[0:64, TT:2 * TT], qbuf[0:64, tsl], AF.Square)
    ctx.bar()
    nc.tensor.matmul(PS[0:1, 3 * TT:4 * TT], nh64, sq2[0:64, 0:TT],
                     start=True, stop=True)
    nc.tensor.matmul(PS[32:33, 3 * TT:4 * TT], nh64, sq2[0:64, TT:2 * TT],
                     start=True, stop=True)
    for j in range(4):
        nc.tensor.transpose(PSb[:, 4096 + j * 64:4096 + (j + 1) * 64],
                            sqt[0:64, j * 128:(j + 1) * 128], identb[0:64, 0:64])
    ctx.bar()
    nc.vector.tensor_reduce(ltmp[0:1, 0:1], PS[0:1, 3 * TT:4 * TT], AX, ALU.min)
    nc.vector.tensor_tensor(rmin[0:1, 0:1], rmin[0:1, 0:1], ltmp[0:1, 0:1], ALU.min)
    for j in range(4):
        kt = t * 4 + j
        nc.vector.tensor_copy(vbuf[:, kt * 65: kt * 65 + 64],
                              PSb[:, 4096 + j * 64:4096 + (j + 1) * 64])
    nc.scalar.activation(qbuf[64:65, tsl], PS[32:33, 3 * TT:4 * TT], AF.Copy)
    ctx.bar()


def emit_attn_std(ctx, sb, cfg, xnag, combine_fn):
    nc = ctx.nc
    PS = ctx.PS
    PSb = ctx.PS.ap().bitcast(BF16)
    identb = ctx.get_identb(sb)
    sb.mark()
    kbuf = sb.abf(8192)
    qbuf = sb.abf(8192)
    vbuf = sb.abf(ATT_NKT * 65 + 63)
    pbuf = [sb.abf(1536), sb.abf(1536)]
    zstage = sb.abf(8192)
    xnt = sb.abf(4096)
    sqt = sb.abf(512)
    sq2 = sb.abf(1024)
    ltmp = sb.af(TT)
    rmin = sb.af(8)
    nh64 = sb.abf(64)[0:64, 0:1]
    ctx.dmag(nh64, ctx.const("neghalf64", np.full((64, 1), -0.5, dtype=BF)))
    patt = None
    if cfg.get("patterns") is not None:
        patt = sb.abf(2048)
        ctx.dmag(patt, cfg["patterns"])
    ctx.dwait()

    causal = cfg["causal"]
    ss = cfg["sqrt_scale"]

    bufs = dict(kbuf=kbuf, qbuf=qbuf, vbuf=vbuf, sqt=sqt, sq2=sq2, ltmp=ltmp,
                rmin=rmin, nh64=nh64, xnt=xnt, identb=identb)

    def std_gen_pair(p):
        nc.vector.memset(vbuf, 1.0)
        nc.vector.memset(rmin[0:1, 0:1], 0.0)
        nc.vector.memset(kbuf[64:65, :], 1.0)
        ctx.bar()
        for t in range(ATT_NQT):
            r, u = t // 4, t % 4
            tsl = slice(t * TT, (t + 1) * TT)
            ctx.dma(xnt, xnag[r * 1024:(r + 1) * 1024,
                              u * TT:(u + 1) * TT].rearrange("(c p) t -> p c t", p=128))
            ctx.dwait()
            for proj in range(3):
                pso = [2 * TT, 0, TT][proj]
                for c in range(NCH):
                    nc.tensor.matmul(PS[0:64, pso:pso + TT],
                                     cfg["lhsT"](proj, p, c),
                                     xnt[:, c * TT:(c + 1) * TT],
                                     start=(c == 0), stop=(c == NCH - 1))
            ctx.bar()
            attn_evac_tile(ctx, sb, cfg, bufs, p, t)
        nc.vector.tensor_scalar(qbuf[64:65, :], qbuf[64:65, :], rmin[0:1, 0:1],
                                None, ALU.add)
        ctx.bar()

    for p in range(4):
        if cfg.get("gen_pair") is None:
            std_gen_pair(p)
        else:
            cfg["gen_pair"](p, bufs)
        if DEBUG_ATTN and p == 0:
            dk = nc.dram_tensor("dbg_k", [65, S], BF16, kind="ExternalOutput").ap()
            dq = nc.dram_tensor("dbg_q", [65, S], BF16, kind="ExternalOutput").ap()
            dv = nc.dram_tensor("dbg_v", [128, ATT_NKT * 65], BF16, kind="ExternalOutput").ap()
            ctx.dma(dk, kbuf[0:65, :])
            ctx.dma(dq, qbuf[0:65, :])
            ctx.dma(dv, vbuf[:, 0:ATT_NKT * 65])
            ctx.dwait()
        half_free_at = [0, 0]
        pbuf_free_at = [0, 0]
        for qt in range(ATT_NQT):
            q0 = qt * TT
            nkt = (q0 // 128 + 4) if causal else ATT_NKT
            units = [list(range(s0, min(s0 + 3, nkt))) for s0 in range(0, nkt, 3)]
            nz = 0
            for ui, unit in enumerate(units):
                h = ui % 2
                off = h * 1536
                w = len(unit) * TT
                nc.tensor.wait_ge(ctx.act_sem, half_free_at[h])
                for j, kt in enumerate(unit):
                    k0 = kt * 128
                    diag = causal and (k0 + 127 > q0)
                    mm = nc.tensor.matmul(PS[:, off + j * TT: off + (j + 1) * TT],
                                          kbuf[0:65, kt * 128:(kt + 1) * 128],
                                          qbuf[0:65, q0:q0 + TT],
                                          start=True, stop=not diag)
                    if diag:
                        rr = (k0 - q0) // 128
                        mm = nc.tensor.matmul(PS[:, off + j * TT: off + (j + 1) * TT],
                                              identb, patt[:, rr * TT:(rr + 1) * TT],
                                              start=False, stop=True)
                mm.then_inc(ctx.pe_sem, 1)
                ctx.pe_cnt += 1
                smm_pe = ctx.pe_cnt
                nc.scalar.wait_ge(ctx.pe_sem, max(smm_pe, pbuf_free_at[h]))
                nc.scalar.activation(pbuf[h][:, 0:w], PS[:, off:off + w],
                                     AF.Exp).then_inc(ctx.act_sem, 1)
                ctx.act_cnt += 1
                half_free_at[h] = ctx.act_cnt
                nc.tensor.wait_ge(ctx.act_sem, ctx.act_cnt)
                last_unit = (ui == len(units) - 1)
                for j, kt in enumerate(unit):
                    mm = nc.tensor.matmul(PS[0:65, 3072:3072 + TT],
                                          vbuf[:, kt * 65:(kt + 1) * 65],
                                          pbuf[h][:, j * TT:(j + 1) * TT],
                                          start=(nz == 0),
                                          stop=(last_unit and j == len(unit) - 1))
                    nz += 1
                mm.then_inc(ctx.pe_sem, 1)
                ctx.pe_cnt += 1
                pbuf_free_at[h] = ctx.pe_cnt
            nc.scalar.wait_ge(ctx.pe_sem, ctx.pe_cnt)
            nc.scalar.wait_ge(ctx.dsem, ctx.dcnt)
            nc.scalar.activation(zstage[0:64, q0:q0 + TT], PS[0:64, 3072:3072 + TT],
                                 AF.Copy)
            nc.scalar.activation(ltmp[0:1, :], PS[64:65, 3072:3072 + TT],
                                 AF.Copy).then_inc(ctx.act_sem, 1)
            ctx.act_cnt += 1
            nc.tensor.wait_ge(ctx.act_sem, ctx.act_cnt)
            nc.sync.wait_ge(ctx.act_sem, ctx.act_cnt)
            ctx.dma(cfg["l_in"].ap()[p:p + 1, q0:q0 + TT], ltmp[0:1, :])
        ctx.bar()
        ctx.dmag(cfg["agz_in"].ap()[p * 64:(p + 1) * 64, :], zstage[0:64, :])
        ctx.dwait()
        if DEBUG_ATTN and p == 0:
            dz = ctx.nc.dram_tensor("dbg_z", [64, S], BF16, kind="ExternalOutput").ap()
            dl = ctx.nc.dram_tensor("dbg_l", [1, S], F32, kind="ExternalOutput").ap()
            ctx.dma(dz, zstage[0:64, :])
            ctx.dwait()
            ctx.dmag(dl, cfg["l_in"].ap()[0:1, :])
            ctx.dwait()
    sb.release()

    ctx.coll("AllGather", ALU.bypass, GROUPS_BATCH,
             cfg["agz_in"].ap(), cfg["agz_out"].ap())
    ctx.coll("AllGather", ALU.bypass, GROUPS_BATCH,
             cfg["l_in"].ap(), cfg["l_out"].ap())
    ctx.cwait()
    emit_attn_out(ctx, sb, cfg, combine_fn)


def emit_attn_out(ctx, sb, cfg, combine_fn):
    nc = ctx.nc
    PS = ctx.PS
    sb.mark()
    zin = sb.abf(8 * TPC)
    lrec = sb.af(TPC)
    Efull = sb.af(1024)[0:16, :]
    ctx.dma(Efull, ctx.const("E16f", cfg["E16"]))
    cid = nc.partition_id()
    qreg = nc.snap(cid % 4)
    for c in range(NCH):
        for hh in range(2):
            h = 2 * c + hh
            r, pp = h // 4, h % 4
            ctx.dma(zin[hh * 64:(hh + 1) * 64, c * TPC:(c + 1) * TPC],
                    cfg["agz_out"].ap()[r * 256 + pp * 64: r * 256 + (pp + 1) * 64,
                                        bass.ts(qreg, TPC)])
    for r in range(4):
        ctx.dma(lrec[0:4, :] if r == 0 else lrec[0:16, :][r * 4:(r + 1) * 4, :],
                cfg["l_out"].ap()[r * 4:(r + 1) * 4, bass.ts(qreg, TPC)])
    ctx.dwait()
    if DEBUG_ATTN:
        dlr = nc.dram_tensor("dbg_lrec", [16, TPC], F32, kind="ExternalOutput").ap()
        ctx.dma(dlr, lrec[0:16, :])
        ctx.dwait()
    nc.vector.reciprocal(lrec[0:16, :], lrec[0:16, :])
    ctx.bar()
    sg = sb.abf(TT)
    for t in range(NTT):
        tsl = slice(t * TT, (t + 1) * TT)
        for c in range(NCH):
            nc.tensor.matmul(PS[:, 3 * TT:4 * TT], Efull[:, c * 128:(c + 1) * 128],
                             lrec[0:16, tsl], start=True, stop=True)
            ctx.bar()
            nc.vector.tensor_tensor(sg, zin[:, c * TPC + t * TT: c * TPC + (t + 1) * TT],
                                    PS[:, 3 * TT:4 * TT], ALU.mult)
            ctx.bar()
            nc.vector.tensor_copy(zin[:, c * TPC + t * TT: c * TPC + (t + 1) * TT], sg)
            ctx.bar()
        for co in range(NCH):
            for ci in range(NCH):
                nc.tensor.matmul(PS[:, 0:TT], cfg["woT"](ci, co),
                                 zin[:, ci * TPC + t * TT: ci * TPC + (t + 1) * TT],
                                 start=(ci == 0), stop=(ci == NCH - 1))
            ctx.bar()
            if DEBUG_ATTN and t == 0 and co == 0:
                dsout = nc.dram_tensor("dbg_s00", [128, TT], F32, kind="ExternalOutput").ap()
                stmp_d = sb.af(TT)
                nc.vector.tensor_copy(stmp_d, PS[:, 0:TT])
                ctx.bar()
                ctx.dma(dsout, stmp_d)
                ctx.dwait()
            combine_fn(co, t, PS[:, 0:TT])
            ctx.bar()
    sb.release()



def causal_patterns():
    pat = np.zeros((128, 2048), dtype=np.float32)
    for r in range(4):
        kk = np.arange(128)[:, None]
        qq = np.arange(512)[None, :]
        pat[:, r * 512:(r + 1) * 512] = np.where(qq < kk + 128 * r, -1e9, 0.0)
    return pat.astype(BF)


def e16_const():
    E = np.zeros((16, 1024), dtype=np.float32)
    for h in range(16):
        E[h, h * 64:(h + 1) * 64] = 1.0
    return E


def attn_dram(ctx, tag):
    nc = ctx.nc
    d = {}
    d["agz_in"] = nc.dram_tensor(f"agzin_{tag}", [256, S], BF16, kind="Internal")
    d["agz_out"] = nc.dram_tensor(f"agzout_{tag}", [1024, S], BF16, kind="Internal")
    d["l_in"] = nc.dram_tensor(f"lin_{tag}", [4, S], F32, kind="Internal")
    d["l_out"] = nc.dram_tensor(f"lout_{tag}", [16, S], F32, kind="Internal")
    return d


def emit_layer2_attn(ctx, sb, xnag, g_dram, tag="l2"):
    nc = ctx.nc
    wqkv = ctx.inp("L2w_qkv_m", [D, 768], F32)
    bqkv = ctx.inp("L2b_qkv_m", [64, 12], F32)
    wo = ctx.inp("L2w_o", [D, D], F32)
    wob = ctx.inp("L2b_o", [128, 8], F32)
    sb.mark()
    Wb = sb.abf(NCH * 768)
    ctx.dmag(Wb, wqkv.rearrange("(c p) o -> p c o", p=128))
    Bia = sb.af(12)[0:64, :]
    ctx.dma(Bia, bqkv)
    Wob = sb.abf(NCH * 1024)
    ctx.dmag(Wob, wo.rearrange("(c p) o -> p c o", p=128))
    Wobi = sb.af(8)
    ctx.dma(Wobi, wob)
    gt = sb.af(TT)
    tmp = sb.af(TT)
    ctx.dwait()

    d = attn_dram(ctx, tag)
    cfg = {
        "causal": True,
        "sqrt_scale": float(1.0 / np.sqrt(np.sqrt(64.0))),
        "lhsT": lambda proj, h, c: Wb[:, c * 768 + proj * 256 + h * 64:
                                      c * 768 + proj * 256 + (h + 1) * 64],
        "bias": lambda proj, h: Bia[:, proj * 4 + h: proj * 4 + h + 1],
        "woT": lambda ci, co: Wob[:, ci * 1024 + co * 128: ci * 1024 + (co + 1) * 128],
        "patterns": ctx.const("causal_patt", causal_patterns()),
        "E16": e16_const(),
    }
    cfg.update(d)

    def combine(co, t, s_psum):
        tsl = slice(t * TT, (t + 1) * TT)
        ctx.dma(gt, g_dram[co * 128:(co + 1) * 128, tsl])
        nc.sync.wait_ge(ctx.dsem, ctx.dcnt)
        ctx.bar()
        nc.vector.tensor_scalar(tmp, s_psum, Wobi[0:128, co:co + 1], None, ALU.add)
        ctx.bar()
        nc.vector.tensor_mul(tmp, tmp, gt)
        ctx.bar()
        nc.vector.tensor_add(sb.x[co][:, tsl], sb.x[co][:, tsl], tmp)

    emit_attn_std(ctx, sb, cfg, xnag, combine)
    sb.release()


def emit_std_attn_layer(ctx, sb, xnag, g_dram, li, with_bias, wo_bias, tag):
    nc = ctx.nc
    wqkv = ctx.inp(f"L{li}w_qkv_m", [D, 768], F32)
    if with_bias:
        bqkv = ctx.inp(f"L{li}b_qkv_m", [64, 12], F32)
    wo = ctx.inp(f"L{li}w_o", [D, D], F32)
    if wo_bias:
        wob = ctx.inp(f"L{li}b_o", [128, 8], F32)
    sb.mark()
    Wb = sb.abf(NCH * 768)
    ctx.dmag(Wb, wqkv.rearrange("(c p) o -> p c o", p=128))
    if with_bias:
        Bia = sb.af(12)[0:64, :]
        ctx.dma(Bia, bqkv)
    Wob = sb.abf(NCH * 1024)
    ctx.dmag(Wob, wo.rearrange("(c p) o -> p c o", p=128))
    if wo_bias:
        Wobi = sb.af(8)
        ctx.dma(Wobi, wob)
    gt = sb.af(TT)
    tmp = sb.af(TT)
    ctx.dwait()

    d = attn_dram(ctx, tag)
    cfg = {
        "causal": True,
        "sqrt_scale": float(1.0 / np.sqrt(np.sqrt(64.0))),
        "lhsT": lambda proj, h, c: Wb[:, c * 768 + proj * 256 + h * 64:
                                      c * 768 + proj * 256 + (h + 1) * 64],
        "bias": (lambda proj, h: Bia[:, proj * 4 + h: proj * 4 + h + 1]) if with_bias
                else (lambda proj, h: None),
        "woT": lambda ci, co: Wob[:, ci * 1024 + co * 128: ci * 1024 + (co + 1) * 128],
        "patterns": ctx.const("causal_patt", causal_patterns()),
        "E16": e16_const(),
    }
    cfg.update(d)

    def combine(co, t, s_psum):
        tsl = slice(t * TT, (t + 1) * TT)
        ctx.dma(gt, g_dram[co * 128:(co + 1) * 128, tsl])
        nc.sync.wait_ge(ctx.dsem, ctx.dcnt)
        ctx.bar()
        if wo_bias:
            nc.vector.tensor_scalar(tmp, s_psum, Wobi[0:128, co:co + 1], None, ALU.add)
            ctx.bar()
            nc.vector.tensor_mul(tmp, tmp, gt)
        else:
            nc.vector.tensor_mul(tmp, s_psum, gt)
        ctx.bar()
        nc.vector.tensor_add(sb.x[co][:, tsl], sb.x[co][:, tsl], tmp)

    emit_attn_std(ctx, sb, cfg, xnag, combine)
    sb.release()


def emit_layer1_attn(ctx, sb, xnag, xn_own, g_dram, tag="l1"):
    nc = ctx.nc
    PS = ctx.PS
    wq_n = ctx.inp("L1w_q_m", [64, 4096], F32)
    wkb_n = ctx.inp("L1w_kb_m", [64, 1024], F32)
    wvbT = ctx.inp("L1w_vbT_m", [256, 256], F32)
    kaT = ctx.inp("L1w_kaT", [D, 256], F32)
    vaT = ctx.inp("L1w_vaT", [D, 256], F32)
    RT = ctx.inp("L1_RT_m", [64, 256], F32)
    woT = ctx.inp("L1w_oT", [D, D], F32)
    sb.mark()
    lv_buf = sb.abf(512)
    ctx.dmag(lv_buf, wvbT.rearrange("(lc p) o -> p lc o", p=128))
    Wob = sb.abf(NCH * 1024)
    ctx.dmag(Wob, woT.rearrange("(c p) o -> p c o", p=128))
    lq_buf = sb.abf(2048)
    lk_buf = sb.abf(512)
    gt = sb.af(TT)
    tmp = sb.af(TT)
    latt = sb.abf(2048)
    sb.mark()
    wqn = sb.abf(4096)[0:64, :]
    ctx.dmag(wqn, wq_n)
    wkbn = sb.abf(1024)[0:64, :]
    ctx.dmag(wkbn, wkb_n)
    kaTb = sb.abf(NCH * 256)
    ctx.dmag(kaTb, kaT.rearrange("(c p) o -> p c o", p=128))
    vaTb = sb.abf(NCH * 256)
    ctx.dmag(vaTb, vaT.rearrange("(c p) o -> p c o", p=128))
    RTb = sb.abf(256)[0:64, :]
    ctx.dmag(RTb, RT)
    ctx.dwait()

    for h in range(4):
        for c in range(NCH):
            nc.tensor.matmul(PS[:, 0:64], wqn[0:64, h * 1024 + c * 128: h * 1024 + (c + 1) * 128],
                             RTb[0:64, h * 64:(h + 1) * 64], start=True, stop=True)
            ctx.bar()
            nc.vector.tensor_copy(lq_buf[:, (h * 8 + c) * 64:(h * 8 + c + 1) * 64],
                                  PS[:, 0:64])
            ctx.bar()
        for lc in range(2):
            nc.tensor.matmul(PS[:, 0:64], wkbn[0:64, h * 256 + lc * 128: h * 256 + (lc + 1) * 128],
                             RTb[0:64, h * 64:(h + 1) * 64], start=True, stop=True)
            ctx.bar()
            nc.vector.tensor_copy(lk_buf[:, (h * 2 + lc) * 64:(h * 2 + lc + 1) * 64],
                                  PS[:, 0:64])
            ctx.bar()

    lat_in = nc.dram_tensor(f"latin_{tag}", [512, TPC], BF16, kind="Internal")
    lat_out = nc.dram_tensor(f"latout_{tag}", [2048, TPC], BF16, kind="Internal")
    xnt_own = sb.abf(4096)
    latk = sb.abf(2 * TPC)
    latv = sb.abf(2 * TPC)
    for t in range(NTT):
        tsl = slice(t * TT, (t + 1) * TT)
        ctx.dma(xnt_own, xn_own[0:1024, tsl].rearrange("(c p) t -> p c t", p=128))
        ctx.dwait()
        for oc in range(2):
            for c in range(NCH):
                nc.tensor.matmul(PS[:, 0:TT],
                                 kaTb[:, c * 256 + oc * 128: c * 256 + (oc + 1) * 128],
                                 xnt_own[:, c * TT:(c + 1) * TT],
                                 start=(c == 0), stop=(c == NCH - 1))
            for c in range(NCH):
                nc.tensor.matmul(PS[:, TT:2 * TT],
                                 vaTb[:, c * 256 + oc * 128: c * 256 + (oc + 1) * 128],
                                 xnt_own[:, c * TT:(c + 1) * TT],
                                 start=(c == 0), stop=(c == NCH - 1))
            ctx.bar()
            nc.vector.tensor_copy(latk[:, oc * TPC + t * TT: oc * TPC + (t + 1) * TT],
                                  PS[:, 0:TT])
            nc.vector.tensor_copy(latv[:, oc * TPC + t * TT: oc * TPC + (t + 1) * TT],
                                  PS[:, TT:2 * TT])
            ctx.bar()
    for oc in range(2):
        ctx.dma(lat_in.ap()[oc * 128:(oc + 1) * 128, :], latk[:, oc * TPC:(oc + 1) * TPC])
        ctx.dma(lat_in.ap()[256 + oc * 128: 256 + (oc + 1) * 128, :],
                latv[:, oc * TPC:(oc + 1) * TPC])
    ctx.dwait()
    sb.release()
    ctx.coll("AllGather", ALU.bypass, GROUPS_BATCH, lat_in.ap(), lat_out.ap())
    ctx.cwait()

    d = attn_dram(ctx, tag)
    cfg = {
        "causal": False,
        "sqrt_scale": float(1.0 / np.sqrt(np.sqrt(64.0))),
        "bias": lambda proj, h: None,
        "woT": lambda ci, co: Wob[:, ci * 1024 + co * 128: ci * 1024 + (co + 1) * 128],
        "patterns": None,
        "E16": e16_const(),
    }
    cfg.update(d)


    def gen_pair(p, bufs):
        kbuf, qbuf, vbuf = bufs["kbuf"], bufs["qbuf"], bufs["vbuf"]
        rmin = bufs["rmin"]
        xnt = bufs["xnt"]
        nc.vector.memset(vbuf, 1.0)
        nc.vector.memset(rmin[0:1, 0:1], 0.0)
        nc.vector.memset(kbuf[64:65, :], 1.0)
        ctx.bar()
        for t in range(ATT_NQT):
            r, u = t // 4, t % 4
            ctx.dma(xnt, xnag[r * 1024:(r + 1) * 1024,
                              u * TT:(u + 1) * TT].rearrange("(c p) t -> p c t", p=128))
            ctx.dma(latt, lat_out.ap()[r * 512:(r + 1) * 512,
                                       u * TT:(u + 1) * TT].rearrange("(kv p) t -> p kv t", p=128))
            ctx.dwait()
            for lc in range(2):
                nc.tensor.matmul(PS[0:64, 0:TT], lk_buf[:, (p * 2 + lc) * 64:(p * 2 + lc + 1) * 64],
                                 latt[:, lc * TT:(lc + 1) * TT],
                                 start=(lc == 0), stop=(lc == 1))
            for lc in range(2):
                nc.tensor.matmul(PS[0:64, TT:2 * TT], lv_buf[:, lc * 256 + p * 64: lc * 256 + (p + 1) * 64],
                                 latt[:, (2 + lc) * TT:(3 + lc) * TT],
                                 start=(lc == 0), stop=(lc == 1))
            for c in range(NCH):
                nc.tensor.matmul(PS[0:64, 2 * TT:3 * TT],
                                 lq_buf[:, (p * 8 + c) * 64:(p * 8 + c + 1) * 64],
                                 xnt[:, c * TT:(c + 1) * TT],
                                 start=(c == 0), stop=(c == NCH - 1))
            ctx.bar()
            attn_evac_tile(ctx, sb, cfg, bufs, p, t)
        nc.vector.tensor_scalar(qbuf[64:65, :], qbuf[64:65, :], rmin[0:1, 0:1],
                                None, ALU.add)
        ctx.bar()

    cfg["gen_pair"] = gen_pair

    def combine(co, t, s_psum):
        tsl = slice(t * TT, (t + 1) * TT)
        ctx.dma(gt, g_dram[co * 128:(co + 1) * 128, tsl])
        nc.sync.wait_ge(ctx.dsem, ctx.dcnt)
        ctx.bar()
        nc.vector.tensor_mul(tmp, s_psum, gt)
        ctx.bar()
        nc.vector.tensor_add(sb.x[co][:, tsl], sb.x[co][:, tsl], tmp)

    emit_attn_std(ctx, sb, cfg, xnag, combine)
    sb.release()



def fp4_qdq(ctx, sb, x_ap, out_bf_ap, F, scr, P=128):
    nc = ctx.nc
    av, v, q, u, sc, rc, sgn = (scr["av"], scr["v"], scr["q"], scr["u"],
                                scr["sc"], scr["rc"], scr["sgn"])
    nb = F // 16
    nc.scalar.activation(av[:, 0:F], x_ap, AF.Abs)
    ctx.edge("scalar", "vector")
    nc.vector.tensor_reduce(u[:, 0:nb], av[:, 0:F].rearrange("p (a b) -> p a b", b=16),
                            AX, ALU.max)
    ctx.bar()
    nc.vector.tensor_scalar(sc[:, 0:nb], u[:, 0:nb], 1.0 / 6.0, 1e-12,
                            ALU.mult, ALU.max)
    ctx.bar()
    nc.vector.reciprocal(rc[:, 0:nb], sc[:, 0:nb])
    ctx.bar()
    rcb = rc[:, 0:nb].unsqueeze(2).broadcast_to([P, nb, 16])
    nc.vector.tensor_tensor(v[:, 0:F].rearrange("p (a b) -> p a b", b=16),
                            av[:, 0:F].rearrange("p (a b) -> p a b", b=16),
                            rcb, ALU.mult)
    ctx.edge("vector", "scalar")
    nc.scalar.activation(sgn[:, 0:F], x_ap, AF.Sign)
    ctx.bar()
    nc.vector.tensor_scalar(q[:, 0:F], v[:, 0:F], 2.0, MAGIC, ALU.mult, ALU.add)
    nc.vector.tensor_scalar(q[:, 0:F], q[:, 0:F], MAGIC, 0.5, ALU.subtract, ALU.mult)
    nc.vector.tensor_scalar(q[:, 0:F], q[:, 0:F], 2.0, None, ALU.min)
    nc.vector.tensor_scalar(u[:, 0:F], v[:, 0:F], 2.5, None, ALU.is_gt)
    nc.vector.tensor_add(q[:, 0:F], q[:, 0:F], u[:, 0:F])
    nc.vector.tensor_scalar(u[:, 0:F], v[:, 0:F], 3.5, None, ALU.is_gt)
    nc.vector.tensor_add(q[:, 0:F], q[:, 0:F], u[:, 0:F])
    nc.vector.tensor_scalar(u[:, 0:F], v[:, 0:F], 5.0, 2.0, ALU.is_gt, ALU.mult)
    nc.vector.tensor_add(q[:, 0:F], q[:, 0:F], u[:, 0:F])
    nc.vector.tensor_mul(q[:, 0:F], q[:, 0:F], sgn[:, 0:F])
    scb = sc[:, 0:nb].unsqueeze(2).broadcast_to([P, nb, 16])
    nc.vector.tensor_tensor(out_bf_ap.rearrange("p (a b) -> p a b", b=16),
                            q[:, 0:F].rearrange("p (a b) -> p a b", b=16),
                            scb, ALU.mult)
    ctx.bar()


def emit_l2_ffn(ctx, sb, stats_dram, g_dram, tag="l2f"):
    nc = ctx.nc
    PS = ctx.PS
    PSb = ctx.PS.ap().bitcast(BF16)
    identb = ctx.get_identb(sb)
    identf = ctx._identf_sb
    wg_in = ctx.inp("L2f_wg_m", [384, 1024], F32)
    wu_in = ctx.inp("L2f_wu_m", [384, 1024], F32)
    wd_in = ctx.inp("L2f_wd_m", [128, 2816], F32)
    HD3 = 3072
    wgT = nc.dram_tensor(f"wgT_{tag}", [8192, 384], BF16, kind="Internal")
    wuT = nc.dram_tensor(f"wuT_{tag}", [8192, 384], BF16, kind="Internal")
    wdT = nc.dram_tensor(f"wdT_{tag}", [8 * 2816, 128], BF16, kind="Internal")
    wgT_l = nc.dram_tensor(f"wgTl_{tag}", [1024, 384], BF16, kind="Internal")
    wuT_l = nc.dram_tensor(f"wuTl_{tag}", [1024, 384], BF16, kind="Internal")
    wdT_l = nc.dram_tensor(f"wdTl_{tag}", [2816, 128], BF16, kind="Internal")
    h_dram = nc.dram_tensor(f"h_{tag}", [HD3, TPC], F32, kind="Internal")
    xq_dram = nc.dram_tensor(f"xq_{tag}", [D, TPC], BF16, kind="Internal")

    sb.mark()
    wt = sb.af(2816)
    wq_b = sb.abf(2816)
    wtr = sb.abf(2816)
    scr = dict(av=sb.af(2816), v=sb.af(2816), q=sb.af(2816), u=sb.af(2816),
               sc=sb.af(192), rc=sb.af(192), sgn=sb.af(2816))
    for (win, wTl) in ((wg_in, wgT_l), (wu_in, wuT_l)):
        for rb in range(3):
            ctx.dma(wt[:, 0:1024], win[rb * 128:(rb + 1) * 128, :])
            ctx.dwait()
            fp4_qdq(ctx, sb, wt[:, 0:1024], wq_b[:, 0:1024], 1024, scr)
            tr_copy(ctx, [(wq_b[:, c * 128:(c + 1) * 128], wtr[:, c * 128:(c + 1) * 128])
                          for c in range(NCH)])
            ctx.dma(wTl.ap()[:, rb * 128:(rb + 1) * 128]
                    .rearrange("(c p) o -> p c o", p=128),
                    wtr[:, 0:1024].rearrange("p (c o) -> p c o", o=128))
            ctx.dwait()
    ctx.dma(wt, wd_in)
    ctx.dwait()
    fp4_qdq(ctx, sb, wt[:, 0:2816], wq_b[:, 0:2816], 2816, scr)
    tr_copy(ctx, [(wq_b[:, cc * 128:(cc + 1) * 128], wtr[:, cc * 128:(cc + 1) * 128])
                  for cc in range(22)])
    ctx.dma(wdT_l.ap().rearrange("(cc p) o -> p cc o", p=128),
            wtr[:, 0:2816].rearrange("p (cc o) -> p cc o", o=128))
    ctx.dwait()
    sb.release()
    ctx.coll("AllGather", ALU.bypass, GROUP_ALL, wgT_l.ap(), wgT.ap())
    ctx.coll("AllGather", ALU.bypass, GROUP_ALL, wuT_l.ap(), wuT.ap())
    ctx.coll("AllGather", ALU.bypass, GROUP_ALL, wdT_l.ap(), wdT.ap())
    ctx.cwait()

    if L2F_PHASES < 2:
        return
    sb.mark()
    strow = sb.af(2 * TT)[0:1, :]
    strowb = sb.abf(2 * TT)[0:1, :]
    onesb2 = sb.abf(128)[0:1, :]
    nc.vector.memset(onesb2, 1.0)
    xnf = sb.af(TT)
    xtok = sb.af(1024)
    xqtok = sb.abf(1024)
    xq_fm = sb.abf(1024)
    scr = dict(av=sb.af(1024), v=sb.af(1024), q=sb.af(1024), u=sb.af(1024),
               sc=sb.af(64), rc=sb.af(64), sgn=sb.af(1024))
    for t in range(NTT):
        tsl = slice(t * TT, (t + 1) * TT)
        ctx.dma(strow[0:1, 0:TT], stats_dram.ap()[0:1, tsl])
        ctx.dma(strow[0:1, TT:2 * TT], stats_dram.ap()[1:2, tsl])
        ctx.dwait()
        nc.vector.tensor_copy(strowb, strow)
        ctx.edge("vector", "tensor")
        nc.tensor.matmul(PS[:, 2 * TT:3 * TT], onesb2, strowb[0:1, 0:TT],
                         start=True, stop=True)
        nc.tensor.matmul(PS[:, 3 * TT:4 * TT], onesb2, strowb[0:1, TT:2 * TT],
                         start=True, stop=True)
        ctx.edge("tensor", "vector")
        for tb in range(4):
            for c in range(NCH):
                ctx.edge("tensor", "vector")
                nc.vector.tensor_tensor(xnf, sb.x[c][:, tsl], PS[:, 2 * TT:3 * TT],
                                        ALU.mult)
                nc.vector.tensor_add(xnf, xnf, PS[:, 3 * TT:4 * TT])
                ctx.edge("vector", "tensor")
                nc.tensor.transpose(PS[:, 0:128], xnf[:, tb * 128:(tb + 1) * 128],
                                    identf)
                ctx.edge("tensor", "vector")
                nc.vector.tensor_copy(xtok[:, c * 128:(c + 1) * 128], PS[:, 0:128])
                ctx.edge("vector", "tensor")
            ctx.edge("vector", "scalar")
            fp4_qdq(ctx, sb, xtok[:, 0:1024], xqtok[:, 0:1024], 1024, scr)
            tr_copy(ctx, [(xqtok[:, c * 128:(c + 1) * 128], xq_fm[:, c * 128:(c + 1) * 128])
                          for c in range(NCH)])
            ctx.dma(xq_dram.ap()[:, t * TT + tb * 128: t * TT + (tb + 1) * 128]
                    .rearrange("(c p) t -> p c t", p=128),
                    xq_fm[:, 0:1024].rearrange("p (c t) -> p c t", t=128))
            nc.sync.wait_ge(ctx.dsem, ctx.dcnt)
            nc.vector.wait_ge(ctx.dsem, ctx.dcnt)
            nc.tensor.wait_ge(ctx.dsem, ctx.dcnt)
    sb.release()

    if L2F_PHASES < 3:
        return
    sb.mark()
    lg = sb.abf(NCH * 128)
    lu = sb.abf(NCH * 128)
    xqt = sb.abf(NCH * TT)
    hsil = sb.af(TT)
    hf = sb.af(TT)
    for oc in range(24):
        r, co = oc // 3, (oc % 3) * 128
        ctx.edge("tensor", "sync")
        ctx.dma(lg, wgT.ap()[r * 1024:(r + 1) * 1024, co:co + 128]
                .rearrange("(c p) o -> p c o", p=128))
        ctx.dma(lu, wuT.ap()[r * 1024:(r + 1) * 1024, co:co + 128]
                .rearrange("(c p) o -> p c o", p=128))
        for t in range(NTT):
            tsl = slice(t * TT, (t + 1) * TT)
            ctx.dma(xqt, xq_dram.ap()[:, tsl].rearrange("(c p) t -> p c t", p=128))
            ctx.dwait()
            for c in range(NCH):
                nc.tensor.matmul(PS[:, 0:TT], lg[:, c * 128:(c + 1) * 128],
                                 xqt[:, c * TT:(c + 1) * TT],
                                 start=(c == 0), stop=(c == NCH - 1))
            for c in range(NCH):
                nc.tensor.matmul(PS[:, TT:2 * TT], lu[:, c * 128:(c + 1) * 128],
                                 xqt[:, c * TT:(c + 1) * TT],
                                 start=(c == 0), stop=(c == NCH - 1))
            ctx.edge("tensor", "scalar")
            nc.scalar.activation(hsil, PS[:, 0:TT], AF.Silu)
            ctx.edge("scalar", "vector", "tensor")
            nc.vector.tensor_tensor(hf, hsil, PS[:, TT:2 * TT], ALU.mult)
            ctx.edge("vector", "sync", "tensor")
            ctx.dma(h_dram.ap()[oc * 128:(oc + 1) * 128, tsl], hf)
            nc.sync.wait_ge(ctx.dsem, ctx.dcnt)
            nc.vector.wait_ge(ctx.dsem, ctx.dcnt)
            nc.scalar.wait_ge(ctx.dsem, ctx.dcnt)
    sb.release()

    if L2F_PHASES < 4:
        return
    sb.mark()
    htok = sb.af(HD3)
    hqtok = sb.abf(HD3)
    hq_fm = sb.abf(22 * TT)
    ld = sb.abf(22 * 128)
    hin = sb.af(HD3)
    _av = sb.af(HD3)
    scr = dict(av=_av, v=sb.af(HD3), q=htok, u=sb.af(HD3),
               sc=sb.af(192), rc=sb.af(192), sgn=_av)
    gt2 = sb.af(TT)
    for t in range(NTT):
        tsl = slice(t * TT, (t + 1) * TT)
        for tb in range(4):
            tk = slice(t * TT + tb * 128, t * TT + (tb + 1) * 128)
            ctx.edge("tensor", "sync")
            ctx.dma(hin.rearrange("p (cc o) -> p cc o", o=128),
                    h_dram.ap()[:, tk].rearrange("(cc p) o -> p cc o", p=128))
            ctx.dwait()
            tr_copy(ctx, [(hin[:, cc * 128:(cc + 1) * 128],
                           htok[:, cc * 128:(cc + 1) * 128]) for cc in range(24)],
                    bf16=False)
            fp4_qdq(ctx, sb, htok[:, 0:HD3], hqtok[:, 0:HD3], HD3, scr)
            tr_copy(ctx, [(hqtok[:, cc * 128:(cc + 1) * 128],
                           hq_fm[:, cc * TT + tb * 128: cc * TT + (tb + 1) * 128])
                          for cc in range(22)])
        for oc in range(NCH):
            ctx.edge("tensor", "sync")
            ctx.dma(ld, wdT.ap()[oc * 2816:(oc + 1) * 2816, :]
                    .rearrange("(cc p) o -> p cc o", p=128))
            ctx.dwait()
            for cc in range(22):
                nc.tensor.matmul(PS[:, 0:TT], ld[:, cc * 128:(cc + 1) * 128],
                                 hq_fm[:, cc * TT:(cc + 1) * TT],
                                 start=(cc == 0), stop=(cc == 21))
            ctx.edge("tensor", "scalar")
            nc.scalar.activation(gt2, PS[:, 0:TT], AF.Silu)
            ctx.edge("scalar", "sync", "tensor")
            ctx.dma(g_dram.ap()[oc * 128:(oc + 1) * 128, tsl], gt2)
            nc.sync.wait_ge(ctx.dsem, ctx.dcnt)
            nc.scalar.wait_ge(ctx.dsem, ctx.dcnt)
    sb.release()


def lpbq_qdq(ctx, sb, x_ap, out_bf_ap, F, scr, block=32):
    nc = ctx.nc
    av, v, q, u, sc, rc = (scr["av"], scr["v"], scr["q"], scr["u"],
                           scr["sc"], scr["rc"])
    nb = F // block
    nc.scalar.activation(av[:, 0:F], x_ap, AF.Abs)
    ctx.edge("scalar", "vector")
    nc.vector.tensor_reduce(u[:, 0:nb], av[:, 0:F].rearrange("p (a b) -> p a b", b=block),
                            AX, ALU.max)
    ctx.bar()
    nc.vector.tensor_scalar(sc[:, 0:nb], u[:, 0:nb], 1.0 / 7.0, 1e-12, ALU.mult, ALU.max)
    ctx.bar()
    nc.vector.reciprocal(rc[:, 0:nb], sc[:, 0:nb])
    ctx.bar()
    rcb = rc[:, 0:nb].unsqueeze(2).broadcast_to([128, nb, block])
    nc.vector.tensor_tensor(v[:, 0:F].rearrange("p (a b) -> p a b", b=block),
                            x_ap.rearrange("p (a b) -> p a b", b=block), rcb, ALU.mult)
    nc.vector.tensor_scalar(v[:, 0:F], v[:, 0:F], MAGIC, None, ALU.add)
    nc.vector.tensor_scalar(v[:, 0:F], v[:, 0:F], MAGIC, None, ALU.subtract)
    scb = sc[:, 0:nb].unsqueeze(2).broadcast_to([128, nb, block])
    nc.vector.tensor_tensor(out_bf_ap.rearrange("p (a b) -> p a b", b=block),
                            v[:, 0:F].rearrange("p (a b) -> p a b", b=block), scb, ALU.mult)
    ctx.bar()


def int_qdq(ctx, in_f32, out_ap, r_ap, s_ap):
    nc = ctx.nc
    nc.vector.tensor_scalar(out_ap, in_f32, r_ap, MAGIC, ALU.mult, ALU.add)
    nc.vector.tensor_scalar(out_ap, out_ap, MAGIC, s_ap, ALU.subtract, ALU.mult)


def gmax_to_slot(ctx, sb, mx128, gdram, col):
    nc = ctx.nc
    sb.mark()
    row = sb.af(128)[0:1, :]
    ctx.edge("vector", "tensor")
    nc.tensor.transpose(ctx.PS[0:1, 3584:3584 + 128], mx128, ctx._identf_sb)
    ctx.edge("tensor", "vector")
    nc.vector.tensor_copy(row, ctx.PS[0:1, 3584:3584 + 128])
    ctx.bar()
    nc.vector.reduce_max(row[0:1, 0:1], row[0:1, 0:128], axis=AX)
    ctx.bar()
    ctx.dma(gdram.ap()[0:1, col:col + 1], row[0:1, 0:1])
    ctx.dwait()
    sb.release()


def scale_bcast(ctx, sb, gdram_out, col, qmax, r_out, s_out, onesb):
    nc = ctx.nc
    sb.mark()
    sv = sb.af(8)[0:1, 0:2]
    svb = sb.abf(8)[0:1, 0:2]
    ctx.dma(sv[0:1, 0:1], gdram_out.ap()[0:1, col:col + 1])
    ctx.dwait()
    nc.vector.tensor_scalar(sv[0:1, 0:1], sv[0:1, 0:1], 1.0 / qmax, None, ALU.mult)
    ctx.bar()
    nc.vector.reciprocal(sv[0:1, 1:2], sv[0:1, 0:1])
    ctx.bar()
    nc.vector.tensor_copy(svb, sv[0:1, 0:2])
    ctx.edge("vector", "tensor")
    nc.tensor.matmul(ctx.PS[:, 3584:3584 + 2], onesb, svb, start=True, stop=True)
    ctx.edge("tensor", "vector")
    nc.vector.tensor_copy(s_out, ctx.PS[:, 3584:3585])
    nc.vector.tensor_copy(r_out, ctx.PS[:, 3585:3586])
    ctx.bar()
    sb.release()


def emit_l1_ffn(ctx, sb, stats_dram, g_dram, xn_absmax_row, tag="l1f"):
    nc = ctx.nc
    PS = ctx.PS
    identb = ctx.get_identb(sb)
    identf = ctx._identf_sb
    wg_in = ctx.inp("L1f_wg_m", [384, 1024], F32)
    wu_in = ctx.inp("L1f_wu_m", [384, 1024], F32)
    wd_in = ctx.inp("L1f_wd_m", [128, 3072], F32)
    wgT = nc.dram_tensor(f"wgT_{tag}", [8192, 384], BF16, kind="Internal")
    wuT = nc.dram_tensor(f"wuT_{tag}", [8192, 384], BF16, kind="Internal")
    wdT = nc.dram_tensor(f"wdT_{tag}", [8 * 3072, 128], BF16, kind="Internal")
    wgT_l = nc.dram_tensor(f"wgTl_{tag}", [1024, 384], BF16, kind="Internal")
    wuT_l = nc.dram_tensor(f"wuTl_{tag}", [1024, 384], BF16, kind="Internal")
    wdT_l = nc.dram_tensor(f"wdTl_{tag}", [3072, 128], BF16, kind="Internal")
    up_dram = nc.dram_tensor(f"up_{tag}", [3072, TPC], F32, kind="Internal")
    gate_dram = nc.dram_tensor(f"gate_{tag}", [3072, TPC], F32, kind="Internal")
    gmax_in = nc.dram_tensor(f"gmi_{tag}", [1, 8], F32, kind="Internal")
    gmax_out = [nc.dram_tensor(f"gmo_{tag}{i}", [1, 8], F32, kind="Internal")
                for i in range(3)]

    sb.mark()
    z8 = sb.af(8)[0:1, :]
    nc.vector.memset(z8, 0.0)
    ctx.bar()
    nc.vector.tensor_copy(z8[0:1, 0:1], xn_absmax_row)
    ctx.bar()
    ctx.dma(gmax_in.ap(), z8)
    ctx.dwait()
    ctx.coll("AllReduce", ALU.max, GROUP_ALL, gmax_in.ap(), gmax_out[0].ap())
    sb.release()

    sb.mark()
    wt = sb.af(3072)
    wq_b = sb.abf(3072)
    wtr = sb.abf(3072)
    scr = dict(av=sb.af(3072), v=sb.af(3072), q=None, u=sb.af(3072),
               sc=sb.af(96), rc=sb.af(96))
    for (win, wTl) in ((wg_in, wgT_l), (wu_in, wuT_l)):
        for rb in range(3):
            ctx.dma(wt[:, 0:1024], win[rb * 128:(rb + 1) * 128, :])
            ctx.dwait()
            lpbq_qdq(ctx, sb, wt[:, 0:1024], wq_b[:, 0:1024], 1024, scr)
            tr_copy(ctx, [(wq_b[:, c * 128:(c + 1) * 128], wtr[:, c * 128:(c + 1) * 128])
                          for c in range(NCH)])
            ctx.dma(wTl.ap()[:, rb * 128:(rb + 1) * 128]
                    .rearrange("(c p) o -> p c o", p=128),
                    wtr[:, 0:1024].rearrange("p (c o) -> p c o", o=128))
            ctx.dwait()
    ctx.dma(wt, wd_in)
    ctx.dwait()
    lpbq_qdq(ctx, sb, wt[:, 0:3072], wq_b[:, 0:3072], 3072, scr)
    tr_copy(ctx, [(wq_b[:, cc * 128:(cc + 1) * 128], wtr[:, cc * 128:(cc + 1) * 128])
                  for cc in range(24)])
    ctx.dma(wdT_l.ap().rearrange("(cc p) o -> p cc o", p=128),
            wtr[:, 0:3072].rearrange("p (cc o) -> p cc o", o=128))
    ctx.dwait()
    sb.release()
    ctx.coll("AllGather", ALU.bypass, GROUP_ALL, wgT_l.ap(), wgT.ap())
    ctx.coll("AllGather", ALU.bypass, GROUP_ALL, wuT_l.ap(), wuT.ap())
    ctx.coll("AllGather", ALU.bypass, GROUP_ALL, wdT_l.ap(), wdT.ap())
    ctx.cwait()

    sb.mark()
    onesb2 = sb.abf(128)[0:1, :]
    nc.vector.memset(onesb2, 1.0)
    ctx.bar()
    r1 = sb.af(8)[:, 0:1]
    s1 = sb.af(8)[:, 0:1]
    scale_bcast(ctx, sb, gmax_out[0], 0, 32767.0, r1, s1, onesb2)
    xq = sb.abf(NCH * TPC)
    strow = sb.af(2 * TT)[0:1, :]
    strowb = sb.abf(2 * TT)[0:1, :]
    xnf = sb.af(TT)
    xqf = sb.af(TT)
    mxu = sb.af(8)[:, 0:1]
    mxg = sb.af(8)[:, 1:2]
    red = sb.af(8)[:, 2:3]
    for t in range(NTT):
        tsl = slice(t * TT, (t + 1) * TT)
        ctx.dma(strow[0:1, 0:TT], stats_dram.ap()[0:1, tsl])
        ctx.dma(strow[0:1, TT:2 * TT], stats_dram.ap()[1:2, tsl])
        ctx.dwait()
        nc.vector.tensor_copy(strowb, strow)
        ctx.edge("vector", "tensor")
        nc.tensor.matmul(PS[:, 2 * TT:3 * TT], onesb2, strowb[0:1, 0:TT],
                         start=True, stop=True)
        nc.tensor.matmul(PS[:, 3 * TT:4 * TT], onesb2, strowb[0:1, TT:2 * TT],
                         start=True, stop=True)
        ctx.edge("tensor", "vector")
        for c in range(NCH):
            nc.vector.tensor_tensor(xnf, sb.x[c][:, tsl], PS[:, 2 * TT:3 * TT], ALU.mult)
            nc.vector.tensor_add(xnf, xnf, PS[:, 3 * TT:4 * TT])
            int_qdq(ctx, xnf, xqf, r1, s1)
            nc.vector.tensor_copy(xq[:, c * TPC + t * TT: c * TPC + (t + 1) * TT], xqf)
        ctx.bar()

    lw = sb.abf(NCH * 128)
    upf = sb.af(TT)
    nc.vector.memset(mxu, 0.0)
    nc.vector.memset(mxg, 0.0)
    ctx.bar()
    for wi, (wT, odram, mx) in enumerate(((wuT, up_dram, mxu), (wgT, gate_dram, mxg))):
        for oc in range(24):
            r, co = oc // 3, (oc % 3) * 128
            ctx.edge("tensor", "sync")
            ctx.dma(lw, wT.ap()[r * 1024:(r + 1) * 1024, co:co + 128]
                    .rearrange("(c p) o -> p c o", p=128))
            ctx.dwait()
            for t in range(NTT):
                tsl = slice(t * TT, (t + 1) * TT)
                for c in range(NCH):
                    nc.tensor.matmul(PS[:, 0:TT], lw[:, c * 128:(c + 1) * 128],
                                     xq[:, c * TPC + t * TT: c * TPC + (t + 1) * TT],
                                     start=(c == 0), stop=(c == NCH - 1))
                ctx.edge("tensor", "scalar")
                nc.scalar.activation(upf, PS[:, 0:TT], AF.Copy)
                ctx.edge("scalar", "sync", "vector", "tensor")
                nc.vector.reduce_max(red, upf, axis=AX, apply_absolute_value=True)
                nc.vector.tensor_max(mx, mx, red)
                ctx.dma(odram.ap()[oc * 128:(oc + 1) * 128, tsl], upf)
                nc.sync.wait_ge(ctx.dsem, ctx.dcnt)
                nc.scalar.wait_ge(ctx.dsem, ctx.dcnt)
    gmax_to_slot(ctx, sb, mxu, gmax_in, 1)
    gmax_to_slot(ctx, sb, mxg, gmax_in, 2)
    ctx.coll("AllReduce", ALU.max, GROUP_ALL, gmax_in.ap(), gmax_out[1].ap())
    ctx.cwait()

    r2 = sb.af(8)[:, 0:1]
    s2 = sb.af(8)[:, 0:1]
    r3 = sb.af(8)[:, 0:1]
    s3 = sb.af(8)[:, 0:1]
    scale_bcast(ctx, sb, gmax_out[1], 1, 32767.0, r2, s2, onesb2)
    scale_bcast(ctx, sb, gmax_out[1], 2, 32767.0, r3, s3, onesb2)
    mx4 = sb.af(8)[:, 3:4]
    nc.vector.memset(mx4, 0.0)
    gq = sb.af(TT)
    sg = sb.af(TT)
    prod = sb.af(TT)
    for oc in range(24):
        for t in range(NTT):
            tsl = slice(t * TT, (t + 1) * TT)
            ctx.edge("vector", "sync")
            ctx.dma(upf, gate_dram.ap()[oc * 128:(oc + 1) * 128, tsl])
            ctx.dwait()
            int_qdq(ctx, upf, gq, r3, s3)
            ctx.edge("vector", "scalar")
            nc.scalar.activation(sg, gq, AF.Sigmoid)
            ctx.edge("scalar", "vector")
            nc.vector.tensor_scalar(sg, sg, 65536.0, MAGIC, ALU.mult, ALU.add)
            nc.vector.tensor_scalar(sg, sg, MAGIC, 65535.0, ALU.subtract, ALU.min)
            nc.vector.tensor_mul(prod, gq, sg)
            nc.vector.tensor_scalar(prod, prod, 1.0 / 65536.0, None, ALU.mult)
            nc.vector.reduce_max(red, prod, axis=AX, apply_absolute_value=True)
            nc.vector.tensor_max(mx4, mx4, red)
            ctx.edge("vector", "sync")
            ctx.dma(gate_dram.ap()[oc * 128:(oc + 1) * 128, tsl], prod)
            nc.sync.wait_ge(ctx.dsem, ctx.dcnt)
            nc.vector.wait_ge(ctx.dsem, ctx.dcnt)
    gmax_to_slot(ctx, sb, mx4, gmax_in, 3)
    ctx.coll("AllReduce", ALU.max, GROUP_ALL, gmax_in.ap(), gmax_out[2].ap())
    ctx.cwait()

    r4 = sb.af(8)[:, 0:1]
    s4 = sb.af(8)[:, 0:1]
    scale_bcast(ctx, sb, gmax_out[2], 3, 32767.0, r4, s4, onesb2)
    mx5 = sb.af(8)[:, 4:5]
    nc.vector.memset(mx5, 0.0)
    for oc in range(24):
        for t in range(NTT):
            tsl = slice(t * TT, (t + 1) * TT)
            ctx.edge("vector", "sync")
            ctx.dma(gq, gate_dram.ap()[oc * 128:(oc + 1) * 128, tsl])
            ctx.dma(upf, up_dram.ap()[oc * 128:(oc + 1) * 128, tsl])
            ctx.dwait()
            int_qdq(ctx, gq, prod, r4, s4)
            int_qdq(ctx, upf, sg, r2, s2)
            nc.vector.tensor_mul(prod, prod, sg)
            nc.vector.reduce_max(red, prod, axis=AX, apply_absolute_value=True)
            nc.vector.tensor_max(mx5, mx5, red)
            ctx.edge("vector", "sync")
            ctx.dma(up_dram.ap()[oc * 128:(oc + 1) * 128, tsl], prod)
            nc.sync.wait_ge(ctx.dsem, ctx.dcnt)
            nc.vector.wait_ge(ctx.dsem, ctx.dcnt)
    gmax_in2 = nc.dram_tensor(f"gmi2_{tag}", [1, 8], F32, kind="Internal")
    gmax_out3 = nc.dram_tensor(f"gmo3_{tag}", [1, 8], F32, kind="Internal")
    gmax_to_slot(ctx, sb, mx5, gmax_in2, 0)
    ctx.coll("AllReduce", ALU.max, GROUP_ALL, gmax_in2.ap(), gmax_out3.ap())
    ctx.cwait()

    r5 = sb.af(8)[:, 0:1]
    s5 = sb.af(8)[:, 0:1]
    scale_bcast(ctx, sb, gmax_out3, 0, 32767.0, r5, s5, onesb2)
    oqfm = sb.abf(24 * TT)
    ld = sb.abf(24 * 128)
    ofl = sb.af(24 * TT // 4) if False else None
    oin = sb.af(TT)
    for t in range(NTT):
        tsl = slice(t * TT, (t + 1) * TT)
        for cc in range(24):
            ctx.edge("tensor", "sync")
            ctx.dma(oin, up_dram.ap()[cc * 128:(cc + 1) * 128, tsl])
            ctx.dwait()
            nc.vector.tensor_scalar(oin, oin, r5, MAGIC, ALU.mult, ALU.add)
            nc.vector.tensor_scalar(oqfm[:, cc * TT:(cc + 1) * TT], oin, MAGIC, s5,
                                    ALU.subtract, ALU.mult)
            ctx.bar()
        for oc in range(NCH):
            ctx.edge("tensor", "sync")
            ctx.dma(ld, wdT.ap()[oc * 3072:(oc + 1) * 3072, :]
                    .rearrange("(cc p) o -> p cc o", p=128))
            ctx.dwait()
            for cc in range(24):
                nc.tensor.matmul(PS[:, 0:TT], ld[:, cc * 128:(cc + 1) * 128],
                                 oqfm[:, cc * TT:(cc + 1) * TT],
                                 start=(cc == 0), stop=(cc == 23))
            ctx.edge("tensor", "scalar")
            nc.scalar.activation(oin, PS[:, 0:TT], AF.Silu)
            ctx.edge("scalar", "sync", "tensor")
            ctx.dma(g_dram.ap()[oc * 128:(oc + 1) * 128, tsl], oin)
            nc.sync.wait_ge(ctx.dsem, ctx.dcnt)
            nc.scalar.wait_ge(ctx.dsem, ctx.dcnt)
    sb.release()



L0W = 2560
L0XW = 2564
L0F_STEP = 10
L0F_GSTEP = 3


def l0_tri_patterns():
    i = np.arange(128)[:, None]
    j = np.arange(128)[None, :]
    lo = np.where(j < i, -1e9, 0.0)
    up = np.where(j > i, -1e9, 0.0)
    return np.concatenate([lo, up], axis=1).astype(BF)


def l0_rot_tables(q):
    s0 = q * 2048
    d = np.arange(64)
    jj = (d % 32).astype(np.float64)
    invf = 10000.0 ** (-(2.0 * jj) / 64.0)
    base = (2.0 * jj + 0.4 * 64.0) / (1.4 * 64.0)
    sgn = np.where(d < 32, -1.0, 1.0)

    def tabs(pos, qside):
        fr = pos[None, :] * invf[:, None]
        sc = base[:, None] ** ((pos[None, :] - 4096.0) / 256.0)
        if not qside:
            sc = 1.0 / sc
        A = np.cos(fr) * sc
        Bt = np.sin(fr) * sc * sgn[:, None]
        return (np.concatenate([A, A], 0).astype(np.float32),
                np.concatenate([Bt, Bt], 0).astype(np.float32))

    Aq, Bq = tabs(np.arange(s0, s0 + 2048, dtype=np.float64), True)
    Ak, Bk = tabs(np.arange(s0 - 512, s0 + 2048, dtype=np.float64), False)
    return Aq, Bq, Ak, Bk


def emit_l0_ffn(ctx, sb, xnag, xn_own, g_dram, tag="l0f"):
    nc = ctx.nc
    PS = ctx.PS
    PSb = ctx.PS.ap().bitcast(BF16)
    identb = ctx.get_identb(sb)
    wqkv_i = [ctx.inp(f"L0f_wqkv{l}", [D, 1536], F32) for l in range(2)]
    wo_i = [ctx.inp(f"L0f_wo{l}", [512, D], F32) for l in range(2)]
    w1_i = [ctx.inp(f"L0f_w1_{l}", [D, 5460], F32) for l in range(2)]
    w2_i = [ctx.inp(f"L0f_w2_{l}", [2730, D], F32) for l in range(2)]
    b1_i = [ctx.inp(f"L0f_b1_{l}", [128, 44], F32) for l in range(2)]
    b2_i = [ctx.inp(f"L0f_b2_{l}", [128, 8], F32) for l in range(2)]
    lng_i = [ctx.inp(f"L0f_lng{l}", [128, 8], F32) for l in range(2)]
    lnb_i = [ctx.inp(f"L0f_lnb{l}", [128, 8], F32) for l in range(2)]
    qs_i = [ctx.inp(f"L0f_qs{l}", [128, 1], F32) for l in range(2)]
    ks_i = [ctx.inp(f"L0f_ks{l}", [128, 1], F32) for l in range(2)]
    rotAq = ctx.inp("L0f_rotAq", [128, TPC], F32)
    rotBq = ctx.inp("L0f_rotBq", [128, TPC], F32)
    rotAk = ctx.inp("L0f_rotAk", [128, L0W], F32)
    rotBk = ctx.inp("L0f_rotBk", [128, L0W], F32)
    halom = ctx.inp("L0f_halomask", [128, TT], F32)
    hvalid = ctx.inp("L0f_hvalid", [128, 1], F32)
    x_spill = nc.dram_tensor(f"xspill_{tag}", [D, TPC], F32, kind="Internal")
    xlb_dram = nc.dram_tensor(f"xlb_{tag}", [D, L0XW], BF16, kind="Internal")
    h3_dram = nc.dram_tensor(f"h3_{tag}", [2730, TPC], BF16, kind="Internal")
    strip_in = nc.dram_tensor(f"strin_{tag}", [D, 520], F32, kind="Internal")
    strip_out = nc.dram_tensor(f"strout_{tag}", [4 * D, 520], F32, kind="Internal")

    for c in range(NCH):
        ctx.dma(x_spill.ap()[c * 128:(c + 1) * 128, :], sb.x[c][:, :])
    ctx.dwait()
    x6f = sb.x[6]
    x6b = sb.x[6].ap().bitcast(BF16)
    x7b = sb.x[7].ap().bitcast(BF16)
    Rb = sb.R.ap().bitcast(BF16)
    ofm = [sb.x[c].ap().bitcast(BF16)[:, 0:TPC] for c in range(4)]
    hnall = [sb.x[4 + c // 2].ap().bitcast(BF16)[:, (c % 2) * TPC:(c % 2 + 1) * TPC]
             for c in range(NCH)]
    sparebf = x6b[:, 0:TT]
    scw = x6f[:, TT:2 * TT]
    ptile = x6b[:, 2048:2688]
    pT = x6b[:, 2688:3328]
    ztb = x6b[0:65, 3328:3456]
    ztk = x6f[:, 1728:1793]
    ztb2 = x6b[:, 3592:3720]
    red = x6f[:, 1880:1881]
    rcp = x6f[:, 1881:1882]
    pairq = x7b[:, 2048:4096]
    qh = x7b[0:64, 0:2048]
    vtok = Rb[:, 0:1365]
    kh = Rb[0:64, 1408:3968]

    sb.mark()
    XL = [sb.af(L0XW) for _ in range(NCH)]
    trip = sb.abf(256)
    ctx.dmag(trip, ctx.const("l0_tri", l0_tri_patterns()))
    hvl = sb.af(8)[:, 0:1]
    ctx.dma(hvl, hvalid)
    hmk = sb.abf(TT)
    ctx.dmag(hmk, halom)
    qsv = sb.af(8)[:, 0:1]
    ksv = sb.af(8)[:, 1:2]
    E2T = sb.abf(2)
    ctx.dmag(E2T, ctx.const("E2T", np.concatenate(
        [np.concatenate([np.ones((64, 1)), np.zeros((64, 1))], 1),
         np.concatenate([np.zeros((64, 1)), np.ones((64, 1))], 1)], 0).astype(BF)))
    E2f = sb.af(128)[0:2, :]
    ctx.dma(E2f, ctx.const("E2f", np.concatenate(
        [np.concatenate([np.ones((1, 64)), np.zeros((1, 64))], 1),
         np.concatenate([np.zeros((1, 64)), np.ones((1, 64))], 1)], 0)
        .astype(np.float32)))
    onesb2 = sb.abf(128)[0:1, :]
    ones1f = sb.af(128)[0:1, :]
    onesv = sb.af(8)[:, 0:1]
    ctx.dma(onesv, ctx.const("ones128f", np.ones((128, 1), dtype=np.float32)))
    b1v = sb.af(44)
    b2v = sb.af(8)
    lngv = sb.af(8)
    lnbv = sb.af(8)
    ctx.dwait()
    nc.vector.memset(onesb2, 1.0)
    nc.vector.memset(ones1f, 1.0)
    ctx.bar()

    cid = nc.partition_id()
    qm1 = nc.snap((cid + 3) % 4)

    sb.mark()
    stg = sb.abf(2048)
    stg2 = sb.abf(520)
    for c in range(NCH):
        ctx.dma(stg, xn_own[c * 128:(c + 1) * 128, :])
        ctx.dma(stg2[:, 0:513],
                xnag[bass.ts(qm1, 1024), 1535:2048]
                .rearrange("(cc p) t -> p cc t", p=128)[:, c, :])
        ctx.dwait()
        nc.vector.tensor_copy(XL[c][:, 513:2561], stg)
        nc.vector.tensor_copy(XL[c][:, 0:513], stg2[:, 0:513])
        ctx.bar()
        nc.vector.tensor_scalar(XL[c][:, 0:513], XL[c][:, 0:513], hvl, None,
                                ALU.mult)
        nc.vector.memset(XL[c][:, 2561:L0XW], 0.0)
        ctx.edge("vector", "sync")
    ctx.bar()
    sb.release()

    def halo_exchange():
        ctx.edge("vector", "sync")
        for c in range(NCH):
            ctx.dma(strip_in.ap()[c * 128:(c + 1) * 128, 0:513],
                    XL[c][:, 2048:2561])
        ctx.dwait()
        ctx.coll("AllGather", ALU.bypass, GROUPS_BATCH, strip_in.ap(),
                 strip_out.ap())
        ctx.cwait()
        for c in range(NCH):
            ctx.dma(XL[c][:, 0:513],
                    strip_out.ap()[bass.ts(qm1, 1024), 0:513]
                    .rearrange("(cc p) t -> p cc t", p=128)[:, c, :])
        ctx.dwait()
        for c in range(NCH):
            nc.vector.tensor_scalar(XL[c][:, 0:513], XL[c][:, 0:513], hvl, None,
                                    ALU.mult)
        ctx.bar()

    def gen_pair(l, pr, pairk, pairv, tabA, tabB, qsp, qsw, nrm, wsl, xlrhs):
        for proj, dst, width, rA, rB in ((1, pairk, L0W, rotAk, rotBk),
                                         (2, pairv, L0W, None, None),
                                         (0, pairq, TPC, rotAq, rotBq)):
            ctx.edge("tensor", "gpsimd")
            ctx.dmag(wsl, wqkv_i[l][:, proj * 512 + pr * 128:
                                    proj * 512 + (pr + 1) * 128]
                     .rearrange("(c p) o -> p c o", p=128))
            ctx.dwait()
            for sp in range(width // TT):
                g0 = sp * TT
                gsp = sp + (1 if proj == 0 else 0)
                ctx.edge("tensor", "sync")
                for c in range(NCH):
                    xoff = (1 if c < 4 else 0) + gsp * TT
                    ctx.dma(xlrhs[:, c * TT:(c + 1) * TT],
                            xlb_dram.ap()[c * 128:(c + 1) * 128,
                                          xoff:xoff + TT])
                nc.tensor.wait_ge(ctx.dsem, ctx.dcnt)
                for c in range(NCH):
                    nc.tensor.matmul(PS[:, 0:TT], wsl[:, c * 128:(c + 1) * 128],
                                     xlrhs[:, c * TT:(c + 1) * TT],
                                     start=(c == 0), stop=(c == NCH - 1))
                ctx.edge("tensor", "vector", "scalar")
                if proj == 2 or L0F_GSTEP < 2:
                    nc.vector.tensor_copy(dst[:, g0:g0 + TT], PS[:, 0:TT])
                    ctx.edge("vector", "tensor")
                    continue
                nc.vector.tensor_copy(qsp, PS[:, 0:TT])
                if L0F_GSTEP == 19:
                    nc.vector.tensor_copy(dst[:, g0:g0 + TT], qsp)
                    ctx.edge("vector", "tensor")
                    continue
                ctx.edge("vector", "scalar")
                nc.scalar.activation(sparebf, qsp, AF.Square)
                ctx.edge("scalar", "tensor")
                if L0F_GSTEP == 20:
                    nc.vector.tensor_copy(dst[:, g0:g0 + TT], qsp)
                    ctx.edge("vector", "tensor")
                    continue
                nc.tensor.matmul(PS[0:2, 2 * TT:3 * TT], E2T, sparebf,
                                 start=True, stop=True)
                ctx.edge("tensor", "vector")
                if L0F_GSTEP == 21:
                    nc.vector.tensor_copy(dst[:, g0:g0 + TT], qsp)
                    ctx.edge("vector", "tensor")
                    continue
                n1 = nrm[0:2, 0:TT]
                n2 = nrm[0:2, TT:2 * TT]
                nc.vector.tensor_scalar(n1, PS[0:2, 2 * TT:3 * TT], 1e-24, None,
                                        ALU.max)
                ctx.edge("vector", "scalar")
                nc.scalar.activation(n2, n1, AF.Sqrt)
                ctx.edge("scalar", "vector")
                nc.vector.reciprocal(n1, n2)
                ctx.edge("vector", "tensor")
                if L0F_GSTEP == 22:
                    nc.vector.tensor_copy(dst[:, g0:g0 + TT], qsp)
                    ctx.edge("vector", "tensor")
                    continue
                nc.tensor.matmul(PS[:, 2 * TT:3 * TT], E2f, n1,
                                 start=True, stop=True)
                ctx.edge("tensor", "vector")
                if L0F_GSTEP == 23:
                    nc.vector.tensor_copy(dst[:, g0:g0 + TT], qsp)
                    ctx.edge("vector", "tensor")
                    continue
                nc.vector.tensor_tensor(qsp, qsp, PS[:, 2 * TT:3 * TT], ALU.mult)
                if proj == 0:
                    nc.vector.tensor_scalar(qsp, qsp, qsv, 8.0, ALU.mult,
                                            ALU.mult)
                else:
                    nc.vector.tensor_scalar(qsp, qsp, ksv, None, ALU.mult)
                if L0F_GSTEP < 3:
                    nc.vector.tensor_copy(dst[:, g0:g0 + TT], qsp)
                    ctx.edge("vector", "tensor")
                    continue
                for bb in range(4):
                    nc.vector.tensor_copy(qsw[32 * bb:32 * bb + 32],
                                          qsp[32 * (bb ^ 1):32 * (bb ^ 1) + 32])
                ctx.edge("vector", "sync")
                ctx.dma(tabA, rA[:, g0:g0 + TT])
                ctx.dma(tabB, rB[:, g0:g0 + TT])
                nc.vector.wait_ge(ctx.dsem, ctx.dcnt)
                nc.vector.tensor_tensor(scw, qsp, tabA, ALU.mult)
                nc.vector.tensor_tensor(qsw, qsw, tabB, ALU.mult)
                nc.vector.tensor_add(scw, scw, qsw)
                nc.vector.tensor_copy(dst[:, g0:g0 + TT], scw)
                ctx.edge("vector", "tensor")

    def attn_head(pr, hh, pairk, pairv):
        rows = slice(hh * 64, hh * 64 + 64)
        nc.vector.tensor_copy(qh, pairq[rows, :])
        nc.vector.tensor_copy(kh, pairk[rows, :])
        nc.vector.memset(vtok, 1.0)
        ctx.edge("vector", "tensor")
        for kt0 in range(0, 20, 4):
            for j in range(4):
                kt = kt0 + j
                nc.tensor.transpose(PSb[:, 4096 + j * 128:4096 + (j + 1) * 128],
                                    pairv[:, kt * 128:(kt + 1) * 128], identb)
            ctx.edge("tensor", "vector")
            for j in range(4):
                kt = kt0 + j
                nc.vector.tensor_copy(vtok[:, kt * 65:kt * 65 + 64],
                                      PSb[:, 4096 + j * 128 + hh * 64:
                                          4096 + j * 128 + hh * 64 + 64])
            ctx.edge("vector", "tensor")
        for qs in range(16):
            k0 = qs * 128
            lq = qh[0:64, qs * 128:(qs + 1) * 128]
            nc.tensor.matmul(PS[:, 0:TT], lq, kh[0:64, k0:k0 + TT],
                             start=True, stop=False)
            if qs < 4:
                wm = TT - qs * 128
                nc.tensor.matmul(PS[:, 0:wm], identb, hmk[:, 0:wm],
                                 start=False, stop=False)
            nc.tensor.matmul(PS[:, 0:128], identb, trip[:, 0:128],
                             start=False, stop=True)
            nc.tensor.matmul(PS[:, TT:TT + 128], lq, kh[0:64, k0 + TT:k0 + 640],
                             start=True, stop=False)
            nc.tensor.matmul(PS[:, TT:TT + 128], identb, trip[:, 128:256],
                             start=False, stop=True)
            ctx.edge("tensor", "vector")
            nc.vector.reduce_max(red, PS[:, 0:640], axis=AX)
            ctx.bar()
            nc.vector.tensor_scalar(red, red, -1.0, None, ALU.mult)
            ctx.edge("vector", "scalar")
            nc.scalar.activation(ptile, PS[:, 0:640], AF.Exp, bias=red,
                                 scale=1.0)
            ctx.edge("scalar", "tensor", "vector")
            for j in range(5):
                nc.tensor.transpose(PSb[:, 4096 + j * 128:4096 + (j + 1) * 128],
                                    ptile[:, j * 128:(j + 1) * 128], identb)
            ctx.edge("tensor", "vector")
            for j in range(5):
                nc.vector.tensor_copy(pT[:, j * 128:(j + 1) * 128],
                                      PSb[:, 4096 + j * 128:4096 + (j + 1) * 128])
            ctx.edge("vector", "tensor")
            for j in range(5):
                kt = k0 // 128 + j
                nc.tensor.matmul(PS[0:65, 3 * TT:3 * TT + 128],
                                 vtok[:, kt * 65:(kt + 1) * 65],
                                 pT[:, j * 128:(j + 1) * 128],
                                 start=(j == 0), stop=(j == 4))
            ctx.edge("tensor", "scalar")
            nc.scalar.activation(ztb, PS[0:65, 3 * TT:3 * TT + 128], AF.Copy)
            ctx.edge("scalar", "tensor")
            nc.tensor.transpose(PSb[:, 4096:4096 + 65], ztb,
                                identb[0:65, 0:65])
            ctx.edge("tensor", "vector")
            nc.vector.tensor_copy(ztk, PSb[:, 4096:4096 + 65])
            ctx.bar()
            nc.vector.reciprocal(rcp, ztk[:, 64:65])
            ctx.bar()
            nc.vector.tensor_scalar(ztk[:, 0:64], ztk[:, 0:64], rcp, None,
                                    ALU.mult)
            ctx.bar()
            nc.vector.tensor_copy(ztb2[:, hh * 64:hh * 64 + 64], ztk[:, 0:64])
            ctx.edge("vector", "tensor")
            nc.tensor.transpose(PSb[:, 4352:4352 + 128], ztb2, identb)
            ctx.edge("tensor", "vector")
            nc.vector.tensor_copy(ofm[pr][rows, qs * 128:(qs + 1) * 128],
                                  PSb[rows, 4352:4352 + 128])
            ctx.edge("vector", "tensor")

    step = L0F_STEP
    for l in range(2 if step >= 10 else 1):
        ctx.dma(qsv, qs_i[l])
        ctx.dma(ksv, ks_i[l])
        ctx.dwait()
        sb.mark()
        pairk = sb.abf(L0W)
        pairv = sb.abf(L0W)
        tabA = sb.af(TT)
        tabB = sb.af(TT)
        qsp = sb.af(TT)
        qsw = sb.af(TT)
        nrm = sb.af(2 * TT)
        wsl = sb.abf(1024)
        xlrhs = sb.abf(4096)
        for c in range(NCH):
            ctx.dmag(xlb_dram.ap()[c * 128:(c + 1) * 128, 0:2561],
                     XL[c][:, 0:2561])
        nc.vector.memset(ztb2, 0.0)
        ctx.dwait()
        if step >= 2:
            for pr in range(4 if step >= 4 else 1):
                gen_pair(l, pr, pairk, pairv, tabA, tabB, qsp, qsw, nrm, wsl,
                         xlrhs)
                if step >= 3:
                    for hh in range(2):
                        attn_head(pr, hh, pairk, pairv)
        if 4 <= step <= 6 and l == 0:
            dbg_o = nc.dram_tensor("dbg_o", [512, TPC], F32,
                                   kind="ExternalOutput").ap()
            ost = qsp
            for pr2 in range(4):
                for t2 in range(NTT):
                    nc.vector.tensor_copy(ost, ofm[pr2][:, t2 * TT:(t2 + 1) * TT])
                    ctx.edge("vector", "sync")
                    ctx.dma(dbg_o[pr2 * 128:(pr2 + 1) * 128,
                                  t2 * TT:(t2 + 1) * TT], ost)
                    nc.vector.wait_ge(ctx.dsem, ctx.dcnt)
        lwo = wsl[:, 0:512]
        for co in range(NCH if step >= 5 else 0):
            ctx.edge("tensor", "gpsimd")
            ctx.dmag(lwo, wo_i[l][:, co * 128:(co + 1) * 128]
                     .rearrange("(c p) o -> p c o", p=128))
            ctx.dwait()
            for t in range(NTT):
                for ci in range(4):
                    nc.tensor.matmul(PS[:, 0:TT], lwo[:, ci * 128:(ci + 1) * 128],
                                     ofm[ci][:, t * TT:(t + 1) * TT],
                                     start=(ci == 0), stop=(ci == 3))
                ctx.edge("tensor", "vector")
                osl = slice(513 + t * TT, 513 + (t + 1) * TT)
                nc.vector.tensor_add(XL[co][:, osl], XL[co][:, osl], PS[:, 0:TT])
                ctx.edge("vector", "tensor")
        sb.release()
        if step < 6:
            break
        halo_exchange()

        ctx.dma(b1v, b1_i[l])
        ctx.dma(b2v, b2_i[l])
        ctx.dma(lngv, lng_i[l])
        ctx.dma(lnbv, lnb_i[l])
        ctx.dwait()
        sb.mark()
        wbuf = sb.abf(4096)
        h3t2 = sb.abf(1024)
        f3sq = sb.af(TT)
        f3gel = sb.af(TT)
        f3scr = sb.abf(TT)
        for t in range(NTT if step >= 7 else 0):
            xoffs = [513 + t * TT - (0 if c < 4 else 1) for c in range(NCH)]
            for c in range(NCH):
                nc.tensor.matmul(PS[0:1, 2 * TT:3 * TT], onesv,
                                 XL[c][:, xoffs[c]:xoffs[c] + TT],
                                 start=(c == 0), stop=(c == NCH - 1))
            for c in range(NCH):
                nc.scalar.activation(f3sq, XL[c][:, xoffs[c]:xoffs[c] + TT],
                                     AF.Square)
                ctx.edge("scalar", "tensor")
                nc.tensor.matmul(PS[0:1, 3 * TT:4 * TT], onesv, f3sq,
                                 start=(c == 0), stop=(c == NCH - 1))
                ctx.edge("tensor", "scalar")
            ctx.edge("tensor", "vector")
            mean = sb.R[0:1, 0:TT]
            var = sb.R[0:1, TT:2 * TT]
            rstd = sb.R[0:1, 2 * TT:3 * TT]
            brow = sb.R[0:1, 3 * TT:4 * TT]
            nc.vector.tensor_scalar(mean, PS[0:1, 2 * TT:3 * TT], 1.0 / D, None,
                                    ALU.mult)
            nc.vector.tensor_scalar(var, PS[0:1, 3 * TT:4 * TT], 1.0 / D, None,
                                    ALU.mult)
            nc.vector.tensor_tensor(rstd, mean, mean, ALU.mult)
            nc.vector.tensor_sub(var, var, rstd)
            nc.vector.tensor_scalar(var, var, 1e-5, None, ALU.add)
            ctx.edge("vector", "scalar")
            nc.scalar.activation(rstd, var, AF.Sqrt)
            ctx.edge("scalar", "vector")
            nc.vector.reciprocal(rstd, rstd)
            nc.vector.tensor_tensor(brow, mean, rstd, ALU.mult)
            nc.vector.tensor_scalar(brow, brow, -1.0, None, ALU.mult)
            ctx.edge("vector", "tensor")
            nc.tensor.matmul(PS[:, 2 * TT:3 * TT], ones1f, rstd,
                             start=True, stop=True)
            nc.tensor.matmul(PS[:, 3 * TT:4 * TT], ones1f, brow,
                             start=True, stop=True)
            ctx.edge("tensor", "vector")
            for c in range(NCH):
                nc.vector.tensor_tensor(f3gel, XL[c][:, xoffs[c]:xoffs[c] + TT],
                                        PS[:, 2 * TT:3 * TT], ALU.mult)
                nc.vector.tensor_add(f3gel, f3gel, PS[:, 3 * TT:4 * TT])
                nc.vector.tensor_scalar(hnall[c][:, t * TT:(t + 1) * TT], f3gel,
                                        lngv[:, c:c + 1], lnbv[:, c:c + 1],
                                        ALU.mult, ALU.add)
            ctx.edge("vector", "tensor")
        gtg = {}

        def f2_issue(j):
            s = j % 2
            wj = 128 if j < 21 else 42
            ctx.dmag(wbuf[:, s * 2048:s * 2048 + NCH * wj]
                     .rearrange("p (c o) -> p c o", o=wj),
                     w1_i[l][:, j * 128:j * 128 + wj]
                     .rearrange("(c p) o -> p c o", p=128))
            ctx.dmag(wbuf[:, s * 2048 + 1024:s * 2048 + 1024 + NCH * wj]
                     .rearrange("p (c o) -> p c o", o=wj),
                     w1_i[l][:, 2730 + j * 128:2730 + j * 128 + wj]
                     .rearrange("(c p) o -> p c o", p=128))
            gtg[j] = ctx.gcnt

        if step >= 8:
            f2_issue(0)
        for j in range(22 if step >= 8 else 0):
            s = j % 2
            wj = 128 if j < 21 else 42
            if j + 1 < 22:
                if j >= 1:
                    ctx.edge("tensor", "gpsimd")
                f2_issue(j + 1)
            nc.tensor.wait_ge(ctx.gsem, gtg[j])
            for t in range(NTT):
                for c in range(NCH):
                    nc.tensor.matmul(PS[0:wj, 0:TT],
                                     wbuf[:, s * 2048 + c * wj:
                                          s * 2048 + (c + 1) * wj],
                                     hnall[c][:, t * TT:(t + 1) * TT],
                                     start=(c == 0), stop=(c == NCH - 1))
                for c in range(NCH):
                    nc.tensor.matmul(PS[0:wj, TT:2 * TT],
                                     wbuf[:, s * 2048 + 1024 + c * wj:
                                          s * 2048 + 1024 + (c + 1) * wj],
                                     hnall[c][:, t * TT:(t + 1) * TT],
                                     start=(c == 0), stop=(c == NCH - 1))
                ctx.edge("tensor", "scalar", "vector")
                nc.vector.tensor_scalar(f3sq[0:wj, :], PS[0:wj, 0:TT],
                                        b1v[0:wj, j:j + 1], None, ALU.add)
                nc.scalar.activation(f3gel[0:wj, :], PS[0:wj, TT:2 * TT],
                                     AF.Gelu, bias=b1v[0:wj, 22 + j:23 + j],
                                     scale=1.0)
                ctx.edge("scalar", "vector")
                nc.vector.wait_ge(ctx.dsem, ctx.dcnt)
                nc.vector.tensor_tensor(f3scr[0:wj, :], f3sq[0:wj, :],
                                        f3gel[0:wj, :], ALU.mult)
                ctx.edge("vector", "sync", "tensor")
                ctx.edge("scalar", "tensor")
                ctx.dma(h3_dram.ap()[j * 128:j * 128 + wj, t * TT:(t + 1) * TT],
                        f3scr[0:wj, :])
        for t in range(NTT if step >= 9 else 0):
            dtg = {}

            def f3_issue(cc):
                s = cc % 2
                wj = 128 if cc < 21 else 42
                ctx.dmag(wbuf[0:wj, s * 1024:(s + 1) * 1024],
                         w2_i[l][cc * 128:cc * 128 + wj, :])
                ctx.dma(h3t2[0:wj, s * TT:(s + 1) * TT],
                        h3_dram.ap()[cc * 128:cc * 128 + wj, t * TT:(t + 1) * TT])
                dtg[cc] = (ctx.dcnt, ctx.gcnt)

            if t == 0:
                nc.sync.wait_ge(ctx.dsem, ctx.dcnt)
                ctx.bar()
            ctx.edge("tensor", "gpsimd", "sync")
            f3_issue(0)
            for cc in range(22):
                s = cc % 2
                wj = 128 if cc < 21 else 42
                if cc + 1 < 22:
                    if cc >= 1:
                        ctx.edge("tensor", "gpsimd", "sync")
                    f3_issue(cc + 1)
                nc.tensor.wait_ge(ctx.dsem, dtg[cc][0])
                nc.tensor.wait_ge(ctx.gsem, dtg[cc][1])
                for co in range(NCH):
                    nc.tensor.matmul(PS[:, co * TT:(co + 1) * TT],
                                     wbuf[0:wj, s * 1024 + co * 128:
                                          s * 1024 + (co + 1) * 128],
                                     h3t2[0:wj, s * TT:(s + 1) * TT],
                                     start=(cc == 0), stop=(cc == 21))
            ctx.edge("tensor", "vector")
            for co in range(NCH):
                osl = slice(513 + t * TT, 513 + (t + 1) * TT)
                nc.vector.tensor_scalar(f3sq, PS[:, co * TT:(co + 1) * TT],
                                        b2v[:, co:co + 1], None, ALU.add)
                ctx.bar()
                if l == 0:
                    nc.vector.tensor_add(XL[co][:, osl], XL[co][:, osl], f3sq)
                    ctx.bar()
                else:
                    nc.vector.tensor_add(f3sq, f3sq, XL[co][:, osl])
                    ctx.edge("vector", "scalar")
                    nc.scalar.activation(f3gel, f3sq, AF.Silu)
                    ctx.edge("scalar", "sync")
                    ctx.dma(g_dram[co * 128:(co + 1) * 128,
                                   t * TT:(t + 1) * TT], f3gel)
                    nc.scalar.wait_ge(ctx.dsem, ctx.dcnt)
                    nc.vector.wait_ge(ctx.dsem, ctx.dcnt)
            ctx.edge("vector", "tensor")
        sb.release()
        if l == 0:
            halo_exchange()

    if step < 10 and step != 4:
        ctx.edge("vector", "sync")
        for c in range(NCH):
            ctx.dma(g_dram[c * 128:(c + 1) * 128, :], XL[c][:, 513:2561])
        ctx.dwait()
    sb.release()
    for c in range(NCH):
        ctx.dma(sb.x[c][:, :], x_spill.ap()[c * 128:(c + 1) * 128, :])
    ctx.dwait()


def l0f_in_map(inp, c):
    q = c % 4
    im = {}
    f32 = lambda a: np.ascontiguousarray(np.asarray(a, np.float32))
    for l in range(2):
        im[f"L0f_wqkv{l}"] = f32(inp["L0_ffn_wqkv"][l])
        im[f"L0f_wo{l}"] = f32(inp["L0_ffn_wo"][l])
        im[f"L0f_w1_{l}"] = f32(inp["L0_ffn_w1"][l])
        im[f"L0f_w2_{l}"] = f32(inp["L0_ffn_w2"][l])
        b1 = f32(inp["L0_ffn_b1"][l])
        Bm = np.zeros((128, 44), np.float32)
        for j in range(22):
            wj = 128 if j < 21 else 42
            Bm[0:wj, j] = b1[j * 128:j * 128 + wj]
            Bm[0:wj, 22 + j] = b1[2730 + j * 128:2730 + j * 128 + wj]
        im[f"L0f_b1_{l}"] = Bm
        im[f"L0f_b2_{l}"] = f32(inp["L0_ffn_b2"][l]).reshape(8, 128).T.copy()
        im[f"L0f_lng{l}"] = f32(inp["L0_ffn_ln_g"][l]).reshape(8, 128).T.copy()
        im[f"L0f_lnb{l}"] = f32(inp["L0_ffn_ln_b"][l]).reshape(8, 128).T.copy()
        im[f"L0f_qs{l}"] = np.tile(f32(inp["L0_ffn_q_scale"][l]), 2).reshape(128, 1)
        im[f"L0f_ks{l}"] = np.tile(f32(inp["L0_ffn_k_scale"][l]), 2).reshape(128, 1)
    Aq, Bq, Ak, Bk = l0_rot_tables(q)
    im["L0f_rotAq"], im["L0f_rotBq"] = Aq, Bq
    im["L0f_rotAk"], im["L0f_rotBk"] = Ak, Bk
    bs = (q == 0)
    im["L0f_halomask"] = np.full((128, TT), -1e9 if bs else 0.0, np.float32)
    im["L0f_hvalid"] = np.full((128, 1), 0.0 if bs else 1.0, np.float32)
    return im


def l3_band_pattern():
    i = np.arange(128)[:, None]
    j = np.arange(134)[None, :]
    return np.where((j >= i) & (j <= i + 6), 0.0, -1e9).astype(BF)


def l3_edge_patterns():
    ln2, ln3 = float(np.log(2.0)), float(np.log(3.0))
    first = np.array([[0, 0, 0, 0, -1e9, -1e9, ln3],
                      [0, 0, 0, 0, 0, -1e9, ln2],
                      [0, 0, 0, 0, 0, 0, 0]], dtype=np.float32)
    last = np.array([[0, 0, 0, 0, 0, 0, 0],
                     [ln2, -1e9, 0, 0, 0, 0, 0],
                     [ln3, -1e9, -1e9, 0, 0, 0, 0]], dtype=np.float32)
    return np.concatenate([first, np.zeros((3, 1), np.float32), last],
                          axis=1).astype(BF)


def emit_l3_attn(ctx, sb, xnag, xn_own, g_dram, tag="l3a"):
    nc = ctx.nc
    PS = ctx.PS
    PSb = ctx.PS.ap().bitcast(BF16)
    identb = ctx.get_identb(sb)
    wqkvT = ctx.inp("L3w_qkvT", [D, 3072], F32)
    bqkv = ctx.inp("L3b_qkv", [128, 24], F32)
    wlatBD = ctx.inp("L3w_latBD", [128, 128], F32)
    blat2 = ctx.inp("L3b_lat2", [128, 1], F32)
    wprojT = ctx.inp("L3w_projT", [D, D], F32)
    bproj = ctx.inp("L3b_proj", [128, 8], F32)
    em_first = ctx.inp("L3_em_first", [128, 1], F32)
    em_last = ctx.inp("L3_em_last", [128, 1], F32)
    sb.mark()
    ss = float(1.0 / np.sqrt(np.sqrt(64.0)))
    Wqs = sb.abf(NCH * 128)
    Bia = sb.af(24)
    ctx.dma(Bia, bqkv)
    BD = sb.abf(128)
    ctx.dmag(BD, wlatBD)
    Bl2 = sb.af(8)[:, 0:1]
    ctx.dma(Bl2, blat2)
    Wp = sb.abf(NCH * 1024)
    ctx.dmag(Wp, wprojT.rearrange("(c p) o -> p c o", p=128))
    Bp = sb.af(8)
    ctx.dma(Bp, bproj)
    emf = sb.af(8)[:, 0:1]
    ctx.dma(emf, em_first)
    eml = sb.af(8)[:, 0:1]
    ctx.dma(eml, em_last)
    bandp = sb.abf(134)
    ctx.dmag(bandp, ctx.const("l3_band", l3_band_pattern()))
    edgep = sb.abf(16)[0:3, :]
    ctx.dmag(edgep[0:3, 0:15], ctx.const("l3_edge", l3_edge_patterns()))
    obuf = [sb.abf(TPC) for _ in range(NCH)]
    ctx.dwait()

    cid = nc.partition_id()
    qm1 = nc.snap((cid + 3) % 4)
    qp1 = nc.snap((cid + 1) % 4)

    xnt = sb.abf(4096)
    xnt_h = sb.abf(48)
    xnt_s = sb.abf(128)
    THW = TPC + 8
    TH = 3
    nc.vector.memset(xnt_h, 0.0)
    nc.vector.memset(xnt_s, 0.0)
    ctx.bar()
    h3 = xnt_h.rearrange("p (c o) -> p c o", o=6)
    s3v = xnt_s.rearrange("p (c o) -> p c o", o=16)
    ctx.dma(h3[:, :, 0:3],
            xnag[bass.ts(qm1, 1024), 2045:2048].rearrange("(c p) t -> p c t", p=128))
    ctx.dma(h3[:, :, 3:6],
            xnag[bass.ts(qp1, 1024), 0:3].rearrange("(c p) t -> p c t", p=128))
    ctx.dma(s3v[:, :, 0:6],
            xnag[0:1024, 0:6].rearrange("(c p) t -> p c t", p=128))
    ctx.dma(s3v[:, :, 6:13],
            xnag[3 * 1024:4 * 1024, 2041:2048].rearrange("(c p) t -> p c t", p=128))
    ctx.dwait()

    qh = [sb.abf(TPC), sb.abf(TPC)]
    kh = [sb.abf(THW), sb.abf(THW)]
    vh = [sb.abf(THW), sb.abf(THW)]
    qsp = [sb.abf(16), sb.abf(16)]
    ksp = [sb.abf(16), sb.abf(16)]
    vsp = [sb.abf(16), sb.abf(16)]
    pair = sb.abf(THW)
    ptile = sb.abf(134 + 10)
    pT1 = sb.abf(128)
    pT2 = sb.abf(128)
    vt1 = sb.abf(64)
    vt2 = sb.abf(64)
    ztk = sb.af(64)
    ztb = sb.abf(64)
    red = sb.af(8)[:, 0:1]
    rcp = sb.af(8)[:, 1:2]
    spm = sb.af(16)
    spmb = sb.abf(16)
    sp_p = sb.abf(16)
    sp_pT = sb.abf(16)
    sp_v = sb.abf(80)
    sp_vT = sb.abf(80)
    sp_z = sb.af(64)
    spzf = sb.af(64)
    sred = sb.af(8)[:, 2:3]
    srcp = sb.af(8)[:, 3:4]
    ebufs = dict(edgep=edgep, sp_p=sp_p, sp_pT=sp_pT, sp_v=sp_v, sp_vT=sp_vT,
                 sp_z=sp_z, spzf=spzf, red=sred, rcp=srcp)

    for hc in range(NCH):
        for sec, width, dst, dsts in ((1, THW, kh, ksp), (2, THW, vh, vsp),
                                      (0, TPC, qh, qsp)):
            bia = Bia[:, sec * 8 + hc: sec * 8 + hc + 1]
            ctx.edge("tensor", "gpsimd")
            ctx.dmag(Wqs, wqkvT[:, sec * 1024 + hc * 128: sec * 1024 + (hc + 1) * 128]
                     .rearrange("(c p) o -> p c o", p=128))
            ctx.dwait()
            for t in range(NTT):
                ctx.edge("tensor", "sync")
                ctx.dma(xnt, xn_own[0:1024, t * TT:(t + 1) * TT]
                        .rearrange("(c p) t -> p c t", p=128))
                ctx.dwait()
                for c in range(NCH):
                    nc.tensor.matmul(PS[:, 0:TT],
                                     Wqs[:, c * 128:(c + 1) * 128],
                                     xnt[:, c * TT:(c + 1) * TT],
                                     start=(c == 0), stop=(c == NCH - 1))
                ctx.edge("tensor", "vector")
                off = TH if sec != 0 else 0
                nc.vector.tensor_scalar(pair[:, off + t * TT: off + (t + 1) * TT],
                                        PS[:, 0:TT], bia, None, ALU.add)
                ctx.edge("vector", "tensor")
            if sec != 0:
                for c in range(NCH):
                    nc.tensor.matmul(PS[:, 3 * TT:3 * TT + 6],
                                     Wqs[:, c * 128:(c + 1) * 128],
                                     xnt_h[:, c * 6:(c + 1) * 6],
                                     start=(c == 0), stop=(c == NCH - 1))
                ctx.edge("tensor", "vector")
                nc.vector.tensor_scalar(spm[:, 0:6], PS[:, 3 * TT:3 * TT + 6],
                                        bia, None, ALU.add)
                ctx.bar()
                nc.vector.tensor_copy(pair[:, 0:3], spm[:, 0:3])
                nc.vector.tensor_copy(pair[:, TH + TPC:TH + TPC + 3], spm[:, 3:6])
                nc.vector.memset(pair[:, TH + TPC + 3:THW], 0.0)
                ctx.edge("vector", "tensor")
            for c in range(NCH):
                nc.tensor.matmul(PS[:, 3 * TT:3 * TT + 16],
                                 Wqs[:, c * 128:(c + 1) * 128],
                                 xnt_s[:, c * 16:(c + 1) * 16],
                                 start=(c == 0), stop=(c == NCH - 1))
            ctx.edge("tensor", "vector")
            nc.vector.tensor_scalar(spm, PS[:, 3 * TT:3 * TT + 16], bia, None, ALU.add)
            ctx.bar()
            if sec != 0:
                mul = ss if sec == 1 else 1.0
                for seg in range(0, THW, TT):
                    w = min(TT, THW - seg)
                    nc.vector.tensor_copy(pair.bitcast(BF16)[:, 0:1] if False else pair[:, seg:seg + 1], pair[:, seg:seg + 1]) if False else None
                    nc.tensor.matmul(PS[:, 0:w], BD, pair[:, seg:seg + w],
                                     start=True, stop=True)
                    ctx.edge("tensor", "vector")
                    nc.vector.tensor_scalar(pair[:, seg:seg + w], PS[:, 0:w],
                                            Bl2, mul, ALU.add, ALU.mult)
                    ctx.edge("vector", "tensor")
                nc.vector.tensor_copy(spmb, spm)
                ctx.edge("vector", "tensor")
                nc.tensor.matmul(PS[:, 0:16], BD, spmb, start=True, stop=True)
                ctx.edge("tensor", "vector")
                nc.vector.tensor_scalar(spm, PS[:, 0:16], Bl2, mul, ALU.add, ALU.mult)
                ctx.bar()
            else:
                nc.vector.tensor_scalar(pair[:, 0:TPC], pair[:, 0:TPC], ss, None,
                                        ALU.mult)
                nc.vector.tensor_scalar(spm, spm, ss, None, ALU.mult)
                ctx.bar()
            for hh in range(2):
                rows = slice(0, 64) if hh == 0 else slice(64, 128)
                nc.vector.tensor_copy(dst[hh][0:64, 0:width], pair[rows, 0:width])
                nc.vector.tensor_copy(dsts[hh][0:64, :], spm[rows, :])
            ctx.bar()
        for hh in range(2):
            for qs in range(16):
                q0 = qs * 128
                nc.tensor.matmul(PS[:, 0:134], qh[hh][0:64, q0:q0 + 128],
                                 kh[hh][0:64, q0:q0 + 134], start=True, stop=False)
                nc.tensor.matmul(PS[:, 0:134], identb, bandp[:, 0:134],
                                 start=False, stop=True)
                ctx.edge("tensor", "vector")
                nc.vector.reduce_max(red, PS[:, 0:134], axis=AX)
                ctx.bar()
                nc.vector.tensor_scalar(red, red, -1.0, None, ALU.mult)
                ctx.edge("vector", "scalar")
                nc.scalar.activation(ptile[:, 0:134], PS[:, 0:134], AF.Exp,
                                     bias=red, scale=1.0)
                ctx.edge("scalar", "vector", "tensor")
                nc.vector.tensor_reduce(red, ptile[:, 0:134], AX, ALU.add)
                ctx.bar()
                nc.vector.reciprocal(rcp, red)
                nc.vector.memset(ptile[:, 134:144], 0.0)
                ctx.edge("vector", "tensor")
                nc.tensor.transpose(PSb[:, 4096:4096 + 128], ptile[:, 0:128], identb)
                nc.tensor.transpose(PSb[0:8, 4352:4352 + 128], ptile[:, 128:136],
                                    identb)
                nc.tensor.transpose(PSb[:, 4608:4608 + 64],
                                    vh[hh][0:64, q0:q0 + 128], identb[0:64, 0:64])
                nc.tensor.transpose(PSb[0:8, 4864:4864 + 64],
                                    vh[hh][0:64, q0 + 128:q0 + 136],
                                    identb[0:64, 0:64])
                ctx.edge("tensor", "vector")
                nc.vector.tensor_copy(pT1, PSb[:, 4096:4096 + 128])
                nc.vector.tensor_copy(pT2[0:8, 0:128], PSb[0:8, 4352:4352 + 128])
                nc.vector.tensor_copy(vt1[:, 0:64], PSb[:, 4608:4608 + 64])
                nc.vector.tensor_copy(vt2[0:8, 0:64], PSb[0:8, 4864:4864 + 64])
                ctx.edge("vector", "tensor")
                nc.tensor.matmul(PS[:, 3 * TT:3 * TT + 64], pT1, vt1[:, 0:64],
                                 start=True, stop=False)
                nc.tensor.matmul(PS[:, 3 * TT:3 * TT + 64], pT2[0:8, 0:128],
                                 vt2[0:8, 0:64], start=False, stop=True)
                ctx.edge("tensor", "vector")
                nc.vector.tensor_scalar(ztk, PS[:, 3 * TT:3 * TT + 64], rcp, None,
                                        ALU.mult)
                ctx.bar()
                if qs == 0:
                    _l3_edge_fix(ctx, sb, qsp[hh], ksp[hh], vsp[hh], ztk, emf,
                                 True, identb, ebufs)
                elif qs == 15:
                    _l3_edge_fix(ctx, sb, qsp[hh], ksp[hh], vsp[hh], ztk, eml,
                                 False, identb, ebufs)
                nc.vector.tensor_copy(ztb, ztk)
                ctx.edge("vector", "tensor")
                nc.tensor.transpose(PSb[0:64, 4096:4096 + 128], ztb[:, 0:64], identb)
                ctx.edge("tensor", "vector")
                nc.vector.tensor_copy(obuf[hc][(0 if hh == 0 else 64):
                                               (64 if hh == 0 else 128), q0:q0 + 128],
                                      PSb[0:64, 4096:4096 + 128])
                ctx.edge("vector", "tensor")

    gt = sb.af(TT)
    tmp = sb.af(TT)
    for t in range(NTT):
        tsl = slice(t * TT, (t + 1) * TT)
        for co in range(NCH):
            for ci in range(NCH):
                nc.tensor.matmul(PS[:, 0:TT], Wp[:, ci * 1024 + co * 128:
                                                 ci * 1024 + (co + 1) * 128],
                                 obuf[ci][:, tsl], start=(ci == 0),
                                 stop=(ci == NCH - 1))
            ctx.edge("tensor", "vector")
            ctx.edge("vector", "sync")
            ctx.dma(gt, g_dram.ap()[co * 128:(co + 1) * 128, tsl])
            nc.sync.wait_ge(ctx.dsem, ctx.dcnt)
            ctx.bar()
            nc.vector.tensor_scalar(tmp, PS[:, 0:TT], Bp[0:128, co:co + 1], None,
                                    ALU.add)
            nc.vector.tensor_mul(tmp, tmp, gt)
            nc.vector.tensor_add(sb.x[co][:, tsl], sb.x[co][:, tsl], tmp)
            ctx.bar()
    sb.release()


def _l3_edge_fix(ctx, sb, qsp, ksp, vsp, ztk, em, is_first, identb, eb):
    nc = ctx.nc
    PS = ctx.PS
    PSb = ctx.PS.ap().bitcast(BF16)
    sp_p, sp_pT, sp_v, sp_vT = eb["sp_p"], eb["sp_pT"], eb["sp_v"], eb["sp_vT"]
    sp_z, spzf, red, rcp, edgep = eb["sp_z"], eb["spzf"], eb["red"], eb["rcp"], eb["edgep"]
    qc = slice(0, 3) if is_first else slice(10, 13)
    sco = 2 * TT
    if is_first:
        nc.tensor.matmul(PS[0:3, sco:sco + 6], qsp[0:64, qc], ksp[0:64, 0:6],
                         start=True, stop=True)
        nc.tensor.matmul(PS[0:3, sco + 6:sco + 7], qsp[0:64, qc], ksp[0:64, 12:13],
                         start=True, stop=True)
        patt = edgep[0:3, 0:7]
        vcols = list(range(0, 6)) + [12]
    else:
        nc.tensor.matmul(PS[0:3, sco:sco + 1], qsp[0:64, qc], ksp[0:64, 0:1],
                         start=True, stop=True)
        nc.tensor.matmul(PS[0:3, sco + 1:sco + 7], qsp[0:64, qc], ksp[0:64, 7:13],
                         start=True, stop=True)
        patt = edgep[0:3, 8:15]
        vcols = [0] + list(range(7, 13))
    ctx.edge("tensor", "vector")
    nc.vector.tensor_copy(sp_p[0:3, 0:7], PS[0:3, sco:sco + 7])
    ctx.bar()
    nc.vector.tensor_tensor(sp_p[0:3, 0:7], sp_p[0:3, 0:7], patt, ALU.add)
    ctx.bar()
    nc.vector.reduce_max(red[0:3, :], sp_p[0:3, 0:7], axis=AX)
    ctx.bar()
    nc.vector.tensor_scalar(red[0:3, :], red[0:3, :], -1.0, None, ALU.mult)
    ctx.edge("vector", "scalar")
    nc.scalar.activation(sp_p[0:3, 0:7], sp_p[0:3, 0:7], AF.Exp, bias=red[0:3, :],
                         scale=1.0)
    ctx.edge("scalar", "vector")
    nc.vector.tensor_reduce(red[0:3, :], sp_p[0:3, 0:7], AX, ALU.add)
    ctx.bar()
    nc.vector.reciprocal(rcp[0:3, :], red[0:3, :])
    for j, vc in enumerate(vcols):
        nc.vector.tensor_copy(sp_v[0:64, j:j + 1], vsp[0:64, vc:vc + 1])
    nc.vector.memset(sp_p[0:3, 7:8], 0.0)
    nc.vector.memset(sp_v[0:64, 7:8], 0.0)
    ctx.edge("vector", "tensor")
    nc.tensor.transpose(PSb[0:8, 4096:4096 + 3], sp_p[0:3, 0:8], identb[0:3, 0:3])
    nc.tensor.transpose(PSb[0:8, 4352:4352 + 64], sp_v[0:64, 0:8],
                        identb[0:64, 0:64])
    ctx.edge("tensor", "vector")
    nc.vector.tensor_copy(sp_pT[0:8, 0:3], PSb[0:8, 4096:4096 + 3])
    nc.vector.tensor_copy(sp_vT[0:8, 0:64], PSb[0:8, 4352:4352 + 64])
    ctx.edge("vector", "tensor")
    nc.tensor.matmul(PS[0:3, sco:sco + 64], sp_pT[0:8, 0:3], sp_vT[0:8, 0:64],
                     start=True, stop=True)
    ctx.edge("tensor", "vector")
    nc.vector.tensor_scalar(sp_z[0:3, 0:64], PS[0:3, sco:sco + 64], rcp[0:3, :],
                            None, ALU.mult)
    ctx.bar()
    nc.vector.memset(spzf[:, 0:64], 0.0)
    ctx.edge("vector", "sync")
    r0 = 0 if is_first else 125
    ctx.dma(spzf[r0:r0 + 3, 0:64], sp_z[0:3, 0:64])
    ctx.dwait()
    nc.vector.tensor_sub(spzf[:, 0:64], spzf[:, 0:64], ztk[:, 0:64])
    nc.vector.tensor_scalar(spzf[:, 0:64], spzf[:, 0:64], em, None, ALU.mult)
    nc.vector.tensor_add(ztk[:, 0:64], ztk[:, 0:64], spzf[:, 0:64])
    ctx.bar()


def emit_l3_ffn(ctx, sb, stats_dram, g_dram, tag="l3f"):
    nc = ctx.nc
    PS = ctx.PS
    cw1 = ctx.inp("L3f_cw1", [1024, 1024], F32)
    cb1 = ctx.inp("L3f_cb1", [128, 8], F32)
    cw2 = ctx.inp("L3f_cw2", [1024, 512], F32)
    cb2 = ctx.inp("L3f_cb2", [128, 4], F32)
    cw3 = ctx.inp("L3f_cw3", [512, 256], F32)
    cb3 = ctx.inp("L3f_cb3", [128, 2], F32)
    cw4 = ctx.inp("L3f_cw4", [128, 2], F32)
    cb4 = ctx.inp("L3f_cb4", [1, 1], F32)
    gw1 = ctx.inp("L3f_gw1", [1024, 512], F32)
    gw2 = ctx.inp("L3f_gw2", [512, 256], F32)
    gw3 = ctx.inp("L3f_gw3", [128, 2], F32)
    gb1 = ctx.inp("L3f_gb1", [128, 4], F32)
    gb2 = ctx.inp("L3f_gb2", [128, 2], F32)
    gb3 = ctx.inp("L3f_gb3", [1, 1], F32)
    arng = ctx.inp("L3f_ar", [1, 1], F32)
    outA = nc.dram_tensor(f"outA_{tag}", [D, TPC], F32, kind="Internal")
    outB = nc.dram_tensor(f"outB_{tag}", [D, TPC], F32, kind="Internal")
    gcin = nc.dram_tensor(f"gci_{tag}", [128, 8], F32, kind="Internal")
    gcout = nc.dram_tensor(f"gco_{tag}", [128, 8], F32, kind="Internal")

    sb.mark()
    W1 = sb.abf(NCH * 1024)
    ctx.dmag(W1, cw1.rearrange("(c p) o -> p c o", p=128))
    W2 = sb.abf(NCH * 512)
    ctx.dmag(W2, cw2.rearrange("(c p) o -> p c o", p=128))
    W3 = sb.abf(4 * 256)
    ctx.dmag(W3, cw3.rearrange("(c p) o -> p c o", p=128))
    W4 = sb.abf(2)
    ctx.dmag(W4, cw4)
    G1 = sb.abf(NCH * 512)
    ctx.dmag(G1, gw1.rearrange("(c p) o -> p c o", p=128))
    G2 = sb.abf(4 * 256)
    ctx.dmag(G2, gw2.rearrange("(c p) o -> p c o", p=128))
    G3 = sb.abf(2)
    ctx.dmag(G3, gw3)
    B1 = sb.af(8)
    ctx.dma(B1, cb1)
    B2 = sb.af(4)
    ctx.dma(B2, cb2)
    B3 = sb.af(2)
    ctx.dma(B3, cb3)
    B4 = sb.af(8)[0:1, 0:1]
    ctx.dma(B4, cb4)
    GB1 = sb.af(4)
    ctx.dma(GB1, gb1)
    GB2 = sb.af(2)
    ctx.dma(GB2, gb2)
    GB3 = sb.af(8)[0:1, 1:2]
    ctx.dma(GB3, gb3)
    AR = sb.af(8)[0:1, 2:3]
    ctx.dma(AR, arng)
    onesb2 = sb.abf(128)[0:1, :]
    strow = sb.af(2 * TT)[0:1, :]
    strowb = sb.abf(2 * TT)[0:1, :]
    ctx.dwait()
    nc.vector.memset(onesb2, 1.0)
    ctx.bar()

    ot = [sb.af(TT) for _ in range(NCH)]
    otb = [sb.abf(TT) for _ in range(NCH)]
    h1t = [sb.abf(TT) for _ in range(NCH)]
    h2t = [sb.abf(TT) for _ in range(4)]
    h3t = [sb.abf(TT) for _ in range(2)]
    twf = sb.af(TT)
    twg = sb.af(TT)
    psum = sb.af(8)
    tmp1 = sb.af(8)[:, 0:1]
    gcb = sb.af(8)
    gcbb = sb.abf(8)
    g1t = [sb.abf(8)[:, i:i + 1] for i in range(4)]
    g2t = [sb.abf(8)[:, 4 + i:5 + i] for i in range(2)]
    trow = sb.af(TT)[0:1, :]
    trow2 = sb.af(TT)[0:1, :]
    trowb = sb.abf(TT)[0:1, :]
    nc.vector.memset(psum, 0.0)
    ctx.bar()
    for t in range(NTT):
        tsl = slice(t * TT, (t + 1) * TT)
        ctx.dma(strow[0:1, 0:TT], stats_dram.ap()[0:1, tsl])
        ctx.dma(strow[0:1, TT:2 * TT], stats_dram.ap()[1:2, tsl])
        ctx.dwait()
        nc.vector.tensor_copy(strowb, strow)
        ctx.edge("vector", "tensor")
        nc.tensor.matmul(PS[:, 2 * TT:3 * TT], onesb2, strowb[0:1, 0:TT],
                         start=True, stop=True)
        nc.tensor.matmul(PS[:, 3 * TT:4 * TT], onesb2, strowb[0:1, TT:2 * TT],
                         start=True, stop=True)
        ctx.edge("tensor", "vector")
        for c in range(NCH):
            nc.vector.tensor_tensor(ot[c], sb.x[c][:, tsl], PS[:, 2 * TT:3 * TT],
                                    ALU.mult)
            nc.vector.tensor_add(ot[c], ot[c], PS[:, 3 * TT:4 * TT])
            nc.vector.reduce_sum(tmp1, ot[c], axis=AX)
            nc.vector.tensor_add(psum[:, c:c + 1], psum[:, c:c + 1], tmp1)
            ctx.edge("vector", "sync")
            ctx.dma(outA.ap()[c * 128:(c + 1) * 128, tsl], ot[c])
        nc.sync.wait_ge(ctx.dsem, ctx.dcnt)
        nc.vector.wait_ge(ctx.dsem, ctx.dcnt)
    ctx.dma(gcin.ap(), psum)
    ctx.dwait()
    ctx.coll("AllReduce", ALU.add, GROUPS_BATCH, gcin.ap(), gcout.ap())
    ctx.cwait()
    ctx.dma(gcb, gcout.ap())
    ctx.dwait()
    nc.vector.tensor_scalar(gcb, gcb, 1.0 / float(S), None, ALU.mult)
    ctx.bar()
    nc.vector.tensor_copy(gcbb, gcb)
    ctx.bar()
    for oc in range(4):
        for c in range(NCH):
            nc.tensor.matmul(PS[:, 3 * TT + oc:3 * TT + oc + 1],
                             G1[:, c * 512 + oc * 128: c * 512 + (oc + 1) * 128],
                             gcbb[:, c:c + 1], start=(c == 0), stop=(c == NCH - 1))
    ctx.edge("tensor", "scalar")
    for oc in range(4):
        nc.scalar.activation(g1t[oc], PS[:, 3 * TT + oc:3 * TT + oc + 1], AF.Gelu,
                             bias=GB1[:, oc:oc + 1], scale=1.0)
    ctx.edge("scalar", "tensor", "vector")
    for oc in range(2):
        for c in range(4):
            nc.tensor.matmul(PS[:, 3 * TT + oc:3 * TT + oc + 1],
                             G2[:, c * 256 + oc * 128: c * 256 + (oc + 1) * 128],
                             g1t[c], start=(c == 0), stop=(c == 3))
    ctx.edge("tensor", "scalar")
    for oc in range(2):
        nc.scalar.activation(g2t[oc], PS[:, 3 * TT + oc:3 * TT + oc + 1], AF.Gelu,
                             bias=GB2[:, oc:oc + 1], scale=1.0)
    ctx.edge("scalar", "tensor", "vector")
    for c in range(2):
        nc.tensor.matmul(PS[0:1, 3 * TT + 4:3 * TT + 5],
                         G3[:, c:c + 1], g2t[c], start=(c == 0), stop=(c == 1))
    ctx.edge("tensor", "scalar")
    gf7 = sb.af(8)[0:1, 3:4]
    ar2 = sb.af(8)[0:1, 4:5]
    nc.scalar.activation(gf7, PS[0:1, 3 * TT + 4:3 * TT + 5], AF.Sigmoid,
                         bias=GB3, scale=1.0)
    ctx.edge("scalar", "vector")
    nc.vector.tensor_scalar(gf7, gf7, 0.7, None, ALU.mult)
    nc.vector.tensor_scalar(ar2, AR, 2.0, None, ALU.mult)
    ctx.bar()

    SIGNS = [1.0, 1.0, -1.0, -1.0, -1.0, -1.0, 1.0, 1.0]
    SIGMA = [4, 5, 6, 7, 0, 1, 2, 3]
    bufs = [outA, outB]
    for cyc in range(3):
        src, dst = bufs[cyc % 2], bufs[(cyc + 1) % 2]
        for t in range(NTT):
            tsl = slice(t * TT, (t + 1) * TT)
            ctx.edge("tensor", "sync")
            ctx.edge("vector", "sync")
            for c in range(NCH):
                ctx.dma(ot[c], src.ap()[c * 128:(c + 1) * 128, tsl])
            ctx.dwait()
            for c in range(NCH):
                nc.vector.tensor_copy(otb[c], ot[c])
            ctx.edge("vector", "tensor")
            for oc in range(NCH):
                for c in range(NCH):
                    nc.tensor.matmul(PS[:, 0:TT],
                                     W1[:, c * 1024 + oc * 128: c * 1024 + (oc + 1) * 128],
                                     otb[c], start=(c == 0), stop=(c == NCH - 1))
                ctx.edge("tensor", "scalar")
                nc.scalar.activation(h1t[oc], PS[:, 0:TT], AF.Gelu,
                                     bias=B1[:, oc:oc + 1], scale=1.0)
                ctx.edge("scalar", "tensor")
            ctx.edge("scalar", "vector")
            for oc in range(4):
                for c in range(NCH):
                    nc.tensor.matmul(PS[:, 0:TT],
                                     W2[:, c * 512 + oc * 128: c * 512 + (oc + 1) * 128],
                                     h1t[c], start=(c == 0), stop=(c == NCH - 1))
                ctx.edge("tensor", "scalar")
                nc.scalar.activation(h2t[oc], PS[:, 0:TT], AF.Gelu,
                                     bias=B2[:, oc:oc + 1], scale=1.0)
                ctx.edge("scalar", "tensor")
            for oc in range(2):
                for c in range(4):
                    nc.tensor.matmul(PS[:, 0:TT],
                                     W3[:, c * 256 + oc * 128: c * 256 + (oc + 1) * 128],
                                     h2t[c], start=(c == 0), stop=(c == 3))
                ctx.edge("tensor", "scalar")
                nc.scalar.activation(h3t[oc], PS[:, 0:TT], AF.Gelu,
                                     bias=B3[:, oc:oc + 1], scale=1.0)
                ctx.edge("scalar", "tensor")
            for c in range(2):
                nc.tensor.matmul(PS[0:1, 2 * TT:3 * TT], W4[:, c:c + 1], h3t[c],
                                 start=(c == 0), stop=(c == 1))
            ctx.edge("tensor", "scalar")
            nc.scalar.activation(trow, PS[0:1, 2 * TT:3 * TT], AF.Sigmoid,
                                 bias=B4, scale=1.0)
            ctx.edge("scalar", "vector")
            nc.vector.tensor_scalar(trow2, trow, 0.3, gf7, ALU.mult, ALU.add)
            nc.vector.tensor_scalar(trow2, trow2, -0.5, None, ALU.add)
            nc.vector.tensor_scalar(trow2, trow2, ar2, 0.1, ALU.mult, ALU.add)
            nc.vector.tensor_copy(trowb, trow2)
            ctx.edge("vector", "tensor")
            nc.tensor.matmul(PS[:, 2 * TT:3 * TT], onesb2, trowb, start=True,
                             stop=True)
            ctx.edge("tensor", "vector")
            last = (cyc == 2)
            for c in range(NCH):
                nc.vector.tensor_tensor(twf, ot[SIGMA[c]], PS[:, 2 * TT:3 * TT],
                                        ALU.mult)
                nc.vector.scalar_tensor_tensor(twg, twf, SIGNS[c], ot[c],
                                               ALU.mult, ALU.add)
                ctx.edge("vector", "sync")
                ctx.dma(dst.ap()[c * 128:(c + 1) * 128, tsl], twg)
                if last:
                    ctx.edge("vector", "scalar")
                    nc.scalar.activation(twf, twg, AF.Silu)
                    ctx.edge("scalar", "sync")
                    ctx.dma(g_dram.ap()[c * 128:(c + 1) * 128, tsl], twf)
                    nc.scalar.wait_ge(ctx.dsem, ctx.dcnt)
                nc.sync.wait_ge(ctx.dsem, ctx.dcnt)
                nc.vector.wait_ge(ctx.dsem, ctx.dcnt)
    sb.release()


def mk_xn_ag(ctx, tag):
    nc = ctx.nc
    xn_in = nc.dram_tensor(f"xnin_{tag}", [D, TPC], BF16, kind="Internal")
    xn_out = nc.dram_tensor(f"xnout_{tag}", [4 * D, TPC], BF16, kind="Internal")
    return xn_in, xn_out


def build_test_l2():
    nc = bass.Bass("TRN2", target_bir_lowering=False, detect_race_conditions=False)
    ctx = Ctx(nc)
    sb = Sbuf(nc)
    x_in = ctx.inp("x_in", [TPC, D], F32)
    g_in = ctx.inp("g_in", [D, TPC], F32)
    x_out = nc.dram_tensor("x_out", [TPC, D], F32, kind="ExternalOutput").ap()

    emit_load_x(ctx, sb, x_in)
    xn_in, xn_out = mk_xn_ag(ctx, "l2")
    emit_layernorm(ctx, sb, xn_in.ap())
    ctx.coll("AllGather", ALU.bypass, GROUPS_BATCH, xn_in.ap(), xn_out.ap())
    ctx.cwait()
    emit_layer2_attn(ctx, sb, xn_out.ap(), g_in)
    emit_store_x(ctx, sb, x_out)
    return nc, ctx



def build_test_layer(li):
    nc = bass.Bass("TRN2", target_bir_lowering=False, detect_race_conditions=False)
    ctx = Ctx(nc)
    sb = Sbuf(nc)
    x_in = ctx.inp("x_in", [TPC, D], F32)
    g_in = ctx.inp("g_in", [D, TPC], F32) if li < 3 else None
    x_out = nc.dram_tensor("x_out", [TPC, D], F32, kind="ExternalOutput").ap()

    emit_load_x(ctx, sb, x_in)
    xn_in, xn_out = mk_xn_ag(ctx, f"l{li}")
    stats3 = nc.dram_tensor("stats3", [2, TPC], F32, kind="Internal") if li == 3 else None
    emit_layernorm(ctx, sb, xn_in.ap(),
                   stats_dram=(stats3.ap() if li == 3 else None))
    ctx.coll("AllGather", ALU.bypass, GROUPS_BATCH, xn_in.ap(), xn_out.ap())
    ctx.cwait()
    if li == 0:
        emit_std_attn_layer(ctx, sb, xn_out.ap(), g_in, 0, False, False, "l0")
    elif li == 1:
        emit_layer1_attn(ctx, sb, xn_out.ap(), xn_in.ap(), g_in)
    elif li == 2:
        emit_std_attn_layer(ctx, sb, xn_out.ap(), g_in, 2, True, True, "l2")
    elif li == 3:
        g_dram = nc.dram_tensor("g3", [D, TPC], F32, kind="Internal")
        emit_l3_ffn(ctx, sb, stats3, g_dram)
        if DEBUG_ATTN:
            gdump = nc.dram_tensor("g_dump", [D, TPC], F32, kind="ExternalOutput").ap()
            sb.mark()
            st_ = sb.af(TPC)
            for c_ in range(NCH):
                ctx.dma(st_, g_dram.ap()[c_ * 128:(c_ + 1) * 128, :])
                ctx.dwait()
                ctx.dma(gdump[c_ * 128:(c_ + 1) * 128, :], st_)
                ctx.dwait()
            sb.release()
        emit_l3_attn(ctx, sb, xn_out.ap(), xn_in.ap(), g_dram)
    emit_store_x(ctx, sb, x_out)
    return nc, ctx



def build_test_ffn(li):
    nc = bass.Bass("TRN2", target_bir_lowering=False, detect_race_conditions=False)
    ctx = Ctx(nc)
    sb = Sbuf(nc)
    x_in = ctx.inp("x_in", [TPC, D], F32)
    g_out = nc.dram_tensor("g_out", [D, TPC], F32, kind="ExternalOutput").ap()
    g_dram = nc.dram_tensor("g_dram", [D, TPC], F32, kind="Internal")
    stats_dram = nc.dram_tensor("stats", [2, TPC], F32, kind="Internal")
    xn_in, xn_out = mk_xn_ag(ctx, f"f{li}")

    emit_load_x(ctx, sb, x_in)
    emit_layernorm(ctx, sb, xn_in.ap(), stats_dram=stats_dram.ap(),
                   track_absmax=(li == 1))
    if li == 0:
        ctx.coll("AllGather", ALU.bypass, GROUPS_BATCH, xn_in.ap(), xn_out.ap())
        ctx.cwait()
        emit_l0_ffn(ctx, sb, xn_out.ap(), xn_in.ap(), g_dram.ap())
    elif li == 2:
        emit_l2_ffn(ctx, sb, stats_dram, g_dram)
    elif li == 1:
        emit_l1_ffn(ctx, sb, stats_dram, g_dram, sb.R[0:1, 2047:2048])
    sb.mark()
    st = sb.af(TPC)
    for c in range(NCH):
        ctx.dma(st, g_dram.ap()[c * 128:(c + 1) * 128, :])
        ctx.dwait()
        ctx.dma(g_out[c * 128:(c + 1) * 128, :], st)
        ctx.dwait()
    sb.release()
    return nc, ctx


def build_test_ln():
    nc = bass.Bass("TRN2", target_bir_lowering=False, detect_race_conditions=False)
    ctx = Ctx(nc)
    sb = Sbuf(nc)
    x_in = ctx.inp("x_in", [TPC, D], F32)
    xn_out = nc.dram_tensor("xn_out", [D, TPC], BF16, kind="ExternalOutput").ap()
    amax_out = nc.dram_tensor("amax_out", [1, 1], F32, kind="ExternalOutput").ap()
    xn_i = nc.dram_tensor("xn_i", [D, TPC], BF16, kind="Internal")

    emit_load_x(ctx, sb, x_in)
    emit_layernorm(ctx, sb, xn_i.ap(), track_absmax=True)
    sb.mark()
    stage = sb.abf(1024)
    for c in range(NCH):
        for t in range(NTT):
            ctx.dma(stage[:, 0:TT], xn_i.ap()[c * 128:(c + 1) * 128, t * TT:(t + 1) * TT])
            ctx.dwait()
            ctx.dma(xn_out[c * 128:(c + 1) * 128, t * TT:(t + 1) * TT], stage[:, 0:TT])
            ctx.dwait()
    ctx.dma(amax_out, sb.R[0:1, 2047:2048])
    ctx.dwait()
    sb.release()
    return nc, ctx



def _f32(a):
    return np.ascontiguousarray(np.asarray(a, np.float32))


def l0a_in_map(inp, c):
    h0 = 4 * (c % 4)
    WK, WQ, WV = inp["L0_attn_W_K"], inp["L0_attn_W_Q"], inp["L0_attn_W_V"]
    cols = []
    for W in (WQ, WK, WV):
        W = _f32(W)
        cols.append(np.concatenate([W[h0 + h].T for h in range(4)], axis=1))
    return {"L0w_qkv_m": np.ascontiguousarray(np.concatenate(cols, axis=1)),
            "L0w_o": np.ascontiguousarray(_f32(inp["L0_attn_W_O"]).T)}


def l1a_in_map(inp, c):
    h0 = 4 * (c % 4)
    wq, wka, wkb = _f32(inp["L1_attn_wq"]), _f32(inp["L1_attn_wk_a"]), \
        _f32(inp["L1_attn_wk_b"])
    wva, wvb, wo = _f32(inp["L1_attn_wv_a"]), _f32(inp["L1_attn_wv_b"]), \
        _f32(inp["L1_attn_wo"])
    im = {}
    im["L1w_q_m"] = np.ascontiguousarray(
        np.concatenate([wq[(h0 + h) * 64:(h0 + h + 1) * 64, :]
                        for h in range(4)], axis=1))
    im["L1w_kb_m"] = np.ascontiguousarray(
        np.concatenate([wkb[(h0 + h) * 64:(h0 + h + 1) * 64, :]
                        for h in range(4)], axis=1))
    im["L1w_vbT_m"] = np.ascontiguousarray(
        np.concatenate([wvb[(h0 + h) * 64:(h0 + h + 1) * 64, :].T
                        for h in range(4)], axis=1))
    im["L1w_kaT"] = np.ascontiguousarray(wka.T)
    im["L1w_vaT"] = np.ascontiguousarray(wva.T)
    im["L1w_oT"] = np.ascontiguousarray(wo.T)
    RTs = []
    for h in range(4):
        hh = h0 + h
        R = np.eye(64, dtype=np.float64)
        for j in range(8):
            a = hh * (1.0 / 10000.0 ** (2.0 * j / 64.0))
            ca, sa = np.cos(a), np.sin(a)
            R[2 * j, 2 * j] = ca
            R[2 * j, 2 * j + 1] = -sa
            R[2 * j + 1, 2 * j] = sa
            R[2 * j + 1, 2 * j + 1] = ca
        RTs.append(R.T.astype(np.float32))
    im["L1_RT_m"] = np.ascontiguousarray(np.concatenate(RTs, axis=1))
    return im


def l2a_in_map(inp, c):
    h0 = 4 * (c % 4)
    qkv_w, qkv_b = _f32(inp["L2_attn_qkv_w"]), _f32(inp["L2_attn_qkv_b"])
    wcols = np.concatenate([qkv_w[:, p * 1024 + h0 * 64: p * 1024 + (h0 + 4) * 64]
                            for p in range(3)], axis=1)
    bcols = np.stack([qkv_b[p * 1024 + (h0 + h) * 64: p * 1024 + (h0 + h + 1) * 64]
                      for p in range(3) for h in range(4)], axis=1)
    return {"L2w_qkv_m": np.ascontiguousarray(wcols),
            "L2b_qkv_m": np.ascontiguousarray(bcols),
            "L2w_o": _f32(inp["L2_attn_out_w"]),
            "L2b_o": np.ascontiguousarray(_f32(inp["L2_attn_out_b"]).reshape(8, 128).T)}


def l1f_in_map(inp, c):
    wg, wu, wd = _f32(inp["L1_ffn_w_gate"]), _f32(inp["L1_ffn_w_up"]), \
        _f32(inp["L1_ffn_w_down"])
    return {"L1f_wg_m": np.ascontiguousarray(wg[c * 384:(c + 1) * 384]),
            "L1f_wu_m": np.ascontiguousarray(wu[c * 384:(c + 1) * 384]),
            "L1f_wd_m": np.ascontiguousarray(wd[c * 128:(c + 1) * 128])}


def l2f_in_map(inp, c):
    wg, wu, wd = _f32(inp["L2_ffn_w_gate"]), _f32(inp["L2_ffn_w_up"]), \
        _f32(inp["L2_ffn_w_down"])
    wgp = np.zeros((3072, 1024), np.float32)
    wgp[0:2816] = wg
    wup = np.zeros((3072, 1024), np.float32)
    wup[0:2816] = wu
    return {"L2f_wg_m": np.ascontiguousarray(wgp[c * 384:(c + 1) * 384]),
            "L2f_wu_m": np.ascontiguousarray(wup[c * 384:(c + 1) * 384]),
            "L2f_wd_m": np.ascontiguousarray(wd[c * 128:(c + 1) * 128])}


def l3_in_map(inp, c):
    q = c % 4
    Wqkv, bq = _f32(inp["L3_attn_W_qkv"]), _f32(inp["L3_attn_b_qkv"])
    Wlat, blat = _f32(inp["L3_attn_W_lat"]), _f32(inp["L3_attn_b_lat"])
    Wproj, bproj = _f32(inp["L3_attn_W_proj"]), _f32(inp["L3_attn_b_proj"])
    im = {}
    bb = np.zeros((128, 24), np.float32)
    for sec in range(3):
        for hc in range(8):
            bb[:, sec * 8 + hc] = bq[sec * 1024 + hc * 128: sec * 1024 + (hc + 1) * 128]
    im["L3w_qkvT"] = np.ascontiguousarray(Wqkv.T)
    im["L3b_qkv"] = bb
    BD = np.zeros((128, 128), np.float32)
    BD[0:64, 0:64] = Wlat.T
    BD[64:128, 64:128] = Wlat.T
    im["L3w_latBD"] = BD
    im["L3b_lat2"] = np.ascontiguousarray(np.concatenate([blat, blat]).reshape(128, 1))
    im["L3w_projT"] = np.ascontiguousarray(Wproj.T)
    im["L3b_proj"] = np.ascontiguousarray(bproj.reshape(8, 128).T)
    emf = np.zeros((128, 1), np.float32)
    eml = np.zeros((128, 1), np.float32)
    if q == 0:
        emf[0:3] = 1.0
    if q == 3:
        eml[125:128] = 1.0
    im["L3_em_first"] = emf
    im["L3_em_last"] = eml
    im["L3f_cw1"] = _f32(inp["L3_ffn_cn_w1"])
    im["L3f_cb1"] = np.ascontiguousarray(_f32(inp["L3_ffn_cn_b1"]).reshape(8, 128).T)
    im["L3f_cw2"] = _f32(inp["L3_ffn_cn_w2"])
    im["L3f_cb2"] = np.ascontiguousarray(_f32(inp["L3_ffn_cn_b2"]).reshape(4, 128).T)
    im["L3f_cw3"] = _f32(inp["L3_ffn_cn_w3"])
    im["L3f_cb3"] = np.ascontiguousarray(_f32(inp["L3_ffn_cn_b3"]).reshape(2, 128).T)
    im["L3f_cw4"] = np.ascontiguousarray(_f32(inp["L3_ffn_cn_w4"]).reshape(2, 128).T)
    im["L3f_cb4"] = np.ascontiguousarray(_f32(inp["L3_ffn_cn_b4"]).reshape(1, 1))
    im["L3f_gw1"] = _f32(inp["L3_ffn_gc_w1"])
    im["L3f_gb1"] = np.ascontiguousarray(_f32(inp["L3_ffn_gc_b1"]).reshape(4, 128).T)
    im["L3f_gw2"] = _f32(inp["L3_ffn_gc_w2"])
    im["L3f_gb2"] = np.ascontiguousarray(_f32(inp["L3_ffn_gc_b2"]).reshape(2, 128).T)
    im["L3f_gw3"] = np.ascontiguousarray(_f32(inp["L3_ffn_gc_w3"]).reshape(2, 128).T)
    im["L3f_gb3"] = np.ascontiguousarray(_f32(inp["L3_ffn_gc_b3"]).reshape(1, 1))
    im["L3f_ar"] = np.ascontiguousarray(_f32(inp["L3_ffn_adaptive_range"]).reshape(1, 1))
    return im



_PROGS = {}


def _prog(name, builder):
    if name not in _PROGS:
        _PROGS[name] = builder()
    return _PROGS[name]


def kernel(**inputs):
    x = np.ascontiguousarray(np.asarray(inputs["L0_attn_x"], np.float32))

    def xsl(c):
        return np.ascontiguousarray(x[c // 4, (c % 4) * TPC:(c % 4 + 1) * TPC, :])

    def launch(prog, maps):
        nc, ctx = prog
        return run_bass_kernel_spmd(nc, maps, core_ids=list(range(8))).results

    for li in range(4):
        if li == 3:
            res = launch(_prog("l3", lambda: build_test_layer(3)),
                         [{"x_in": xsl(c), **l3_in_map(inputs, c)}
                          for c in range(NCORE)])
        else:
            fmap = (l0f_in_map, l1f_in_map, l2f_in_map)[li]
            amap = (l0a_in_map, l1a_in_map, l2a_in_map)[li]
            resf = launch(_prog(f"f{li}", lambda li=li: build_test_ffn(li)),
                          [{"x_in": xsl(c), **fmap(inputs, c)}
                           for c in range(NCORE)])
            res = launch(_prog(f"a{li}", lambda li=li: build_test_layer(li)),
                         [{"x_in": xsl(c), "g_in": resf[c]["g_out"],
                           **amap(inputs, c)} for c in range(NCORE)])
        xo = np.empty_like(x)
        for c in range(NCORE):
            xo[c // 4, (c % 4) * TPC:(c % 4 + 1) * TPC, :] = res[c]["x_out"]
        x = xo
    return x


if __name__ == "__main__":
    pass


# version 03 (final)
# speedup vs baseline: 1.0810x; 1.0810x over previous
import numpy as np
import ml_dtypes
import concourse.bass as bass
import concourse.mybir as mybir
from concourse.bass_utils import run_bass_kernel_spmd

F32 = mybir.dt.float32
BF16 = mybir.dt.bfloat16
U8 = mybir.dt.uint8
AF = mybir.ActivationFunctionType
ALU = mybir.AluOpType
AX = mybir.AxisListType.X

B, S, D = 2, 8192, 1024
NCORE = 8
TPC = 2048
NCH = 8
NTT = 4
TT = 512
MAGIC = 12582912.0
GROUPS_BATCH = [[0, 1, 2, 3], [4, 5, 6, 7]]
GROUP_ALL = [list(range(8))]
BF = ml_dtypes.bfloat16

class Ctx:

    def __init__(self, nc):
        self.nc = nc
        self.dsem = nc.alloc_semaphore("dsem")
        self.dcnt = 0
        self.gsem = nc.alloc_semaphore("gsem")
        self.gcnt = 0
        self.csem = nc.alloc_semaphore("csem")
        self.ccnt = 0
        self.pe_sem = nc.alloc_semaphore("pe_sem")
        self.pe_cnt = 0
        self.act_sem = nc.alloc_semaphore("act_sem")
        self.act_cnt = 0
        self.esem = {}
        self.ecnt = {}
        for e in ("tensor", "vector", "scalar", "sync", "gpsimd"):
            self.esem[e] = nc.alloc_semaphore(f"esem_{e}")
            self.ecnt[e] = 0
        self.PS = nc.alloc_psum_tensor("PS", [128, 4096], F32)
        self.consts = {}
        self.input_specs = {}
        self._uid = 0

    def uid(self):
        self._uid += 1
        return self._uid

    def dma(self, out, in_, **kw):
        self.nc.sync.dma_start(out=out, in_=in_, **kw).then_inc(self.dsem, 16)
        self.dcnt += 16

    def dmag(self, out, in_, **kw):
        self.nc.gpsimd.dma_start(out=out, in_=in_, **kw).then_inc(self.gsem, 16)
        self.gcnt += 16

    def dma_tr(self, out, in_):
        self.nc.sync.dma_start_transpose(out, in_).then_inc(self.dsem, 16)
        self.dcnt += 16

    def dwait(self):
        self.nc.sync.wait_ge(self.dsem, self.dcnt)
        self.nc.gpsimd.wait_ge(self.gsem, self.gcnt)
        self.nc.all_engine_barrier()

    def bar(self):
        self.nc.all_engine_barrier()

    def edge(self, prod, *cons):
        getattr(self.nc, prod).drain().then_inc(self.esem[prod], 1)
        self.ecnt[prod] += 1
        for cn in cons:
            getattr(self.nc, cn).wait_ge(self.esem[prod], self.ecnt[prod])

    def coll(self, kind, op, groups, in_ap, out_ap):
        self.nc.gpsimd.collective_compute(
            kind, op, replica_groups=groups, ins=[in_ap], outs=[out_ap]
        ).then_inc(self.csem)
        self.ccnt += 1

    def cwait(self):
        self.nc.gpsimd.wait_ge(self.csem, self.ccnt)
        self.nc.all_engine_barrier()

    def qsnap(self, which):
        if not hasattr(self, "_qsnaps"):
            self._qsnaps = {}
        if which not in self._qsnaps:
            cid = self.nc.partition_id()
            d = {"own": 0, "m1": 3, "p1": 1}[which]
            self._qsnaps[which] = self.nc.snap((cid + d) % 4)
        return self._qsnaps[which]

    def inp(self, name, shape, dtype=F32):
        h = self.nc.dram_tensor(name, list(shape), dtype, kind="ExternalInput")
        self.input_specs[name] = (tuple(shape), dtype)
        return h.ap()

    def const(self, name, arr):
        if name in self.consts:
            return self.consts[name]
        h = self.nc.inline_tensor(np.ascontiguousarray(arr), name=name)
        self.consts[name] = h.ap()
        return self.consts[name]



class Sbuf:
    def __init__(self, nc):
        self.nc = nc
        self.x = [nc.alloc_sbuf_tensor(f"x{c}", [128, TPC], F32) for c in range(NCH)]
        self.A = nc.alloc_sbuf_tensor("arenaA", [128, 30208], F32)
        self.R = nc.alloc_sbuf_tensor("rows", [128, 2048], F32)
        self._amark = []
        self._aoff = 0

    def a(self, off, n):
        return self.A[:, off:off + n]

    def ab(self, off_f32, n_bf16):
        return self.A.ap().bitcast(BF16)[:, 2 * off_f32: 2 * off_f32 + n_bf16]

    def mark(self):
        self._amark.append(self._aoff)

    def release(self):
        self._aoff = self._amark.pop()

    def falloc(self, n_f32):
        off = self._aoff
        n = (n_f32 + 7) & ~7
        assert off + n <= 29952, f"arena A overflow: {off}+{n}"
        self._aoff = off + n
        return off

    def af(self, n_f32):
        return self.a(self.falloc(n_f32), n_f32)

    def abf(self, n_bf16):
        nf = (n_bf16 + 1) // 2
        return self.ab(self.falloc(nf), n_bf16)



def emit_load_x(ctx, sb, x_in):
    nc = ctx.nc
    sb.mark()
    idf = sb.af(128)
    ctx.dma(idf, ctx.const("ident128f", np.eye(128, dtype=np.float32)))
    stage = sb.af(1024)
    ctx.dwait()
    for t in range(TPC // 128):
        ctx.dma(stage, x_in[t * 128:(t + 1) * 128, :])
        ctx.dwait()
        for c in range(NCH):
            nc.tensor.transpose(ctx.PS[:, 0:128], stage[:, c * 128:(c + 1) * 128], idf)
            ctx.bar()
            nc.vector.tensor_copy(sb.x[c][:, t * 128:(t + 1) * 128], ctx.PS[:, 0:128])
            ctx.bar()
    sb.release()


def emit_store_x(ctx, sb, x_out):
    nc = ctx.nc
    sb.mark()
    idf = sb.af(128)
    ctx.dma(idf, ctx.const("ident128f", np.eye(128, dtype=np.float32)))
    stage = sb.af(1024)
    ctx.dwait()
    for t in range(TPC // 128):
        for c in range(NCH):
            nc.tensor.transpose(ctx.PS[:, 0:128], sb.x[c][:, t * 128:(t + 1) * 128], idf)
            ctx.bar()
            nc.vector.tensor_copy(stage[:, c * 128:(c + 1) * 128], ctx.PS[:, 0:128])
            ctx.bar()
        ctx.dma(x_out[t * 128:(t + 1) * 128, :], stage)
        ctx.dwait()
    sb.release()


def emit_layernorm(ctx, sb, xn_bf_dram, stats_dram=None, track_absmax=False):
    nc = ctx.nc
    PS = ctx.PS
    sb.mark()
    onesv = sb.af(8)[:, 0:1]
    ctx.dma(onesv, ctx.const("ones128f", np.ones((128, 1), dtype=np.float32)))
    sq = sb.af(TT)
    xnf = sb.af(TT)
    am = sb.af(8)[:, 0:1]
    amr = sb.af(128)
    onesb = sb.abf(128)[0:1, :]
    rstdb = sb.abf(TT)[0:1, :]
    browb = sb.abf(TT)[0:1, :]
    xnb = sb.abf(TT)
    ctx.dwait()
    nc.vector.memset(onesb, 1.0)
    if track_absmax:
        nc.vector.memset(am, 0.0)
    for t in range(NTT):
        tsl = slice(t * TT, (t + 1) * TT)
        for c in range(NCH):
            nc.tensor.matmul(PS[0:1, 0:TT], onesv, sb.x[c][:, tsl],
                             start=(c == 0), stop=(c == NCH - 1))
        for c in range(NCH):
            nc.scalar.activation(sq, sb.x[c][:, tsl], AF.Square)
            ctx.bar()
            nc.tensor.matmul(PS[0:1, TT:2 * TT], onesv, sq,
                             start=(c == 0), stop=(c == NCH - 1))
            ctx.bar()
        mean = sb.R[0:1, 0:TT]
        var = sb.R[0:1, TT:2 * TT]
        rstd = sb.R[0:1, 2 * TT:3 * TT]
        brow = sb.R[0:1, 3 * TT:4 * TT]
        nc.vector.tensor_scalar(mean, PS[0:1, 0:TT], 1.0 / D, None, ALU.mult)
        nc.vector.tensor_scalar(var, PS[0:1, TT:2 * TT], 1.0 / D, None, ALU.mult)
        nc.vector.tensor_tensor(rstd, mean, mean, ALU.mult)
        nc.vector.tensor_sub(var, var, rstd)
        nc.vector.tensor_scalar(var, var, 1e-5, None, ALU.add)
        ctx.bar()
        nc.scalar.activation(rstd, var, AF.Sqrt)
        ctx.bar()
        nc.vector.reciprocal(rstd, rstd)
        nc.vector.tensor_tensor(brow, mean, rstd, ALU.mult)
        nc.vector.tensor_scalar(brow, brow, -1.0, None, ALU.mult)
        ctx.bar()
        nc.vector.tensor_copy(rstdb, rstd)
        nc.vector.tensor_copy(browb, brow)
        ctx.bar()
        nc.tensor.matmul(PS[:, 2 * TT:3 * TT], onesb, rstdb, start=True, stop=True)
        nc.tensor.matmul(PS[:, 3 * TT:4 * TT], onesb, browb, start=True, stop=True)
        ctx.bar()
        if stats_dram is not None:
            ctx.dma(stats_dram[0:1, tsl], rstd)
            ctx.dma(stats_dram[1:2, tsl], brow)
        for c in range(NCH):
            nc.vector.tensor_tensor(xnf, sb.x[c][:, tsl], PS[:, 2 * TT:3 * TT], ALU.mult)
            nc.vector.tensor_add(xnf, xnf, PS[:, 3 * TT:4 * TT])
            nc.vector.tensor_copy(xnb, xnf)
            if track_absmax:
                nc.vector.reduce_max(sq[:, 0:1], xnf, axis=AX, apply_absolute_value=True)
                nc.vector.tensor_max(am, am, sq[:, 0:1])
            ctx.bar()
            ctx.dmag(xn_bf_dram[c * 128:(c + 1) * 128, tsl], xnb)
            nc.gpsimd.wait_ge(ctx.gsem, ctx.gcnt)
            ctx.bar()
    if track_absmax:
        identb = ctx.get_identb(sb)
        amb = sb.abf(128)
        nc.vector.tensor_copy(amb[:, 0:1], am)
        ctx.bar()
        nc.tensor.transpose(ctx.PS.ap().bitcast(BF16)[0:1, 0:128], amb[:, 0:1], identb)
        ctx.bar()
        nc.vector.tensor_copy(amr[0:1, 0:128], ctx.PS.ap().bitcast(BF16)[0:1, 0:128])
        ctx.bar()
        nc.vector.reduce_max(sb.R[0:1, 2047:2048], amr[0:1, 0:128], axis=AX)
        ctx.bar()
    sb.release()


def _ctx_get_identb(ctx, sb):
    if not hasattr(ctx, "_identb"):
        ctx._identb = sb.ab(30144, 128)
        ctx.dmag(ctx._identb, ctx.const("ident128b", np.eye(128, dtype=BF)))
        ctx._identf_sb = sb.a(29952, 128)
        ctx.dma(ctx._identf_sb, ctx.const("ident128f", np.eye(128, dtype=np.float32)))
        ctx.dwait()
    return ctx._identb

Ctx.get_identb = _ctx_get_identb




DEBUG_ATTN = False
L2F_PHASES = 4

ATT_NKT = S // 128
ATT_NQT = S // TT


def tr_copy(ctx, items, bf16=True, slots=4):
    nc = ctx.nc
    PSb = ctx.PS.ap().bitcast(BF16)
    PSf = ctx.PS
    ident = ctx._identb if bf16 else ctx._identf_sb
    n = len(items)
    i = 0
    while i < n:
        grp = items[i:i + slots]
        for j, (src, dst) in enumerate(grp):
            pp, ff = src.shape[0], src.shape[1]
            if bf16:
                out = PSb[0:ff, 7 * 1024 + j * 128: 7 * 1024 + j * 128 + pp]
                nc.tensor.transpose(out, src, ident[0:pp, 0:pp])
            else:
                out = PSf[0:ff, 7 * 512 + j * 128: 7 * 512 + j * 128 + pp]
                nc.tensor.transpose(out, src, ident[0:pp, 0:pp])
        ctx.edge("tensor", "vector")
        for j, (src, dst) in enumerate(grp):
            pp, ff = src.shape[0], src.shape[1]
            if bf16:
                nc.vector.tensor_copy(dst, PSb[0:ff, 7 * 1024 + j * 128: 7 * 1024 + j * 128 + pp])
            else:
                nc.vector.tensor_copy(dst, PSf[0:ff, 7 * 512 + j * 128: 7 * 512 + j * 128 + pp])
        ctx.edge("vector", "tensor")
        i += slots
    ctx.edge("vector", "scalar", "sync", "gpsimd")


def attn_evac_tile(ctx, sb, cfg, bufs, p, t):
    nc = ctx.nc
    PS = ctx.PS
    PSb = ctx.PS.ap().bitcast(BF16)
    kbuf, qbuf, vbuf = bufs["kbuf"], bufs["qbuf"], bufs["vbuf"]
    sqt, sq2, ltmp, rmin, nh64 = (bufs["sqt"], bufs["sq2"], bufs["ltmp"],
                                  bufs["rmin"], bufs["nh64"])
    identb = bufs["identb"]
    ss = cfg["sqrt_scale"]
    tsl = slice(t * TT, (t + 1) * TT)
    bq, bk, bv = cfg["bias"](0, p), cfg["bias"](1, p), cfg["bias"](2, p)
    if bk is not None:
        nc.vector.tensor_scalar(kbuf[0:64, tsl], PS[0:64, 0:TT], bk, ss,
                                ALU.add, ALU.mult)
    else:
        nc.vector.tensor_scalar(kbuf[0:64, tsl], PS[0:64, 0:TT], ss, None, ALU.mult)
    if bv is not None:
        nc.vector.tensor_scalar(sqt[0:64, :], PS[0:64, TT:2 * TT], bv, None, ALU.add)
    else:
        nc.vector.tensor_copy(sqt[0:64, :], PS[0:64, TT:2 * TT])
    if bq is not None:
        nc.vector.tensor_scalar(qbuf[0:64, tsl], PS[0:64, 2 * TT:3 * TT], bq, ss,
                                ALU.add, ALU.mult)
    else:
        nc.vector.tensor_scalar(qbuf[0:64, tsl], PS[0:64, 2 * TT:3 * TT], ss, None,
                                ALU.mult)
    ctx.bar()
    nc.scalar.activation(sq2[0:64, 0:TT], kbuf[0:64, tsl], AF.Square)
    nc.scalar.activation(sq2[0:64, TT:2 * TT], qbuf[0:64, tsl], AF.Square)
    ctx.bar()
    nc.tensor.matmul(PS[0:1, 3 * TT:4 * TT], nh64, sq2[0:64, 0:TT],
                     start=True, stop=True)
    nc.tensor.matmul(PS[32:33, 3 * TT:4 * TT], nh64, sq2[0:64, TT:2 * TT],
                     start=True, stop=True)
    for j in range(4):
        nc.tensor.transpose(PSb[:, 4096 + j * 64:4096 + (j + 1) * 64],
                            sqt[0:64, j * 128:(j + 1) * 128], identb[0:64, 0:64])
    ctx.bar()
    nc.vector.tensor_reduce(ltmp[0:1, 0:1], PS[0:1, 3 * TT:4 * TT], AX, ALU.min)
    nc.vector.tensor_tensor(rmin[0:1, 0:1], rmin[0:1, 0:1], ltmp[0:1, 0:1], ALU.min)
    for j in range(4):
        kt = t * 4 + j
        nc.vector.tensor_copy(vbuf[:, kt * 65: kt * 65 + 64],
                              PSb[:, 4096 + j * 64:4096 + (j + 1) * 64])
    nc.scalar.activation(qbuf[64:65, tsl], PS[32:33, 3 * TT:4 * TT], AF.Copy)
    ctx.bar()


def emit_attn_std(ctx, sb, cfg, xnag, combine_fn):
    nc = ctx.nc
    PS = ctx.PS
    PSb = ctx.PS.ap().bitcast(BF16)
    identb = ctx.get_identb(sb)
    sb.mark()
    kbuf = sb.abf(8192)
    qbuf = sb.abf(8192)
    vbuf = sb.abf(ATT_NKT * 65 + 63)
    pbuf = [sb.abf(1536), sb.abf(1536)]
    zstage = sb.abf(8192)
    xnt = sb.abf(4096)
    sqt = sb.abf(512)
    sq2 = sb.abf(1024)
    ltmp = sb.af(TT)
    rmin = sb.af(8)
    nh64 = sb.abf(64)[0:64, 0:1]
    ctx.dmag(nh64, ctx.const("neghalf64", np.full((64, 1), -0.5, dtype=BF)))
    patt = None
    if cfg.get("patterns") is not None:
        patt = sb.abf(2048)
        ctx.dmag(patt, cfg["patterns"])
    ctx.dwait()

    causal = cfg["causal"]
    ss = cfg["sqrt_scale"]

    bufs = dict(kbuf=kbuf, qbuf=qbuf, vbuf=vbuf, sqt=sqt, sq2=sq2, ltmp=ltmp,
                rmin=rmin, nh64=nh64, xnt=xnt, identb=identb)

    def std_gen_pair(p):
        nc.vector.memset(vbuf, 1.0)
        nc.vector.memset(rmin[0:1, 0:1], 0.0)
        nc.vector.memset(kbuf[64:65, :], 1.0)
        ctx.bar()
        for t in range(ATT_NQT):
            r, u = t // 4, t % 4
            tsl = slice(t * TT, (t + 1) * TT)
            ctx.dma(xnt, xnag[r * 1024:(r + 1) * 1024,
                              u * TT:(u + 1) * TT].rearrange("(c p) t -> p c t", p=128))
            ctx.dwait()
            for proj in range(3):
                pso = [2 * TT, 0, TT][proj]
                for c in range(NCH):
                    nc.tensor.matmul(PS[0:64, pso:pso + TT],
                                     cfg["lhsT"](proj, p, c),
                                     xnt[:, c * TT:(c + 1) * TT],
                                     start=(c == 0), stop=(c == NCH - 1))
            ctx.bar()
            attn_evac_tile(ctx, sb, cfg, bufs, p, t)
        nc.vector.tensor_scalar(qbuf[64:65, :], qbuf[64:65, :], rmin[0:1, 0:1],
                                None, ALU.add)
        ctx.bar()

    for p in range(4):
        if cfg.get("gen_pair") is None:
            std_gen_pair(p)
        else:
            cfg["gen_pair"](p, bufs)
        if DEBUG_ATTN and p == 0:
            dk = nc.dram_tensor("dbg_k", [65, S], BF16, kind="ExternalOutput").ap()
            dq = nc.dram_tensor("dbg_q", [65, S], BF16, kind="ExternalOutput").ap()
            dv = nc.dram_tensor("dbg_v", [128, ATT_NKT * 65], BF16, kind="ExternalOutput").ap()
            ctx.dma(dk, kbuf[0:65, :])
            ctx.dma(dq, qbuf[0:65, :])
            ctx.dma(dv, vbuf[:, 0:ATT_NKT * 65])
            ctx.dwait()
        half_free_at = [0, 0]
        pbuf_free_at = [0, 0]
        for qt in range(ATT_NQT):
            q0 = qt * TT
            nkt = (q0 // 128 + 4) if causal else ATT_NKT
            units = [list(range(s0, min(s0 + 3, nkt))) for s0 in range(0, nkt, 3)]
            nz = 0
            for ui, unit in enumerate(units):
                h = ui % 2
                off = h * 1536
                w = len(unit) * TT
                nc.tensor.wait_ge(ctx.act_sem, half_free_at[h])
                for j, kt in enumerate(unit):
                    k0 = kt * 128
                    diag = causal and (k0 + 127 > q0)
                    mm = nc.tensor.matmul(PS[:, off + j * TT: off + (j + 1) * TT],
                                          kbuf[0:65, kt * 128:(kt + 1) * 128],
                                          qbuf[0:65, q0:q0 + TT],
                                          start=True, stop=not diag)
                    if diag:
                        rr = (k0 - q0) // 128
                        mm = nc.tensor.matmul(PS[:, off + j * TT: off + (j + 1) * TT],
                                              identb, patt[:, rr * TT:(rr + 1) * TT],
                                              start=False, stop=True)
                mm.then_inc(ctx.pe_sem, 1)
                ctx.pe_cnt += 1
                smm_pe = ctx.pe_cnt
                nc.scalar.wait_ge(ctx.pe_sem, max(smm_pe, pbuf_free_at[h]))
                nc.scalar.activation(pbuf[h][:, 0:w], PS[:, off:off + w],
                                     AF.Exp).then_inc(ctx.act_sem, 1)
                ctx.act_cnt += 1
                half_free_at[h] = ctx.act_cnt
                nc.tensor.wait_ge(ctx.act_sem, ctx.act_cnt)
                last_unit = (ui == len(units) - 1)
                for j, kt in enumerate(unit):
                    mm = nc.tensor.matmul(PS[0:65, 3072:3072 + TT],
                                          vbuf[:, kt * 65:(kt + 1) * 65],
                                          pbuf[h][:, j * TT:(j + 1) * TT],
                                          start=(nz == 0),
                                          stop=(last_unit and j == len(unit) - 1))
                    nz += 1
                mm.then_inc(ctx.pe_sem, 1)
                ctx.pe_cnt += 1
                pbuf_free_at[h] = ctx.pe_cnt
            nc.scalar.wait_ge(ctx.pe_sem, ctx.pe_cnt)
            nc.scalar.wait_ge(ctx.dsem, ctx.dcnt)
            nc.scalar.activation(zstage[0:64, q0:q0 + TT], PS[0:64, 3072:3072 + TT],
                                 AF.Copy)
            nc.scalar.activation(ltmp[0:1, :], PS[64:65, 3072:3072 + TT],
                                 AF.Copy).then_inc(ctx.act_sem, 1)
            ctx.act_cnt += 1
            nc.tensor.wait_ge(ctx.act_sem, ctx.act_cnt)
            nc.sync.wait_ge(ctx.act_sem, ctx.act_cnt)
            ctx.dma(cfg["l_in"].ap()[p:p + 1, q0:q0 + TT], ltmp[0:1, :])
        ctx.bar()
        ctx.dmag(cfg["agz_in"].ap()[p * 64:(p + 1) * 64, :], zstage[0:64, :])
        ctx.dwait()
        if DEBUG_ATTN and p == 0:
            dz = ctx.nc.dram_tensor("dbg_z", [64, S], BF16, kind="ExternalOutput").ap()
            dl = ctx.nc.dram_tensor("dbg_l", [1, S], F32, kind="ExternalOutput").ap()
            ctx.dma(dz, zstage[0:64, :])
            ctx.dwait()
            ctx.dmag(dl, cfg["l_in"].ap()[0:1, :])
            ctx.dwait()
    sb.release()

    ctx.coll("AllGather", ALU.bypass, GROUPS_BATCH,
             cfg["agz_in"].ap(), cfg["agz_out"].ap())
    ctx.coll("AllGather", ALU.bypass, GROUPS_BATCH,
             cfg["l_in"].ap(), cfg["l_out"].ap())
    ctx.cwait()
    emit_attn_out(ctx, sb, cfg, combine_fn)


def emit_attn_out(ctx, sb, cfg, combine_fn):
    nc = ctx.nc
    PS = ctx.PS
    sb.mark()
    zin = sb.abf(8 * TPC)
    lrec = sb.af(TPC)
    Efull = sb.af(1024)[0:16, :]
    ctx.dma(Efull, ctx.const("E16f", cfg["E16"]))
    qreg = ctx.qsnap("own")
    ctx.dma(zin.rearrange("p (r pph t) -> p r pph t", r=4, pph=2),
            cfg["agz_out"].ap()[0:1024, bass.ts(qreg, TPC)]
            .rearrange("(r pph pp2 d) t -> (pp2 d) r pph t", r=4, pph=2, pp2=2))
    ctx.dma(lrec[0:16, :], cfg["l_out"].ap()[0:16, bass.ts(qreg, TPC)])
    ctx.dwait()
    if DEBUG_ATTN:
        dlr = nc.dram_tensor("dbg_lrec", [16, TPC], F32, kind="ExternalOutput").ap()
        ctx.dma(dlr, lrec[0:16, :])
        ctx.dwait()
    nc.vector.reciprocal(lrec[0:16, :], lrec[0:16, :])
    ctx.bar()
    sg = sb.abf(TT)
    for t in range(NTT):
        tsl = slice(t * TT, (t + 1) * TT)
        for c in range(NCH):
            nc.tensor.matmul(PS[:, 3 * TT:4 * TT], Efull[:, c * 128:(c + 1) * 128],
                             lrec[0:16, tsl], start=True, stop=True)
            ctx.bar()
            nc.vector.tensor_tensor(sg, zin[:, c * TPC + t * TT: c * TPC + (t + 1) * TT],
                                    PS[:, 3 * TT:4 * TT], ALU.mult)
            ctx.bar()
            nc.vector.tensor_copy(zin[:, c * TPC + t * TT: c * TPC + (t + 1) * TT], sg)
            ctx.bar()
        for co in range(NCH):
            for ci in range(NCH):
                nc.tensor.matmul(PS[:, 0:TT], cfg["woT"](ci, co),
                                 zin[:, ci * TPC + t * TT: ci * TPC + (t + 1) * TT],
                                 start=(ci == 0), stop=(ci == NCH - 1))
            ctx.bar()
            if DEBUG_ATTN and t == 0 and co == 0:
                dsout = nc.dram_tensor("dbg_s00", [128, TT], F32, kind="ExternalOutput").ap()
                stmp_d = sb.af(TT)
                nc.vector.tensor_copy(stmp_d, PS[:, 0:TT])
                ctx.bar()
                ctx.dma(dsout, stmp_d)
                ctx.dwait()
            combine_fn(co, t, PS[:, 0:TT])
            ctx.bar()
    sb.release()



def causal_patterns():
    pat = np.zeros((128, 2048), dtype=np.float32)
    for r in range(4):
        kk = np.arange(128)[:, None]
        qq = np.arange(512)[None, :]
        pat[:, r * 512:(r + 1) * 512] = np.where(qq < kk + 128 * r, -1e9, 0.0)
    return pat.astype(BF)


def e16_const():
    E = np.zeros((16, 1024), dtype=np.float32)
    for h in range(16):
        E[h, h * 64:(h + 1) * 64] = 1.0
    return E


def attn_dram(ctx, tag):
    nc = ctx.nc
    d = {}
    d["agz_in"] = nc.dram_tensor(f"agzin_{tag}", [256, S], BF16, kind="Internal")
    d["agz_out"] = nc.dram_tensor(f"agzout_{tag}", [1024, S], BF16, kind="Internal")
    d["l_in"] = nc.dram_tensor(f"lin_{tag}", [4, S], F32, kind="Internal")
    d["l_out"] = nc.dram_tensor(f"lout_{tag}", [16, S], F32, kind="Internal")
    return d


def emit_layer2_attn(ctx, sb, xnag, g_dram, tag="l2"):
    nc = ctx.nc
    wqkv = ctx.inp("L2w_qkv_m", [D, 768], F32)
    bqkv = ctx.inp("L2b_qkv_m", [64, 12], F32)
    wo = ctx.inp("L2w_o", [D, D], F32)
    wob = ctx.inp("L2b_o", [128, 8], F32)
    sb.mark()
    Wb = sb.abf(NCH * 768)
    ctx.dmag(Wb, wqkv.rearrange("(c p) o -> p c o", p=128))
    Bia = sb.af(12)[0:64, :]
    ctx.dma(Bia, bqkv)
    Wob = sb.abf(NCH * 1024)
    ctx.dmag(Wob, wo.rearrange("(c p) o -> p c o", p=128))
    Wobi = sb.af(8)
    ctx.dma(Wobi, wob)
    gt = sb.af(TT)
    tmp = sb.af(TT)
    ctx.dwait()

    d = attn_dram(ctx, tag)
    cfg = {
        "causal": True,
        "sqrt_scale": float(1.0 / np.sqrt(np.sqrt(64.0))),
        "lhsT": lambda proj, h, c: Wb[:, c * 768 + proj * 256 + h * 64:
                                      c * 768 + proj * 256 + (h + 1) * 64],
        "bias": lambda proj, h: Bia[:, proj * 4 + h: proj * 4 + h + 1],
        "woT": lambda ci, co: Wob[:, ci * 1024 + co * 128: ci * 1024 + (co + 1) * 128],
        "patterns": ctx.const("causal_patt", causal_patterns()),
        "E16": e16_const(),
    }
    cfg.update(d)

    def combine(co, t, s_psum):
        tsl = slice(t * TT, (t + 1) * TT)
        ctx.dma(gt, g_dram[co * 128:(co + 1) * 128, tsl])
        nc.sync.wait_ge(ctx.dsem, ctx.dcnt)
        ctx.bar()
        nc.vector.tensor_scalar(tmp, s_psum, Wobi[0:128, co:co + 1], None, ALU.add)
        ctx.bar()
        nc.vector.tensor_mul(tmp, tmp, gt)
        ctx.bar()
        nc.vector.tensor_add(sb.x[co][:, tsl], sb.x[co][:, tsl], tmp)

    emit_attn_std(ctx, sb, cfg, xnag, combine)
    sb.release()


def emit_std_attn_layer(ctx, sb, xnag, g_dram, li, with_bias, wo_bias, tag):
    nc = ctx.nc
    wqkv = ctx.inp(f"L{li}w_qkv_m", [D, 768], F32)
    if with_bias:
        bqkv = ctx.inp(f"L{li}b_qkv_m", [64, 12], F32)
    wo = ctx.inp(f"L{li}w_o", [D, D], F32)
    if wo_bias:
        wob = ctx.inp(f"L{li}b_o", [128, 8], F32)
    sb.mark()
    Wb = sb.abf(NCH * 768)
    ctx.dmag(Wb, wqkv.rearrange("(c p) o -> p c o", p=128))
    if with_bias:
        Bia = sb.af(12)[0:64, :]
        ctx.dma(Bia, bqkv)
    Wob = sb.abf(NCH * 1024)
    ctx.dmag(Wob, wo.rearrange("(c p) o -> p c o", p=128))
    if wo_bias:
        Wobi = sb.af(8)
        ctx.dma(Wobi, wob)
    gt = sb.af(TT)
    tmp = sb.af(TT)
    ctx.dwait()

    d = attn_dram(ctx, tag)
    cfg = {
        "causal": True,
        "sqrt_scale": float(1.0 / np.sqrt(np.sqrt(64.0))),
        "lhsT": lambda proj, h, c: Wb[:, c * 768 + proj * 256 + h * 64:
                                      c * 768 + proj * 256 + (h + 1) * 64],
        "bias": (lambda proj, h: Bia[:, proj * 4 + h: proj * 4 + h + 1]) if with_bias
                else (lambda proj, h: None),
        "woT": lambda ci, co: Wob[:, ci * 1024 + co * 128: ci * 1024 + (co + 1) * 128],
        "patterns": ctx.const("causal_patt", causal_patterns()),
        "E16": e16_const(),
    }
    cfg.update(d)

    def combine(co, t, s_psum):
        tsl = slice(t * TT, (t + 1) * TT)
        ctx.dma(gt, g_dram[co * 128:(co + 1) * 128, tsl])
        nc.sync.wait_ge(ctx.dsem, ctx.dcnt)
        ctx.bar()
        if wo_bias:
            nc.vector.tensor_scalar(tmp, s_psum, Wobi[0:128, co:co + 1], None, ALU.add)
            ctx.bar()
            nc.vector.tensor_mul(tmp, tmp, gt)
        else:
            nc.vector.tensor_mul(tmp, s_psum, gt)
        ctx.bar()
        nc.vector.tensor_add(sb.x[co][:, tsl], sb.x[co][:, tsl], tmp)

    emit_attn_std(ctx, sb, cfg, xnag, combine)
    sb.release()


def emit_layer1_attn(ctx, sb, xnag, xn_own, g_dram, tag="l1"):
    nc = ctx.nc
    PS = ctx.PS
    wq_n = ctx.inp("L1w_q_m", [64, 4096], F32)
    wkb_n = ctx.inp("L1w_kb_m", [64, 1024], F32)
    wvbT = ctx.inp("L1w_vbT_m", [256, 256], F32)
    kaT = ctx.inp("L1w_kaT", [D, 256], F32)
    vaT = ctx.inp("L1w_vaT", [D, 256], F32)
    RT = ctx.inp("L1_RT_m", [64, 256], F32)
    woT = ctx.inp("L1w_oT", [D, D], F32)
    sb.mark()
    lv_buf = sb.abf(512)
    ctx.dmag(lv_buf, wvbT.rearrange("(lc p) o -> p lc o", p=128))
    Wob = sb.abf(NCH * 1024)
    ctx.dmag(Wob, woT.rearrange("(c p) o -> p c o", p=128))
    lq_buf = sb.abf(2048)
    lk_buf = sb.abf(512)
    gt = sb.af(TT)
    tmp = sb.af(TT)
    latt = sb.abf(2048)
    sb.mark()
    wqn = sb.abf(4096)[0:64, :]
    ctx.dmag(wqn, wq_n)
    wkbn = sb.abf(1024)[0:64, :]
    ctx.dmag(wkbn, wkb_n)
    kaTb = sb.abf(NCH * 256)
    ctx.dmag(kaTb, kaT.rearrange("(c p) o -> p c o", p=128))
    vaTb = sb.abf(NCH * 256)
    ctx.dmag(vaTb, vaT.rearrange("(c p) o -> p c o", p=128))
    RTb = sb.abf(256)[0:64, :]
    ctx.dmag(RTb, RT)
    ctx.dwait()

    for h in range(4):
        for c in range(NCH):
            nc.tensor.matmul(PS[:, 0:64], wqn[0:64, h * 1024 + c * 128: h * 1024 + (c + 1) * 128],
                             RTb[0:64, h * 64:(h + 1) * 64], start=True, stop=True)
            ctx.bar()
            nc.vector.tensor_copy(lq_buf[:, (h * 8 + c) * 64:(h * 8 + c + 1) * 64],
                                  PS[:, 0:64])
            ctx.bar()
        for lc in range(2):
            nc.tensor.matmul(PS[:, 0:64], wkbn[0:64, h * 256 + lc * 128: h * 256 + (lc + 1) * 128],
                             RTb[0:64, h * 64:(h + 1) * 64], start=True, stop=True)
            ctx.bar()
            nc.vector.tensor_copy(lk_buf[:, (h * 2 + lc) * 64:(h * 2 + lc + 1) * 64],
                                  PS[:, 0:64])
            ctx.bar()

    lat_in = nc.dram_tensor(f"latin_{tag}", [512, TPC], BF16, kind="Internal")
    lat_out = nc.dram_tensor(f"latout_{tag}", [2048, TPC], BF16, kind="Internal")
    xnt_own = sb.abf(4096)
    latk = sb.abf(2 * TPC)
    latv = sb.abf(2 * TPC)
    for t in range(NTT):
        tsl = slice(t * TT, (t + 1) * TT)
        ctx.dma(xnt_own, xn_own[0:1024, tsl].rearrange("(c p) t -> p c t", p=128))
        ctx.dwait()
        for oc in range(2):
            for c in range(NCH):
                nc.tensor.matmul(PS[:, 0:TT],
                                 kaTb[:, c * 256 + oc * 128: c * 256 + (oc + 1) * 128],
                                 xnt_own[:, c * TT:(c + 1) * TT],
                                 start=(c == 0), stop=(c == NCH - 1))
            for c in range(NCH):
                nc.tensor.matmul(PS[:, TT:2 * TT],
                                 vaTb[:, c * 256 + oc * 128: c * 256 + (oc + 1) * 128],
                                 xnt_own[:, c * TT:(c + 1) * TT],
                                 start=(c == 0), stop=(c == NCH - 1))
            ctx.bar()
            nc.vector.tensor_copy(latk[:, oc * TPC + t * TT: oc * TPC + (t + 1) * TT],
                                  PS[:, 0:TT])
            nc.vector.tensor_copy(latv[:, oc * TPC + t * TT: oc * TPC + (t + 1) * TT],
                                  PS[:, TT:2 * TT])
            ctx.bar()
    for oc in range(2):
        ctx.dma(lat_in.ap()[oc * 128:(oc + 1) * 128, :], latk[:, oc * TPC:(oc + 1) * TPC])
        ctx.dma(lat_in.ap()[256 + oc * 128: 256 + (oc + 1) * 128, :],
                latv[:, oc * TPC:(oc + 1) * TPC])
    ctx.dwait()
    sb.release()
    ctx.coll("AllGather", ALU.bypass, GROUPS_BATCH, lat_in.ap(), lat_out.ap())
    ctx.cwait()

    d = attn_dram(ctx, tag)
    cfg = {
        "causal": False,
        "sqrt_scale": float(1.0 / np.sqrt(np.sqrt(64.0))),
        "bias": lambda proj, h: None,
        "woT": lambda ci, co: Wob[:, ci * 1024 + co * 128: ci * 1024 + (co + 1) * 128],
        "patterns": None,
        "E16": e16_const(),
    }
    cfg.update(d)


    def gen_pair(p, bufs):
        kbuf, qbuf, vbuf = bufs["kbuf"], bufs["qbuf"], bufs["vbuf"]
        rmin = bufs["rmin"]
        xnt = bufs["xnt"]
        nc.vector.memset(vbuf, 1.0)
        nc.vector.memset(rmin[0:1, 0:1], 0.0)
        nc.vector.memset(kbuf[64:65, :], 1.0)
        ctx.bar()
        for t in range(ATT_NQT):
            r, u = t // 4, t % 4
            ctx.dma(xnt, xnag[r * 1024:(r + 1) * 1024,
                              u * TT:(u + 1) * TT].rearrange("(c p) t -> p c t", p=128))
            ctx.dma(latt, lat_out.ap()[r * 512:(r + 1) * 512,
                                       u * TT:(u + 1) * TT].rearrange("(kv p) t -> p kv t", p=128))
            ctx.dwait()
            for lc in range(2):
                nc.tensor.matmul(PS[0:64, 0:TT], lk_buf[:, (p * 2 + lc) * 64:(p * 2 + lc + 1) * 64],
                                 latt[:, lc * TT:(lc + 1) * TT],
                                 start=(lc == 0), stop=(lc == 1))
            for lc in range(2):
                nc.tensor.matmul(PS[0:64, TT:2 * TT], lv_buf[:, lc * 256 + p * 64: lc * 256 + (p + 1) * 64],
                                 latt[:, (2 + lc) * TT:(3 + lc) * TT],
                                 start=(lc == 0), stop=(lc == 1))
            for c in range(NCH):
                nc.tensor.matmul(PS[0:64, 2 * TT:3 * TT],
                                 lq_buf[:, (p * 8 + c) * 64:(p * 8 + c + 1) * 64],
                                 xnt[:, c * TT:(c + 1) * TT],
                                 start=(c == 0), stop=(c == NCH - 1))
            ctx.bar()
            attn_evac_tile(ctx, sb, cfg, bufs, p, t)
        nc.vector.tensor_scalar(qbuf[64:65, :], qbuf[64:65, :], rmin[0:1, 0:1],
                                None, ALU.add)
        ctx.bar()

    cfg["gen_pair"] = gen_pair

    def combine(co, t, s_psum):
        tsl = slice(t * TT, (t + 1) * TT)
        ctx.dma(gt, g_dram[co * 128:(co + 1) * 128, tsl])
        nc.sync.wait_ge(ctx.dsem, ctx.dcnt)
        ctx.bar()
        nc.vector.tensor_mul(tmp, s_psum, gt)
        ctx.bar()
        nc.vector.tensor_add(sb.x[co][:, tsl], sb.x[co][:, tsl], tmp)

    emit_attn_std(ctx, sb, cfg, xnag, combine)
    sb.release()



def fp4_qdq(ctx, sb, x_ap, out_bf_ap, F, scr, P=128):
    nc = ctx.nc
    av, v, q, u, sc, rc, sgn = (scr["av"], scr["v"], scr["q"], scr["u"],
                                scr["sc"], scr["rc"], scr["sgn"])
    nb = F // 16
    nc.scalar.activation(av[:, 0:F], x_ap, AF.Abs)
    ctx.edge("scalar", "vector")
    nc.vector.tensor_reduce(u[:, 0:nb], av[:, 0:F].rearrange("p (a b) -> p a b", b=16),
                            AX, ALU.max)
    ctx.bar()
    nc.vector.tensor_scalar(sc[:, 0:nb], u[:, 0:nb], 1.0 / 6.0, 1e-12,
                            ALU.mult, ALU.max)
    ctx.bar()
    nc.vector.reciprocal(rc[:, 0:nb], sc[:, 0:nb])
    ctx.bar()
    rcb = rc[:, 0:nb].unsqueeze(2).broadcast_to([P, nb, 16])
    nc.vector.tensor_tensor(v[:, 0:F].rearrange("p (a b) -> p a b", b=16),
                            av[:, 0:F].rearrange("p (a b) -> p a b", b=16),
                            rcb, ALU.mult)
    ctx.edge("vector", "scalar")
    nc.scalar.activation(sgn[:, 0:F], x_ap, AF.Sign)
    ctx.bar()
    nc.vector.tensor_scalar(q[:, 0:F], v[:, 0:F], 2.0, MAGIC, ALU.mult, ALU.add)
    nc.vector.tensor_scalar(q[:, 0:F], q[:, 0:F], MAGIC, 0.5, ALU.subtract, ALU.mult)
    nc.vector.tensor_scalar(q[:, 0:F], q[:, 0:F], 2.0, None, ALU.min)
    nc.vector.tensor_scalar(u[:, 0:F], v[:, 0:F], 2.5, None, ALU.is_gt)
    nc.vector.tensor_add(q[:, 0:F], q[:, 0:F], u[:, 0:F])
    nc.vector.tensor_scalar(u[:, 0:F], v[:, 0:F], 3.5, None, ALU.is_gt)
    nc.vector.tensor_add(q[:, 0:F], q[:, 0:F], u[:, 0:F])
    nc.vector.tensor_scalar(u[:, 0:F], v[:, 0:F], 5.0, 2.0, ALU.is_gt, ALU.mult)
    nc.vector.tensor_add(q[:, 0:F], q[:, 0:F], u[:, 0:F])
    nc.vector.tensor_mul(q[:, 0:F], q[:, 0:F], sgn[:, 0:F])
    scb = sc[:, 0:nb].unsqueeze(2).broadcast_to([P, nb, 16])
    nc.vector.tensor_tensor(out_bf_ap.rearrange("p (a b) -> p a b", b=16),
                            q[:, 0:F].rearrange("p (a b) -> p a b", b=16),
                            scb, ALU.mult)
    ctx.bar()


def emit_l2_ffn(ctx, sb, stats_dram, g_dram, tag="l2f"):
    nc = ctx.nc
    PS = ctx.PS
    PSb = ctx.PS.ap().bitcast(BF16)
    identb = ctx.get_identb(sb)
    identf = ctx._identf_sb
    wg_in = ctx.inp("L2f_wg_m", [384, 1024], F32)
    wu_in = ctx.inp("L2f_wu_m", [384, 1024], F32)
    wd_in = ctx.inp("L2f_wd_m", [128, 2816], F32)
    HD3 = 3072
    wgT = nc.dram_tensor(f"wgT_{tag}", [8192, 384], BF16, kind="Internal")
    wuT = nc.dram_tensor(f"wuT_{tag}", [8192, 384], BF16, kind="Internal")
    wdT = nc.dram_tensor(f"wdT_{tag}", [8 * 2816, 128], BF16, kind="Internal")
    wgT_l = nc.dram_tensor(f"wgTl_{tag}", [1024, 384], BF16, kind="Internal")
    wuT_l = nc.dram_tensor(f"wuTl_{tag}", [1024, 384], BF16, kind="Internal")
    wdT_l = nc.dram_tensor(f"wdTl_{tag}", [2816, 128], BF16, kind="Internal")
    h_dram = nc.dram_tensor(f"h_{tag}", [HD3, TPC], F32, kind="Internal")
    xq_dram = nc.dram_tensor(f"xq_{tag}", [D, TPC], BF16, kind="Internal")

    sb.mark()
    wt = sb.af(2816)
    wq_b = sb.abf(2816)
    wtr = sb.abf(2816)
    scr = dict(av=sb.af(2816), v=sb.af(2816), q=sb.af(2816), u=sb.af(2816),
               sc=sb.af(192), rc=sb.af(192), sgn=sb.af(2816))
    for (win, wTl) in ((wg_in, wgT_l), (wu_in, wuT_l)):
        for rb in range(3):
            ctx.dma(wt[:, 0:1024], win[rb * 128:(rb + 1) * 128, :])
            ctx.dwait()
            fp4_qdq(ctx, sb, wt[:, 0:1024], wq_b[:, 0:1024], 1024, scr)
            tr_copy(ctx, [(wq_b[:, c * 128:(c + 1) * 128], wtr[:, c * 128:(c + 1) * 128])
                          for c in range(NCH)])
            ctx.dma(wTl.ap()[:, rb * 128:(rb + 1) * 128]
                    .rearrange("(c p) o -> p c o", p=128),
                    wtr[:, 0:1024].rearrange("p (c o) -> p c o", o=128))
            ctx.dwait()
    ctx.dma(wt, wd_in)
    ctx.dwait()
    fp4_qdq(ctx, sb, wt[:, 0:2816], wq_b[:, 0:2816], 2816, scr)
    tr_copy(ctx, [(wq_b[:, cc * 128:(cc + 1) * 128], wtr[:, cc * 128:(cc + 1) * 128])
                  for cc in range(22)])
    ctx.dma(wdT_l.ap().rearrange("(cc p) o -> p cc o", p=128),
            wtr[:, 0:2816].rearrange("p (cc o) -> p cc o", o=128))
    ctx.dwait()
    sb.release()
    ctx.coll("AllGather", ALU.bypass, GROUP_ALL, wgT_l.ap(), wgT.ap())
    ctx.coll("AllGather", ALU.bypass, GROUP_ALL, wuT_l.ap(), wuT.ap())
    ctx.coll("AllGather", ALU.bypass, GROUP_ALL, wdT_l.ap(), wdT.ap())
    ctx.cwait()

    if L2F_PHASES < 2:
        return
    sb.mark()
    strow = sb.af(2 * TT)[0:1, :]
    strowb = sb.abf(2 * TT)[0:1, :]
    onesb2 = sb.abf(128)[0:1, :]
    nc.vector.memset(onesb2, 1.0)
    xnf = sb.af(TT)
    xtok = sb.af(1024)
    xqtok = sb.abf(1024)
    xq_fm = sb.abf(1024)
    scr = dict(av=sb.af(1024), v=sb.af(1024), q=sb.af(1024), u=sb.af(1024),
               sc=sb.af(64), rc=sb.af(64), sgn=sb.af(1024))
    for t in range(NTT):
        tsl = slice(t * TT, (t + 1) * TT)
        ctx.dma(strow[0:1, 0:TT], stats_dram.ap()[0:1, tsl])
        ctx.dma(strow[0:1, TT:2 * TT], stats_dram.ap()[1:2, tsl])
        ctx.dwait()
        nc.vector.tensor_copy(strowb, strow)
        ctx.edge("vector", "tensor")
        nc.tensor.matmul(PS[:, 2 * TT:3 * TT], onesb2, strowb[0:1, 0:TT],
                         start=True, stop=True)
        nc.tensor.matmul(PS[:, 3 * TT:4 * TT], onesb2, strowb[0:1, TT:2 * TT],
                         start=True, stop=True)
        ctx.edge("tensor", "vector")
        for tb in range(4):
            for c in range(NCH):
                ctx.edge("tensor", "vector")
                nc.vector.tensor_tensor(xnf, sb.x[c][:, tsl], PS[:, 2 * TT:3 * TT],
                                        ALU.mult)
                nc.vector.tensor_add(xnf, xnf, PS[:, 3 * TT:4 * TT])
                ctx.edge("vector", "tensor")
                nc.tensor.transpose(PS[:, 0:128], xnf[:, tb * 128:(tb + 1) * 128],
                                    identf)
                ctx.edge("tensor", "vector")
                nc.vector.tensor_copy(xtok[:, c * 128:(c + 1) * 128], PS[:, 0:128])
                ctx.edge("vector", "tensor")
            ctx.edge("vector", "scalar")
            fp4_qdq(ctx, sb, xtok[:, 0:1024], xqtok[:, 0:1024], 1024, scr)
            tr_copy(ctx, [(xqtok[:, c * 128:(c + 1) * 128], xq_fm[:, c * 128:(c + 1) * 128])
                          for c in range(NCH)])
            ctx.dma(xq_dram.ap()[:, t * TT + tb * 128: t * TT + (tb + 1) * 128]
                    .rearrange("(c p) t -> p c t", p=128),
                    xq_fm[:, 0:1024].rearrange("p (c t) -> p c t", t=128))
            nc.sync.wait_ge(ctx.dsem, ctx.dcnt)
            nc.vector.wait_ge(ctx.dsem, ctx.dcnt)
            nc.tensor.wait_ge(ctx.dsem, ctx.dcnt)
    sb.release()

    if L2F_PHASES < 3:
        return
    sb.mark()
    lg = sb.abf(NCH * 128)
    lu = sb.abf(NCH * 128)
    xqt = sb.abf(NCH * TT)
    hsil = sb.af(TT)
    hf = sb.af(TT)
    for oc in range(24):
        r, co = oc // 3, (oc % 3) * 128
        ctx.edge("tensor", "sync")
        ctx.dma(lg, wgT.ap()[r * 1024:(r + 1) * 1024, co:co + 128]
                .rearrange("(c p) o -> p c o", p=128))
        ctx.dma(lu, wuT.ap()[r * 1024:(r + 1) * 1024, co:co + 128]
                .rearrange("(c p) o -> p c o", p=128))
        for t in range(NTT):
            tsl = slice(t * TT, (t + 1) * TT)
            ctx.dma(xqt, xq_dram.ap()[:, tsl].rearrange("(c p) t -> p c t", p=128))
            ctx.dwait()
            for c in range(NCH):
                nc.tensor.matmul(PS[:, 0:TT], lg[:, c * 128:(c + 1) * 128],
                                 xqt[:, c * TT:(c + 1) * TT],
                                 start=(c == 0), stop=(c == NCH - 1))
            for c in range(NCH):
                nc.tensor.matmul(PS[:, TT:2 * TT], lu[:, c * 128:(c + 1) * 128],
                                 xqt[:, c * TT:(c + 1) * TT],
                                 start=(c == 0), stop=(c == NCH - 1))
            ctx.edge("tensor", "scalar")
            nc.scalar.activation(hsil, PS[:, 0:TT], AF.Silu)
            ctx.edge("scalar", "vector", "tensor")
            nc.vector.tensor_tensor(hf, hsil, PS[:, TT:2 * TT], ALU.mult)
            ctx.edge("vector", "sync", "tensor")
            ctx.dma(h_dram.ap()[oc * 128:(oc + 1) * 128, tsl], hf)
            nc.sync.wait_ge(ctx.dsem, ctx.dcnt)
            nc.vector.wait_ge(ctx.dsem, ctx.dcnt)
            nc.scalar.wait_ge(ctx.dsem, ctx.dcnt)
    sb.release()

    if L2F_PHASES < 4:
        return
    sb.mark()
    htok = sb.af(HD3)
    hqtok = sb.abf(HD3)
    hq_fm = sb.abf(22 * TT)
    ld = sb.abf(22 * 128)
    hin = sb.af(HD3)
    _av = sb.af(HD3)
    scr = dict(av=_av, v=sb.af(HD3), q=htok, u=sb.af(HD3),
               sc=sb.af(192), rc=sb.af(192), sgn=_av)
    gt2 = sb.af(TT)
    for t in range(NTT):
        tsl = slice(t * TT, (t + 1) * TT)
        for tb in range(4):
            tk = slice(t * TT + tb * 128, t * TT + (tb + 1) * 128)
            ctx.edge("tensor", "sync")
            ctx.dma(hin.rearrange("p (cc o) -> p cc o", o=128),
                    h_dram.ap()[:, tk].rearrange("(cc p) o -> p cc o", p=128))
            ctx.dwait()
            tr_copy(ctx, [(hin[:, cc * 128:(cc + 1) * 128],
                           htok[:, cc * 128:(cc + 1) * 128]) for cc in range(24)],
                    bf16=False)
            fp4_qdq(ctx, sb, htok[:, 0:HD3], hqtok[:, 0:HD3], HD3, scr)
            tr_copy(ctx, [(hqtok[:, cc * 128:(cc + 1) * 128],
                           hq_fm[:, cc * TT + tb * 128: cc * TT + (tb + 1) * 128])
                          for cc in range(22)])
        for oc in range(NCH):
            ctx.edge("tensor", "sync")
            ctx.dma(ld, wdT.ap()[oc * 2816:(oc + 1) * 2816, :]
                    .rearrange("(cc p) o -> p cc o", p=128))
            ctx.dwait()
            for cc in range(22):
                nc.tensor.matmul(PS[:, 0:TT], ld[:, cc * 128:(cc + 1) * 128],
                                 hq_fm[:, cc * TT:(cc + 1) * TT],
                                 start=(cc == 0), stop=(cc == 21))
            ctx.edge("tensor", "scalar")
            nc.scalar.activation(gt2, PS[:, 0:TT], AF.Silu)
            ctx.edge("scalar", "sync", "tensor")
            ctx.dma(g_dram.ap()[oc * 128:(oc + 1) * 128, tsl], gt2)
            nc.sync.wait_ge(ctx.dsem, ctx.dcnt)
            nc.scalar.wait_ge(ctx.dsem, ctx.dcnt)
    sb.release()


def lpbq_qdq(ctx, sb, x_ap, out_bf_ap, F, scr, block=32):
    nc = ctx.nc
    av, v, q, u, sc, rc = (scr["av"], scr["v"], scr["q"], scr["u"],
                           scr["sc"], scr["rc"])
    nb = F // block
    nc.scalar.activation(av[:, 0:F], x_ap, AF.Abs)
    ctx.edge("scalar", "vector")
    nc.vector.tensor_reduce(u[:, 0:nb], av[:, 0:F].rearrange("p (a b) -> p a b", b=block),
                            AX, ALU.max)
    ctx.bar()
    nc.vector.tensor_scalar(sc[:, 0:nb], u[:, 0:nb], 1.0 / 7.0, 1e-12, ALU.mult, ALU.max)
    ctx.bar()
    nc.vector.reciprocal(rc[:, 0:nb], sc[:, 0:nb])
    ctx.bar()
    rcb = rc[:, 0:nb].unsqueeze(2).broadcast_to([128, nb, block])
    nc.vector.tensor_tensor(v[:, 0:F].rearrange("p (a b) -> p a b", b=block),
                            x_ap.rearrange("p (a b) -> p a b", b=block), rcb, ALU.mult)
    nc.vector.tensor_scalar(v[:, 0:F], v[:, 0:F], MAGIC, None, ALU.add)
    nc.vector.tensor_scalar(v[:, 0:F], v[:, 0:F], MAGIC, None, ALU.subtract)
    scb = sc[:, 0:nb].unsqueeze(2).broadcast_to([128, nb, block])
    nc.vector.tensor_tensor(out_bf_ap.rearrange("p (a b) -> p a b", b=block),
                            v[:, 0:F].rearrange("p (a b) -> p a b", b=block), scb, ALU.mult)
    ctx.bar()


def int_qdq(ctx, in_f32, out_ap, r_ap, s_ap):
    nc = ctx.nc
    nc.vector.tensor_scalar(out_ap, in_f32, r_ap, MAGIC, ALU.mult, ALU.add)
    nc.vector.tensor_scalar(out_ap, out_ap, MAGIC, s_ap, ALU.subtract, ALU.mult)


def gmax_to_slot(ctx, sb, mx128, gdram, col):
    nc = ctx.nc
    sb.mark()
    row = sb.af(128)[0:1, :]
    ctx.edge("vector", "tensor")
    nc.tensor.transpose(ctx.PS[0:1, 3584:3584 + 128], mx128, ctx._identf_sb)
    ctx.edge("tensor", "vector")
    nc.vector.tensor_copy(row, ctx.PS[0:1, 3584:3584 + 128])
    ctx.bar()
    nc.vector.reduce_max(row[0:1, 0:1], row[0:1, 0:128], axis=AX)
    ctx.bar()
    ctx.dma(gdram.ap()[0:1, col:col + 1], row[0:1, 0:1])
    ctx.dwait()
    sb.release()


def scale_bcast(ctx, sb, gdram_out, col, qmax, r_out, s_out, onesb):
    nc = ctx.nc
    sb.mark()
    sv = sb.af(8)[0:1, 0:2]
    svb = sb.abf(8)[0:1, 0:2]
    ctx.dma(sv[0:1, 0:1], gdram_out.ap()[0:1, col:col + 1])
    ctx.dwait()
    nc.vector.tensor_scalar(sv[0:1, 0:1], sv[0:1, 0:1], 1.0 / qmax, None, ALU.mult)
    ctx.bar()
    nc.vector.reciprocal(sv[0:1, 1:2], sv[0:1, 0:1])
    ctx.bar()
    nc.vector.tensor_copy(svb, sv[0:1, 0:2])
    ctx.edge("vector", "tensor")
    nc.tensor.matmul(ctx.PS[:, 3584:3584 + 2], onesb, svb, start=True, stop=True)
    ctx.edge("tensor", "vector")
    nc.vector.tensor_copy(s_out, ctx.PS[:, 3584:3585])
    nc.vector.tensor_copy(r_out, ctx.PS[:, 3585:3586])
    ctx.bar()
    sb.release()


def emit_l1_ffn(ctx, sb, stats_dram, g_dram, xn_absmax_row, tag="l1f"):
    nc = ctx.nc
    PS = ctx.PS
    identb = ctx.get_identb(sb)
    identf = ctx._identf_sb
    wg_in = ctx.inp("L1f_wg_m", [384, 1024], F32)
    wu_in = ctx.inp("L1f_wu_m", [384, 1024], F32)
    wd_in = ctx.inp("L1f_wd_m", [128, 3072], F32)
    wgT = nc.dram_tensor(f"wgT_{tag}", [8192, 384], BF16, kind="Internal")
    wuT = nc.dram_tensor(f"wuT_{tag}", [8192, 384], BF16, kind="Internal")
    wdT = nc.dram_tensor(f"wdT_{tag}", [8 * 3072, 128], BF16, kind="Internal")
    wgT_l = nc.dram_tensor(f"wgTl_{tag}", [1024, 384], BF16, kind="Internal")
    wuT_l = nc.dram_tensor(f"wuTl_{tag}", [1024, 384], BF16, kind="Internal")
    wdT_l = nc.dram_tensor(f"wdTl_{tag}", [3072, 128], BF16, kind="Internal")
    up_dram = nc.dram_tensor(f"up_{tag}", [3072, TPC], F32, kind="Internal")
    gate_dram = nc.dram_tensor(f"gate_{tag}", [3072, TPC], F32, kind="Internal")
    gmax_in = nc.dram_tensor(f"gmi_{tag}", [1, 8], F32, kind="Internal")
    gmax_out = [nc.dram_tensor(f"gmo_{tag}{i}", [1, 8], F32, kind="Internal")
                for i in range(3)]

    sb.mark()
    z8 = sb.af(8)[0:1, :]
    nc.vector.memset(z8, 0.0)
    ctx.bar()
    nc.vector.tensor_copy(z8[0:1, 0:1], xn_absmax_row)
    ctx.bar()
    ctx.dma(gmax_in.ap(), z8)
    ctx.dwait()
    ctx.coll("AllReduce", ALU.max, GROUP_ALL, gmax_in.ap(), gmax_out[0].ap())
    sb.release()

    sb.mark()
    wt = sb.af(3072)
    wq_b = sb.abf(3072)
    wtr = sb.abf(3072)
    scr = dict(av=sb.af(3072), v=sb.af(3072), q=None, u=sb.af(3072),
               sc=sb.af(96), rc=sb.af(96))
    for (win, wTl) in ((wg_in, wgT_l), (wu_in, wuT_l)):
        for rb in range(3):
            ctx.dma(wt[:, 0:1024], win[rb * 128:(rb + 1) * 128, :])
            ctx.dwait()
            lpbq_qdq(ctx, sb, wt[:, 0:1024], wq_b[:, 0:1024], 1024, scr)
            tr_copy(ctx, [(wq_b[:, c * 128:(c + 1) * 128], wtr[:, c * 128:(c + 1) * 128])
                          for c in range(NCH)])
            ctx.dma(wTl.ap()[:, rb * 128:(rb + 1) * 128]
                    .rearrange("(c p) o -> p c o", p=128),
                    wtr[:, 0:1024].rearrange("p (c o) -> p c o", o=128))
            ctx.dwait()
    ctx.dma(wt, wd_in)
    ctx.dwait()
    lpbq_qdq(ctx, sb, wt[:, 0:3072], wq_b[:, 0:3072], 3072, scr)
    tr_copy(ctx, [(wq_b[:, cc * 128:(cc + 1) * 128], wtr[:, cc * 128:(cc + 1) * 128])
                  for cc in range(24)])
    ctx.dma(wdT_l.ap().rearrange("(cc p) o -> p cc o", p=128),
            wtr[:, 0:3072].rearrange("p (cc o) -> p cc o", o=128))
    ctx.dwait()
    sb.release()
    ctx.coll("AllGather", ALU.bypass, GROUP_ALL, wgT_l.ap(), wgT.ap())
    ctx.coll("AllGather", ALU.bypass, GROUP_ALL, wuT_l.ap(), wuT.ap())
    ctx.coll("AllGather", ALU.bypass, GROUP_ALL, wdT_l.ap(), wdT.ap())
    ctx.cwait()

    sb.mark()
    onesb2 = sb.abf(128)[0:1, :]
    nc.vector.memset(onesb2, 1.0)
    ctx.bar()
    r1 = sb.af(8)[:, 0:1]
    s1 = sb.af(8)[:, 0:1]
    scale_bcast(ctx, sb, gmax_out[0], 0, 32767.0, r1, s1, onesb2)
    xq = sb.abf(NCH * TPC)
    strow = sb.af(2 * TT)[0:1, :]
    strowb = sb.abf(2 * TT)[0:1, :]
    xnf = sb.af(TT)
    xqf = sb.af(TT)
    mxu = sb.af(8)[:, 0:1]
    mxg = sb.af(8)[:, 1:2]
    red = sb.af(8)[:, 2:3]
    for t in range(NTT):
        tsl = slice(t * TT, (t + 1) * TT)
        ctx.dma(strow[0:1, 0:TT], stats_dram.ap()[0:1, tsl])
        ctx.dma(strow[0:1, TT:2 * TT], stats_dram.ap()[1:2, tsl])
        ctx.dwait()
        nc.vector.tensor_copy(strowb, strow)
        ctx.edge("vector", "tensor")
        nc.tensor.matmul(PS[:, 2 * TT:3 * TT], onesb2, strowb[0:1, 0:TT],
                         start=True, stop=True)
        nc.tensor.matmul(PS[:, 3 * TT:4 * TT], onesb2, strowb[0:1, TT:2 * TT],
                         start=True, stop=True)
        ctx.edge("tensor", "vector")
        for c in range(NCH):
            nc.vector.tensor_tensor(xnf, sb.x[c][:, tsl], PS[:, 2 * TT:3 * TT], ALU.mult)
            nc.vector.tensor_add(xnf, xnf, PS[:, 3 * TT:4 * TT])
            int_qdq(ctx, xnf, xqf, r1, s1)
            nc.vector.tensor_copy(xq[:, c * TPC + t * TT: c * TPC + (t + 1) * TT], xqf)
        ctx.bar()

    lw = sb.abf(NCH * 128)
    upf = sb.af(TT)
    nc.vector.memset(mxu, 0.0)
    nc.vector.memset(mxg, 0.0)
    ctx.bar()
    for wi, (wT, odram, mx) in enumerate(((wuT, up_dram, mxu), (wgT, gate_dram, mxg))):
        for oc in range(24):
            r, co = oc // 3, (oc % 3) * 128
            ctx.edge("tensor", "sync")
            ctx.dma(lw, wT.ap()[r * 1024:(r + 1) * 1024, co:co + 128]
                    .rearrange("(c p) o -> p c o", p=128))
            ctx.dwait()
            for t in range(NTT):
                tsl = slice(t * TT, (t + 1) * TT)
                for c in range(NCH):
                    nc.tensor.matmul(PS[:, 0:TT], lw[:, c * 128:(c + 1) * 128],
                                     xq[:, c * TPC + t * TT: c * TPC + (t + 1) * TT],
                                     start=(c == 0), stop=(c == NCH - 1))
                ctx.edge("tensor", "scalar")
                nc.scalar.activation(upf, PS[:, 0:TT], AF.Copy)
                ctx.edge("scalar", "sync", "vector", "tensor")
                nc.vector.reduce_max(red, upf, axis=AX, apply_absolute_value=True)
                nc.vector.tensor_max(mx, mx, red)
                ctx.dma(odram.ap()[oc * 128:(oc + 1) * 128, tsl], upf)
                nc.sync.wait_ge(ctx.dsem, ctx.dcnt)
                nc.scalar.wait_ge(ctx.dsem, ctx.dcnt)
    gmax_to_slot(ctx, sb, mxu, gmax_in, 1)
    gmax_to_slot(ctx, sb, mxg, gmax_in, 2)
    ctx.coll("AllReduce", ALU.max, GROUP_ALL, gmax_in.ap(), gmax_out[1].ap())
    ctx.cwait()

    r2 = sb.af(8)[:, 0:1]
    s2 = sb.af(8)[:, 0:1]
    r3 = sb.af(8)[:, 0:1]
    s3 = sb.af(8)[:, 0:1]
    scale_bcast(ctx, sb, gmax_out[1], 1, 32767.0, r2, s2, onesb2)
    scale_bcast(ctx, sb, gmax_out[1], 2, 32767.0, r3, s3, onesb2)
    mx4 = sb.af(8)[:, 3:4]
    nc.vector.memset(mx4, 0.0)
    gq = sb.af(TT)
    sg = sb.af(TT)
    prod = sb.af(TT)
    for oc in range(24):
        for t in range(NTT):
            tsl = slice(t * TT, (t + 1) * TT)
            ctx.edge("vector", "sync")
            ctx.dma(upf, gate_dram.ap()[oc * 128:(oc + 1) * 128, tsl])
            ctx.dwait()
            int_qdq(ctx, upf, gq, r3, s3)
            ctx.edge("vector", "scalar")
            nc.scalar.activation(sg, gq, AF.Sigmoid)
            ctx.edge("scalar", "vector")
            nc.vector.tensor_scalar(sg, sg, 65536.0, MAGIC, ALU.mult, ALU.add)
            nc.vector.tensor_scalar(sg, sg, MAGIC, 65535.0, ALU.subtract, ALU.min)
            nc.vector.tensor_mul(prod, gq, sg)
            nc.vector.tensor_scalar(prod, prod, 1.0 / 65536.0, None, ALU.mult)
            nc.vector.reduce_max(red, prod, axis=AX, apply_absolute_value=True)
            nc.vector.tensor_max(mx4, mx4, red)
            ctx.edge("vector", "sync")
            ctx.dma(gate_dram.ap()[oc * 128:(oc + 1) * 128, tsl], prod)
            nc.sync.wait_ge(ctx.dsem, ctx.dcnt)
            nc.vector.wait_ge(ctx.dsem, ctx.dcnt)
    gmax_to_slot(ctx, sb, mx4, gmax_in, 3)
    ctx.coll("AllReduce", ALU.max, GROUP_ALL, gmax_in.ap(), gmax_out[2].ap())
    ctx.cwait()

    r4 = sb.af(8)[:, 0:1]
    s4 = sb.af(8)[:, 0:1]
    scale_bcast(ctx, sb, gmax_out[2], 3, 32767.0, r4, s4, onesb2)
    mx5 = sb.af(8)[:, 4:5]
    nc.vector.memset(mx5, 0.0)
    for oc in range(24):
        for t in range(NTT):
            tsl = slice(t * TT, (t + 1) * TT)
            ctx.edge("vector", "sync")
            ctx.dma(gq, gate_dram.ap()[oc * 128:(oc + 1) * 128, tsl])
            ctx.dma(upf, up_dram.ap()[oc * 128:(oc + 1) * 128, tsl])
            ctx.dwait()
            int_qdq(ctx, gq, prod, r4, s4)
            int_qdq(ctx, upf, sg, r2, s2)
            nc.vector.tensor_mul(prod, prod, sg)
            nc.vector.reduce_max(red, prod, axis=AX, apply_absolute_value=True)
            nc.vector.tensor_max(mx5, mx5, red)
            ctx.edge("vector", "sync")
            ctx.dma(up_dram.ap()[oc * 128:(oc + 1) * 128, tsl], prod)
            nc.sync.wait_ge(ctx.dsem, ctx.dcnt)
            nc.vector.wait_ge(ctx.dsem, ctx.dcnt)
    gmax_in2 = nc.dram_tensor(f"gmi2_{tag}", [1, 8], F32, kind="Internal")
    gmax_out3 = nc.dram_tensor(f"gmo3_{tag}", [1, 8], F32, kind="Internal")
    gmax_to_slot(ctx, sb, mx5, gmax_in2, 0)
    ctx.coll("AllReduce", ALU.max, GROUP_ALL, gmax_in2.ap(), gmax_out3.ap())
    ctx.cwait()

    r5 = sb.af(8)[:, 0:1]
    s5 = sb.af(8)[:, 0:1]
    scale_bcast(ctx, sb, gmax_out3, 0, 32767.0, r5, s5, onesb2)
    oqfm = sb.abf(24 * TT)
    ld = sb.abf(24 * 128)
    ofl = sb.af(24 * TT // 4) if False else None
    oin = sb.af(TT)
    for t in range(NTT):
        tsl = slice(t * TT, (t + 1) * TT)
        for cc in range(24):
            ctx.edge("tensor", "sync")
            ctx.dma(oin, up_dram.ap()[cc * 128:(cc + 1) * 128, tsl])
            ctx.dwait()
            nc.vector.tensor_scalar(oin, oin, r5, MAGIC, ALU.mult, ALU.add)
            nc.vector.tensor_scalar(oqfm[:, cc * TT:(cc + 1) * TT], oin, MAGIC, s5,
                                    ALU.subtract, ALU.mult)
            ctx.bar()
        for oc in range(NCH):
            ctx.edge("tensor", "sync")
            ctx.dma(ld, wdT.ap()[oc * 3072:(oc + 1) * 3072, :]
                    .rearrange("(cc p) o -> p cc o", p=128))
            ctx.dwait()
            for cc in range(24):
                nc.tensor.matmul(PS[:, 0:TT], ld[:, cc * 128:(cc + 1) * 128],
                                 oqfm[:, cc * TT:(cc + 1) * TT],
                                 start=(cc == 0), stop=(cc == 23))
            ctx.edge("tensor", "scalar")
            nc.scalar.activation(oin, PS[:, 0:TT], AF.Silu)
            ctx.edge("scalar", "sync", "tensor")
            ctx.dma(g_dram.ap()[oc * 128:(oc + 1) * 128, tsl], oin)
            nc.sync.wait_ge(ctx.dsem, ctx.dcnt)
            nc.scalar.wait_ge(ctx.dsem, ctx.dcnt)
    sb.release()



L0W = 2560
L0XW = 2564
L0F_STEP = 10
L0F_GSTEP = 3


def l0_tri_patterns():
    i = np.arange(128)[:, None]
    j = np.arange(128)[None, :]
    lo = np.where(j < i, -1e9, 0.0)
    up = np.where(j > i, -1e9, 0.0)
    return np.concatenate([lo, up], axis=1).astype(BF)


def l0_rot_tables(q):
    s0 = q * 2048
    d = np.arange(64)
    jj = (d % 32).astype(np.float64)
    invf = 10000.0 ** (-(2.0 * jj) / 64.0)
    base = (2.0 * jj + 0.4 * 64.0) / (1.4 * 64.0)
    sgn = np.where(d < 32, -1.0, 1.0)

    def tabs(pos, qside):
        fr = pos[None, :] * invf[:, None]
        sc = base[:, None] ** ((pos[None, :] - 4096.0) / 256.0)
        if not qside:
            sc = 1.0 / sc
        A = np.cos(fr) * sc
        Bt = np.sin(fr) * sc * sgn[:, None]
        return (np.concatenate([A, A], 0).astype(np.float32),
                np.concatenate([Bt, Bt], 0).astype(np.float32))

    Aq, Bq = tabs(np.arange(s0, s0 + 2048, dtype=np.float64), True)
    Ak, Bk = tabs(np.arange(s0 - 512, s0 + 2048, dtype=np.float64), False)
    return Aq, Bq, Ak, Bk


def emit_l0_ffn(ctx, sb, xnag, xn_own, g_dram, tag="l0f"):
    nc = ctx.nc
    PS = ctx.PS
    PSb = ctx.PS.ap().bitcast(BF16)
    identb = ctx.get_identb(sb)
    wqkv_i = [ctx.inp(f"L0f_wqkv{l}", [D, 1536], F32) for l in range(2)]
    wo_i = [ctx.inp(f"L0f_wo{l}", [512, D], F32) for l in range(2)]
    w1_i = [ctx.inp(f"L0f_w1_{l}", [D, 5460], F32) for l in range(2)]
    w2_i = [ctx.inp(f"L0f_w2_{l}", [2730, D], F32) for l in range(2)]
    b1_i = [ctx.inp(f"L0f_b1_{l}", [128, 44], F32) for l in range(2)]
    b2_i = [ctx.inp(f"L0f_b2_{l}", [128, 8], F32) for l in range(2)]
    lng_i = [ctx.inp(f"L0f_lng{l}", [128, 8], F32) for l in range(2)]
    lnb_i = [ctx.inp(f"L0f_lnb{l}", [128, 8], F32) for l in range(2)]
    qs_i = [ctx.inp(f"L0f_qs{l}", [128, 1], F32) for l in range(2)]
    ks_i = [ctx.inp(f"L0f_ks{l}", [128, 1], F32) for l in range(2)]
    rotAq = ctx.inp("L0f_rotAq", [128, TPC], F32)
    rotBq = ctx.inp("L0f_rotBq", [128, TPC], F32)
    rotAk = ctx.inp("L0f_rotAk", [128, L0W], F32)
    rotBk = ctx.inp("L0f_rotBk", [128, L0W], F32)
    halom = ctx.inp("L0f_halomask", [128, TT], F32)
    hvalid = ctx.inp("L0f_hvalid", [128, 1], F32)
    x_spill = nc.dram_tensor(f"xspill_{tag}", [D, TPC], F32, kind="Internal")
    xlb_dram = nc.dram_tensor(f"xlb_{tag}", [D, L0XW], BF16, kind="Internal")
    h3_dram = nc.dram_tensor(f"h3_{tag}", [2730, TPC], BF16, kind="Internal")
    strip_in = nc.dram_tensor(f"strin_{tag}", [D, 520], F32, kind="Internal")
    strip_out = nc.dram_tensor(f"strout_{tag}", [4 * D, 520], F32, kind="Internal")

    for c in range(NCH):
        ctx.dma(x_spill.ap()[c * 128:(c + 1) * 128, :], sb.x[c][:, :])
    ctx.dwait()
    x6f = sb.x[6]
    x6b = sb.x[6].ap().bitcast(BF16)
    x7b = sb.x[7].ap().bitcast(BF16)
    Rb = sb.R.ap().bitcast(BF16)
    ofm = [sb.x[c].ap().bitcast(BF16)[:, 0:TPC] for c in range(4)]
    hnall = [sb.x[4 + c // 2].ap().bitcast(BF16)[:, (c % 2) * TPC:(c % 2 + 1) * TPC]
             for c in range(NCH)]
    sparebf = x6b[:, 0:TT]
    scw = x6f[:, TT:2 * TT]
    ptile = x6b[:, 2048:2688]
    pT = x6b[:, 2688:3328]
    ztb = x6b[0:65, 3328:3456]
    ztk = x6f[:, 1728:1793]
    ztb2 = x6b[:, 3592:3720]
    red = x6f[:, 1880:1881]
    rcp = x6f[:, 1881:1882]
    pairq = x7b[:, 2048:4096]
    qh = x7b[0:64, 0:2048]
    vtok = Rb[:, 0:1365]
    kh = Rb[0:64, 1408:3968]

    sb.mark()
    xl0 = sb._aoff
    XL = [sb.af(L0XW) for _ in range(NCH)]
    XLW = (L0XW + 7) & ~7
    xl_halo3 = sb.A[:, xl0: xl0 + NCH * XLW] \
        .rearrange("p (c w) -> p c w", w=XLW)[:, :, 0:513]
    trip = sb.abf(256)
    ctx.dmag(trip, ctx.const("l0_tri", l0_tri_patterns()))
    hvl = sb.af(8)[:, 0:1]
    ctx.dma(hvl, hvalid)
    hmk = sb.abf(TT)
    ctx.dmag(hmk, halom)
    qsv = sb.af(8)[:, 0:1]
    ksv = sb.af(8)[:, 1:2]
    E2T = sb.abf(2)
    ctx.dmag(E2T, ctx.const("E2T", np.concatenate(
        [np.concatenate([np.ones((64, 1)), np.zeros((64, 1))], 1),
         np.concatenate([np.zeros((64, 1)), np.ones((64, 1))], 1)], 0).astype(BF)))
    E2f = sb.af(128)[0:2, :]
    ctx.dma(E2f, ctx.const("E2f", np.concatenate(
        [np.concatenate([np.ones((1, 64)), np.zeros((1, 64))], 1),
         np.concatenate([np.zeros((1, 64)), np.ones((1, 64))], 1)], 0)
        .astype(np.float32)))
    onesb2 = sb.abf(128)[0:1, :]
    ones1f = sb.af(128)[0:1, :]
    onesv = sb.af(8)[:, 0:1]
    ctx.dma(onesv, ctx.const("ones128f", np.ones((128, 1), dtype=np.float32)))
    b1v = sb.af(44)
    b2v = sb.af(8)
    lngv = sb.af(8)
    lnbv = sb.af(8)
    ctx.dwait()
    nc.vector.memset(onesb2, 1.0)
    nc.vector.memset(ones1f, 1.0)
    ctx.bar()

    qm1 = ctx.qsnap("m1")

    sb.mark()
    stg = sb.abf(2048)
    stgH = sb.abf(NCH * 520)
    stgH3 = stgH.rearrange("p (c w) -> p c w", w=520)[:, :, 0:513]
    ctx.dma(stgH3, xnag[bass.ts(qm1, 1024), 1535:2048]
            .rearrange("(cc p) t -> p cc t", p=128))
    for c in range(NCH):
        ctx.dma(stg, xn_own[c * 128:(c + 1) * 128, :])
        ctx.dwait()
        nc.vector.tensor_copy(XL[c][:, 513:2561], stg)
        ctx.edge("vector", "sync")
    nc.vector.tensor_copy(xl_halo3, stgH3)
    ctx.bar()
    for c in range(NCH):
        nc.vector.tensor_scalar(XL[c][:, 0:513], XL[c][:, 0:513], hvl, None,
                                ALU.mult)
        nc.vector.memset(XL[c][:, 2561:L0XW], 0.0)
    ctx.bar()
    sb.release()

    def halo_exchange():
        ctx.edge("vector", "sync")
        for c in range(NCH):
            ctx.dma(strip_in.ap()[c * 128:(c + 1) * 128, 0:513],
                    XL[c][:, 2048:2561])
        ctx.dwait()
        ctx.coll("AllGather", ALU.bypass, GROUPS_BATCH, strip_in.ap(),
                 strip_out.ap())
        ctx.cwait()
        ctx.dma(xl_halo3, strip_out.ap()[bass.ts(qm1, 1024), 0:513]
                .rearrange("(cc p) t -> p cc t", p=128))
        ctx.dwait()
        for c in range(NCH):
            nc.vector.tensor_scalar(XL[c][:, 0:513], XL[c][:, 0:513], hvl, None,
                                    ALU.mult)
        ctx.bar()

    def gen_pair(l, pr, pairk, pairv, tabA, tabB, qsp, qsw, nrm, wsl, xlrhs):
        for proj, dst, width, rA, rB in ((1, pairk, L0W, rotAk, rotBk),
                                         (2, pairv, L0W, None, None),
                                         (0, pairq, TPC, rotAq, rotBq)):
            ctx.edge("tensor", "gpsimd")
            ctx.dmag(wsl, wqkv_i[l][:, proj * 512 + pr * 128:
                                    proj * 512 + (pr + 1) * 128]
                     .rearrange("(c p) o -> p c o", p=128))
            ctx.dwait()
            for sp in range(width // TT):
                g0 = sp * TT
                gsp = sp + (1 if proj == 0 else 0)
                ctx.edge("tensor", "sync")
                for c in range(NCH):
                    xoff = (1 if c < 4 else 0) + gsp * TT
                    ctx.dma(xlrhs[:, c * TT:(c + 1) * TT],
                            xlb_dram.ap()[c * 128:(c + 1) * 128,
                                          xoff:xoff + TT])
                nc.tensor.wait_ge(ctx.dsem, ctx.dcnt)
                for c in range(NCH):
                    nc.tensor.matmul(PS[:, 0:TT], wsl[:, c * 128:(c + 1) * 128],
                                     xlrhs[:, c * TT:(c + 1) * TT],
                                     start=(c == 0), stop=(c == NCH - 1))
                ctx.edge("tensor", "vector", "scalar")
                if proj == 2 or L0F_GSTEP < 2:
                    nc.vector.tensor_copy(dst[:, g0:g0 + TT], PS[:, 0:TT])
                    ctx.edge("vector", "tensor")
                    continue
                nc.vector.tensor_copy(qsp, PS[:, 0:TT])
                if L0F_GSTEP == 19:
                    nc.vector.tensor_copy(dst[:, g0:g0 + TT], qsp)
                    ctx.edge("vector", "tensor")
                    continue
                ctx.edge("vector", "scalar")
                nc.scalar.activation(sparebf, qsp, AF.Square)
                ctx.edge("scalar", "tensor")
                if L0F_GSTEP == 20:
                    nc.vector.tensor_copy(dst[:, g0:g0 + TT], qsp)
                    ctx.edge("vector", "tensor")
                    continue
                nc.tensor.matmul(PS[0:2, 2 * TT:3 * TT], E2T, sparebf,
                                 start=True, stop=True)
                ctx.edge("tensor", "vector")
                if L0F_GSTEP == 21:
                    nc.vector.tensor_copy(dst[:, g0:g0 + TT], qsp)
                    ctx.edge("vector", "tensor")
                    continue
                n1 = nrm[0:2, 0:TT]
                n2 = nrm[0:2, TT:2 * TT]
                nc.vector.tensor_scalar(n1, PS[0:2, 2 * TT:3 * TT], 1e-24, None,
                                        ALU.max)
                ctx.edge("vector", "scalar")
                nc.scalar.activation(n2, n1, AF.Sqrt)
                ctx.edge("scalar", "vector")
                nc.vector.reciprocal(n1, n2)
                ctx.edge("vector", "tensor")
                if L0F_GSTEP == 22:
                    nc.vector.tensor_copy(dst[:, g0:g0 + TT], qsp)
                    ctx.edge("vector", "tensor")
                    continue
                nc.tensor.matmul(PS[:, 2 * TT:3 * TT], E2f, n1,
                                 start=True, stop=True)
                ctx.edge("tensor", "vector")
                if L0F_GSTEP == 23:
                    nc.vector.tensor_copy(dst[:, g0:g0 + TT], qsp)
                    ctx.edge("vector", "tensor")
                    continue
                nc.vector.tensor_tensor(qsp, qsp, PS[:, 2 * TT:3 * TT], ALU.mult)
                if proj == 0:
                    nc.vector.tensor_scalar(qsp, qsp, qsv, 8.0, ALU.mult,
                                            ALU.mult)
                else:
                    nc.vector.tensor_scalar(qsp, qsp, ksv, None, ALU.mult)
                if L0F_GSTEP < 3:
                    nc.vector.tensor_copy(dst[:, g0:g0 + TT], qsp)
                    ctx.edge("vector", "tensor")
                    continue
                for bb in range(4):
                    nc.vector.tensor_copy(qsw[32 * bb:32 * bb + 32],
                                          qsp[32 * (bb ^ 1):32 * (bb ^ 1) + 32])
                ctx.edge("vector", "sync")
                ctx.dma(tabA, rA[:, g0:g0 + TT])
                ctx.dma(tabB, rB[:, g0:g0 + TT])
                nc.vector.wait_ge(ctx.dsem, ctx.dcnt)
                nc.vector.tensor_tensor(scw, qsp, tabA, ALU.mult)
                nc.vector.tensor_tensor(qsw, qsw, tabB, ALU.mult)
                nc.vector.tensor_add(scw, scw, qsw)
                nc.vector.tensor_copy(dst[:, g0:g0 + TT], scw)
                ctx.edge("vector", "tensor")

    def attn_head(pr, hh, pairk, pairv):
        rows = slice(hh * 64, hh * 64 + 64)
        nc.vector.tensor_copy(qh, pairq[rows, :])
        nc.vector.tensor_copy(kh, pairk[rows, :])
        nc.vector.memset(vtok, 1.0)
        ctx.edge("vector", "tensor")
        for kt0 in range(0, 20, 4):
            for j in range(4):
                kt = kt0 + j
                nc.tensor.transpose(PSb[:, 4096 + j * 128:4096 + (j + 1) * 128],
                                    pairv[:, kt * 128:(kt + 1) * 128], identb)
            ctx.edge("tensor", "vector")
            for j in range(4):
                kt = kt0 + j
                nc.vector.tensor_copy(vtok[:, kt * 65:kt * 65 + 64],
                                      PSb[:, 4096 + j * 128 + hh * 64:
                                          4096 + j * 128 + hh * 64 + 64])
            ctx.edge("vector", "tensor")
        for qs in range(16):
            k0 = qs * 128
            lq = qh[0:64, qs * 128:(qs + 1) * 128]
            nc.tensor.matmul(PS[:, 0:TT], lq, kh[0:64, k0:k0 + TT],
                             start=True, stop=False)
            if qs < 4:
                wm = TT - qs * 128
                nc.tensor.matmul(PS[:, 0:wm], identb, hmk[:, 0:wm],
                                 start=False, stop=False)
            nc.tensor.matmul(PS[:, 0:128], identb, trip[:, 0:128],
                             start=False, stop=True)
            nc.tensor.matmul(PS[:, TT:TT + 128], lq, kh[0:64, k0 + TT:k0 + 640],
                             start=True, stop=False)
            nc.tensor.matmul(PS[:, TT:TT + 128], identb, trip[:, 128:256],
                             start=False, stop=True)
            ctx.edge("tensor", "vector")
            nc.vector.reduce_max(red, PS[:, 0:640], axis=AX)
            ctx.bar()
            nc.vector.tensor_scalar(red, red, -1.0, None, ALU.mult)
            ctx.edge("vector", "scalar")
            nc.scalar.activation(ptile, PS[:, 0:640], AF.Exp, bias=red,
                                 scale=1.0)
            ctx.edge("scalar", "tensor", "vector")
            for j in range(5):
                nc.tensor.transpose(PSb[:, 4096 + j * 128:4096 + (j + 1) * 128],
                                    ptile[:, j * 128:(j + 1) * 128], identb)
            ctx.edge("tensor", "vector")
            for j in range(5):
                nc.vector.tensor_copy(pT[:, j * 128:(j + 1) * 128],
                                      PSb[:, 4096 + j * 128:4096 + (j + 1) * 128])
            ctx.edge("vector", "tensor")
            for j in range(5):
                kt = k0 // 128 + j
                nc.tensor.matmul(PS[0:65, 3 * TT:3 * TT + 128],
                                 vtok[:, kt * 65:(kt + 1) * 65],
                                 pT[:, j * 128:(j + 1) * 128],
                                 start=(j == 0), stop=(j == 4))
            ctx.edge("tensor", "scalar")
            nc.scalar.activation(ztb, PS[0:65, 3 * TT:3 * TT + 128], AF.Copy)
            ctx.edge("scalar", "tensor")
            nc.tensor.transpose(PSb[:, 4096:4096 + 65], ztb,
                                identb[0:65, 0:65])
            ctx.edge("tensor", "vector")
            nc.vector.tensor_copy(ztk, PSb[:, 4096:4096 + 65])
            ctx.bar()
            nc.vector.reciprocal(rcp, ztk[:, 64:65])
            ctx.bar()
            nc.vector.tensor_scalar(ztk[:, 0:64], ztk[:, 0:64], rcp, None,
                                    ALU.mult)
            ctx.bar()
            nc.vector.tensor_copy(ztb2[:, hh * 64:hh * 64 + 64], ztk[:, 0:64])
            ctx.edge("vector", "tensor")
            nc.tensor.transpose(PSb[:, 4352:4352 + 128], ztb2, identb)
            ctx.edge("tensor", "vector")
            nc.vector.tensor_copy(ofm[pr][rows, qs * 128:(qs + 1) * 128],
                                  PSb[rows, 4352:4352 + 128])
            ctx.edge("vector", "tensor")

    step = L0F_STEP
    for l in range(2 if step >= 10 else 1):
        ctx.dma(qsv, qs_i[l])
        ctx.dma(ksv, ks_i[l])
        ctx.dwait()
        sb.mark()
        pairk = sb.abf(L0W)
        pairv = sb.abf(L0W)
        tabA = sb.af(TT)
        tabB = sb.af(TT)
        qsp = sb.af(TT)
        qsw = sb.af(TT)
        nrm = sb.af(2 * TT)
        wsl = sb.abf(1024)
        xlrhs = sb.abf(4096)
        for c in range(NCH):
            ctx.dmag(xlb_dram.ap()[c * 128:(c + 1) * 128, 0:2561],
                     XL[c][:, 0:2561])
        nc.vector.memset(ztb2, 0.0)
        ctx.dwait()
        if step >= 2:
            for pr in range(4 if step >= 4 else 1):
                gen_pair(l, pr, pairk, pairv, tabA, tabB, qsp, qsw, nrm, wsl,
                         xlrhs)
                if step >= 3:
                    for hh in range(2):
                        attn_head(pr, hh, pairk, pairv)
        if 4 <= step <= 6 and l == 0:
            dbg_o = nc.dram_tensor("dbg_o", [512, TPC], F32,
                                   kind="ExternalOutput").ap()
            ost = qsp
            for pr2 in range(4):
                for t2 in range(NTT):
                    nc.vector.tensor_copy(ost, ofm[pr2][:, t2 * TT:(t2 + 1) * TT])
                    ctx.edge("vector", "sync")
                    ctx.dma(dbg_o[pr2 * 128:(pr2 + 1) * 128,
                                  t2 * TT:(t2 + 1) * TT], ost)
                    nc.vector.wait_ge(ctx.dsem, ctx.dcnt)
        lwo = wsl[:, 0:512]
        for co in range(NCH if step >= 5 else 0):
            ctx.edge("tensor", "gpsimd")
            ctx.dmag(lwo, wo_i[l][:, co * 128:(co + 1) * 128]
                     .rearrange("(c p) o -> p c o", p=128))
            ctx.dwait()
            for t in range(NTT):
                for ci in range(4):
                    nc.tensor.matmul(PS[:, 0:TT], lwo[:, ci * 128:(ci + 1) * 128],
                                     ofm[ci][:, t * TT:(t + 1) * TT],
                                     start=(ci == 0), stop=(ci == 3))
                ctx.edge("tensor", "vector")
                osl = slice(513 + t * TT, 513 + (t + 1) * TT)
                nc.vector.tensor_add(XL[co][:, osl], XL[co][:, osl], PS[:, 0:TT])
                ctx.edge("vector", "tensor")
        sb.release()
        if step < 6:
            break
        halo_exchange()

        ctx.dma(b1v, b1_i[l])
        ctx.dma(b2v, b2_i[l])
        ctx.dma(lngv, lng_i[l])
        ctx.dma(lnbv, lnb_i[l])
        ctx.dwait()
        sb.mark()
        wbuf = sb.abf(4096)
        h3t2 = sb.abf(1024)
        f3sq = sb.af(TT)
        f3gel = sb.af(TT)
        f3scr = sb.abf(TT)
        for t in range(NTT if step >= 7 else 0):
            xoffs = [513 + t * TT - (0 if c < 4 else 1) for c in range(NCH)]
            for c in range(NCH):
                nc.tensor.matmul(PS[0:1, 2 * TT:3 * TT], onesv,
                                 XL[c][:, xoffs[c]:xoffs[c] + TT],
                                 start=(c == 0), stop=(c == NCH - 1))
            for c in range(NCH):
                nc.scalar.activation(f3sq, XL[c][:, xoffs[c]:xoffs[c] + TT],
                                     AF.Square)
                ctx.edge("scalar", "tensor")
                nc.tensor.matmul(PS[0:1, 3 * TT:4 * TT], onesv, f3sq,
                                 start=(c == 0), stop=(c == NCH - 1))
                ctx.edge("tensor", "scalar")
            ctx.edge("tensor", "vector")
            mean = sb.R[0:1, 0:TT]
            var = sb.R[0:1, TT:2 * TT]
            rstd = sb.R[0:1, 2 * TT:3 * TT]
            brow = sb.R[0:1, 3 * TT:4 * TT]
            nc.vector.tensor_scalar(mean, PS[0:1, 2 * TT:3 * TT], 1.0 / D, None,
                                    ALU.mult)
            nc.vector.tensor_scalar(var, PS[0:1, 3 * TT:4 * TT], 1.0 / D, None,
                                    ALU.mult)
            nc.vector.tensor_tensor(rstd, mean, mean, ALU.mult)
            nc.vector.tensor_sub(var, var, rstd)
            nc.vector.tensor_scalar(var, var, 1e-5, None, ALU.add)
            ctx.edge("vector", "scalar")
            nc.scalar.activation(rstd, var, AF.Sqrt)
            ctx.edge("scalar", "vector")
            nc.vector.reciprocal(rstd, rstd)
            nc.vector.tensor_tensor(brow, mean, rstd, ALU.mult)
            nc.vector.tensor_scalar(brow, brow, -1.0, None, ALU.mult)
            ctx.edge("vector", "tensor")
            nc.tensor.matmul(PS[:, 2 * TT:3 * TT], ones1f, rstd,
                             start=True, stop=True)
            nc.tensor.matmul(PS[:, 3 * TT:4 * TT], ones1f, brow,
                             start=True, stop=True)
            ctx.edge("tensor", "vector")
            for c in range(NCH):
                nc.vector.tensor_tensor(f3gel, XL[c][:, xoffs[c]:xoffs[c] + TT],
                                        PS[:, 2 * TT:3 * TT], ALU.mult)
                nc.vector.tensor_add(f3gel, f3gel, PS[:, 3 * TT:4 * TT])
                nc.vector.tensor_scalar(hnall[c][:, t * TT:(t + 1) * TT], f3gel,
                                        lngv[:, c:c + 1], lnbv[:, c:c + 1],
                                        ALU.mult, ALU.add)
            ctx.edge("vector", "tensor")
        gtg = {}

        def f2_issue(j):
            s = j % 2
            wj = 128 if j < 21 else 42
            ctx.dmag(wbuf[:, s * 2048:s * 2048 + NCH * wj]
                     .rearrange("p (c o) -> p c o", o=wj),
                     w1_i[l][:, j * 128:j * 128 + wj]
                     .rearrange("(c p) o -> p c o", p=128))
            ctx.dmag(wbuf[:, s * 2048 + 1024:s * 2048 + 1024 + NCH * wj]
                     .rearrange("p (c o) -> p c o", o=wj),
                     w1_i[l][:, 2730 + j * 128:2730 + j * 128 + wj]
                     .rearrange("(c p) o -> p c o", p=128))
            gtg[j] = ctx.gcnt

        if step >= 8:
            f2_issue(0)
        for j in range(22 if step >= 8 else 0):
            s = j % 2
            wj = 128 if j < 21 else 42
            if j + 1 < 22:
                if j >= 1:
                    ctx.edge("tensor", "gpsimd")
                f2_issue(j + 1)
            nc.tensor.wait_ge(ctx.gsem, gtg[j])
            for t in range(NTT):
                for c in range(NCH):
                    nc.tensor.matmul(PS[0:wj, 0:TT],
                                     wbuf[:, s * 2048 + c * wj:
                                          s * 2048 + (c + 1) * wj],
                                     hnall[c][:, t * TT:(t + 1) * TT],
                                     start=(c == 0), stop=(c == NCH - 1))
                for c in range(NCH):
                    nc.tensor.matmul(PS[0:wj, TT:2 * TT],
                                     wbuf[:, s * 2048 + 1024 + c * wj:
                                          s * 2048 + 1024 + (c + 1) * wj],
                                     hnall[c][:, t * TT:(t + 1) * TT],
                                     start=(c == 0), stop=(c == NCH - 1))
                ctx.edge("tensor", "scalar", "vector")
                nc.vector.tensor_scalar(f3sq[0:wj, :], PS[0:wj, 0:TT],
                                        b1v[0:wj, j:j + 1], None, ALU.add)
                nc.scalar.activation(f3gel[0:wj, :], PS[0:wj, TT:2 * TT],
                                     AF.Gelu, bias=b1v[0:wj, 22 + j:23 + j],
                                     scale=1.0)
                ctx.edge("scalar", "vector")
                nc.vector.wait_ge(ctx.dsem, ctx.dcnt)
                nc.vector.tensor_tensor(f3scr[0:wj, :], f3sq[0:wj, :],
                                        f3gel[0:wj, :], ALU.mult)
                ctx.edge("vector", "sync", "tensor")
                ctx.edge("scalar", "tensor")
                ctx.dma(h3_dram.ap()[j * 128:j * 128 + wj, t * TT:(t + 1) * TT],
                        f3scr[0:wj, :])
        for t in range(NTT if step >= 9 else 0):
            dtg = {}

            def f3_issue(cc):
                s = cc % 2
                wj = 128 if cc < 21 else 42
                ctx.dmag(wbuf[0:wj, s * 1024:(s + 1) * 1024],
                         w2_i[l][cc * 128:cc * 128 + wj, :])
                ctx.dma(h3t2[0:wj, s * TT:(s + 1) * TT],
                        h3_dram.ap()[cc * 128:cc * 128 + wj, t * TT:(t + 1) * TT])
                dtg[cc] = (ctx.dcnt, ctx.gcnt)

            if t == 0:
                nc.sync.wait_ge(ctx.dsem, ctx.dcnt)
                ctx.bar()
            ctx.edge("tensor", "gpsimd", "sync")
            f3_issue(0)
            for cc in range(22):
                s = cc % 2
                wj = 128 if cc < 21 else 42
                if cc + 1 < 22:
                    if cc >= 1:
                        ctx.edge("tensor", "gpsimd", "sync")
                    f3_issue(cc + 1)
                nc.tensor.wait_ge(ctx.dsem, dtg[cc][0])
                nc.tensor.wait_ge(ctx.gsem, dtg[cc][1])
                for co in range(NCH):
                    nc.tensor.matmul(PS[:, co * TT:(co + 1) * TT],
                                     wbuf[0:wj, s * 1024 + co * 128:
                                          s * 1024 + (co + 1) * 128],
                                     h3t2[0:wj, s * TT:(s + 1) * TT],
                                     start=(cc == 0), stop=(cc == 21))
            ctx.edge("tensor", "vector")
            for co in range(NCH):
                osl = slice(513 + t * TT, 513 + (t + 1) * TT)
                nc.vector.tensor_scalar(f3sq, PS[:, co * TT:(co + 1) * TT],
                                        b2v[:, co:co + 1], None, ALU.add)
                ctx.bar()
                if l == 0:
                    nc.vector.tensor_add(XL[co][:, osl], XL[co][:, osl], f3sq)
                    ctx.bar()
                else:
                    nc.vector.tensor_add(f3sq, f3sq, XL[co][:, osl])
                    ctx.edge("vector", "scalar")
                    nc.scalar.activation(f3gel, f3sq, AF.Silu)
                    ctx.edge("scalar", "sync")
                    ctx.dma(g_dram[co * 128:(co + 1) * 128,
                                   t * TT:(t + 1) * TT], f3gel)
                    nc.scalar.wait_ge(ctx.dsem, ctx.dcnt)
                    nc.vector.wait_ge(ctx.dsem, ctx.dcnt)
            ctx.edge("vector", "tensor")
        sb.release()
        if l == 0:
            halo_exchange()

    if step < 10 and step != 4:
        ctx.edge("vector", "sync")
        for c in range(NCH):
            ctx.dma(g_dram[c * 128:(c + 1) * 128, :], XL[c][:, 513:2561])
        ctx.dwait()
    sb.release()
    for c in range(NCH):
        ctx.dma(sb.x[c][:, :], x_spill.ap()[c * 128:(c + 1) * 128, :])
    ctx.dwait()


def l0f_in_map(inp, c):
    q = c % 4
    im = {}
    f32 = lambda a: np.ascontiguousarray(np.asarray(a, np.float32))
    for l in range(2):
        im[f"L0f_wqkv{l}"] = f32(inp["L0_ffn_wqkv"][l])
        im[f"L0f_wo{l}"] = f32(inp["L0_ffn_wo"][l])
        im[f"L0f_w1_{l}"] = f32(inp["L0_ffn_w1"][l])
        im[f"L0f_w2_{l}"] = f32(inp["L0_ffn_w2"][l])
        b1 = f32(inp["L0_ffn_b1"][l])
        Bm = np.zeros((128, 44), np.float32)
        for j in range(22):
            wj = 128 if j < 21 else 42
            Bm[0:wj, j] = b1[j * 128:j * 128 + wj]
            Bm[0:wj, 22 + j] = b1[2730 + j * 128:2730 + j * 128 + wj]
        im[f"L0f_b1_{l}"] = Bm
        im[f"L0f_b2_{l}"] = f32(inp["L0_ffn_b2"][l]).reshape(8, 128).T.copy()
        im[f"L0f_lng{l}"] = f32(inp["L0_ffn_ln_g"][l]).reshape(8, 128).T.copy()
        im[f"L0f_lnb{l}"] = f32(inp["L0_ffn_ln_b"][l]).reshape(8, 128).T.copy()
        im[f"L0f_qs{l}"] = np.tile(f32(inp["L0_ffn_q_scale"][l]), 2).reshape(128, 1)
        im[f"L0f_ks{l}"] = np.tile(f32(inp["L0_ffn_k_scale"][l]), 2).reshape(128, 1)
    Aq, Bq, Ak, Bk = l0_rot_tables(q)
    im["L0f_rotAq"], im["L0f_rotBq"] = Aq, Bq
    im["L0f_rotAk"], im["L0f_rotBk"] = Ak, Bk
    bs = (q == 0)
    im["L0f_halomask"] = np.full((128, TT), -1e9 if bs else 0.0, np.float32)
    im["L0f_hvalid"] = np.full((128, 1), 0.0 if bs else 1.0, np.float32)
    return im


def l3_band_pattern():
    i = np.arange(128)[:, None]
    j = np.arange(134)[None, :]
    return np.where((j >= i) & (j <= i + 6), 0.0, -1e9).astype(BF)


def l3_edge_patterns():
    ln2, ln3 = float(np.log(2.0)), float(np.log(3.0))
    first = np.array([[0, 0, 0, 0, -1e9, -1e9, ln3],
                      [0, 0, 0, 0, 0, -1e9, ln2],
                      [0, 0, 0, 0, 0, 0, 0]], dtype=np.float32)
    last = np.array([[0, 0, 0, 0, 0, 0, 0],
                     [ln2, -1e9, 0, 0, 0, 0, 0],
                     [ln3, -1e9, -1e9, 0, 0, 0, 0]], dtype=np.float32)
    return np.concatenate([first, np.zeros((3, 1), np.float32), last],
                          axis=1).astype(BF)


def emit_l3_attn(ctx, sb, xnag, xn_own, g_dram, tag="l3a"):
    nc = ctx.nc
    PS = ctx.PS
    PSb = ctx.PS.ap().bitcast(BF16)
    identb = ctx.get_identb(sb)
    wqkvT = ctx.inp("L3w_qkvT", [D, 3072], F32)
    bqkv = ctx.inp("L3b_qkv", [128, 24], F32)
    wlatBD = ctx.inp("L3w_latBD", [128, 128], F32)
    blat2 = ctx.inp("L3b_lat2", [128, 1], F32)
    wprojT = ctx.inp("L3w_projT", [D, D], F32)
    bproj = ctx.inp("L3b_proj", [128, 8], F32)
    em_first = ctx.inp("L3_em_first", [128, 1], F32)
    em_last = ctx.inp("L3_em_last", [128, 1], F32)
    sb.mark()
    ss = float(1.0 / np.sqrt(np.sqrt(64.0)))
    Wqs = sb.abf(NCH * 128)
    Bia = sb.af(24)
    ctx.dma(Bia, bqkv)
    BD = sb.abf(128)
    ctx.dmag(BD, wlatBD)
    Bl2 = sb.af(8)[:, 0:1]
    ctx.dma(Bl2, blat2)
    Wp = sb.abf(NCH * 1024)
    ctx.dmag(Wp, wprojT.rearrange("(c p) o -> p c o", p=128))
    Bp = sb.af(8)
    ctx.dma(Bp, bproj)
    emf = sb.af(8)[:, 0:1]
    ctx.dma(emf, em_first)
    eml = sb.af(8)[:, 0:1]
    ctx.dma(eml, em_last)
    bandp = sb.abf(134)
    ctx.dmag(bandp, ctx.const("l3_band", l3_band_pattern()))
    edgep = sb.abf(16)[0:3, :]
    ctx.dmag(edgep[0:3, 0:15], ctx.const("l3_edge", l3_edge_patterns()))
    obuf = [sb.abf(TPC) for _ in range(NCH)]
    ctx.dwait()

    qm1 = ctx.qsnap("m1")
    qp1 = ctx.qsnap("p1")

    xnt = sb.abf(4096)
    xnt_h = sb.abf(48)
    xnt_s = sb.abf(128)
    THW = TPC + 8
    TH = 3
    nc.vector.memset(xnt_h, 0.0)
    nc.vector.memset(xnt_s, 0.0)
    ctx.bar()
    h3 = xnt_h.rearrange("p (c o) -> p c o", o=6)
    s3v = xnt_s.rearrange("p (c o) -> p c o", o=16)
    ctx.dma(h3[:, :, 0:3],
            xnag[bass.ts(qm1, 1024), 2045:2048].rearrange("(c p) t -> p c t", p=128))
    ctx.dma(h3[:, :, 3:6],
            xnag[bass.ts(qp1, 1024), 0:3].rearrange("(c p) t -> p c t", p=128))
    ctx.dma(s3v[:, :, 0:6],
            xnag[0:1024, 0:6].rearrange("(c p) t -> p c t", p=128))
    ctx.dma(s3v[:, :, 6:13],
            xnag[3 * 1024:4 * 1024, 2041:2048].rearrange("(c p) t -> p c t", p=128))
    ctx.dwait()

    qh = [sb.abf(TPC), sb.abf(TPC)]
    kh = [sb.abf(THW), sb.abf(THW)]
    vh = [sb.abf(THW), sb.abf(THW)]
    qsp = [sb.abf(16), sb.abf(16)]
    ksp = [sb.abf(16), sb.abf(16)]
    vsp = [sb.abf(16), sb.abf(16)]
    pair = sb.abf(THW)
    ptile = sb.abf(134 + 10)
    pT1 = sb.abf(128)
    pT2 = sb.abf(128)
    vt1 = sb.abf(64)
    vt2 = sb.abf(64)
    ztk = sb.af(64)
    ztb = sb.abf(64)
    red = sb.af(8)[:, 0:1]
    rcp = sb.af(8)[:, 1:2]
    spm = sb.af(16)
    spmb = sb.abf(16)
    sp_p = sb.abf(16)
    sp_pT = sb.abf(16)
    sp_v = sb.abf(80)
    sp_vT = sb.abf(80)
    sp_z = sb.af(64)
    spzf = sb.af(64)
    sred = sb.af(8)[:, 2:3]
    srcp = sb.af(8)[:, 3:4]
    ebufs = dict(edgep=edgep, sp_p=sp_p, sp_pT=sp_pT, sp_v=sp_v, sp_vT=sp_vT,
                 sp_z=sp_z, spzf=spzf, red=sred, rcp=srcp)

    for hc in range(NCH):
        for sec, width, dst, dsts in ((1, THW, kh, ksp), (2, THW, vh, vsp),
                                      (0, TPC, qh, qsp)):
            bia = Bia[:, sec * 8 + hc: sec * 8 + hc + 1]
            ctx.edge("tensor", "gpsimd")
            ctx.dmag(Wqs, wqkvT[:, sec * 1024 + hc * 128: sec * 1024 + (hc + 1) * 128]
                     .rearrange("(c p) o -> p c o", p=128))
            ctx.dwait()
            for t in range(NTT):
                ctx.edge("tensor", "sync")
                ctx.dma(xnt, xn_own[0:1024, t * TT:(t + 1) * TT]
                        .rearrange("(c p) t -> p c t", p=128))
                ctx.dwait()
                for c in range(NCH):
                    nc.tensor.matmul(PS[:, 0:TT],
                                     Wqs[:, c * 128:(c + 1) * 128],
                                     xnt[:, c * TT:(c + 1) * TT],
                                     start=(c == 0), stop=(c == NCH - 1))
                ctx.edge("tensor", "vector")
                off = TH if sec != 0 else 0
                nc.vector.tensor_scalar(pair[:, off + t * TT: off + (t + 1) * TT],
                                        PS[:, 0:TT], bia, None, ALU.add)
                ctx.edge("vector", "tensor")
            if sec != 0:
                for c in range(NCH):
                    nc.tensor.matmul(PS[:, 3 * TT:3 * TT + 6],
                                     Wqs[:, c * 128:(c + 1) * 128],
                                     xnt_h[:, c * 6:(c + 1) * 6],
                                     start=(c == 0), stop=(c == NCH - 1))
                ctx.edge("tensor", "vector")
                nc.vector.tensor_scalar(spm[:, 0:6], PS[:, 3 * TT:3 * TT + 6],
                                        bia, None, ALU.add)
                ctx.bar()
                nc.vector.tensor_copy(pair[:, 0:3], spm[:, 0:3])
                nc.vector.tensor_copy(pair[:, TH + TPC:TH + TPC + 3], spm[:, 3:6])
                nc.vector.memset(pair[:, TH + TPC + 3:THW], 0.0)
                ctx.edge("vector", "tensor")
            for c in range(NCH):
                nc.tensor.matmul(PS[:, 3 * TT:3 * TT + 16],
                                 Wqs[:, c * 128:(c + 1) * 128],
                                 xnt_s[:, c * 16:(c + 1) * 16],
                                 start=(c == 0), stop=(c == NCH - 1))
            ctx.edge("tensor", "vector")
            nc.vector.tensor_scalar(spm, PS[:, 3 * TT:3 * TT + 16], bia, None, ALU.add)
            ctx.bar()
            if sec != 0:
                mul = ss if sec == 1 else 1.0
                for seg in range(0, THW, TT):
                    w = min(TT, THW - seg)
                    nc.vector.tensor_copy(pair.bitcast(BF16)[:, 0:1] if False else pair[:, seg:seg + 1], pair[:, seg:seg + 1]) if False else None
                    nc.tensor.matmul(PS[:, 0:w], BD, pair[:, seg:seg + w],
                                     start=True, stop=True)
                    ctx.edge("tensor", "vector")
                    nc.vector.tensor_scalar(pair[:, seg:seg + w], PS[:, 0:w],
                                            Bl2, mul, ALU.add, ALU.mult)
                    ctx.edge("vector", "tensor")
                nc.vector.tensor_copy(spmb, spm)
                ctx.edge("vector", "tensor")
                nc.tensor.matmul(PS[:, 0:16], BD, spmb, start=True, stop=True)
                ctx.edge("tensor", "vector")
                nc.vector.tensor_scalar(spm, PS[:, 0:16], Bl2, mul, ALU.add, ALU.mult)
                ctx.bar()
            else:
                nc.vector.tensor_scalar(pair[:, 0:TPC], pair[:, 0:TPC], ss, None,
                                        ALU.mult)
                nc.vector.tensor_scalar(spm, spm, ss, None, ALU.mult)
                ctx.bar()
            for hh in range(2):
                rows = slice(0, 64) if hh == 0 else slice(64, 128)
                nc.vector.tensor_copy(dst[hh][0:64, 0:width], pair[rows, 0:width])
                nc.vector.tensor_copy(dsts[hh][0:64, :], spm[rows, :])
            ctx.bar()
        for hh in range(2):
            for qs in range(16):
                q0 = qs * 128
                nc.tensor.matmul(PS[:, 0:134], qh[hh][0:64, q0:q0 + 128],
                                 kh[hh][0:64, q0:q0 + 134], start=True, stop=False)
                nc.tensor.matmul(PS[:, 0:134], identb, bandp[:, 0:134],
                                 start=False, stop=True)
                ctx.edge("tensor", "vector")
                nc.vector.reduce_max(red, PS[:, 0:134], axis=AX)
                ctx.bar()
                nc.vector.tensor_scalar(red, red, -1.0, None, ALU.mult)
                ctx.edge("vector", "scalar")
                nc.scalar.activation(ptile[:, 0:134], PS[:, 0:134], AF.Exp,
                                     bias=red, scale=1.0)
                ctx.edge("scalar", "vector", "tensor")
                nc.vector.tensor_reduce(red, ptile[:, 0:134], AX, ALU.add)
                ctx.bar()
                nc.vector.reciprocal(rcp, red)
                nc.vector.memset(ptile[:, 134:144], 0.0)
                ctx.edge("vector", "tensor")
                nc.tensor.transpose(PSb[:, 4096:4096 + 128], ptile[:, 0:128], identb)
                nc.tensor.transpose(PSb[0:8, 4352:4352 + 128], ptile[:, 128:136],
                                    identb)
                nc.tensor.transpose(PSb[:, 4608:4608 + 64],
                                    vh[hh][0:64, q0:q0 + 128], identb[0:64, 0:64])
                nc.tensor.transpose(PSb[0:8, 4864:4864 + 64],
                                    vh[hh][0:64, q0 + 128:q0 + 136],
                                    identb[0:64, 0:64])
                ctx.edge("tensor", "vector")
                nc.vector.tensor_copy(pT1, PSb[:, 4096:4096 + 128])
                nc.vector.tensor_copy(pT2[0:8, 0:128], PSb[0:8, 4352:4352 + 128])
                nc.vector.tensor_copy(vt1[:, 0:64], PSb[:, 4608:4608 + 64])
                nc.vector.tensor_copy(vt2[0:8, 0:64], PSb[0:8, 4864:4864 + 64])
                ctx.edge("vector", "tensor")
                nc.tensor.matmul(PS[:, 3 * TT:3 * TT + 64], pT1, vt1[:, 0:64],
                                 start=True, stop=False)
                nc.tensor.matmul(PS[:, 3 * TT:3 * TT + 64], pT2[0:8, 0:128],
                                 vt2[0:8, 0:64], start=False, stop=True)
                ctx.edge("tensor", "vector")
                nc.vector.tensor_scalar(ztk, PS[:, 3 * TT:3 * TT + 64], rcp, None,
                                        ALU.mult)
                ctx.bar()
                if qs == 0:
                    _l3_edge_fix(ctx, sb, qsp[hh], ksp[hh], vsp[hh], ztk, emf,
                                 True, identb, ebufs)
                elif qs == 15:
                    _l3_edge_fix(ctx, sb, qsp[hh], ksp[hh], vsp[hh], ztk, eml,
                                 False, identb, ebufs)
                nc.vector.tensor_copy(ztb, ztk)
                ctx.edge("vector", "tensor")
                nc.tensor.transpose(PSb[0:64, 4096:4096 + 128], ztb[:, 0:64], identb)
                ctx.edge("tensor", "vector")
                nc.vector.tensor_copy(obuf[hc][(0 if hh == 0 else 64):
                                               (64 if hh == 0 else 128), q0:q0 + 128],
                                      PSb[0:64, 4096:4096 + 128])
                ctx.edge("vector", "tensor")

    gt = sb.af(TT)
    tmp = sb.af(TT)
    for t in range(NTT):
        tsl = slice(t * TT, (t + 1) * TT)
        for co in range(NCH):
            for ci in range(NCH):
                nc.tensor.matmul(PS[:, 0:TT], Wp[:, ci * 1024 + co * 128:
                                                 ci * 1024 + (co + 1) * 128],
                                 obuf[ci][:, tsl], start=(ci == 0),
                                 stop=(ci == NCH - 1))
            ctx.edge("tensor", "vector")
            ctx.edge("vector", "sync")
            ctx.dma(gt, g_dram.ap()[co * 128:(co + 1) * 128, tsl])
            nc.sync.wait_ge(ctx.dsem, ctx.dcnt)
            ctx.bar()
            nc.vector.tensor_scalar(tmp, PS[:, 0:TT], Bp[0:128, co:co + 1], None,
                                    ALU.add)
            nc.vector.tensor_mul(tmp, tmp, gt)
            nc.vector.tensor_add(sb.x[co][:, tsl], sb.x[co][:, tsl], tmp)
            ctx.bar()
    sb.release()


def _l3_edge_fix(ctx, sb, qsp, ksp, vsp, ztk, em, is_first, identb, eb):
    nc = ctx.nc
    PS = ctx.PS
    PSb = ctx.PS.ap().bitcast(BF16)
    sp_p, sp_pT, sp_v, sp_vT = eb["sp_p"], eb["sp_pT"], eb["sp_v"], eb["sp_vT"]
    sp_z, spzf, red, rcp, edgep = eb["sp_z"], eb["spzf"], eb["red"], eb["rcp"], eb["edgep"]
    qc = slice(0, 3) if is_first else slice(10, 13)
    sco = 2 * TT
    if is_first:
        nc.tensor.matmul(PS[0:3, sco:sco + 6], qsp[0:64, qc], ksp[0:64, 0:6],
                         start=True, stop=True)
        nc.tensor.matmul(PS[0:3, sco + 6:sco + 7], qsp[0:64, qc], ksp[0:64, 12:13],
                         start=True, stop=True)
        patt = edgep[0:3, 0:7]
        vcols = list(range(0, 6)) + [12]
    else:
        nc.tensor.matmul(PS[0:3, sco:sco + 1], qsp[0:64, qc], ksp[0:64, 0:1],
                         start=True, stop=True)
        nc.tensor.matmul(PS[0:3, sco + 1:sco + 7], qsp[0:64, qc], ksp[0:64, 7:13],
                         start=True, stop=True)
        patt = edgep[0:3, 8:15]
        vcols = [0] + list(range(7, 13))
    ctx.edge("tensor", "vector")
    nc.vector.tensor_copy(sp_p[0:3, 0:7], PS[0:3, sco:sco + 7])
    ctx.bar()
    nc.vector.tensor_tensor(sp_p[0:3, 0:7], sp_p[0:3, 0:7], patt, ALU.add)
    ctx.bar()
    nc.vector.reduce_max(red[0:3, :], sp_p[0:3, 0:7], axis=AX)
    ctx.bar()
    nc.vector.tensor_scalar(red[0:3, :], red[0:3, :], -1.0, None, ALU.mult)
    ctx.edge("vector", "scalar")
    nc.scalar.activation(sp_p[0:3, 0:7], sp_p[0:3, 0:7], AF.Exp, bias=red[0:3, :],
                         scale=1.0)
    ctx.edge("scalar", "vector")
    nc.vector.tensor_reduce(red[0:3, :], sp_p[0:3, 0:7], AX, ALU.add)
    ctx.bar()
    nc.vector.reciprocal(rcp[0:3, :], red[0:3, :])
    for j, vc in enumerate(vcols):
        nc.vector.tensor_copy(sp_v[0:64, j:j + 1], vsp[0:64, vc:vc + 1])
    nc.vector.memset(sp_p[0:3, 7:8], 0.0)
    nc.vector.memset(sp_v[0:64, 7:8], 0.0)
    ctx.edge("vector", "tensor")
    nc.tensor.transpose(PSb[0:8, 4096:4096 + 3], sp_p[0:3, 0:8], identb[0:3, 0:3])
    nc.tensor.transpose(PSb[0:8, 4352:4352 + 64], sp_v[0:64, 0:8],
                        identb[0:64, 0:64])
    ctx.edge("tensor", "vector")
    nc.vector.tensor_copy(sp_pT[0:8, 0:3], PSb[0:8, 4096:4096 + 3])
    nc.vector.tensor_copy(sp_vT[0:8, 0:64], PSb[0:8, 4352:4352 + 64])
    ctx.edge("vector", "tensor")
    nc.tensor.matmul(PS[0:3, sco:sco + 64], sp_pT[0:8, 0:3], sp_vT[0:8, 0:64],
                     start=True, stop=True)
    ctx.edge("tensor", "vector")
    nc.vector.tensor_scalar(sp_z[0:3, 0:64], PS[0:3, sco:sco + 64], rcp[0:3, :],
                            None, ALU.mult)
    ctx.bar()
    nc.vector.memset(spzf[:, 0:64], 0.0)
    ctx.edge("vector", "sync")
    r0 = 0 if is_first else 125
    ctx.dma(spzf[r0:r0 + 3, 0:64], sp_z[0:3, 0:64])
    ctx.dwait()
    nc.vector.tensor_sub(spzf[:, 0:64], spzf[:, 0:64], ztk[:, 0:64])
    nc.vector.tensor_scalar(spzf[:, 0:64], spzf[:, 0:64], em, None, ALU.mult)
    nc.vector.tensor_add(ztk[:, 0:64], ztk[:, 0:64], spzf[:, 0:64])
    ctx.bar()


def emit_l3_ffn(ctx, sb, stats_dram, g_dram, tag="l3f"):
    nc = ctx.nc
    PS = ctx.PS
    cw1 = ctx.inp("L3f_cw1", [1024, 1024], F32)
    cb1 = ctx.inp("L3f_cb1", [128, 8], F32)
    cw2 = ctx.inp("L3f_cw2", [1024, 512], F32)
    cb2 = ctx.inp("L3f_cb2", [128, 4], F32)
    cw3 = ctx.inp("L3f_cw3", [512, 256], F32)
    cb3 = ctx.inp("L3f_cb3", [128, 2], F32)
    cw4 = ctx.inp("L3f_cw4", [128, 2], F32)
    cb4 = ctx.inp("L3f_cb4", [1, 1], F32)
    gw1 = ctx.inp("L3f_gw1", [1024, 512], F32)
    gw2 = ctx.inp("L3f_gw2", [512, 256], F32)
    gw3 = ctx.inp("L3f_gw3", [128, 2], F32)
    gb1 = ctx.inp("L3f_gb1", [128, 4], F32)
    gb2 = ctx.inp("L3f_gb2", [128, 2], F32)
    gb3 = ctx.inp("L3f_gb3", [1, 1], F32)
    arng = ctx.inp("L3f_ar", [1, 1], F32)
    outA = nc.dram_tensor(f"outA_{tag}", [D, TPC], F32, kind="Internal")
    outB = nc.dram_tensor(f"outB_{tag}", [D, TPC], F32, kind="Internal")
    gcin = nc.dram_tensor(f"gci_{tag}", [128, 8], F32, kind="Internal")
    gcout = nc.dram_tensor(f"gco_{tag}", [128, 8], F32, kind="Internal")

    sb.mark()
    W1 = sb.abf(NCH * 1024)
    ctx.dmag(W1, cw1.rearrange("(c p) o -> p c o", p=128))
    W2 = sb.abf(NCH * 512)
    ctx.dmag(W2, cw2.rearrange("(c p) o -> p c o", p=128))
    W3 = sb.abf(4 * 256)
    ctx.dmag(W3, cw3.rearrange("(c p) o -> p c o", p=128))
    W4 = sb.abf(2)
    ctx.dmag(W4, cw4)
    G1 = sb.abf(NCH * 512)
    ctx.dmag(G1, gw1.rearrange("(c p) o -> p c o", p=128))
    G2 = sb.abf(4 * 256)
    ctx.dmag(G2, gw2.rearrange("(c p) o -> p c o", p=128))
    G3 = sb.abf(2)
    ctx.dmag(G3, gw3)
    B1 = sb.af(8)
    ctx.dma(B1, cb1)
    B2 = sb.af(4)
    ctx.dma(B2, cb2)
    B3 = sb.af(2)
    ctx.dma(B3, cb3)
    B4 = sb.af(8)[0:1, 0:1]
    ctx.dma(B4, cb4)
    GB1 = sb.af(4)
    ctx.dma(GB1, gb1)
    GB2 = sb.af(2)
    ctx.dma(GB2, gb2)
    GB3 = sb.af(8)[0:1, 1:2]
    ctx.dma(GB3, gb3)
    AR = sb.af(8)[0:1, 2:3]
    ctx.dma(AR, arng)
    onesb2 = sb.abf(128)[0:1, :]
    strow = sb.af(2 * TT)[0:1, :]
    strowb = sb.abf(2 * TT)[0:1, :]
    ctx.dwait()
    nc.vector.memset(onesb2, 1.0)
    ctx.bar()

    ot = [sb.af(TT) for _ in range(NCH)]
    otb = [sb.abf(TT) for _ in range(NCH)]
    h1t = [sb.abf(TT) for _ in range(NCH)]
    h2t = [sb.abf(TT) for _ in range(4)]
    h3t = [sb.abf(TT) for _ in range(2)]
    twf = sb.af(TT)
    twg = sb.af(TT)
    psum = sb.af(8)
    tmp1 = sb.af(8)[:, 0:1]
    gcb = sb.af(8)
    gcbb = sb.abf(8)
    g1t = [sb.abf(8)[:, i:i + 1] for i in range(4)]
    g2t = [sb.abf(8)[:, 4 + i:5 + i] for i in range(2)]
    trow = sb.af(TT)[0:1, :]
    trow2 = sb.af(TT)[0:1, :]
    trowb = sb.abf(TT)[0:1, :]
    nc.vector.memset(psum, 0.0)
    ctx.bar()
    for t in range(NTT):
        tsl = slice(t * TT, (t + 1) * TT)
        ctx.dma(strow[0:1, 0:TT], stats_dram.ap()[0:1, tsl])
        ctx.dma(strow[0:1, TT:2 * TT], stats_dram.ap()[1:2, tsl])
        ctx.dwait()
        nc.vector.tensor_copy(strowb, strow)
        ctx.edge("vector", "tensor")
        nc.tensor.matmul(PS[:, 2 * TT:3 * TT], onesb2, strowb[0:1, 0:TT],
                         start=True, stop=True)
        nc.tensor.matmul(PS[:, 3 * TT:4 * TT], onesb2, strowb[0:1, TT:2 * TT],
                         start=True, stop=True)
        ctx.edge("tensor", "vector")
        for c in range(NCH):
            nc.vector.tensor_tensor(ot[c], sb.x[c][:, tsl], PS[:, 2 * TT:3 * TT],
                                    ALU.mult)
            nc.vector.tensor_add(ot[c], ot[c], PS[:, 3 * TT:4 * TT])
            nc.vector.reduce_sum(tmp1, ot[c], axis=AX)
            nc.vector.tensor_add(psum[:, c:c + 1], psum[:, c:c + 1], tmp1)
            ctx.edge("vector", "sync")
            ctx.dma(outA.ap()[c * 128:(c + 1) * 128, tsl], ot[c])
        nc.sync.wait_ge(ctx.dsem, ctx.dcnt)
        nc.vector.wait_ge(ctx.dsem, ctx.dcnt)
    ctx.dma(gcin.ap(), psum)
    ctx.dwait()
    ctx.coll("AllReduce", ALU.add, GROUPS_BATCH, gcin.ap(), gcout.ap())
    ctx.cwait()
    ctx.dma(gcb, gcout.ap())
    ctx.dwait()
    nc.vector.tensor_scalar(gcb, gcb, 1.0 / float(S), None, ALU.mult)
    ctx.bar()
    nc.vector.tensor_copy(gcbb, gcb)
    ctx.bar()
    for oc in range(4):
        for c in range(NCH):
            nc.tensor.matmul(PS[:, 3 * TT + oc:3 * TT + oc + 1],
                             G1[:, c * 512 + oc * 128: c * 512 + (oc + 1) * 128],
                             gcbb[:, c:c + 1], start=(c == 0), stop=(c == NCH - 1))
    ctx.edge("tensor", "scalar")
    for oc in range(4):
        nc.scalar.activation(g1t[oc], PS[:, 3 * TT + oc:3 * TT + oc + 1], AF.Gelu,
                             bias=GB1[:, oc:oc + 1], scale=1.0)
    ctx.edge("scalar", "tensor", "vector")
    for oc in range(2):
        for c in range(4):
            nc.tensor.matmul(PS[:, 3 * TT + oc:3 * TT + oc + 1],
                             G2[:, c * 256 + oc * 128: c * 256 + (oc + 1) * 128],
                             g1t[c], start=(c == 0), stop=(c == 3))
    ctx.edge("tensor", "scalar")
    for oc in range(2):
        nc.scalar.activation(g2t[oc], PS[:, 3 * TT + oc:3 * TT + oc + 1], AF.Gelu,
                             bias=GB2[:, oc:oc + 1], scale=1.0)
    ctx.edge("scalar", "tensor", "vector")
    for c in range(2):
        nc.tensor.matmul(PS[0:1, 3 * TT + 4:3 * TT + 5],
                         G3[:, c:c + 1], g2t[c], start=(c == 0), stop=(c == 1))
    ctx.edge("tensor", "scalar")
    gf7 = sb.af(8)[0:1, 3:4]
    ar2 = sb.af(8)[0:1, 4:5]
    nc.scalar.activation(gf7, PS[0:1, 3 * TT + 4:3 * TT + 5], AF.Sigmoid,
                         bias=GB3, scale=1.0)
    ctx.edge("scalar", "vector")
    nc.vector.tensor_scalar(gf7, gf7, 0.7, None, ALU.mult)
    nc.vector.tensor_scalar(ar2, AR, 2.0, None, ALU.mult)
    ctx.bar()

    SIGNS = [1.0, 1.0, -1.0, -1.0, -1.0, -1.0, 1.0, 1.0]
    SIGMA = [4, 5, 6, 7, 0, 1, 2, 3]
    bufs = [outA, outB]
    for cyc in range(3):
        src, dst = bufs[cyc % 2], bufs[(cyc + 1) % 2]
        for t in range(NTT):
            tsl = slice(t * TT, (t + 1) * TT)
            ctx.edge("tensor", "sync")
            ctx.edge("vector", "sync")
            for c in range(NCH):
                ctx.dma(ot[c], src.ap()[c * 128:(c + 1) * 128, tsl])
            ctx.dwait()
            for c in range(NCH):
                nc.vector.tensor_copy(otb[c], ot[c])
            ctx.edge("vector", "tensor")
            for oc in range(NCH):
                for c in range(NCH):
                    nc.tensor.matmul(PS[:, 0:TT],
                                     W1[:, c * 1024 + oc * 128: c * 1024 + (oc + 1) * 128],
                                     otb[c], start=(c == 0), stop=(c == NCH - 1))
                ctx.edge("tensor", "scalar")
                nc.scalar.activation(h1t[oc], PS[:, 0:TT], AF.Gelu,
                                     bias=B1[:, oc:oc + 1], scale=1.0)
                ctx.edge("scalar", "tensor")
            ctx.edge("scalar", "vector")
            for oc in range(4):
                for c in range(NCH):
                    nc.tensor.matmul(PS[:, 0:TT],
                                     W2[:, c * 512 + oc * 128: c * 512 + (oc + 1) * 128],
                                     h1t[c], start=(c == 0), stop=(c == NCH - 1))
                ctx.edge("tensor", "scalar")
                nc.scalar.activation(h2t[oc], PS[:, 0:TT], AF.Gelu,
                                     bias=B2[:, oc:oc + 1], scale=1.0)
                ctx.edge("scalar", "tensor")
            for oc in range(2):
                for c in range(4):
                    nc.tensor.matmul(PS[:, 0:TT],
                                     W3[:, c * 256 + oc * 128: c * 256 + (oc + 1) * 128],
                                     h2t[c], start=(c == 0), stop=(c == 3))
                ctx.edge("tensor", "scalar")
                nc.scalar.activation(h3t[oc], PS[:, 0:TT], AF.Gelu,
                                     bias=B3[:, oc:oc + 1], scale=1.0)
                ctx.edge("scalar", "tensor")
            for c in range(2):
                nc.tensor.matmul(PS[0:1, 2 * TT:3 * TT], W4[:, c:c + 1], h3t[c],
                                 start=(c == 0), stop=(c == 1))
            ctx.edge("tensor", "scalar")
            nc.scalar.activation(trow, PS[0:1, 2 * TT:3 * TT], AF.Sigmoid,
                                 bias=B4, scale=1.0)
            ctx.edge("scalar", "vector")
            nc.vector.tensor_scalar(trow2, trow, 0.3, gf7, ALU.mult, ALU.add)
            nc.vector.tensor_scalar(trow2, trow2, -0.5, None, ALU.add)
            nc.vector.tensor_scalar(trow2, trow2, ar2, 0.1, ALU.mult, ALU.add)
            nc.vector.tensor_copy(trowb, trow2)
            ctx.edge("vector", "tensor")
            nc.tensor.matmul(PS[:, 2 * TT:3 * TT], onesb2, trowb, start=True,
                             stop=True)
            ctx.edge("tensor", "vector")
            last = (cyc == 2)
            for c in range(NCH):
                nc.vector.tensor_tensor(twf, ot[SIGMA[c]], PS[:, 2 * TT:3 * TT],
                                        ALU.mult)
                nc.vector.scalar_tensor_tensor(twg, twf, SIGNS[c], ot[c],
                                               ALU.mult, ALU.add)
                ctx.edge("vector", "sync")
                ctx.dma(dst.ap()[c * 128:(c + 1) * 128, tsl], twg)
                if last:
                    ctx.edge("vector", "scalar")
                    nc.scalar.activation(twf, twg, AF.Silu)
                    ctx.edge("scalar", "sync")
                    ctx.dma(g_dram.ap()[c * 128:(c + 1) * 128, tsl], twf)
                    nc.scalar.wait_ge(ctx.dsem, ctx.dcnt)
                nc.sync.wait_ge(ctx.dsem, ctx.dcnt)
                nc.vector.wait_ge(ctx.dsem, ctx.dcnt)
    sb.release()


def mk_xn_ag(ctx, tag):
    nc = ctx.nc
    xn_in = nc.dram_tensor(f"xnin_{tag}", [D, TPC], BF16, kind="Internal")
    xn_out = nc.dram_tensor(f"xnout_{tag}", [4 * D, TPC], BF16, kind="Internal")
    return xn_in, xn_out


def build_test_l2():
    nc = bass.Bass("TRN2", target_bir_lowering=False, detect_race_conditions=False)
    ctx = Ctx(nc)
    sb = Sbuf(nc)
    x_in = ctx.inp("x_in", [TPC, D], F32)
    g_in = ctx.inp("g_in", [D, TPC], F32)
    x_out = nc.dram_tensor("x_out", [TPC, D], F32, kind="ExternalOutput").ap()

    emit_load_x(ctx, sb, x_in)
    xn_in, xn_out = mk_xn_ag(ctx, "l2")
    emit_layernorm(ctx, sb, xn_in.ap())
    ctx.coll("AllGather", ALU.bypass, GROUPS_BATCH, xn_in.ap(), xn_out.ap())
    ctx.cwait()
    emit_layer2_attn(ctx, sb, xn_out.ap(), g_in)
    emit_store_x(ctx, sb, x_out)
    return nc, ctx



def build_test_layer(li):
    nc = bass.Bass("TRN2", target_bir_lowering=False, detect_race_conditions=False)
    ctx = Ctx(nc)
    sb = Sbuf(nc)
    x_in = ctx.inp("x_in", [TPC, D], F32)
    g_in = ctx.inp("g_in", [D, TPC], F32) if li < 3 else None
    x_out = nc.dram_tensor("x_out", [TPC, D], F32, kind="ExternalOutput").ap()

    emit_load_x(ctx, sb, x_in)
    xn_in, xn_out = mk_xn_ag(ctx, f"l{li}")
    stats3 = nc.dram_tensor("stats3", [2, TPC], F32, kind="Internal") if li == 3 else None
    emit_layernorm(ctx, sb, xn_in.ap(),
                   stats_dram=(stats3.ap() if li == 3 else None))
    ctx.coll("AllGather", ALU.bypass, GROUPS_BATCH, xn_in.ap(), xn_out.ap())
    ctx.cwait()
    if li == 0:
        emit_std_attn_layer(ctx, sb, xn_out.ap(), g_in, 0, False, False, "l0")
    elif li == 1:
        emit_layer1_attn(ctx, sb, xn_out.ap(), xn_in.ap(), g_in)
    elif li == 2:
        emit_std_attn_layer(ctx, sb, xn_out.ap(), g_in, 2, True, True, "l2")
    elif li == 3:
        g_dram = nc.dram_tensor("g3", [D, TPC], F32, kind="Internal")
        emit_l3_ffn(ctx, sb, stats3, g_dram)
        if DEBUG_ATTN:
            gdump = nc.dram_tensor("g_dump", [D, TPC], F32, kind="ExternalOutput").ap()
            sb.mark()
            st_ = sb.af(TPC)
            for c_ in range(NCH):
                ctx.dma(st_, g_dram.ap()[c_ * 128:(c_ + 1) * 128, :])
                ctx.dwait()
                ctx.dma(gdump[c_ * 128:(c_ + 1) * 128, :], st_)
                ctx.dwait()
            sb.release()
        emit_l3_attn(ctx, sb, xn_out.ap(), xn_in.ap(), g_dram)
    emit_store_x(ctx, sb, x_out)
    return nc, ctx



def build_test_ffn(li):
    nc = bass.Bass("TRN2", target_bir_lowering=False, detect_race_conditions=False)
    ctx = Ctx(nc)
    sb = Sbuf(nc)
    x_in = ctx.inp("x_in", [TPC, D], F32)
    g_out = nc.dram_tensor("g_out", [D, TPC], F32, kind="ExternalOutput").ap()
    g_dram = nc.dram_tensor("g_dram", [D, TPC], F32, kind="Internal")
    stats_dram = nc.dram_tensor("stats", [2, TPC], F32, kind="Internal")
    xn_in, xn_out = mk_xn_ag(ctx, f"f{li}")

    emit_load_x(ctx, sb, x_in)
    emit_layernorm(ctx, sb, xn_in.ap(), stats_dram=stats_dram.ap(),
                   track_absmax=(li == 1))
    if li == 0:
        ctx.coll("AllGather", ALU.bypass, GROUPS_BATCH, xn_in.ap(), xn_out.ap())
        ctx.cwait()
        emit_l0_ffn(ctx, sb, xn_out.ap(), xn_in.ap(), g_dram.ap())
    elif li == 2:
        emit_l2_ffn(ctx, sb, stats_dram, g_dram)
    elif li == 1:
        emit_l1_ffn(ctx, sb, stats_dram, g_dram, sb.R[0:1, 2047:2048])
    sb.mark()
    st = sb.af(TPC)
    for c in range(NCH):
        ctx.dma(st, g_dram.ap()[c * 128:(c + 1) * 128, :])
        ctx.dwait()
        ctx.dma(g_out[c * 128:(c + 1) * 128, :], st)
        ctx.dwait()
    sb.release()
    return nc, ctx


def build_test_ln():
    nc = bass.Bass("TRN2", target_bir_lowering=False, detect_race_conditions=False)
    ctx = Ctx(nc)
    sb = Sbuf(nc)
    x_in = ctx.inp("x_in", [TPC, D], F32)
    xn_out = nc.dram_tensor("xn_out", [D, TPC], BF16, kind="ExternalOutput").ap()
    amax_out = nc.dram_tensor("amax_out", [1, 1], F32, kind="ExternalOutput").ap()
    xn_i = nc.dram_tensor("xn_i", [D, TPC], BF16, kind="Internal")

    emit_load_x(ctx, sb, x_in)
    emit_layernorm(ctx, sb, xn_i.ap(), track_absmax=True)
    sb.mark()
    stage = sb.abf(1024)
    for c in range(NCH):
        for t in range(NTT):
            ctx.dma(stage[:, 0:TT], xn_i.ap()[c * 128:(c + 1) * 128, t * TT:(t + 1) * TT])
            ctx.dwait()
            ctx.dma(xn_out[c * 128:(c + 1) * 128, t * TT:(t + 1) * TT], stage[:, 0:TT])
            ctx.dwait()
    ctx.dma(amax_out, sb.R[0:1, 2047:2048])
    ctx.dwait()
    sb.release()
    return nc, ctx



def _f32(a):
    return np.ascontiguousarray(np.asarray(a, np.float32))


def l0a_in_map(inp, c):
    h0 = 4 * (c % 4)
    WK, WQ, WV = inp["L0_attn_W_K"], inp["L0_attn_W_Q"], inp["L0_attn_W_V"]
    cols = []
    for W in (WQ, WK, WV):
        W = _f32(W)
        cols.append(np.concatenate([W[h0 + h].T for h in range(4)], axis=1))
    return {"L0w_qkv_m": np.ascontiguousarray(np.concatenate(cols, axis=1)),
            "L0w_o": np.ascontiguousarray(_f32(inp["L0_attn_W_O"]).T)}


def l1a_in_map(inp, c):
    h0 = 4 * (c % 4)
    wq, wka, wkb = _f32(inp["L1_attn_wq"]), _f32(inp["L1_attn_wk_a"]), \
        _f32(inp["L1_attn_wk_b"])
    wva, wvb, wo = _f32(inp["L1_attn_wv_a"]), _f32(inp["L1_attn_wv_b"]), \
        _f32(inp["L1_attn_wo"])
    im = {}
    im["L1w_q_m"] = np.ascontiguousarray(
        np.concatenate([wq[(h0 + h) * 64:(h0 + h + 1) * 64, :]
                        for h in range(4)], axis=1))
    im["L1w_kb_m"] = np.ascontiguousarray(
        np.concatenate([wkb[(h0 + h) * 64:(h0 + h + 1) * 64, :]
                        for h in range(4)], axis=1))
    im["L1w_vbT_m"] = np.ascontiguousarray(
        np.concatenate([wvb[(h0 + h) * 64:(h0 + h + 1) * 64, :].T
                        for h in range(4)], axis=1))
    im["L1w_kaT"] = np.ascontiguousarray(wka.T)
    im["L1w_vaT"] = np.ascontiguousarray(wva.T)
    im["L1w_oT"] = np.ascontiguousarray(wo.T)
    RTs = []
    for h in range(4):
        hh = h0 + h
        R = np.eye(64, dtype=np.float64)
        for j in range(8):
            a = hh * (1.0 / 10000.0 ** (2.0 * j / 64.0))
            ca, sa = np.cos(a), np.sin(a)
            R[2 * j, 2 * j] = ca
            R[2 * j, 2 * j + 1] = -sa
            R[2 * j + 1, 2 * j] = sa
            R[2 * j + 1, 2 * j + 1] = ca
        RTs.append(R.T.astype(np.float32))
    im["L1_RT_m"] = np.ascontiguousarray(np.concatenate(RTs, axis=1))
    return im


def l2a_in_map(inp, c):
    h0 = 4 * (c % 4)
    qkv_w, qkv_b = _f32(inp["L2_attn_qkv_w"]), _f32(inp["L2_attn_qkv_b"])
    wcols = np.concatenate([qkv_w[:, p * 1024 + h0 * 64: p * 1024 + (h0 + 4) * 64]
                            for p in range(3)], axis=1)
    bcols = np.stack([qkv_b[p * 1024 + (h0 + h) * 64: p * 1024 + (h0 + h + 1) * 64]
                      for p in range(3) for h in range(4)], axis=1)
    return {"L2w_qkv_m": np.ascontiguousarray(wcols),
            "L2b_qkv_m": np.ascontiguousarray(bcols),
            "L2w_o": _f32(inp["L2_attn_out_w"]),
            "L2b_o": np.ascontiguousarray(_f32(inp["L2_attn_out_b"]).reshape(8, 128).T)}


def l1f_in_map(inp, c):
    wg, wu, wd = _f32(inp["L1_ffn_w_gate"]), _f32(inp["L1_ffn_w_up"]), \
        _f32(inp["L1_ffn_w_down"])
    return {"L1f_wg_m": np.ascontiguousarray(wg[c * 384:(c + 1) * 384]),
            "L1f_wu_m": np.ascontiguousarray(wu[c * 384:(c + 1) * 384]),
            "L1f_wd_m": np.ascontiguousarray(wd[c * 128:(c + 1) * 128])}


def l2f_in_map(inp, c):
    wg, wu, wd = _f32(inp["L2_ffn_w_gate"]), _f32(inp["L2_ffn_w_up"]), \
        _f32(inp["L2_ffn_w_down"])
    wgp = np.zeros((3072, 1024), np.float32)
    wgp[0:2816] = wg
    wup = np.zeros((3072, 1024), np.float32)
    wup[0:2816] = wu
    return {"L2f_wg_m": np.ascontiguousarray(wgp[c * 384:(c + 1) * 384]),
            "L2f_wu_m": np.ascontiguousarray(wup[c * 384:(c + 1) * 384]),
            "L2f_wd_m": np.ascontiguousarray(wd[c * 128:(c + 1) * 128])}


def l3_in_map(inp, c):
    q = c % 4
    Wqkv, bq = _f32(inp["L3_attn_W_qkv"]), _f32(inp["L3_attn_b_qkv"])
    Wlat, blat = _f32(inp["L3_attn_W_lat"]), _f32(inp["L3_attn_b_lat"])
    Wproj, bproj = _f32(inp["L3_attn_W_proj"]), _f32(inp["L3_attn_b_proj"])
    im = {}
    bb = np.zeros((128, 24), np.float32)
    for sec in range(3):
        for hc in range(8):
            bb[:, sec * 8 + hc] = bq[sec * 1024 + hc * 128: sec * 1024 + (hc + 1) * 128]
    im["L3w_qkvT"] = np.ascontiguousarray(Wqkv.T)
    im["L3b_qkv"] = bb
    BD = np.zeros((128, 128), np.float32)
    BD[0:64, 0:64] = Wlat.T
    BD[64:128, 64:128] = Wlat.T
    im["L3w_latBD"] = BD
    im["L3b_lat2"] = np.ascontiguousarray(np.concatenate([blat, blat]).reshape(128, 1))
    im["L3w_projT"] = np.ascontiguousarray(Wproj.T)
    im["L3b_proj"] = np.ascontiguousarray(bproj.reshape(8, 128).T)
    emf = np.zeros((128, 1), np.float32)
    eml = np.zeros((128, 1), np.float32)
    if q == 0:
        emf[0:3] = 1.0
    if q == 3:
        eml[125:128] = 1.0
    im["L3_em_first"] = emf
    im["L3_em_last"] = eml
    im["L3f_cw1"] = _f32(inp["L3_ffn_cn_w1"])
    im["L3f_cb1"] = np.ascontiguousarray(_f32(inp["L3_ffn_cn_b1"]).reshape(8, 128).T)
    im["L3f_cw2"] = _f32(inp["L3_ffn_cn_w2"])
    im["L3f_cb2"] = np.ascontiguousarray(_f32(inp["L3_ffn_cn_b2"]).reshape(4, 128).T)
    im["L3f_cw3"] = _f32(inp["L3_ffn_cn_w3"])
    im["L3f_cb3"] = np.ascontiguousarray(_f32(inp["L3_ffn_cn_b3"]).reshape(2, 128).T)
    im["L3f_cw4"] = np.ascontiguousarray(_f32(inp["L3_ffn_cn_w4"]).reshape(2, 128).T)
    im["L3f_cb4"] = np.ascontiguousarray(_f32(inp["L3_ffn_cn_b4"]).reshape(1, 1))
    im["L3f_gw1"] = _f32(inp["L3_ffn_gc_w1"])
    im["L3f_gb1"] = np.ascontiguousarray(_f32(inp["L3_ffn_gc_b1"]).reshape(4, 128).T)
    im["L3f_gw2"] = _f32(inp["L3_ffn_gc_w2"])
    im["L3f_gb2"] = np.ascontiguousarray(_f32(inp["L3_ffn_gc_b2"]).reshape(2, 128).T)
    im["L3f_gw3"] = np.ascontiguousarray(_f32(inp["L3_ffn_gc_w3"]).reshape(2, 128).T)
    im["L3f_gb3"] = np.ascontiguousarray(_f32(inp["L3_ffn_gc_b3"]).reshape(1, 1))
    im["L3f_ar"] = np.ascontiguousarray(_f32(inp["L3_ffn_adaptive_range"]).reshape(1, 1))
    return im



def build_fused():
    nc = bass.Bass("TRN2", target_bir_lowering=False, detect_race_conditions=False)
    ctx = Ctx(nc)
    sb = Sbuf(nc)
    x_in = ctx.inp("x_in", [TPC, D], F32)
    x_out = nc.dram_tensor("x_out", [TPC, D], F32, kind="ExternalOutput").ap()

    emit_load_x(ctx, sb, x_in)
    for li in range(4):
        xn_in, xn_out = mk_xn_ag(ctx, f"L{li}")
        stats = nc.dram_tensor(f"stats_L{li}", [2, TPC], F32, kind="Internal") \
            if li in (1, 2, 3) else None
        g_dram = nc.dram_tensor(f"g_L{li}", [D, TPC], F32, kind="Internal")
        emit_layernorm(ctx, sb, xn_in.ap(),
                       stats_dram=(stats.ap() if stats is not None else None),
                       track_absmax=(li == 1))
        ctx.coll("AllGather", ALU.bypass, GROUPS_BATCH, xn_in.ap(), xn_out.ap())
        ctx.cwait()
        if li == 0:
            emit_l0_ffn(ctx, sb, xn_out.ap(), xn_in.ap(), g_dram.ap())
            emit_std_attn_layer(ctx, sb, xn_out.ap(), g_dram.ap(), 0,
                                False, False, "l0")
        elif li == 1:
            emit_l1_ffn(ctx, sb, stats, g_dram, sb.R[0:1, 2047:2048])
            emit_layer1_attn(ctx, sb, xn_out.ap(), xn_in.ap(), g_dram.ap())
        elif li == 2:
            emit_l2_ffn(ctx, sb, stats, g_dram)
            emit_layer2_attn(ctx, sb, xn_out.ap(), g_dram.ap())
        else:
            emit_l3_ffn(ctx, sb, stats, g_dram)
            emit_l3_attn(ctx, sb, xn_out.ap(), xn_in.ap(), g_dram)
    emit_store_x(ctx, sb, x_out)
    return nc, ctx


def fused_in_map(inputs, c):
    im = {}
    im.update(l0f_in_map(inputs, c))
    im.update(l0a_in_map(inputs, c))
    im.update(l1f_in_map(inputs, c))
    im.update(l1a_in_map(inputs, c))
    im.update(l2f_in_map(inputs, c))
    im.update(l2a_in_map(inputs, c))
    im.update(l3_in_map(inputs, c))
    return im



FUSED = True
_PROGS = {}


def _prog(name, builder):
    if name not in _PROGS:
        _PROGS[name] = builder()
    return _PROGS[name]


def kernel(**inputs):
    x = np.ascontiguousarray(np.asarray(inputs["L0_attn_x"], np.float32))

    def xsl(c):
        return np.ascontiguousarray(x[c // 4, (c % 4) * TPC:(c % 4 + 1) * TPC, :])

    def launch(prog, maps):
        nc, ctx = prog
        return run_bass_kernel_spmd(nc, maps, core_ids=list(range(8))).results

    if FUSED:
        res = launch(_prog("fused", build_fused),
                     [{"x_in": xsl(c), **fused_in_map(inputs, c)}
                      for c in range(NCORE)])
        xo = np.empty_like(x)
        for c in range(NCORE):
            xo[c // 4, (c % 4) * TPC:(c % 4 + 1) * TPC, :] = res[c]["x_out"]
        return xo

    for li in range(4):
        if li == 3:
            res = launch(_prog("l3", lambda: build_test_layer(3)),
                         [{"x_in": xsl(c), **l3_in_map(inputs, c)}
                          for c in range(NCORE)])
        else:
            fmap = (l0f_in_map, l1f_in_map, l2f_in_map)[li]
            amap = (l0a_in_map, l1a_in_map, l2a_in_map)[li]
            resf = launch(_prog(f"f{li}", lambda li=li: build_test_ffn(li)),
                          [{"x_in": xsl(c), **fmap(inputs, c)}
                           for c in range(NCORE)])
            res = launch(_prog(f"a{li}", lambda li=li: build_test_layer(li)),
                         [{"x_in": xsl(c), "g_in": resf[c]["g_out"],
                           **amap(inputs, c)} for c in range(NCORE)])
        xo = np.empty_like(x)
        for c in range(NCORE):
            xo[c // 4, (c % 4) * TPC:(c % 4 + 1) * TPC, :] = res[c]["x_out"]
        x = xo
    return x


if __name__ == "__main__":
    pass


# version 04
# speedup vs baseline: 1.0818x; 1.0007x over previous
import numpy as np
import ml_dtypes
import concourse.bass as bass
import concourse.mybir as mybir
from concourse.bass_utils import run_bass_kernel_spmd

F32 = mybir.dt.float32
BF16 = mybir.dt.bfloat16
U8 = mybir.dt.uint8
AF = mybir.ActivationFunctionType
ALU = mybir.AluOpType
AX = mybir.AxisListType.X

B, S, D = 2, 8192, 1024
NCORE = 8
TPC = 2048
NCH = 8
NTT = 4
TT = 512
MAGIC = 12582912.0
GROUPS_BATCH = [[0, 1, 2, 3], [4, 5, 6, 7]]
GROUP_ALL = [list(range(8))]
BF = ml_dtypes.bfloat16

class Ctx:

    def __init__(self, nc):
        self.nc = nc
        self.dsem = nc.alloc_semaphore("dsem")
        self.dcnt = 0
        self.gsem = nc.alloc_semaphore("gsem")
        self.gcnt = 0
        self.csem = nc.alloc_semaphore("csem")
        self.ccnt = 0
        self.pe_sem = nc.alloc_semaphore("pe_sem")
        self.pe_cnt = 0
        self.act_sem = nc.alloc_semaphore("act_sem")
        self.act_cnt = 0
        self.esem = {}
        self.ecnt = {}
        for e in ("tensor", "vector", "scalar", "sync", "gpsimd"):
            self.esem[e] = nc.alloc_semaphore(f"esem_{e}")
            self.ecnt[e] = 0
        self.PS = nc.alloc_psum_tensor("PS", [128, 4096], F32)
        self.consts = {}
        self.input_specs = {}
        self._uid = 0

    def uid(self):
        self._uid += 1
        return self._uid

    def dma(self, out, in_, **kw):
        self.nc.sync.dma_start(out=out, in_=in_, **kw).then_inc(self.dsem, 16)
        self.dcnt += 16

    def dmag(self, out, in_, **kw):
        self.nc.gpsimd.dma_start(out=out, in_=in_, **kw).then_inc(self.gsem, 16)
        self.gcnt += 16

    def dma_tr(self, out, in_):
        self.nc.sync.dma_start_transpose(out, in_).then_inc(self.dsem, 16)
        self.dcnt += 16

    def dwait(self):
        self.nc.sync.wait_ge(self.dsem, self.dcnt)
        self.nc.gpsimd.wait_ge(self.gsem, self.gcnt)
        self.nc.all_engine_barrier()

    def bar(self):
        self.nc.all_engine_barrier()

    def edge(self, prod, *cons):
        getattr(self.nc, prod).drain().then_inc(self.esem[prod], 1)
        self.ecnt[prod] += 1
        for cn in cons:
            getattr(self.nc, cn).wait_ge(self.esem[prod], self.ecnt[prod])

    def coll(self, kind, op, groups, in_ap, out_ap):
        self.nc.gpsimd.collective_compute(
            kind, op, replica_groups=groups, ins=[in_ap], outs=[out_ap]
        ).then_inc(self.csem)
        self.ccnt += 1

    def cwait(self):
        self.nc.gpsimd.wait_ge(self.csem, self.ccnt)
        self.nc.all_engine_barrier()

    def qsnap(self, which):
        if not hasattr(self, "_qsnaps"):
            self._qsnaps = {}
        if which not in self._qsnaps:
            cid = self.nc.partition_id()
            d = {"own": 0, "m1": 3, "p1": 1}[which]
            self._qsnaps[which] = self.nc.snap((cid + d) % 4)
        return self._qsnaps[which]

    def inp(self, name, shape, dtype=F32):
        h = self.nc.dram_tensor(name, list(shape), dtype, kind="ExternalInput")
        self.input_specs[name] = (tuple(shape), dtype)
        return h.ap()

    def const(self, name, arr):
        if name in self.consts:
            return self.consts[name]
        h = self.nc.inline_tensor(np.ascontiguousarray(arr), name=name)
        self.consts[name] = h.ap()
        return self.consts[name]



class Sbuf:
    def __init__(self, nc):
        self.nc = nc
        self.x = [nc.alloc_sbuf_tensor(f"x{c}", [128, TPC], F32) for c in range(NCH)]
        self.A = nc.alloc_sbuf_tensor("arenaA", [128, 30208], F32)
        self.R = nc.alloc_sbuf_tensor("rows", [128, 2048], F32)
        self._amark = []
        self._aoff = 0

    def a(self, off, n):
        return self.A[:, off:off + n]

    def ab(self, off_f32, n_bf16):
        return self.A.ap().bitcast(BF16)[:, 2 * off_f32: 2 * off_f32 + n_bf16]

    def mark(self):
        self._amark.append(self._aoff)

    def release(self):
        self._aoff = self._amark.pop()

    def falloc(self, n_f32):
        off = self._aoff
        n = (n_f32 + 7) & ~7
        assert off + n <= 29952, f"arena A overflow: {off}+{n}"
        self._aoff = off + n
        return off

    def af(self, n_f32):
        return self.a(self.falloc(n_f32), n_f32)

    def abf(self, n_bf16):
        nf = (n_bf16 + 1) // 2
        return self.ab(self.falloc(nf), n_bf16)



def emit_load_x(ctx, sb, x_in):
    nc = ctx.nc
    sb.mark()
    idf = sb.af(128)
    ctx.dma(idf, ctx.const("ident128f", np.eye(128, dtype=np.float32)))
    stage = sb.af(1024)
    ctx.dwait()
    for t in range(TPC // 128):
        ctx.dma(stage, x_in[t * 128:(t + 1) * 128, :])
        ctx.dwait()
        for c in range(NCH):
            nc.tensor.transpose(ctx.PS[:, 0:128], stage[:, c * 128:(c + 1) * 128], idf)
            ctx.bar()
            nc.vector.tensor_copy(sb.x[c][:, t * 128:(t + 1) * 128], ctx.PS[:, 0:128])
            ctx.bar()
    sb.release()


def emit_store_x(ctx, sb, x_out):
    nc = ctx.nc
    sb.mark()
    idf = sb.af(128)
    ctx.dma(idf, ctx.const("ident128f", np.eye(128, dtype=np.float32)))
    stage = sb.af(1024)
    ctx.dwait()
    for t in range(TPC // 128):
        for c in range(NCH):
            nc.tensor.transpose(ctx.PS[:, 0:128], sb.x[c][:, t * 128:(t + 1) * 128], idf)
            ctx.bar()
            nc.vector.tensor_copy(stage[:, c * 128:(c + 1) * 128], ctx.PS[:, 0:128])
            ctx.bar()
        ctx.dma(x_out[t * 128:(t + 1) * 128, :], stage)
        ctx.dwait()
    sb.release()


def emit_layernorm(ctx, sb, xn_bf_dram, stats_dram=None, track_absmax=False):
    nc = ctx.nc
    PS = ctx.PS
    sb.mark()
    onesv = sb.af(8)[:, 0:1]
    ctx.dma(onesv, ctx.const("ones128f", np.ones((128, 1), dtype=np.float32)))
    sq = sb.af(TT)
    xnf = sb.af(TT)
    am = sb.af(8)[:, 0:1]
    amr = sb.af(128)
    onesb = sb.abf(128)[0:1, :]
    rstdb = sb.abf(TT)[0:1, :]
    browb = sb.abf(TT)[0:1, :]
    xnb = sb.abf(TT)
    ctx.dwait()
    nc.vector.memset(onesb, 1.0)
    if track_absmax:
        nc.vector.memset(am, 0.0)
    for t in range(NTT):
        tsl = slice(t * TT, (t + 1) * TT)
        for c in range(NCH):
            nc.tensor.matmul(PS[0:1, 0:TT], onesv, sb.x[c][:, tsl],
                             start=(c == 0), stop=(c == NCH - 1))
        for c in range(NCH):
            nc.scalar.activation(sq, sb.x[c][:, tsl], AF.Square)
            ctx.bar()
            nc.tensor.matmul(PS[0:1, TT:2 * TT], onesv, sq,
                             start=(c == 0), stop=(c == NCH - 1))
            ctx.bar()
        mean = sb.R[0:1, 0:TT]
        var = sb.R[0:1, TT:2 * TT]
        rstd = sb.R[0:1, 2 * TT:3 * TT]
        brow = sb.R[0:1, 3 * TT:4 * TT]
        nc.vector.tensor_scalar(mean, PS[0:1, 0:TT], 1.0 / D, None, ALU.mult)
        nc.vector.tensor_scalar(var, PS[0:1, TT:2 * TT], 1.0 / D, None, ALU.mult)
        nc.vector.tensor_tensor(rstd, mean, mean, ALU.mult)
        nc.vector.tensor_sub(var, var, rstd)
        nc.vector.tensor_scalar(var, var, 1e-5, None, ALU.add)
        ctx.bar()
        nc.scalar.activation(rstd, var, AF.Sqrt)
        ctx.bar()
        nc.vector.reciprocal(rstd, rstd)
        nc.vector.tensor_tensor(brow, mean, rstd, ALU.mult)
        nc.vector.tensor_scalar(brow, brow, -1.0, None, ALU.mult)
        ctx.bar()
        nc.vector.tensor_copy(rstdb, rstd)
        nc.vector.tensor_copy(browb, brow)
        ctx.bar()
        nc.tensor.matmul(PS[:, 2 * TT:3 * TT], onesb, rstdb, start=True, stop=True)
        nc.tensor.matmul(PS[:, 3 * TT:4 * TT], onesb, browb, start=True, stop=True)
        ctx.bar()
        if stats_dram is not None:
            ctx.dma(stats_dram[0:1, tsl], rstd)
            ctx.dma(stats_dram[1:2, tsl], brow)
        for c in range(NCH):
            nc.vector.tensor_tensor(xnf, sb.x[c][:, tsl], PS[:, 2 * TT:3 * TT], ALU.mult)
            nc.vector.tensor_add(xnf, xnf, PS[:, 3 * TT:4 * TT])
            nc.vector.tensor_copy(xnb, xnf)
            if track_absmax:
                nc.vector.reduce_max(sq[:, 0:1], xnf, axis=AX, apply_absolute_value=True)
                nc.vector.tensor_max(am, am, sq[:, 0:1])
            ctx.bar()
            ctx.dmag(xn_bf_dram[c * 128:(c + 1) * 128, tsl], xnb)
            nc.gpsimd.wait_ge(ctx.gsem, ctx.gcnt)
            ctx.bar()
    if track_absmax:
        identb = ctx.get_identb(sb)
        amb = sb.abf(128)
        nc.vector.tensor_copy(amb[:, 0:1], am)
        ctx.bar()
        nc.tensor.transpose(ctx.PS.ap().bitcast(BF16)[0:1, 0:128], amb[:, 0:1], identb)
        ctx.bar()
        nc.vector.tensor_copy(amr[0:1, 0:128], ctx.PS.ap().bitcast(BF16)[0:1, 0:128])
        ctx.bar()
        nc.vector.reduce_max(sb.R[0:1, 2047:2048], amr[0:1, 0:128], axis=AX)
        ctx.bar()
    sb.release()


def _ctx_get_identb(ctx, sb):
    if not hasattr(ctx, "_identb"):
        ctx._identb = sb.ab(30144, 128)
        ctx.dmag(ctx._identb, ctx.const("ident128b", np.eye(128, dtype=BF)))
        ctx._identf_sb = sb.a(29952, 128)
        ctx.dma(ctx._identf_sb, ctx.const("ident128f", np.eye(128, dtype=np.float32)))
        ctx.dwait()
    return ctx._identb

Ctx.get_identb = _ctx_get_identb




DEBUG_ATTN = False
L2F_PHASES = 4

ATT_NKT = S // 128
ATT_NQT = S // TT


def tr_copy(ctx, items, bf16=True, slots=4):
    nc = ctx.nc
    PSb = ctx.PS.ap().bitcast(BF16)
    PSf = ctx.PS
    ident = ctx._identb if bf16 else ctx._identf_sb
    n = len(items)
    i = 0
    while i < n:
        grp = items[i:i + slots]
        for j, (src, dst) in enumerate(grp):
            pp, ff = src.shape[0], src.shape[1]
            if bf16:
                out = PSb[0:ff, 7 * 1024 + j * 128: 7 * 1024 + j * 128 + pp]
                nc.tensor.transpose(out, src, ident[0:pp, 0:pp])
            else:
                out = PSf[0:ff, 7 * 512 + j * 128: 7 * 512 + j * 128 + pp]
                nc.tensor.transpose(out, src, ident[0:pp, 0:pp])
        ctx.edge("tensor", "vector")
        for j, (src, dst) in enumerate(grp):
            pp, ff = src.shape[0], src.shape[1]
            if bf16:
                nc.vector.tensor_copy(dst, PSb[0:ff, 7 * 1024 + j * 128: 7 * 1024 + j * 128 + pp])
            else:
                nc.vector.tensor_copy(dst, PSf[0:ff, 7 * 512 + j * 128: 7 * 512 + j * 128 + pp])
        ctx.edge("vector", "tensor")
        i += slots
    ctx.edge("vector", "scalar", "sync", "gpsimd")


def attn_evac_tile(ctx, sb, cfg, bufs, p, t):
    nc = ctx.nc
    PS = ctx.PS
    PSb = ctx.PS.ap().bitcast(BF16)
    kbuf, qbuf, vbuf = bufs["kbuf"], bufs["qbuf"], bufs["vbuf"]
    sqt, sq2, ltmp, rmin, nh64 = (bufs["sqt"], bufs["sq2"], bufs["ltmp"],
                                  bufs["rmin"], bufs["nh64"])
    identb = bufs["identb"]
    ss = cfg["sqrt_scale"]
    tsl = slice(t * TT, (t + 1) * TT)
    bq, bk, bv = cfg["bias"](0, p), cfg["bias"](1, p), cfg["bias"](2, p)
    if bk is not None:
        nc.vector.tensor_scalar(kbuf[0:64, tsl], PS[0:64, 0:TT], bk, ss,
                                ALU.add, ALU.mult)
    else:
        nc.vector.tensor_scalar(kbuf[0:64, tsl], PS[0:64, 0:TT], ss, None, ALU.mult)
    if bv is not None:
        nc.vector.tensor_scalar(sqt[0:64, :], PS[0:64, TT:2 * TT], bv, None, ALU.add)
    else:
        nc.vector.tensor_copy(sqt[0:64, :], PS[0:64, TT:2 * TT])
    if bq is not None:
        nc.vector.tensor_scalar(qbuf[0:64, tsl], PS[0:64, 2 * TT:3 * TT], bq, ss,
                                ALU.add, ALU.mult)
    else:
        nc.vector.tensor_scalar(qbuf[0:64, tsl], PS[0:64, 2 * TT:3 * TT], ss, None,
                                ALU.mult)
    ctx.bar()
    nc.scalar.activation(sq2[0:64, 0:TT], kbuf[0:64, tsl], AF.Square)
    nc.scalar.activation(sq2[0:64, TT:2 * TT], qbuf[0:64, tsl], AF.Square)
    ctx.bar()
    nc.tensor.matmul(PS[0:1, 3 * TT:4 * TT], nh64, sq2[0:64, 0:TT],
                     start=True, stop=True)
    nc.tensor.matmul(PS[32:33, 3 * TT:4 * TT], nh64, sq2[0:64, TT:2 * TT],
                     start=True, stop=True)
    for j in range(4):
        nc.tensor.transpose(PSb[:, 4096 + j * 64:4096 + (j + 1) * 64],
                            sqt[0:64, j * 128:(j + 1) * 128], identb[0:64, 0:64])
    ctx.bar()
    nc.vector.tensor_reduce(ltmp[0:1, 0:1], PS[0:1, 3 * TT:4 * TT], AX, ALU.min)
    nc.vector.tensor_tensor(rmin[0:1, 0:1], rmin[0:1, 0:1], ltmp[0:1, 0:1], ALU.min)
    for j in range(4):
        kt = t * 4 + j
        nc.vector.tensor_copy(vbuf[:, kt * 65: kt * 65 + 64],
                              PSb[:, 4096 + j * 64:4096 + (j + 1) * 64])
    nc.scalar.activation(qbuf[64:65, tsl], PS[32:33, 3 * TT:4 * TT], AF.Copy)
    ctx.bar()


def emit_attn_std(ctx, sb, cfg, xnag, combine_fn):
    nc = ctx.nc
    PS = ctx.PS
    PSb = ctx.PS.ap().bitcast(BF16)
    identb = ctx.get_identb(sb)
    sb.mark()
    kbuf = sb.abf(8192)
    qbuf = sb.abf(8192)
    vbuf = sb.abf(ATT_NKT * 65 + 63)
    pbuf = [sb.abf(1536), sb.abf(1536)]
    zstage = sb.abf(8192)
    xnt = sb.abf(4096)
    sqt = sb.abf(512)
    sq2 = sb.abf(1024)
    ltmp = sb.af(TT)
    rmin = sb.af(8)
    nh64 = sb.abf(64)[0:64, 0:1]
    ctx.dmag(nh64, ctx.const("neghalf64", np.full((64, 1), -0.5, dtype=BF)))
    patt = None
    if cfg.get("patterns") is not None:
        patt = sb.abf(2048)
        ctx.dmag(patt, cfg["patterns"])
    ctx.dwait()

    causal = cfg["causal"]
    ss = cfg["sqrt_scale"]

    bufs = dict(kbuf=kbuf, qbuf=qbuf, vbuf=vbuf, sqt=sqt, sq2=sq2, ltmp=ltmp,
                rmin=rmin, nh64=nh64, xnt=xnt, identb=identb)

    def std_gen_pair(p):
        nc.vector.memset(vbuf, 1.0)
        nc.vector.memset(rmin[0:1, 0:1], 0.0)
        nc.vector.memset(kbuf[64:65, :], 1.0)
        ctx.bar()
        for t in range(ATT_NQT):
            r, u = t // 4, t % 4
            tsl = slice(t * TT, (t + 1) * TT)
            ctx.dma(xnt, xnag[r * 1024:(r + 1) * 1024,
                              u * TT:(u + 1) * TT].rearrange("(c p) t -> p c t", p=128))
            ctx.dwait()
            for proj in range(3):
                pso = [2 * TT, 0, TT][proj]
                for c in range(NCH):
                    nc.tensor.matmul(PS[0:64, pso:pso + TT],
                                     cfg["lhsT"](proj, p, c),
                                     xnt[:, c * TT:(c + 1) * TT],
                                     start=(c == 0), stop=(c == NCH - 1))
            ctx.bar()
            attn_evac_tile(ctx, sb, cfg, bufs, p, t)
        nc.vector.tensor_scalar(qbuf[64:65, :], qbuf[64:65, :], rmin[0:1, 0:1],
                                None, ALU.add)
        ctx.bar()

    for p in range(4):
        if cfg.get("gen_pair") is None:
            std_gen_pair(p)
        else:
            cfg["gen_pair"](p, bufs)
        if DEBUG_ATTN and p == 0:
            dk = nc.dram_tensor("dbg_k", [65, S], BF16, kind="ExternalOutput").ap()
            dq = nc.dram_tensor("dbg_q", [65, S], BF16, kind="ExternalOutput").ap()
            dv = nc.dram_tensor("dbg_v", [128, ATT_NKT * 65], BF16, kind="ExternalOutput").ap()
            ctx.dma(dk, kbuf[0:65, :])
            ctx.dma(dq, qbuf[0:65, :])
            ctx.dma(dv, vbuf[:, 0:ATT_NKT * 65])
            ctx.dwait()
        half_free_at = [0, 0]
        pbuf_free_at = [0, 0]
        for qt in range(ATT_NQT):
            q0 = qt * TT
            nkt = (q0 // 128 + 4) if causal else ATT_NKT
            units = [list(range(s0, min(s0 + 3, nkt))) for s0 in range(0, nkt, 3)]
            nz = 0
            for ui, unit in enumerate(units):
                h = ui % 2
                off = h * 1536
                w = len(unit) * TT
                nc.tensor.wait_ge(ctx.act_sem, half_free_at[h])
                for j, kt in enumerate(unit):
                    k0 = kt * 128
                    diag = causal and (k0 + 127 > q0)
                    mm = nc.tensor.matmul(PS[:, off + j * TT: off + (j + 1) * TT],
                                          kbuf[0:65, kt * 128:(kt + 1) * 128],
                                          qbuf[0:65, q0:q0 + TT],
                                          start=True, stop=not diag)
                    if diag:
                        rr = (k0 - q0) // 128
                        mm = nc.tensor.matmul(PS[:, off + j * TT: off + (j + 1) * TT],
                                              identb, patt[:, rr * TT:(rr + 1) * TT],
                                              start=False, stop=True)
                mm.then_inc(ctx.pe_sem, 1)
                ctx.pe_cnt += 1
                smm_pe = ctx.pe_cnt
                nc.scalar.wait_ge(ctx.pe_sem, max(smm_pe, pbuf_free_at[h]))
                nc.scalar.activation(pbuf[h][:, 0:w], PS[:, off:off + w],
                                     AF.Exp).then_inc(ctx.act_sem, 1)
                ctx.act_cnt += 1
                half_free_at[h] = ctx.act_cnt
                nc.tensor.wait_ge(ctx.act_sem, ctx.act_cnt)
                last_unit = (ui == len(units) - 1)
                for j, kt in enumerate(unit):
                    mm = nc.tensor.matmul(PS[0:65, 3072:3072 + TT],
                                          vbuf[:, kt * 65:(kt + 1) * 65],
                                          pbuf[h][:, j * TT:(j + 1) * TT],
                                          start=(nz == 0),
                                          stop=(last_unit and j == len(unit) - 1))
                    nz += 1
                mm.then_inc(ctx.pe_sem, 1)
                ctx.pe_cnt += 1
                pbuf_free_at[h] = ctx.pe_cnt
            nc.scalar.wait_ge(ctx.pe_sem, ctx.pe_cnt)
            nc.scalar.wait_ge(ctx.dsem, ctx.dcnt)
            nc.scalar.activation(zstage[0:64, q0:q0 + TT], PS[0:64, 3072:3072 + TT],
                                 AF.Copy)
            nc.scalar.activation(ltmp[0:1, :], PS[64:65, 3072:3072 + TT],
                                 AF.Copy).then_inc(ctx.act_sem, 1)
            ctx.act_cnt += 1
            nc.tensor.wait_ge(ctx.act_sem, ctx.act_cnt)
            nc.sync.wait_ge(ctx.act_sem, ctx.act_cnt)
            ctx.dma(cfg["l_in"].ap()[p:p + 1, q0:q0 + TT], ltmp[0:1, :])
        ctx.bar()
        ctx.dmag(cfg["agz_in"].ap()[p * 64:(p + 1) * 64, :], zstage[0:64, :])
        ctx.dwait()
        if DEBUG_ATTN and p == 0:
            dz = ctx.nc.dram_tensor("dbg_z", [64, S], BF16, kind="ExternalOutput").ap()
            dl = ctx.nc.dram_tensor("dbg_l", [1, S], F32, kind="ExternalOutput").ap()
            ctx.dma(dz, zstage[0:64, :])
            ctx.dwait()
            ctx.dmag(dl, cfg["l_in"].ap()[0:1, :])
            ctx.dwait()
    sb.release()

    ctx.coll("AllGather", ALU.bypass, GROUPS_BATCH,
             cfg["agz_in"].ap(), cfg["agz_out"].ap())
    ctx.coll("AllGather", ALU.bypass, GROUPS_BATCH,
             cfg["l_in"].ap(), cfg["l_out"].ap())
    ctx.cwait()
    emit_attn_out(ctx, sb, cfg, combine_fn)


def emit_attn_out(ctx, sb, cfg, combine_fn):
    nc = ctx.nc
    PS = ctx.PS
    sb.mark()
    zin = sb.abf(8 * TPC)
    lrec = sb.af(TPC)
    Efull = sb.af(1024)[0:16, :]
    ctx.dma(Efull, ctx.const("E16f", cfg["E16"]))
    qreg = ctx.qsnap("own")
    ctx.dma(zin.rearrange("p (r pph t) -> p r pph t", r=4, pph=2),
            cfg["agz_out"].ap()[0:1024, bass.ts(qreg, TPC)]
            .rearrange("(r pph pp2 d) t -> (pp2 d) r pph t", r=4, pph=2, pp2=2))
    ctx.dma(lrec[0:16, :], cfg["l_out"].ap()[0:16, bass.ts(qreg, TPC)])
    ctx.dwait()
    if DEBUG_ATTN:
        dlr = nc.dram_tensor("dbg_lrec", [16, TPC], F32, kind="ExternalOutput").ap()
        ctx.dma(dlr, lrec[0:16, :])
        ctx.dwait()
    nc.vector.reciprocal(lrec[0:16, :], lrec[0:16, :])
    ctx.bar()
    sg = sb.abf(TT)
    for t in range(NTT):
        tsl = slice(t * TT, (t + 1) * TT)
        for c in range(NCH):
            nc.tensor.matmul(PS[:, 3 * TT:4 * TT], Efull[:, c * 128:(c + 1) * 128],
                             lrec[0:16, tsl], start=True, stop=True)
            ctx.bar()
            nc.vector.tensor_tensor(sg, zin[:, c * TPC + t * TT: c * TPC + (t + 1) * TT],
                                    PS[:, 3 * TT:4 * TT], ALU.mult)
            ctx.bar()
            nc.vector.tensor_copy(zin[:, c * TPC + t * TT: c * TPC + (t + 1) * TT], sg)
            ctx.bar()
        for co in range(NCH):
            for ci in range(NCH):
                nc.tensor.matmul(PS[:, 0:TT], cfg["woT"](ci, co),
                                 zin[:, ci * TPC + t * TT: ci * TPC + (t + 1) * TT],
                                 start=(ci == 0), stop=(ci == NCH - 1))
            ctx.bar()
            if DEBUG_ATTN and t == 0 and co == 0:
                dsout = nc.dram_tensor("dbg_s00", [128, TT], F32, kind="ExternalOutput").ap()
                stmp_d = sb.af(TT)
                nc.vector.tensor_copy(stmp_d, PS[:, 0:TT])
                ctx.bar()
                ctx.dma(dsout, stmp_d)
                ctx.dwait()
            combine_fn(co, t, PS[:, 0:TT])
            ctx.bar()
    sb.release()



def causal_patterns():
    pat = np.zeros((128, 2048), dtype=np.float32)
    for r in range(4):
        kk = np.arange(128)[:, None]
        qq = np.arange(512)[None, :]
        pat[:, r * 512:(r + 1) * 512] = np.where(qq < kk + 128 * r, -1e9, 0.0)
    return pat.astype(BF)


def e16_const():
    E = np.zeros((16, 1024), dtype=np.float32)
    for h in range(16):
        E[h, h * 64:(h + 1) * 64] = 1.0
    return E


def attn_dram(ctx, tag):
    nc = ctx.nc
    d = {}
    d["agz_in"] = nc.dram_tensor(f"agzin_{tag}", [256, S], BF16, kind="Internal")
    d["agz_out"] = nc.dram_tensor(f"agzout_{tag}", [1024, S], BF16, kind="Internal")
    d["l_in"] = nc.dram_tensor(f"lin_{tag}", [4, S], F32, kind="Internal")
    d["l_out"] = nc.dram_tensor(f"lout_{tag}", [16, S], F32, kind="Internal")
    return d


def emit_layer2_attn(ctx, sb, xnag, g_dram, tag="l2"):
    nc = ctx.nc
    wqkv = ctx.inp("L2w_qkv_m", [D, 768], F32)
    bqkv = ctx.inp("L2b_qkv_m", [64, 12], F32)
    wo = ctx.inp("L2w_o", [D, D], F32)
    wob = ctx.inp("L2b_o", [128, 8], F32)
    sb.mark()
    Wb = sb.abf(NCH * 768)
    ctx.dmag(Wb, wqkv.rearrange("(c p) o -> p c o", p=128))
    Bia = sb.af(12)[0:64, :]
    ctx.dma(Bia, bqkv)
    Wob = sb.abf(NCH * 1024)
    ctx.dmag(Wob, wo.rearrange("(c p) o -> p c o", p=128))
    Wobi = sb.af(8)
    ctx.dma(Wobi, wob)
    gt = sb.af(TT)
    tmp = sb.af(TT)
    ctx.dwait()

    d = attn_dram(ctx, tag)
    cfg = {
        "causal": True,
        "sqrt_scale": float(1.0 / np.sqrt(np.sqrt(64.0))),
        "lhsT": lambda proj, h, c: Wb[:, c * 768 + proj * 256 + h * 64:
                                      c * 768 + proj * 256 + (h + 1) * 64],
        "bias": lambda proj, h: Bia[:, proj * 4 + h: proj * 4 + h + 1],
        "woT": lambda ci, co: Wob[:, ci * 1024 + co * 128: ci * 1024 + (co + 1) * 128],
        "patterns": ctx.const("causal_patt", causal_patterns()),
        "E16": e16_const(),
    }
    cfg.update(d)

    def combine(co, t, s_psum):
        tsl = slice(t * TT, (t + 1) * TT)
        ctx.dma(gt, g_dram[co * 128:(co + 1) * 128, tsl])
        nc.sync.wait_ge(ctx.dsem, ctx.dcnt)
        ctx.bar()
        nc.vector.tensor_scalar(tmp, s_psum, Wobi[0:128, co:co + 1], None, ALU.add)
        ctx.bar()
        nc.vector.tensor_mul(tmp, tmp, gt)
        ctx.bar()
        nc.vector.tensor_add(sb.x[co][:, tsl], sb.x[co][:, tsl], tmp)

    emit_attn_std(ctx, sb, cfg, xnag, combine)
    sb.release()


def emit_std_attn_layer(ctx, sb, xnag, g_dram, li, with_bias, wo_bias, tag):
    nc = ctx.nc
    wqkv = ctx.inp(f"L{li}w_qkv_m", [D, 768], F32)
    if with_bias:
        bqkv = ctx.inp(f"L{li}b_qkv_m", [64, 12], F32)
    wo = ctx.inp(f"L{li}w_o", [D, D], F32)
    if wo_bias:
        wob = ctx.inp(f"L{li}b_o", [128, 8], F32)
    sb.mark()
    Wb = sb.abf(NCH * 768)
    ctx.dmag(Wb, wqkv.rearrange("(c p) o -> p c o", p=128))
    if with_bias:
        Bia = sb.af(12)[0:64, :]
        ctx.dma(Bia, bqkv)
    Wob = sb.abf(NCH * 1024)
    ctx.dmag(Wob, wo.rearrange("(c p) o -> p c o", p=128))
    if wo_bias:
        Wobi = sb.af(8)
        ctx.dma(Wobi, wob)
    gt = sb.af(TT)
    tmp = sb.af(TT)
    ctx.dwait()

    d = attn_dram(ctx, tag)
    cfg = {
        "causal": True,
        "sqrt_scale": float(1.0 / np.sqrt(np.sqrt(64.0))),
        "lhsT": lambda proj, h, c: Wb[:, c * 768 + proj * 256 + h * 64:
                                      c * 768 + proj * 256 + (h + 1) * 64],
        "bias": (lambda proj, h: Bia[:, proj * 4 + h: proj * 4 + h + 1]) if with_bias
                else (lambda proj, h: None),
        "woT": lambda ci, co: Wob[:, ci * 1024 + co * 128: ci * 1024 + (co + 1) * 128],
        "patterns": ctx.const("causal_patt", causal_patterns()),
        "E16": e16_const(),
    }
    cfg.update(d)

    def combine(co, t, s_psum):
        tsl = slice(t * TT, (t + 1) * TT)
        ctx.dma(gt, g_dram[co * 128:(co + 1) * 128, tsl])
        nc.sync.wait_ge(ctx.dsem, ctx.dcnt)
        ctx.bar()
        if wo_bias:
            nc.vector.tensor_scalar(tmp, s_psum, Wobi[0:128, co:co + 1], None, ALU.add)
            ctx.bar()
            nc.vector.tensor_mul(tmp, tmp, gt)
        else:
            nc.vector.tensor_mul(tmp, s_psum, gt)
        ctx.bar()
        nc.vector.tensor_add(sb.x[co][:, tsl], sb.x[co][:, tsl], tmp)

    emit_attn_std(ctx, sb, cfg, xnag, combine)
    sb.release()


def emit_layer1_attn(ctx, sb, xnag, xn_own, g_dram, tag="l1"):
    nc = ctx.nc
    PS = ctx.PS
    wq_n = ctx.inp("L1w_q_m", [64, 4096], F32)
    wkb_n = ctx.inp("L1w_kb_m", [64, 1024], F32)
    wvbT = ctx.inp("L1w_vbT_m", [256, 256], F32)
    kaT = ctx.inp("L1w_kaT", [D, 256], F32)
    vaT = ctx.inp("L1w_vaT", [D, 256], F32)
    RT = ctx.inp("L1_RT_m", [64, 256], F32)
    woT = ctx.inp("L1w_oT", [D, D], F32)
    sb.mark()
    lv_buf = sb.abf(512)
    ctx.dmag(lv_buf, wvbT.rearrange("(lc p) o -> p lc o", p=128))
    Wob = sb.abf(NCH * 1024)
    ctx.dmag(Wob, woT.rearrange("(c p) o -> p c o", p=128))
    lq_buf = sb.abf(2048)
    lk_buf = sb.abf(512)
    gt = sb.af(TT)
    tmp = sb.af(TT)
    latt = sb.abf(2048)
    sb.mark()
    wqn = sb.abf(4096)[0:64, :]
    ctx.dmag(wqn, wq_n)
    wkbn = sb.abf(1024)[0:64, :]
    ctx.dmag(wkbn, wkb_n)
    kaTb = sb.abf(NCH * 256)
    ctx.dmag(kaTb, kaT.rearrange("(c p) o -> p c o", p=128))
    vaTb = sb.abf(NCH * 256)
    ctx.dmag(vaTb, vaT.rearrange("(c p) o -> p c o", p=128))
    RTb = sb.abf(256)[0:64, :]
    ctx.dmag(RTb, RT)
    ctx.dwait()

    for h in range(4):
        for c in range(NCH):
            nc.tensor.matmul(PS[:, 0:64], wqn[0:64, h * 1024 + c * 128: h * 1024 + (c + 1) * 128],
                             RTb[0:64, h * 64:(h + 1) * 64], start=True, stop=True)
            ctx.bar()
            nc.vector.tensor_copy(lq_buf[:, (h * 8 + c) * 64:(h * 8 + c + 1) * 64],
                                  PS[:, 0:64])
            ctx.bar()
        for lc in range(2):
            nc.tensor.matmul(PS[:, 0:64], wkbn[0:64, h * 256 + lc * 128: h * 256 + (lc + 1) * 128],
                             RTb[0:64, h * 64:(h + 1) * 64], start=True, stop=True)
            ctx.bar()
            nc.vector.tensor_copy(lk_buf[:, (h * 2 + lc) * 64:(h * 2 + lc + 1) * 64],
                                  PS[:, 0:64])
            ctx.bar()

    lat_in = nc.dram_tensor(f"latin_{tag}", [512, TPC], BF16, kind="Internal")
    lat_out = nc.dram_tensor(f"latout_{tag}", [2048, TPC], BF16, kind="Internal")
    xnt_own = sb.abf(4096)
    latk = sb.abf(2 * TPC)
    latv = sb.abf(2 * TPC)
    for t in range(NTT):
        tsl = slice(t * TT, (t + 1) * TT)
        ctx.dma(xnt_own, xn_own[0:1024, tsl].rearrange("(c p) t -> p c t", p=128))
        ctx.dwait()
        for oc in range(2):
            for c in range(NCH):
                nc.tensor.matmul(PS[:, 0:TT],
                                 kaTb[:, c * 256 + oc * 128: c * 256 + (oc + 1) * 128],
                                 xnt_own[:, c * TT:(c + 1) * TT],
                                 start=(c == 0), stop=(c == NCH - 1))
            for c in range(NCH):
                nc.tensor.matmul(PS[:, TT:2 * TT],
                                 vaTb[:, c * 256 + oc * 128: c * 256 + (oc + 1) * 128],
                                 xnt_own[:, c * TT:(c + 1) * TT],
                                 start=(c == 0), stop=(c == NCH - 1))
            ctx.bar()
            nc.vector.tensor_copy(latk[:, oc * TPC + t * TT: oc * TPC + (t + 1) * TT],
                                  PS[:, 0:TT])
            nc.vector.tensor_copy(latv[:, oc * TPC + t * TT: oc * TPC + (t + 1) * TT],
                                  PS[:, TT:2 * TT])
            ctx.bar()
    for oc in range(2):
        ctx.dma(lat_in.ap()[oc * 128:(oc + 1) * 128, :], latk[:, oc * TPC:(oc + 1) * TPC])
        ctx.dma(lat_in.ap()[256 + oc * 128: 256 + (oc + 1) * 128, :],
                latv[:, oc * TPC:(oc + 1) * TPC])
    ctx.dwait()
    sb.release()
    ctx.coll("AllGather", ALU.bypass, GROUPS_BATCH, lat_in.ap(), lat_out.ap())
    ctx.cwait()

    d = attn_dram(ctx, tag)
    cfg = {
        "causal": False,
        "sqrt_scale": float(1.0 / np.sqrt(np.sqrt(64.0))),
        "bias": lambda proj, h: None,
        "woT": lambda ci, co: Wob[:, ci * 1024 + co * 128: ci * 1024 + (co + 1) * 128],
        "patterns": None,
        "E16": e16_const(),
    }
    cfg.update(d)


    def gen_pair(p, bufs):
        kbuf, qbuf, vbuf = bufs["kbuf"], bufs["qbuf"], bufs["vbuf"]
        rmin = bufs["rmin"]
        xnt = bufs["xnt"]
        nc.vector.memset(vbuf, 1.0)
        nc.vector.memset(rmin[0:1, 0:1], 0.0)
        nc.vector.memset(kbuf[64:65, :], 1.0)
        ctx.bar()
        for t in range(ATT_NQT):
            r, u = t // 4, t % 4
            ctx.dma(xnt, xnag[r * 1024:(r + 1) * 1024,
                              u * TT:(u + 1) * TT].rearrange("(c p) t -> p c t", p=128))
            ctx.dma(latt, lat_out.ap()[r * 512:(r + 1) * 512,
                                       u * TT:(u + 1) * TT].rearrange("(kv p) t -> p kv t", p=128))
            ctx.dwait()
            for lc in range(2):
                nc.tensor.matmul(PS[0:64, 0:TT], lk_buf[:, (p * 2 + lc) * 64:(p * 2 + lc + 1) * 64],
                                 latt[:, lc * TT:(lc + 1) * TT],
                                 start=(lc == 0), stop=(lc == 1))
            for lc in range(2):
                nc.tensor.matmul(PS[0:64, TT:2 * TT], lv_buf[:, lc * 256 + p * 64: lc * 256 + (p + 1) * 64],
                                 latt[:, (2 + lc) * TT:(3 + lc) * TT],
                                 start=(lc == 0), stop=(lc == 1))
            for c in range(NCH):
                nc.tensor.matmul(PS[0:64, 2 * TT:3 * TT],
                                 lq_buf[:, (p * 8 + c) * 64:(p * 8 + c + 1) * 64],
                                 xnt[:, c * TT:(c + 1) * TT],
                                 start=(c == 0), stop=(c == NCH - 1))
            ctx.bar()
            attn_evac_tile(ctx, sb, cfg, bufs, p, t)
        nc.vector.tensor_scalar(qbuf[64:65, :], qbuf[64:65, :], rmin[0:1, 0:1],
                                None, ALU.add)
        ctx.bar()

    cfg["gen_pair"] = gen_pair

    def combine(co, t, s_psum):
        tsl = slice(t * TT, (t + 1) * TT)
        ctx.dma(gt, g_dram[co * 128:(co + 1) * 128, tsl])
        nc.sync.wait_ge(ctx.dsem, ctx.dcnt)
        ctx.bar()
        nc.vector.tensor_mul(tmp, s_psum, gt)
        ctx.bar()
        nc.vector.tensor_add(sb.x[co][:, tsl], sb.x[co][:, tsl], tmp)

    emit_attn_std(ctx, sb, cfg, xnag, combine)
    sb.release()



def fp4_qdq(ctx, sb, x_ap, out_bf_ap, F, scr, P=128):
    nc = ctx.nc
    av, v, q, u, sc, rc, sgn = (scr["av"], scr["v"], scr["q"], scr["u"],
                                scr["sc"], scr["rc"], scr["sgn"])
    nb = F // 16
    nc.scalar.activation(av[:, 0:F], x_ap, AF.Abs)
    ctx.edge("scalar", "vector")
    nc.vector.tensor_reduce(u[:, 0:nb], av[:, 0:F].rearrange("p (a b) -> p a b", b=16),
                            AX, ALU.max)
    ctx.bar()
    nc.vector.tensor_scalar(sc[:, 0:nb], u[:, 0:nb], 1.0 / 6.0, 1e-12,
                            ALU.mult, ALU.max)
    ctx.bar()
    nc.vector.reciprocal(rc[:, 0:nb], sc[:, 0:nb])
    ctx.bar()
    rcb = rc[:, 0:nb].unsqueeze(2).broadcast_to([P, nb, 16])
    nc.vector.tensor_tensor(v[:, 0:F].rearrange("p (a b) -> p a b", b=16),
                            av[:, 0:F].rearrange("p (a b) -> p a b", b=16),
                            rcb, ALU.mult)
    ctx.edge("vector", "scalar")
    nc.scalar.activation(sgn[:, 0:F], x_ap, AF.Sign)
    ctx.bar()
    nc.vector.tensor_scalar(q[:, 0:F], v[:, 0:F], 2.0, MAGIC, ALU.mult, ALU.add)
    nc.vector.tensor_scalar(q[:, 0:F], q[:, 0:F], MAGIC, 0.5, ALU.subtract, ALU.mult)
    nc.vector.tensor_scalar(q[:, 0:F], q[:, 0:F], 2.0, None, ALU.min)
    nc.vector.tensor_scalar(u[:, 0:F], v[:, 0:F], 2.5, None, ALU.is_gt)
    nc.vector.tensor_add(q[:, 0:F], q[:, 0:F], u[:, 0:F])
    nc.vector.tensor_scalar(u[:, 0:F], v[:, 0:F], 3.5, None, ALU.is_gt)
    nc.vector.tensor_add(q[:, 0:F], q[:, 0:F], u[:, 0:F])
    nc.vector.tensor_scalar(u[:, 0:F], v[:, 0:F], 5.0, 2.0, ALU.is_gt, ALU.mult)
    nc.vector.tensor_add(q[:, 0:F], q[:, 0:F], u[:, 0:F])
    nc.vector.tensor_mul(q[:, 0:F], q[:, 0:F], sgn[:, 0:F])
    scb = sc[:, 0:nb].unsqueeze(2).broadcast_to([P, nb, 16])
    nc.vector.tensor_tensor(out_bf_ap.rearrange("p (a b) -> p a b", b=16),
                            q[:, 0:F].rearrange("p (a b) -> p a b", b=16),
                            scb, ALU.mult)
    ctx.bar()


def emit_l2_ffn(ctx, sb, stats_dram, g_dram, tag="l2f"):
    nc = ctx.nc
    PS = ctx.PS
    PSb = ctx.PS.ap().bitcast(BF16)
    identb = ctx.get_identb(sb)
    identf = ctx._identf_sb
    wg_in = ctx.inp("L2f_wg_m", [384, 1024], F32)
    wu_in = ctx.inp("L2f_wu_m", [384, 1024], F32)
    wd_in = ctx.inp("L2f_wd_m", [128, 2816], F32)
    HD3 = 3072
    wgT = nc.dram_tensor(f"wgT_{tag}", [8192, 384], BF16, kind="Internal")
    wuT = nc.dram_tensor(f"wuT_{tag}", [8192, 384], BF16, kind="Internal")
    wdT = nc.dram_tensor(f"wdT_{tag}", [8 * 2816, 128], BF16, kind="Internal")
    wgT_l = nc.dram_tensor(f"wgTl_{tag}", [1024, 384], BF16, kind="Internal")
    wuT_l = nc.dram_tensor(f"wuTl_{tag}", [1024, 384], BF16, kind="Internal")
    wdT_l = nc.dram_tensor(f"wdTl_{tag}", [2816, 128], BF16, kind="Internal")
    h_dram = nc.dram_tensor(f"h_{tag}", [HD3, TPC], F32, kind="Internal")
    xq_dram = nc.dram_tensor(f"xq_{tag}", [D, TPC], BF16, kind="Internal")

    sb.mark()
    wt = sb.af(2816)
    wq_b = sb.abf(2816)
    wtr = sb.abf(2816)
    scr = dict(av=sb.af(2816), v=sb.af(2816), q=sb.af(2816), u=sb.af(2816),
               sc=sb.af(192), rc=sb.af(192), sgn=sb.af(2816))
    for (win, wTl) in ((wg_in, wgT_l), (wu_in, wuT_l)):
        for rb in range(3):
            ctx.dma(wt[:, 0:1024], win[rb * 128:(rb + 1) * 128, :])
            ctx.dwait()
            fp4_qdq(ctx, sb, wt[:, 0:1024], wq_b[:, 0:1024], 1024, scr)
            tr_copy(ctx, [(wq_b[:, c * 128:(c + 1) * 128], wtr[:, c * 128:(c + 1) * 128])
                          for c in range(NCH)])
            ctx.dma(wTl.ap()[:, rb * 128:(rb + 1) * 128]
                    .rearrange("(c p) o -> p c o", p=128),
                    wtr[:, 0:1024].rearrange("p (c o) -> p c o", o=128))
            ctx.dwait()
    ctx.dma(wt, wd_in)
    ctx.dwait()
    fp4_qdq(ctx, sb, wt[:, 0:2816], wq_b[:, 0:2816], 2816, scr)
    tr_copy(ctx, [(wq_b[:, cc * 128:(cc + 1) * 128], wtr[:, cc * 128:(cc + 1) * 128])
                  for cc in range(22)])
    ctx.dma(wdT_l.ap().rearrange("(cc p) o -> p cc o", p=128),
            wtr[:, 0:2816].rearrange("p (cc o) -> p cc o", o=128))
    ctx.dwait()
    sb.release()
    ctx.coll("AllGather", ALU.bypass, GROUP_ALL, wgT_l.ap(), wgT.ap())
    ctx.coll("AllGather", ALU.bypass, GROUP_ALL, wuT_l.ap(), wuT.ap())
    ctx.coll("AllGather", ALU.bypass, GROUP_ALL, wdT_l.ap(), wdT.ap())
    ctx.cwait()

    if L2F_PHASES < 2:
        return
    sb.mark()
    strow = sb.af(2 * TT)[0:1, :]
    strowb = sb.abf(2 * TT)[0:1, :]
    onesb2 = sb.abf(128)[0:1, :]
    nc.vector.memset(onesb2, 1.0)
    xnf = sb.af(TT)
    xtok = sb.af(1024)
    xqtok = sb.abf(1024)
    xq_fm = sb.abf(1024)
    scr = dict(av=sb.af(1024), v=sb.af(1024), q=sb.af(1024), u=sb.af(1024),
               sc=sb.af(64), rc=sb.af(64), sgn=sb.af(1024))
    for t in range(NTT):
        tsl = slice(t * TT, (t + 1) * TT)
        ctx.dma(strow[0:1, 0:TT], stats_dram.ap()[0:1, tsl])
        ctx.dma(strow[0:1, TT:2 * TT], stats_dram.ap()[1:2, tsl])
        ctx.dwait()
        nc.vector.tensor_copy(strowb, strow)
        ctx.edge("vector", "tensor")
        nc.tensor.matmul(PS[:, 2 * TT:3 * TT], onesb2, strowb[0:1, 0:TT],
                         start=True, stop=True)
        nc.tensor.matmul(PS[:, 3 * TT:4 * TT], onesb2, strowb[0:1, TT:2 * TT],
                         start=True, stop=True)
        ctx.edge("tensor", "vector")
        for tb in range(4):
            for c in range(NCH):
                ctx.edge("tensor", "vector")
                nc.vector.tensor_tensor(xnf, sb.x[c][:, tsl], PS[:, 2 * TT:3 * TT],
                                        ALU.mult)
                nc.vector.tensor_add(xnf, xnf, PS[:, 3 * TT:4 * TT])
                ctx.edge("vector", "tensor")
                nc.tensor.transpose(PS[:, 0:128], xnf[:, tb * 128:(tb + 1) * 128],
                                    identf)
                ctx.edge("tensor", "vector")
                nc.vector.tensor_copy(xtok[:, c * 128:(c + 1) * 128], PS[:, 0:128])
                ctx.edge("vector", "tensor")
            ctx.edge("vector", "scalar")
            fp4_qdq(ctx, sb, xtok[:, 0:1024], xqtok[:, 0:1024], 1024, scr)
            tr_copy(ctx, [(xqtok[:, c * 128:(c + 1) * 128], xq_fm[:, c * 128:(c + 1) * 128])
                          for c in range(NCH)])
            ctx.dma(xq_dram.ap()[:, t * TT + tb * 128: t * TT + (tb + 1) * 128]
                    .rearrange("(c p) t -> p c t", p=128),
                    xq_fm[:, 0:1024].rearrange("p (c t) -> p c t", t=128))
            nc.sync.wait_ge(ctx.dsem, ctx.dcnt)
            nc.vector.wait_ge(ctx.dsem, ctx.dcnt)
            nc.tensor.wait_ge(ctx.dsem, ctx.dcnt)
    sb.release()

    if L2F_PHASES < 3:
        return
    sb.mark()
    lg = sb.abf(NCH * 128)
    lu = sb.abf(NCH * 128)
    xqt = sb.abf(NCH * TT)
    hsil = sb.af(TT)
    hf = sb.af(TT)
    for oc in range(24):
        r, co = oc // 3, (oc % 3) * 128
        ctx.edge("tensor", "sync")
        ctx.dma(lg, wgT.ap()[r * 1024:(r + 1) * 1024, co:co + 128]
                .rearrange("(c p) o -> p c o", p=128))
        ctx.dma(lu, wuT.ap()[r * 1024:(r + 1) * 1024, co:co + 128]
                .rearrange("(c p) o -> p c o", p=128))
        for t in range(NTT):
            tsl = slice(t * TT, (t + 1) * TT)
            ctx.dma(xqt, xq_dram.ap()[:, tsl].rearrange("(c p) t -> p c t", p=128))
            ctx.dwait()
            for c in range(NCH):
                nc.tensor.matmul(PS[:, 0:TT], lg[:, c * 128:(c + 1) * 128],
                                 xqt[:, c * TT:(c + 1) * TT],
                                 start=(c == 0), stop=(c == NCH - 1))
            for c in range(NCH):
                nc.tensor.matmul(PS[:, TT:2 * TT], lu[:, c * 128:(c + 1) * 128],
                                 xqt[:, c * TT:(c + 1) * TT],
                                 start=(c == 0), stop=(c == NCH - 1))
            ctx.edge("tensor", "scalar")
            nc.scalar.activation(hsil, PS[:, 0:TT], AF.Silu)
            ctx.edge("scalar", "vector", "tensor")
            nc.vector.tensor_tensor(hf, hsil, PS[:, TT:2 * TT], ALU.mult)
            ctx.edge("vector", "sync", "tensor")
            ctx.dma(h_dram.ap()[oc * 128:(oc + 1) * 128, tsl], hf)
            nc.sync.wait_ge(ctx.dsem, ctx.dcnt)
            nc.vector.wait_ge(ctx.dsem, ctx.dcnt)
            nc.scalar.wait_ge(ctx.dsem, ctx.dcnt)
    sb.release()

    if L2F_PHASES < 4:
        return
    sb.mark()
    htok = sb.af(HD3)
    hqtok = sb.abf(HD3)
    hq_fm = sb.abf(22 * TT)
    ld = sb.abf(22 * 128)
    hin = sb.af(HD3)
    _av = sb.af(HD3)
    scr = dict(av=_av, v=sb.af(HD3), q=htok, u=sb.af(HD3),
               sc=sb.af(192), rc=sb.af(192), sgn=_av)
    gt2 = sb.af(TT)
    for t in range(NTT):
        tsl = slice(t * TT, (t + 1) * TT)
        for tb in range(4):
            tk = slice(t * TT + tb * 128, t * TT + (tb + 1) * 128)
            ctx.edge("tensor", "sync")
            ctx.dma(hin.rearrange("p (cc o) -> p cc o", o=128),
                    h_dram.ap()[:, tk].rearrange("(cc p) o -> p cc o", p=128))
            ctx.dwait()
            tr_copy(ctx, [(hin[:, cc * 128:(cc + 1) * 128],
                           htok[:, cc * 128:(cc + 1) * 128]) for cc in range(24)],
                    bf16=False)
            fp4_qdq(ctx, sb, htok[:, 0:HD3], hqtok[:, 0:HD3], HD3, scr)
            tr_copy(ctx, [(hqtok[:, cc * 128:(cc + 1) * 128],
                           hq_fm[:, cc * TT + tb * 128: cc * TT + (tb + 1) * 128])
                          for cc in range(22)])
        for oc in range(NCH):
            ctx.edge("tensor", "sync")
            ctx.dma(ld, wdT.ap()[oc * 2816:(oc + 1) * 2816, :]
                    .rearrange("(cc p) o -> p cc o", p=128))
            ctx.dwait()
            for cc in range(22):
                nc.tensor.matmul(PS[:, 0:TT], ld[:, cc * 128:(cc + 1) * 128],
                                 hq_fm[:, cc * TT:(cc + 1) * TT],
                                 start=(cc == 0), stop=(cc == 21))
            ctx.edge("tensor", "scalar")
            nc.scalar.activation(gt2, PS[:, 0:TT], AF.Silu)
            ctx.edge("scalar", "sync", "tensor")
            ctx.dma(g_dram.ap()[oc * 128:(oc + 1) * 128, tsl], gt2)
            nc.sync.wait_ge(ctx.dsem, ctx.dcnt)
            nc.scalar.wait_ge(ctx.dsem, ctx.dcnt)
    sb.release()


def lpbq_qdq(ctx, sb, x_ap, out_bf_ap, F, scr, block=32):
    nc = ctx.nc
    av, v, q, u, sc, rc = (scr["av"], scr["v"], scr["q"], scr["u"],
                           scr["sc"], scr["rc"])
    nb = F // block
    nc.scalar.activation(av[:, 0:F], x_ap, AF.Abs)
    ctx.edge("scalar", "vector")
    nc.vector.tensor_reduce(u[:, 0:nb], av[:, 0:F].rearrange("p (a b) -> p a b", b=block),
                            AX, ALU.max)
    ctx.bar()
    nc.vector.tensor_scalar(sc[:, 0:nb], u[:, 0:nb], 1.0 / 7.0, 1e-12, ALU.mult, ALU.max)
    ctx.bar()
    nc.vector.reciprocal(rc[:, 0:nb], sc[:, 0:nb])
    ctx.bar()
    rcb = rc[:, 0:nb].unsqueeze(2).broadcast_to([128, nb, block])
    nc.vector.tensor_tensor(v[:, 0:F].rearrange("p (a b) -> p a b", b=block),
                            x_ap.rearrange("p (a b) -> p a b", b=block), rcb, ALU.mult)
    nc.vector.tensor_scalar(v[:, 0:F], v[:, 0:F], MAGIC, None, ALU.add)
    nc.vector.tensor_scalar(v[:, 0:F], v[:, 0:F], MAGIC, None, ALU.subtract)
    scb = sc[:, 0:nb].unsqueeze(2).broadcast_to([128, nb, block])
    nc.vector.tensor_tensor(out_bf_ap.rearrange("p (a b) -> p a b", b=block),
                            v[:, 0:F].rearrange("p (a b) -> p a b", b=block), scb, ALU.mult)
    ctx.bar()


def int_qdq(ctx, in_f32, out_ap, r_ap, s_ap):
    nc = ctx.nc
    nc.vector.tensor_scalar(out_ap, in_f32, r_ap, MAGIC, ALU.mult, ALU.add)
    nc.vector.tensor_scalar(out_ap, out_ap, MAGIC, s_ap, ALU.subtract, ALU.mult)


def gmax_to_slot(ctx, sb, mx128, gdram, col):
    nc = ctx.nc
    sb.mark()
    row = sb.af(128)[0:1, :]
    ctx.edge("vector", "tensor")
    nc.tensor.transpose(ctx.PS[0:1, 3584:3584 + 128], mx128, ctx._identf_sb)
    ctx.edge("tensor", "vector")
    nc.vector.tensor_copy(row, ctx.PS[0:1, 3584:3584 + 128])
    ctx.bar()
    nc.vector.reduce_max(row[0:1, 0:1], row[0:1, 0:128], axis=AX)
    ctx.bar()
    ctx.dma(gdram.ap()[0:1, col:col + 1], row[0:1, 0:1])
    ctx.dwait()
    sb.release()


def scale_bcast(ctx, sb, gdram_out, col, qmax, r_out, s_out, onesb):
    nc = ctx.nc
    sb.mark()
    sv = sb.af(8)[0:1, 0:2]
    svb = sb.abf(8)[0:1, 0:2]
    ctx.dma(sv[0:1, 0:1], gdram_out.ap()[0:1, col:col + 1])
    ctx.dwait()
    nc.vector.tensor_scalar(sv[0:1, 0:1], sv[0:1, 0:1], 1.0 / qmax, None, ALU.mult)
    ctx.bar()
    nc.vector.reciprocal(sv[0:1, 1:2], sv[0:1, 0:1])
    ctx.bar()
    nc.vector.tensor_copy(svb, sv[0:1, 0:2])
    ctx.edge("vector", "tensor")
    nc.tensor.matmul(ctx.PS[:, 3584:3584 + 2], onesb, svb, start=True, stop=True)
    ctx.edge("tensor", "vector")
    nc.vector.tensor_copy(s_out, ctx.PS[:, 3584:3585])
    nc.vector.tensor_copy(r_out, ctx.PS[:, 3585:3586])
    ctx.bar()
    sb.release()


def emit_l1_ffn(ctx, sb, stats_dram, g_dram, xn_absmax_row, tag="l1f"):
    nc = ctx.nc
    PS = ctx.PS
    identb = ctx.get_identb(sb)
    identf = ctx._identf_sb
    wg_in = ctx.inp("L1f_wg_m", [384, 1024], F32)
    wu_in = ctx.inp("L1f_wu_m", [384, 1024], F32)
    wd_in = ctx.inp("L1f_wd_m", [128, 3072], F32)
    wgT = nc.dram_tensor(f"wgT_{tag}", [8192, 384], BF16, kind="Internal")
    wuT = nc.dram_tensor(f"wuT_{tag}", [8192, 384], BF16, kind="Internal")
    wdT = nc.dram_tensor(f"wdT_{tag}", [8 * 3072, 128], BF16, kind="Internal")
    wgT_l = nc.dram_tensor(f"wgTl_{tag}", [1024, 384], BF16, kind="Internal")
    wuT_l = nc.dram_tensor(f"wuTl_{tag}", [1024, 384], BF16, kind="Internal")
    wdT_l = nc.dram_tensor(f"wdTl_{tag}", [3072, 128], BF16, kind="Internal")
    up_dram = nc.dram_tensor(f"up_{tag}", [3072, TPC], F32, kind="Internal")
    gate_dram = nc.dram_tensor(f"gate_{tag}", [3072, TPC], F32, kind="Internal")
    gmax_in = nc.dram_tensor(f"gmi_{tag}", [1, 8], F32, kind="Internal")
    gmax_out = [nc.dram_tensor(f"gmo_{tag}{i}", [1, 8], F32, kind="Internal")
                for i in range(3)]

    sb.mark()
    z8 = sb.af(8)[0:1, :]
    nc.vector.memset(z8, 0.0)
    ctx.bar()
    nc.vector.tensor_copy(z8[0:1, 0:1], xn_absmax_row)
    ctx.bar()
    ctx.dma(gmax_in.ap(), z8)
    ctx.dwait()
    ctx.coll("AllReduce", ALU.max, GROUP_ALL, gmax_in.ap(), gmax_out[0].ap())
    sb.release()

    sb.mark()
    wt = sb.af(3072)
    wq_b = sb.abf(3072)
    wtr = sb.abf(3072)
    scr = dict(av=sb.af(3072), v=sb.af(3072), q=None, u=sb.af(3072),
               sc=sb.af(96), rc=sb.af(96))
    for (win, wTl) in ((wg_in, wgT_l), (wu_in, wuT_l)):
        for rb in range(3):
            ctx.dma(wt[:, 0:1024], win[rb * 128:(rb + 1) * 128, :])
            ctx.dwait()
            lpbq_qdq(ctx, sb, wt[:, 0:1024], wq_b[:, 0:1024], 1024, scr)
            tr_copy(ctx, [(wq_b[:, c * 128:(c + 1) * 128], wtr[:, c * 128:(c + 1) * 128])
                          for c in range(NCH)])
            ctx.dma(wTl.ap()[:, rb * 128:(rb + 1) * 128]
                    .rearrange("(c p) o -> p c o", p=128),
                    wtr[:, 0:1024].rearrange("p (c o) -> p c o", o=128))
            ctx.dwait()
    ctx.dma(wt, wd_in)
    ctx.dwait()
    lpbq_qdq(ctx, sb, wt[:, 0:3072], wq_b[:, 0:3072], 3072, scr)
    tr_copy(ctx, [(wq_b[:, cc * 128:(cc + 1) * 128], wtr[:, cc * 128:(cc + 1) * 128])
                  for cc in range(24)])
    ctx.dma(wdT_l.ap().rearrange("(cc p) o -> p cc o", p=128),
            wtr[:, 0:3072].rearrange("p (cc o) -> p cc o", o=128))
    ctx.dwait()
    sb.release()
    ctx.coll("AllGather", ALU.bypass, GROUP_ALL, wgT_l.ap(), wgT.ap())
    ctx.coll("AllGather", ALU.bypass, GROUP_ALL, wuT_l.ap(), wuT.ap())
    ctx.coll("AllGather", ALU.bypass, GROUP_ALL, wdT_l.ap(), wdT.ap())
    ctx.cwait()

    sb.mark()
    onesb2 = sb.abf(128)[0:1, :]
    nc.vector.memset(onesb2, 1.0)
    ctx.bar()
    r1 = sb.af(8)[:, 0:1]
    s1 = sb.af(8)[:, 0:1]
    scale_bcast(ctx, sb, gmax_out[0], 0, 32767.0, r1, s1, onesb2)
    xq = sb.abf(NCH * TPC)
    strow = sb.af(2 * TT)[0:1, :]
    strowb = sb.abf(2 * TT)[0:1, :]
    xnf = sb.af(TT)
    xqf = sb.af(TT)
    mxu = sb.af(8)[:, 0:1]
    mxg = sb.af(8)[:, 1:2]
    red = sb.af(8)[:, 2:3]
    for t in range(NTT):
        tsl = slice(t * TT, (t + 1) * TT)
        ctx.dma(strow[0:1, 0:TT], stats_dram.ap()[0:1, tsl])
        ctx.dma(strow[0:1, TT:2 * TT], stats_dram.ap()[1:2, tsl])
        ctx.dwait()
        nc.vector.tensor_copy(strowb, strow)
        ctx.edge("vector", "tensor")
        nc.tensor.matmul(PS[:, 2 * TT:3 * TT], onesb2, strowb[0:1, 0:TT],
                         start=True, stop=True)
        nc.tensor.matmul(PS[:, 3 * TT:4 * TT], onesb2, strowb[0:1, TT:2 * TT],
                         start=True, stop=True)
        ctx.edge("tensor", "vector")
        for c in range(NCH):
            nc.vector.tensor_tensor(xnf, sb.x[c][:, tsl], PS[:, 2 * TT:3 * TT], ALU.mult)
            nc.vector.tensor_add(xnf, xnf, PS[:, 3 * TT:4 * TT])
            int_qdq(ctx, xnf, xqf, r1, s1)
            nc.vector.tensor_copy(xq[:, c * TPC + t * TT: c * TPC + (t + 1) * TT], xqf)
        ctx.bar()

    lw = sb.abf(NCH * 128)
    upf = sb.af(TT)
    nc.vector.memset(mxu, 0.0)
    nc.vector.memset(mxg, 0.0)
    ctx.bar()
    for wi, (wT, odram, mx) in enumerate(((wuT, up_dram, mxu), (wgT, gate_dram, mxg))):
        for oc in range(24):
            r, co = oc // 3, (oc % 3) * 128
            ctx.edge("tensor", "sync")
            ctx.dma(lw, wT.ap()[r * 1024:(r + 1) * 1024, co:co + 128]
                    .rearrange("(c p) o -> p c o", p=128))
            ctx.dwait()
            for t in range(NTT):
                tsl = slice(t * TT, (t + 1) * TT)
                for c in range(NCH):
                    nc.tensor.matmul(PS[:, 0:TT], lw[:, c * 128:(c + 1) * 128],
                                     xq[:, c * TPC + t * TT: c * TPC + (t + 1) * TT],
                                     start=(c == 0), stop=(c == NCH - 1))
                ctx.edge("tensor", "scalar")
                nc.scalar.activation(upf, PS[:, 0:TT], AF.Copy)
                ctx.edge("scalar", "sync", "vector", "tensor")
                nc.vector.reduce_max(red, upf, axis=AX, apply_absolute_value=True)
                nc.vector.tensor_max(mx, mx, red)
                ctx.dma(odram.ap()[oc * 128:(oc + 1) * 128, tsl], upf)
                nc.sync.wait_ge(ctx.dsem, ctx.dcnt)
                nc.scalar.wait_ge(ctx.dsem, ctx.dcnt)
    gmax_to_slot(ctx, sb, mxu, gmax_in, 1)
    gmax_to_slot(ctx, sb, mxg, gmax_in, 2)
    ctx.coll("AllReduce", ALU.max, GROUP_ALL, gmax_in.ap(), gmax_out[1].ap())
    ctx.cwait()

    r2 = sb.af(8)[:, 0:1]
    s2 = sb.af(8)[:, 0:1]
    r3 = sb.af(8)[:, 0:1]
    s3 = sb.af(8)[:, 0:1]
    scale_bcast(ctx, sb, gmax_out[1], 1, 32767.0, r2, s2, onesb2)
    scale_bcast(ctx, sb, gmax_out[1], 2, 32767.0, r3, s3, onesb2)
    mx4 = sb.af(8)[:, 3:4]
    nc.vector.memset(mx4, 0.0)
    gq = sb.af(TT)
    sg = sb.af(TT)
    prod = sb.af(TT)
    for oc in range(24):
        for t in range(NTT):
            tsl = slice(t * TT, (t + 1) * TT)
            ctx.edge("vector", "sync")
            ctx.dma(upf, gate_dram.ap()[oc * 128:(oc + 1) * 128, tsl])
            ctx.dwait()
            int_qdq(ctx, upf, gq, r3, s3)
            ctx.edge("vector", "scalar")
            nc.scalar.activation(sg, gq, AF.Sigmoid)
            ctx.edge("scalar", "vector")
            nc.vector.tensor_scalar(sg, sg, 65536.0, MAGIC, ALU.mult, ALU.add)
            nc.vector.tensor_scalar(sg, sg, MAGIC, 65535.0, ALU.subtract, ALU.min)
            nc.vector.tensor_mul(prod, gq, sg)
            nc.vector.tensor_scalar(prod, prod, 1.0 / 65536.0, None, ALU.mult)
            nc.vector.reduce_max(red, prod, axis=AX, apply_absolute_value=True)
            nc.vector.tensor_max(mx4, mx4, red)
            ctx.edge("vector", "sync")
            ctx.dma(gate_dram.ap()[oc * 128:(oc + 1) * 128, tsl], prod)
            nc.sync.wait_ge(ctx.dsem, ctx.dcnt)
            nc.vector.wait_ge(ctx.dsem, ctx.dcnt)
    gmax_to_slot(ctx, sb, mx4, gmax_in, 3)
    ctx.coll("AllReduce", ALU.max, GROUP_ALL, gmax_in.ap(), gmax_out[2].ap())
    ctx.cwait()

    r4 = sb.af(8)[:, 0:1]
    s4 = sb.af(8)[:, 0:1]
    scale_bcast(ctx, sb, gmax_out[2], 3, 32767.0, r4, s4, onesb2)
    mx5 = sb.af(8)[:, 4:5]
    nc.vector.memset(mx5, 0.0)
    for oc in range(24):
        for t in range(NTT):
            tsl = slice(t * TT, (t + 1) * TT)
            ctx.edge("vector", "sync")
            ctx.dma(gq, gate_dram.ap()[oc * 128:(oc + 1) * 128, tsl])
            ctx.dma(upf, up_dram.ap()[oc * 128:(oc + 1) * 128, tsl])
            ctx.dwait()
            int_qdq(ctx, gq, prod, r4, s4)
            int_qdq(ctx, upf, sg, r2, s2)
            nc.vector.tensor_mul(prod, prod, sg)
            nc.vector.reduce_max(red, prod, axis=AX, apply_absolute_value=True)
            nc.vector.tensor_max(mx5, mx5, red)
            ctx.edge("vector", "sync")
            ctx.dma(up_dram.ap()[oc * 128:(oc + 1) * 128, tsl], prod)
            nc.sync.wait_ge(ctx.dsem, ctx.dcnt)
            nc.vector.wait_ge(ctx.dsem, ctx.dcnt)
    gmax_in2 = nc.dram_tensor(f"gmi2_{tag}", [1, 8], F32, kind="Internal")
    gmax_out3 = nc.dram_tensor(f"gmo3_{tag}", [1, 8], F32, kind="Internal")
    gmax_to_slot(ctx, sb, mx5, gmax_in2, 0)
    ctx.coll("AllReduce", ALU.max, GROUP_ALL, gmax_in2.ap(), gmax_out3.ap())
    ctx.cwait()

    r5 = sb.af(8)[:, 0:1]
    s5 = sb.af(8)[:, 0:1]
    scale_bcast(ctx, sb, gmax_out3, 0, 32767.0, r5, s5, onesb2)
    oqfm = sb.abf(24 * TT)
    ld = sb.abf(24 * 128)
    ofl = sb.af(24 * TT // 4) if False else None
    oin = sb.af(TT)
    for t in range(NTT):
        tsl = slice(t * TT, (t + 1) * TT)
        for cc in range(24):
            ctx.edge("tensor", "sync")
            ctx.dma(oin, up_dram.ap()[cc * 128:(cc + 1) * 128, tsl])
            ctx.dwait()
            nc.vector.tensor_scalar(oin, oin, r5, MAGIC, ALU.mult, ALU.add)
            nc.vector.tensor_scalar(oqfm[:, cc * TT:(cc + 1) * TT], oin, MAGIC, s5,
                                    ALU.subtract, ALU.mult)
            ctx.bar()
        for oc in range(NCH):
            ctx.edge("tensor", "sync")
            ctx.dma(ld, wdT.ap()[oc * 3072:(oc + 1) * 3072, :]
                    .rearrange("(cc p) o -> p cc o", p=128))
            ctx.dwait()
            for cc in range(24):
                nc.tensor.matmul(PS[:, 0:TT], ld[:, cc * 128:(cc + 1) * 128],
                                 oqfm[:, cc * TT:(cc + 1) * TT],
                                 start=(cc == 0), stop=(cc == 23))
            ctx.edge("tensor", "scalar")
            nc.scalar.activation(oin, PS[:, 0:TT], AF.Silu)
            ctx.edge("scalar", "sync", "tensor")
            ctx.dma(g_dram.ap()[oc * 128:(oc + 1) * 128, tsl], oin)
            nc.sync.wait_ge(ctx.dsem, ctx.dcnt)
            nc.scalar.wait_ge(ctx.dsem, ctx.dcnt)
    sb.release()



L0W = 2560
L0XW = 2564
L0F_STEP = 10
L0F_GSTEP = 3


def l0_tri_patterns():
    i = np.arange(128)[:, None]
    j = np.arange(128)[None, :]
    lo = np.where(j < i, -1e9, 0.0)
    up = np.where(j > i, -1e9, 0.0)
    return np.concatenate([lo, up], axis=1).astype(BF)


def l0_rot_tables(q):
    s0 = q * 2048
    d = np.arange(64)
    jj = (d % 32).astype(np.float64)
    invf = 10000.0 ** (-(2.0 * jj) / 64.0)
    base = (2.0 * jj + 0.4 * 64.0) / (1.4 * 64.0)
    sgn = np.where(d < 32, -1.0, 1.0)

    def tabs(pos, qside):
        fr = pos[None, :] * invf[:, None]
        sc = base[:, None] ** ((pos[None, :] - 4096.0) / 256.0)
        if not qside:
            sc = 1.0 / sc
        A = np.cos(fr) * sc
        Bt = np.sin(fr) * sc * sgn[:, None]
        return (np.concatenate([A, A], 0).astype(np.float32),
                np.concatenate([Bt, Bt], 0).astype(np.float32))

    Aq, Bq = tabs(np.arange(s0, s0 + 2048, dtype=np.float64), True)
    Ak, Bk = tabs(np.arange(s0 - 512, s0 + 2048, dtype=np.float64), False)
    return Aq, Bq, Ak, Bk


def emit_l0_ffn(ctx, sb, xnag, xn_own, g_dram, tag="l0f"):
    nc = ctx.nc
    PS = ctx.PS
    PSb = ctx.PS.ap().bitcast(BF16)
    identb = ctx.get_identb(sb)
    wqkv_i = [ctx.inp(f"L0f_wqkv{l}", [D, 1536], F32) for l in range(2)]
    wo_i = [ctx.inp(f"L0f_wo{l}", [512, D], F32) for l in range(2)]
    w1_i = [ctx.inp(f"L0f_w1_{l}", [D, 5460], F32) for l in range(2)]
    w2_i = [ctx.inp(f"L0f_w2_{l}", [2730, D], F32) for l in range(2)]
    b1_i = [ctx.inp(f"L0f_b1_{l}", [128, 44], F32) for l in range(2)]
    b2_i = [ctx.inp(f"L0f_b2_{l}", [128, 8], F32) for l in range(2)]
    lng_i = [ctx.inp(f"L0f_lng{l}", [128, 8], F32) for l in range(2)]
    lnb_i = [ctx.inp(f"L0f_lnb{l}", [128, 8], F32) for l in range(2)]
    qs_i = [ctx.inp(f"L0f_qs{l}", [128, 1], F32) for l in range(2)]
    ks_i = [ctx.inp(f"L0f_ks{l}", [128, 1], F32) for l in range(2)]
    rotAq = ctx.inp("L0f_rotAq", [128, TPC], F32)
    rotBq = ctx.inp("L0f_rotBq", [128, TPC], F32)
    rotAk = ctx.inp("L0f_rotAk", [128, L0W], F32)
    rotBk = ctx.inp("L0f_rotBk", [128, L0W], F32)
    halom = ctx.inp("L0f_halomask", [128, TT], F32)
    hvalid = ctx.inp("L0f_hvalid", [128, 1], F32)
    x_spill = nc.dram_tensor(f"xspill_{tag}", [D, TPC], F32, kind="Internal")
    xlb_dram = nc.dram_tensor(f"xlb_{tag}", [D, L0XW], BF16, kind="Internal")
    h3_dram = nc.dram_tensor(f"h3_{tag}", [2730, TPC], BF16, kind="Internal")
    strip_in = nc.dram_tensor(f"strin_{tag}", [D, 520], F32, kind="Internal")
    strip_out = nc.dram_tensor(f"strout_{tag}", [4 * D, 520], F32, kind="Internal")

    for c in range(NCH):
        ctx.dma(x_spill.ap()[c * 128:(c + 1) * 128, :], sb.x[c][:, :])
    ctx.dwait()
    x6f = sb.x[6]
    x6b = sb.x[6].ap().bitcast(BF16)
    x7b = sb.x[7].ap().bitcast(BF16)
    Rb = sb.R.ap().bitcast(BF16)
    ofm = [sb.x[c].ap().bitcast(BF16)[:, 0:TPC] for c in range(4)]
    hnall = [sb.x[4 + c // 2].ap().bitcast(BF16)[:, (c % 2) * TPC:(c % 2 + 1) * TPC]
             for c in range(NCH)]
    sparebf = x6b[:, 0:TT]
    scw = x6f[:, TT:2 * TT]
    ptile = x6b[:, 2048:2688]
    pT = x6b[:, 2688:3328]
    ztb = x6b[0:65, 3328:3456]
    ztk = x6f[:, 1728:1793]
    ztb2 = x6b[:, 3592:3720]
    red = x6f[:, 1880:1881]
    rcp = x6f[:, 1881:1882]
    pairq = x7b[:, 2048:4096]
    qh = x7b[0:64, 0:2048]
    vtok = Rb[:, 0:1365]
    kh = Rb[0:64, 1408:3968]

    sb.mark()
    xl0 = sb._aoff
    XL = [sb.af(L0XW) for _ in range(NCH)]
    XLW = (L0XW + 7) & ~7
    xl_halo3 = sb.A[:, xl0: xl0 + NCH * XLW] \
        .rearrange("p (c w) -> p c w", w=XLW)[:, :, 0:513]
    trip = sb.abf(256)
    ctx.dmag(trip, ctx.const("l0_tri", l0_tri_patterns()))
    hvl = sb.af(8)[:, 0:1]
    ctx.dma(hvl, hvalid)
    hmk = sb.abf(TT)
    ctx.dmag(hmk, halom)
    qsv = sb.af(8)[:, 0:1]
    ksv = sb.af(8)[:, 1:2]
    E2T = sb.abf(2)
    ctx.dmag(E2T, ctx.const("E2T", np.concatenate(
        [np.concatenate([np.ones((64, 1)), np.zeros((64, 1))], 1),
         np.concatenate([np.zeros((64, 1)), np.ones((64, 1))], 1)], 0).astype(BF)))
    E2f = sb.af(128)[0:2, :]
    ctx.dma(E2f, ctx.const("E2f", np.concatenate(
        [np.concatenate([np.ones((1, 64)), np.zeros((1, 64))], 1),
         np.concatenate([np.zeros((1, 64)), np.ones((1, 64))], 1)], 0)
        .astype(np.float32)))
    onesb2 = sb.abf(128)[0:1, :]
    ones1f = sb.af(128)[0:1, :]
    onesv = sb.af(8)[:, 0:1]
    ctx.dma(onesv, ctx.const("ones128f", np.ones((128, 1), dtype=np.float32)))
    b1v = sb.af(44)
    b2v = sb.af(8)
    lngv = sb.af(8)
    lnbv = sb.af(8)
    ctx.dwait()
    nc.vector.memset(onesb2, 1.0)
    nc.vector.memset(ones1f, 1.0)
    ctx.bar()

    qm1 = ctx.qsnap("m1")

    sb.mark()
    stg = sb.abf(2048)
    stgH = sb.abf(NCH * 520)
    stgH3 = stgH.rearrange("p (c w) -> p c w", w=520)[:, :, 0:513]
    ctx.dma(stgH3, xnag[bass.ts(qm1, 1024), 1535:2048]
            .rearrange("(cc p) t -> p cc t", p=128))
    for c in range(NCH):
        ctx.dma(stg, xn_own[c * 128:(c + 1) * 128, :])
        ctx.dwait()
        nc.vector.tensor_copy(XL[c][:, 513:2561], stg)
        ctx.edge("vector", "sync")
    nc.vector.tensor_copy(xl_halo3, stgH3)
    ctx.bar()
    for c in range(NCH):
        nc.vector.tensor_scalar(XL[c][:, 0:513], XL[c][:, 0:513], hvl, None,
                                ALU.mult)
        nc.vector.memset(XL[c][:, 2561:L0XW], 0.0)
    ctx.bar()
    sb.release()

    def halo_exchange():
        ctx.edge("vector", "sync")
        for c in range(NCH):
            ctx.dma(strip_in.ap()[c * 128:(c + 1) * 128, 0:513],
                    XL[c][:, 2048:2561])
        ctx.dwait()
        ctx.coll("AllGather", ALU.bypass, GROUPS_BATCH, strip_in.ap(),
                 strip_out.ap())
        ctx.cwait()
        ctx.dma(xl_halo3, strip_out.ap()[bass.ts(qm1, 1024), 0:513]
                .rearrange("(cc p) t -> p cc t", p=128))
        ctx.dwait()
        for c in range(NCH):
            nc.vector.tensor_scalar(XL[c][:, 0:513], XL[c][:, 0:513], hvl, None,
                                    ALU.mult)
        ctx.bar()

    def gen_pair(l, pr, pairk, pairv, tabA, tabB, qsp, qsw, nrm, wsl, xlrhs):
        for proj, dst, width, rA, rB in ((1, pairk, L0W, rotAk, rotBk),
                                         (2, pairv, L0W, None, None),
                                         (0, pairq, TPC, rotAq, rotBq)):
            ctx.edge("tensor", "gpsimd")
            ctx.dmag(wsl, wqkv_i[l][:, proj * 512 + pr * 128:
                                    proj * 512 + (pr + 1) * 128]
                     .rearrange("(c p) o -> p c o", p=128))
            ctx.dwait()
            for sp in range(width // TT):
                g0 = sp * TT
                gsp = sp + (1 if proj == 0 else 0)
                ctx.edge("tensor", "sync")
                for c in range(NCH):
                    xoff = (1 if c < 4 else 0) + gsp * TT
                    ctx.dma(xlrhs[:, c * TT:(c + 1) * TT],
                            xlb_dram.ap()[c * 128:(c + 1) * 128,
                                          xoff:xoff + TT])
                nc.tensor.wait_ge(ctx.dsem, ctx.dcnt)
                for c in range(NCH):
                    nc.tensor.matmul(PS[:, 0:TT], wsl[:, c * 128:(c + 1) * 128],
                                     xlrhs[:, c * TT:(c + 1) * TT],
                                     start=(c == 0), stop=(c == NCH - 1))
                ctx.edge("tensor", "vector", "scalar")
                if proj == 2 or L0F_GSTEP < 2:
                    nc.vector.tensor_copy(dst[:, g0:g0 + TT], PS[:, 0:TT])
                    ctx.edge("vector", "tensor")
                    continue
                nc.vector.tensor_copy(qsp, PS[:, 0:TT])
                if L0F_GSTEP == 19:
                    nc.vector.tensor_copy(dst[:, g0:g0 + TT], qsp)
                    ctx.edge("vector", "tensor")
                    continue
                ctx.edge("vector", "scalar")
                nc.scalar.activation(sparebf, qsp, AF.Square)
                ctx.edge("scalar", "tensor")
                if L0F_GSTEP == 20:
                    nc.vector.tensor_copy(dst[:, g0:g0 + TT], qsp)
                    ctx.edge("vector", "tensor")
                    continue
                nc.tensor.matmul(PS[0:2, 2 * TT:3 * TT], E2T, sparebf,
                                 start=True, stop=True)
                ctx.edge("tensor", "vector")
                if L0F_GSTEP == 21:
                    nc.vector.tensor_copy(dst[:, g0:g0 + TT], qsp)
                    ctx.edge("vector", "tensor")
                    continue
                n1 = nrm[0:2, 0:TT]
                n2 = nrm[0:2, TT:2 * TT]
                nc.vector.tensor_scalar(n1, PS[0:2, 2 * TT:3 * TT], 1e-24, None,
                                        ALU.max)
                ctx.edge("vector", "scalar")
                nc.scalar.activation(n2, n1, AF.Sqrt)
                ctx.edge("scalar", "vector")
                nc.vector.reciprocal(n1, n2)
                ctx.edge("vector", "tensor")
                if L0F_GSTEP == 22:
                    nc.vector.tensor_copy(dst[:, g0:g0 + TT], qsp)
                    ctx.edge("vector", "tensor")
                    continue
                nc.tensor.matmul(PS[:, 2 * TT:3 * TT], E2f, n1,
                                 start=True, stop=True)
                ctx.edge("tensor", "vector")
                if L0F_GSTEP == 23:
                    nc.vector.tensor_copy(dst[:, g0:g0 + TT], qsp)
                    ctx.edge("vector", "tensor")
                    continue
                nc.vector.tensor_tensor(qsp, qsp, PS[:, 2 * TT:3 * TT], ALU.mult)
                if proj == 0:
                    nc.vector.tensor_scalar(qsp, qsp, qsv, 8.0, ALU.mult,
                                            ALU.mult)
                else:
                    nc.vector.tensor_scalar(qsp, qsp, ksv, None, ALU.mult)
                if L0F_GSTEP < 3:
                    nc.vector.tensor_copy(dst[:, g0:g0 + TT], qsp)
                    ctx.edge("vector", "tensor")
                    continue
                for bb in range(4):
                    nc.vector.tensor_copy(qsw[32 * bb:32 * bb + 32],
                                          qsp[32 * (bb ^ 1):32 * (bb ^ 1) + 32])
                ctx.edge("vector", "sync")
                ctx.dma(tabA, rA[:, g0:g0 + TT])
                ctx.dma(tabB, rB[:, g0:g0 + TT])
                nc.vector.wait_ge(ctx.dsem, ctx.dcnt)
                nc.vector.tensor_tensor(scw, qsp, tabA, ALU.mult)
                nc.vector.tensor_tensor(qsw, qsw, tabB, ALU.mult)
                nc.vector.tensor_add(scw, scw, qsw)
                nc.vector.tensor_copy(dst[:, g0:g0 + TT], scw)
                ctx.edge("vector", "tensor")

    def attn_head(pr, hh, pairk, pairv):
        rows = slice(hh * 64, hh * 64 + 64)
        nc.vector.tensor_copy(qh, pairq[rows, :])
        nc.vector.tensor_copy(kh, pairk[rows, :])
        nc.vector.memset(vtok, 1.0)
        ctx.edge("vector", "tensor")
        for kt0 in range(0, 20, 4):
            for j in range(4):
                kt = kt0 + j
                nc.tensor.transpose(PSb[:, 4096 + j * 128:4096 + (j + 1) * 128],
                                    pairv[:, kt * 128:(kt + 1) * 128], identb)
            ctx.edge("tensor", "vector")
            for j in range(4):
                kt = kt0 + j
                nc.vector.tensor_copy(vtok[:, kt * 65:kt * 65 + 64],
                                      PSb[:, 4096 + j * 128 + hh * 64:
                                          4096 + j * 128 + hh * 64 + 64])
            ctx.edge("vector", "tensor")
        for qs in range(16):
            k0 = qs * 128
            lq = qh[0:64, qs * 128:(qs + 1) * 128]
            nc.tensor.matmul(PS[:, 0:TT], lq, kh[0:64, k0:k0 + TT],
                             start=True, stop=False)
            if qs < 4:
                wm = TT - qs * 128
                nc.tensor.matmul(PS[:, 0:wm], identb, hmk[:, 0:wm],
                                 start=False, stop=False)
            nc.tensor.matmul(PS[:, 0:128], identb, trip[:, 0:128],
                             start=False, stop=True)
            nc.tensor.matmul(PS[:, TT:TT + 128], lq, kh[0:64, k0 + TT:k0 + 640],
                             start=True, stop=False)
            nc.tensor.matmul(PS[:, TT:TT + 128], identb, trip[:, 128:256],
                             start=False, stop=True)
            ctx.edge("tensor", "vector")
            nc.vector.reduce_max(red, PS[:, 0:640], axis=AX)
            ctx.bar()
            nc.vector.tensor_scalar(red, red, -1.0, None, ALU.mult)
            ctx.edge("vector", "scalar")
            nc.scalar.activation(ptile, PS[:, 0:640], AF.Exp, bias=red,
                                 scale=1.0)
            ctx.edge("scalar", "tensor", "vector")
            for j in range(5):
                nc.tensor.transpose(PSb[:, 4096 + j * 128:4096 + (j + 1) * 128],
                                    ptile[:, j * 128:(j + 1) * 128], identb)
            ctx.edge("tensor", "vector")
            for j in range(5):
                nc.vector.tensor_copy(pT[:, j * 128:(j + 1) * 128],
                                      PSb[:, 4096 + j * 128:4096 + (j + 1) * 128])
            ctx.edge("vector", "tensor")
            for j in range(5):
                kt = k0 // 128 + j
                nc.tensor.matmul(PS[0:65, 3 * TT:3 * TT + 128],
                                 vtok[:, kt * 65:(kt + 1) * 65],
                                 pT[:, j * 128:(j + 1) * 128],
                                 start=(j == 0), stop=(j == 4))
            ctx.edge("tensor", "scalar")
            nc.scalar.activation(ztb, PS[0:65, 3 * TT:3 * TT + 128], AF.Copy)
            ctx.edge("scalar", "tensor")
            nc.tensor.transpose(PSb[:, 4096:4096 + 65], ztb,
                                identb[0:65, 0:65])
            ctx.edge("tensor", "vector")
            nc.vector.tensor_copy(ztk, PSb[:, 4096:4096 + 65])
            ctx.bar()
            nc.vector.reciprocal(rcp, ztk[:, 64:65])
            ctx.bar()
            nc.vector.tensor_scalar(ztk[:, 0:64], ztk[:, 0:64], rcp, None,
                                    ALU.mult)
            ctx.bar()
            nc.vector.tensor_copy(ztb2[:, hh * 64:hh * 64 + 64], ztk[:, 0:64])
            ctx.edge("vector", "tensor")
            nc.tensor.transpose(PSb[:, 4352:4352 + 128], ztb2, identb)
            ctx.edge("tensor", "vector")
            nc.vector.tensor_copy(ofm[pr][rows, qs * 128:(qs + 1) * 128],
                                  PSb[rows, 4352:4352 + 128])
            ctx.edge("vector", "tensor")

    step = L0F_STEP
    for l in range(2 if step >= 10 else 1):
        ctx.dma(qsv, qs_i[l])
        ctx.dma(ksv, ks_i[l])
        ctx.dwait()
        sb.mark()
        pairk = sb.abf(L0W)
        pairv = sb.abf(L0W)
        tabA = sb.af(TT)
        tabB = sb.af(TT)
        qsp = sb.af(TT)
        qsw = sb.af(TT)
        nrm = sb.af(2 * TT)
        wsl = sb.abf(1024)
        xlrhs = sb.abf(4096)
        for c in range(NCH):
            ctx.dmag(xlb_dram.ap()[c * 128:(c + 1) * 128, 0:2561],
                     XL[c][:, 0:2561])
        nc.vector.memset(ztb2, 0.0)
        ctx.dwait()
        if step >= 2:
            for pr in range(4 if step >= 4 else 1):
                gen_pair(l, pr, pairk, pairv, tabA, tabB, qsp, qsw, nrm, wsl,
                         xlrhs)
                if step >= 3:
                    for hh in range(2):
                        attn_head(pr, hh, pairk, pairv)
        if 4 <= step <= 6 and l == 0:
            dbg_o = nc.dram_tensor("dbg_o", [512, TPC], F32,
                                   kind="ExternalOutput").ap()
            ost = qsp
            for pr2 in range(4):
                for t2 in range(NTT):
                    nc.vector.tensor_copy(ost, ofm[pr2][:, t2 * TT:(t2 + 1) * TT])
                    ctx.edge("vector", "sync")
                    ctx.dma(dbg_o[pr2 * 128:(pr2 + 1) * 128,
                                  t2 * TT:(t2 + 1) * TT], ost)
                    nc.vector.wait_ge(ctx.dsem, ctx.dcnt)
        lwo = wsl[:, 0:512]
        for co in range(NCH if step >= 5 else 0):
            ctx.edge("tensor", "gpsimd")
            ctx.dmag(lwo, wo_i[l][:, co * 128:(co + 1) * 128]
                     .rearrange("(c p) o -> p c o", p=128))
            ctx.dwait()
            for t in range(NTT):
                for ci in range(4):
                    nc.tensor.matmul(PS[:, 0:TT], lwo[:, ci * 128:(ci + 1) * 128],
                                     ofm[ci][:, t * TT:(t + 1) * TT],
                                     start=(ci == 0), stop=(ci == 3))
                ctx.edge("tensor", "vector")
                osl = slice(513 + t * TT, 513 + (t + 1) * TT)
                nc.vector.tensor_add(XL[co][:, osl], XL[co][:, osl], PS[:, 0:TT])
                ctx.edge("vector", "tensor")
        sb.release()
        if step < 6:
            break
        halo_exchange()

        ctx.dma(b1v, b1_i[l])
        ctx.dma(b2v, b2_i[l])
        ctx.dma(lngv, lng_i[l])
        ctx.dma(lnbv, lnb_i[l])
        ctx.dwait()
        sb.mark()
        wbuf = sb.abf(4096)
        h3t2 = sb.abf(1024)
        f3sq = sb.af(TT)
        f3gel = sb.af(TT)
        f3scr = sb.abf(TT)
        for t in range(NTT if step >= 7 else 0):
            xoffs = [513 + t * TT - (0 if c < 4 else 1) for c in range(NCH)]
            for c in range(NCH):
                nc.tensor.matmul(PS[0:1, 2 * TT:3 * TT], onesv,
                                 XL[c][:, xoffs[c]:xoffs[c] + TT],
                                 start=(c == 0), stop=(c == NCH - 1))
            for c in range(NCH):
                nc.scalar.activation(f3sq, XL[c][:, xoffs[c]:xoffs[c] + TT],
                                     AF.Square)
                ctx.edge("scalar", "tensor")
                nc.tensor.matmul(PS[0:1, 3 * TT:4 * TT], onesv, f3sq,
                                 start=(c == 0), stop=(c == NCH - 1))
                ctx.edge("tensor", "scalar")
            ctx.edge("tensor", "vector")
            mean = sb.R[0:1, 0:TT]
            var = sb.R[0:1, TT:2 * TT]
            rstd = sb.R[0:1, 2 * TT:3 * TT]
            brow = sb.R[0:1, 3 * TT:4 * TT]
            nc.vector.tensor_scalar(mean, PS[0:1, 2 * TT:3 * TT], 1.0 / D, None,
                                    ALU.mult)
            nc.vector.tensor_scalar(var, PS[0:1, 3 * TT:4 * TT], 1.0 / D, None,
                                    ALU.mult)
            nc.vector.tensor_tensor(rstd, mean, mean, ALU.mult)
            nc.vector.tensor_sub(var, var, rstd)
            nc.vector.tensor_scalar(var, var, 1e-5, None, ALU.add)
            ctx.edge("vector", "scalar")
            nc.scalar.activation(rstd, var, AF.Sqrt)
            ctx.edge("scalar", "vector")
            nc.vector.reciprocal(rstd, rstd)
            nc.vector.tensor_tensor(brow, mean, rstd, ALU.mult)
            nc.vector.tensor_scalar(brow, brow, -1.0, None, ALU.mult)
            ctx.edge("vector", "tensor")
            nc.tensor.matmul(PS[:, 2 * TT:3 * TT], ones1f, rstd,
                             start=True, stop=True)
            nc.tensor.matmul(PS[:, 3 * TT:4 * TT], ones1f, brow,
                             start=True, stop=True)
            ctx.edge("tensor", "vector")
            for c in range(NCH):
                nc.vector.tensor_tensor(f3gel, XL[c][:, xoffs[c]:xoffs[c] + TT],
                                        PS[:, 2 * TT:3 * TT], ALU.mult)
                nc.vector.tensor_add(f3gel, f3gel, PS[:, 3 * TT:4 * TT])
                nc.vector.tensor_scalar(hnall[c][:, t * TT:(t + 1) * TT], f3gel,
                                        lngv[:, c:c + 1], lnbv[:, c:c + 1],
                                        ALU.mult, ALU.add)
            ctx.edge("vector", "tensor")
        gtg = {}

        def f2_issue(j):
            s = j % 2
            wj = 128 if j < 21 else 42
            ctx.dmag(wbuf[:, s * 2048:s * 2048 + NCH * wj]
                     .rearrange("p (c o) -> p c o", o=wj),
                     w1_i[l][:, j * 128:j * 128 + wj]
                     .rearrange("(c p) o -> p c o", p=128))
            ctx.dmag(wbuf[:, s * 2048 + 1024:s * 2048 + 1024 + NCH * wj]
                     .rearrange("p (c o) -> p c o", o=wj),
                     w1_i[l][:, 2730 + j * 128:2730 + j * 128 + wj]
                     .rearrange("(c p) o -> p c o", p=128))
            gtg[j] = ctx.gcnt

        if step >= 8:
            f2_issue(0)
        for j in range(22 if step >= 8 else 0):
            s = j % 2
            wj = 128 if j < 21 else 42
            if j + 1 < 22:
                if j >= 1:
                    ctx.edge("tensor", "gpsimd")
                f2_issue(j + 1)
            nc.tensor.wait_ge(ctx.gsem, gtg[j])
            for t in range(NTT):
                for c in range(NCH):
                    nc.tensor.matmul(PS[0:wj, 0:TT],
                                     wbuf[:, s * 2048 + c * wj:
                                          s * 2048 + (c + 1) * wj],
                                     hnall[c][:, t * TT:(t + 1) * TT],
                                     start=(c == 0), stop=(c == NCH - 1))
                for c in range(NCH):
                    nc.tensor.matmul(PS[0:wj, TT:2 * TT],
                                     wbuf[:, s * 2048 + 1024 + c * wj:
                                          s * 2048 + 1024 + (c + 1) * wj],
                                     hnall[c][:, t * TT:(t + 1) * TT],
                                     start=(c == 0), stop=(c == NCH - 1))
                ctx.edge("tensor", "scalar", "vector")
                nc.vector.tensor_scalar(f3sq[0:wj, :], PS[0:wj, 0:TT],
                                        b1v[0:wj, j:j + 1], None, ALU.add)
                nc.scalar.activation(f3gel[0:wj, :], PS[0:wj, TT:2 * TT],
                                     AF.Gelu, bias=b1v[0:wj, 22 + j:23 + j],
                                     scale=1.0)
                ctx.edge("scalar", "vector")
                nc.vector.wait_ge(ctx.dsem, ctx.dcnt)
                nc.vector.tensor_tensor(f3scr[0:wj, :], f3sq[0:wj, :],
                                        f3gel[0:wj, :], ALU.mult)
                ctx.edge("vector", "sync", "tensor")
                ctx.edge("scalar", "tensor")
                ctx.dma(h3_dram.ap()[j * 128:j * 128 + wj, t * TT:(t + 1) * TT],
                        f3scr[0:wj, :])
        for t in range(NTT if step >= 9 else 0):
            dtg = {}

            def f3_issue(cc):
                s = cc % 2
                wj = 128 if cc < 21 else 42
                ctx.dmag(wbuf[0:wj, s * 1024:(s + 1) * 1024],
                         w2_i[l][cc * 128:cc * 128 + wj, :])
                ctx.dma(h3t2[0:wj, s * TT:(s + 1) * TT],
                        h3_dram.ap()[cc * 128:cc * 128 + wj, t * TT:(t + 1) * TT])
                dtg[cc] = (ctx.dcnt, ctx.gcnt)

            if t == 0:
                nc.sync.wait_ge(ctx.dsem, ctx.dcnt)
                ctx.bar()
            ctx.edge("tensor", "gpsimd", "sync")
            f3_issue(0)
            for cc in range(22):
                s = cc % 2
                wj = 128 if cc < 21 else 42
                if cc + 1 < 22:
                    if cc >= 1:
                        ctx.edge("tensor", "gpsimd", "sync")
                    f3_issue(cc + 1)
                nc.tensor.wait_ge(ctx.dsem, dtg[cc][0])
                nc.tensor.wait_ge(ctx.gsem, dtg[cc][1])
                for co in range(NCH):
                    nc.tensor.matmul(PS[:, co * TT:(co + 1) * TT],
                                     wbuf[0:wj, s * 1024 + co * 128:
                                          s * 1024 + (co + 1) * 128],
                                     h3t2[0:wj, s * TT:(s + 1) * TT],
                                     start=(cc == 0), stop=(cc == 21))
            ctx.edge("tensor", "vector")
            for co in range(NCH):
                osl = slice(513 + t * TT, 513 + (t + 1) * TT)
                nc.vector.tensor_scalar(f3sq, PS[:, co * TT:(co + 1) * TT],
                                        b2v[:, co:co + 1], None, ALU.add)
                ctx.bar()
                if l == 0:
                    nc.vector.tensor_add(XL[co][:, osl], XL[co][:, osl], f3sq)
                    ctx.bar()
                else:
                    nc.vector.tensor_add(f3sq, f3sq, XL[co][:, osl])
                    ctx.edge("vector", "scalar")
                    nc.scalar.activation(f3gel, f3sq, AF.Silu)
                    ctx.edge("scalar", "sync")
                    ctx.dma(g_dram[co * 128:(co + 1) * 128,
                                   t * TT:(t + 1) * TT], f3gel)
                    nc.scalar.wait_ge(ctx.dsem, ctx.dcnt)
                    nc.vector.wait_ge(ctx.dsem, ctx.dcnt)
            ctx.edge("vector", "tensor")
        sb.release()
        if l == 0:
            halo_exchange()

    if step < 10 and step != 4:
        ctx.edge("vector", "sync")
        for c in range(NCH):
            ctx.dma(g_dram[c * 128:(c + 1) * 128, :], XL[c][:, 513:2561])
        ctx.dwait()
    sb.release()
    for c in range(NCH):
        ctx.dma(sb.x[c][:, :], x_spill.ap()[c * 128:(c + 1) * 128, :])
    ctx.dwait()


def l0f_in_map(inp, c):
    q = c % 4
    im = {}
    f32 = lambda a: np.ascontiguousarray(np.asarray(a, np.float32))
    for l in range(2):
        im[f"L0f_wqkv{l}"] = f32(inp["L0_ffn_wqkv"][l])
        im[f"L0f_wo{l}"] = f32(inp["L0_ffn_wo"][l])
        im[f"L0f_w1_{l}"] = f32(inp["L0_ffn_w1"][l])
        im[f"L0f_w2_{l}"] = f32(inp["L0_ffn_w2"][l])
        b1 = f32(inp["L0_ffn_b1"][l])
        Bm = np.zeros((128, 44), np.float32)
        for j in range(22):
            wj = 128 if j < 21 else 42
            Bm[0:wj, j] = b1[j * 128:j * 128 + wj]
            Bm[0:wj, 22 + j] = b1[2730 + j * 128:2730 + j * 128 + wj]
        im[f"L0f_b1_{l}"] = Bm
        im[f"L0f_b2_{l}"] = f32(inp["L0_ffn_b2"][l]).reshape(8, 128).T.copy()
        im[f"L0f_lng{l}"] = f32(inp["L0_ffn_ln_g"][l]).reshape(8, 128).T.copy()
        im[f"L0f_lnb{l}"] = f32(inp["L0_ffn_ln_b"][l]).reshape(8, 128).T.copy()
        im[f"L0f_qs{l}"] = np.tile(f32(inp["L0_ffn_q_scale"][l]), 2).reshape(128, 1)
        im[f"L0f_ks{l}"] = np.tile(f32(inp["L0_ffn_k_scale"][l]), 2).reshape(128, 1)
    Aq, Bq, Ak, Bk = l0_rot_tables(q)
    im["L0f_rotAq"], im["L0f_rotBq"] = Aq, Bq
    im["L0f_rotAk"], im["L0f_rotBk"] = Ak, Bk
    bs = (q == 0)
    im["L0f_halomask"] = np.full((128, TT), -1e9 if bs else 0.0, np.float32)
    im["L0f_hvalid"] = np.full((128, 1), 0.0 if bs else 1.0, np.float32)
    return im


def l3_band_pattern():
    i = np.arange(128)[:, None]
    j = np.arange(134)[None, :]
    return np.where((j >= i) & (j <= i + 6), 0.0, -1e9).astype(BF)


def l3_edge_patterns():
    ln2, ln3 = float(np.log(2.0)), float(np.log(3.0))
    first = np.array([[0, 0, 0, 0, -1e9, -1e9, ln3],
                      [0, 0, 0, 0, 0, -1e9, ln2],
                      [0, 0, 0, 0, 0, 0, 0]], dtype=np.float32)
    last = np.array([[0, 0, 0, 0, 0, 0, 0],
                     [ln2, -1e9, 0, 0, 0, 0, 0],
                     [ln3, -1e9, -1e9, 0, 0, 0, 0]], dtype=np.float32)
    return np.concatenate([first, np.zeros((3, 1), np.float32), last],
                          axis=1).astype(BF)


def emit_l3_attn(ctx, sb, xnag, xn_own, g_dram, tag="l3a"):
    nc = ctx.nc
    PS = ctx.PS
    PSb = ctx.PS.ap().bitcast(BF16)
    identb = ctx.get_identb(sb)
    wqkvT = ctx.inp("L3w_qkvT", [D, 3072], F32)
    bqkv = ctx.inp("L3b_qkv", [128, 24], F32)
    wlatBD = ctx.inp("L3w_latBD", [128, 128], F32)
    blat2 = ctx.inp("L3b_lat2", [128, 1], F32)
    wprojT = ctx.inp("L3w_projT", [D, D], F32)
    bproj = ctx.inp("L3b_proj", [128, 8], F32)
    em_first = ctx.inp("L3_em_first", [128, 1], F32)
    em_last = ctx.inp("L3_em_last", [128, 1], F32)
    sb.mark()
    ss = float(1.0 / np.sqrt(np.sqrt(64.0)))
    Wqs = sb.abf(NCH * 128)
    Bia = sb.af(24)
    ctx.dma(Bia, bqkv)
    BD = sb.abf(128)
    ctx.dmag(BD, wlatBD)
    Bl2 = sb.af(8)[:, 0:1]
    ctx.dma(Bl2, blat2)
    Wp = sb.abf(NCH * 1024)
    ctx.dmag(Wp, wprojT.rearrange("(c p) o -> p c o", p=128))
    Bp = sb.af(8)
    ctx.dma(Bp, bproj)
    emf = sb.af(8)[:, 0:1]
    ctx.dma(emf, em_first)
    eml = sb.af(8)[:, 0:1]
    ctx.dma(eml, em_last)
    bandp = sb.abf(134)
    ctx.dmag(bandp, ctx.const("l3_band", l3_band_pattern()))
    edgep = sb.abf(16)[0:3, :]
    ctx.dmag(edgep[0:3, 0:15], ctx.const("l3_edge", l3_edge_patterns()))
    obuf = [sb.abf(TPC) for _ in range(NCH)]
    ctx.dwait()

    qm1 = ctx.qsnap("m1")
    qp1 = ctx.qsnap("p1")

    xnt = sb.abf(4096)
    xnt_h = sb.abf(48)
    xnt_s = sb.abf(128)
    THW = TPC + 8
    TH = 3
    nc.vector.memset(xnt_h, 0.0)
    nc.vector.memset(xnt_s, 0.0)
    ctx.bar()
    h3 = xnt_h.rearrange("p (c o) -> p c o", o=6)
    s3v = xnt_s.rearrange("p (c o) -> p c o", o=16)
    ctx.dma(h3[:, :, 0:3],
            xnag[bass.ts(qm1, 1024), 2045:2048].rearrange("(c p) t -> p c t", p=128))
    ctx.dma(h3[:, :, 3:6],
            xnag[bass.ts(qp1, 1024), 0:3].rearrange("(c p) t -> p c t", p=128))
    ctx.dma(s3v[:, :, 0:6],
            xnag[0:1024, 0:6].rearrange("(c p) t -> p c t", p=128))
    ctx.dma(s3v[:, :, 6:13],
            xnag[3 * 1024:4 * 1024, 2041:2048].rearrange("(c p) t -> p c t", p=128))
    ctx.dwait()

    qh = [sb.abf(TPC), sb.abf(TPC)]
    kh = [sb.abf(THW), sb.abf(THW)]
    vh = [sb.abf(THW), sb.abf(THW)]
    qsp = [sb.abf(16), sb.abf(16)]
    ksp = [sb.abf(16), sb.abf(16)]
    vsp = [sb.abf(16), sb.abf(16)]
    pair = sb.abf(THW)
    ptile = sb.abf(134 + 10)
    pT1 = sb.abf(128)
    pT2 = sb.abf(128)
    vt1 = sb.abf(64)
    vt2 = sb.abf(64)
    ztk = sb.af(64)
    ztb = sb.abf(64)
    red = sb.af(8)[:, 0:1]
    rcp = sb.af(8)[:, 1:2]
    spm = sb.af(16)
    spmb = sb.abf(16)
    sp_p = sb.abf(16)
    sp_pT = sb.abf(16)
    sp_v = sb.abf(80)
    sp_vT = sb.abf(80)
    sp_z = sb.af(64)
    spzf = sb.af(64)
    sred = sb.af(8)[:, 2:3]
    srcp = sb.af(8)[:, 3:4]
    ebufs = dict(edgep=edgep, sp_p=sp_p, sp_pT=sp_pT, sp_v=sp_v, sp_vT=sp_vT,
                 sp_z=sp_z, spzf=spzf, red=sred, rcp=srcp)

    for hc in range(NCH):
        for sec, width, dst, dsts in ((1, THW, kh, ksp), (2, THW, vh, vsp),
                                      (0, TPC, qh, qsp)):
            bia = Bia[:, sec * 8 + hc: sec * 8 + hc + 1]
            ctx.edge("tensor", "gpsimd")
            ctx.dmag(Wqs, wqkvT[:, sec * 1024 + hc * 128: sec * 1024 + (hc + 1) * 128]
                     .rearrange("(c p) o -> p c o", p=128))
            ctx.dwait()
            for t in range(NTT):
                ctx.edge("tensor", "sync")
                ctx.dma(xnt, xn_own[0:1024, t * TT:(t + 1) * TT]
                        .rearrange("(c p) t -> p c t", p=128))
                ctx.dwait()
                for c in range(NCH):
                    nc.tensor.matmul(PS[:, 0:TT],
                                     Wqs[:, c * 128:(c + 1) * 128],
                                     xnt[:, c * TT:(c + 1) * TT],
                                     start=(c == 0), stop=(c == NCH - 1))
                ctx.edge("tensor", "vector")
                off = TH if sec != 0 else 0
                nc.vector.tensor_scalar(pair[:, off + t * TT: off + (t + 1) * TT],
                                        PS[:, 0:TT], bia, None, ALU.add)
                ctx.edge("vector", "tensor")
            if sec != 0:
                for c in range(NCH):
                    nc.tensor.matmul(PS[:, 3 * TT:3 * TT + 6],
                                     Wqs[:, c * 128:(c + 1) * 128],
                                     xnt_h[:, c * 6:(c + 1) * 6],
                                     start=(c == 0), stop=(c == NCH - 1))
                ctx.edge("tensor", "vector")
                nc.vector.tensor_scalar(spm[:, 0:6], PS[:, 3 * TT:3 * TT + 6],
                                        bia, None, ALU.add)
                ctx.bar()
                nc.vector.tensor_copy(pair[:, 0:3], spm[:, 0:3])
                nc.vector.tensor_copy(pair[:, TH + TPC:TH + TPC + 3], spm[:, 3:6])
                nc.vector.memset(pair[:, TH + TPC + 3:THW], 0.0)
                ctx.edge("vector", "tensor")
            for c in range(NCH):
                nc.tensor.matmul(PS[:, 3 * TT:3 * TT + 16],
                                 Wqs[:, c * 128:(c + 1) * 128],
                                 xnt_s[:, c * 16:(c + 1) * 16],
                                 start=(c == 0), stop=(c == NCH - 1))
            ctx.edge("tensor", "vector")
            nc.vector.tensor_scalar(spm, PS[:, 3 * TT:3 * TT + 16], bia, None, ALU.add)
            ctx.bar()
            if sec != 0:
                mul = ss if sec == 1 else 1.0
                for seg in range(0, THW, TT):
                    w = min(TT, THW - seg)
                    nc.vector.tensor_copy(pair.bitcast(BF16)[:, 0:1] if False else pair[:, seg:seg + 1], pair[:, seg:seg + 1]) if False else None
                    nc.tensor.matmul(PS[:, 0:w], BD, pair[:, seg:seg + w],
                                     start=True, stop=True)
                    ctx.edge("tensor", "vector")
                    nc.vector.tensor_scalar(pair[:, seg:seg + w], PS[:, 0:w],
                                            Bl2, mul, ALU.add, ALU.mult)
                    ctx.edge("vector", "tensor")
                nc.vector.tensor_copy(spmb, spm)
                ctx.edge("vector", "tensor")
                nc.tensor.matmul(PS[:, 0:16], BD, spmb, start=True, stop=True)
                ctx.edge("tensor", "vector")
                nc.vector.tensor_scalar(spm, PS[:, 0:16], Bl2, mul, ALU.add, ALU.mult)
                ctx.bar()
            else:
                nc.vector.tensor_scalar(pair[:, 0:TPC], pair[:, 0:TPC], ss, None,
                                        ALU.mult)
                nc.vector.tensor_scalar(spm, spm, ss, None, ALU.mult)
                ctx.bar()
            for hh in range(2):
                rows = slice(0, 64) if hh == 0 else slice(64, 128)
                nc.vector.tensor_copy(dst[hh][0:64, 0:width], pair[rows, 0:width])
                nc.vector.tensor_copy(dsts[hh][0:64, :], spm[rows, :])
            ctx.bar()
        for hh in range(2):
            for qs in range(16):
                q0 = qs * 128
                nc.tensor.matmul(PS[:, 0:134], qh[hh][0:64, q0:q0 + 128],
                                 kh[hh][0:64, q0:q0 + 134], start=True, stop=False)
                nc.tensor.matmul(PS[:, 0:134], identb, bandp[:, 0:134],
                                 start=False, stop=True)
                ctx.edge("tensor", "vector")
                nc.vector.reduce_max(red, PS[:, 0:134], axis=AX)
                ctx.bar()
                nc.vector.tensor_scalar(red, red, -1.0, None, ALU.mult)
                ctx.edge("vector", "scalar")
                nc.scalar.activation(ptile[:, 0:134], PS[:, 0:134], AF.Exp,
                                     bias=red, scale=1.0)
                ctx.edge("scalar", "vector", "tensor")
                nc.vector.tensor_reduce(red, ptile[:, 0:134], AX, ALU.add)
                ctx.bar()
                nc.vector.reciprocal(rcp, red)
                nc.vector.memset(ptile[:, 134:144], 0.0)
                ctx.edge("vector", "tensor")
                nc.tensor.transpose(PSb[:, 4096:4096 + 128], ptile[:, 0:128], identb)
                nc.tensor.transpose(PSb[0:8, 4352:4352 + 128], ptile[:, 128:136],
                                    identb)
                nc.tensor.transpose(PSb[:, 4608:4608 + 64],
                                    vh[hh][0:64, q0:q0 + 128], identb[0:64, 0:64])
                nc.tensor.transpose(PSb[0:8, 4864:4864 + 64],
                                    vh[hh][0:64, q0 + 128:q0 + 136],
                                    identb[0:64, 0:64])
                ctx.edge("tensor", "vector")
                nc.vector.tensor_copy(pT1, PSb[:, 4096:4096 + 128])
                nc.vector.tensor_copy(pT2[0:8, 0:128], PSb[0:8, 4352:4352 + 128])
                nc.vector.tensor_copy(vt1[:, 0:64], PSb[:, 4608:4608 + 64])
                nc.vector.tensor_copy(vt2[0:8, 0:64], PSb[0:8, 4864:4864 + 64])
                ctx.edge("vector", "tensor")
                nc.tensor.matmul(PS[:, 3 * TT:3 * TT + 64], pT1, vt1[:, 0:64],
                                 start=True, stop=False)
                nc.tensor.matmul(PS[:, 3 * TT:3 * TT + 64], pT2[0:8, 0:128],
                                 vt2[0:8, 0:64], start=False, stop=True)
                ctx.edge("tensor", "vector")
                nc.vector.tensor_scalar(ztk, PS[:, 3 * TT:3 * TT + 64], rcp, None,
                                        ALU.mult)
                ctx.bar()
                if qs == 0:
                    _l3_edge_fix(ctx, sb, qsp[hh], ksp[hh], vsp[hh], ztk, emf,
                                 True, identb, ebufs)
                elif qs == 15:
                    _l3_edge_fix(ctx, sb, qsp[hh], ksp[hh], vsp[hh], ztk, eml,
                                 False, identb, ebufs)
                nc.vector.tensor_copy(ztb, ztk)
                ctx.edge("vector", "tensor")
                nc.tensor.transpose(PSb[0:64, 4096:4096 + 128], ztb[:, 0:64], identb)
                ctx.edge("tensor", "vector")
                nc.vector.tensor_copy(obuf[hc][(0 if hh == 0 else 64):
                                               (64 if hh == 0 else 128), q0:q0 + 128],
                                      PSb[0:64, 4096:4096 + 128])
                ctx.edge("vector", "tensor")

    gt = sb.af(TT)
    tmp = sb.af(TT)
    for t in range(NTT):
        tsl = slice(t * TT, (t + 1) * TT)
        for co in range(NCH):
            for ci in range(NCH):
                nc.tensor.matmul(PS[:, 0:TT], Wp[:, ci * 1024 + co * 128:
                                                 ci * 1024 + (co + 1) * 128],
                                 obuf[ci][:, tsl], start=(ci == 0),
                                 stop=(ci == NCH - 1))
            ctx.edge("tensor", "vector")
            ctx.edge("vector", "sync")
            ctx.dma(gt, g_dram.ap()[co * 128:(co + 1) * 128, tsl])
            nc.sync.wait_ge(ctx.dsem, ctx.dcnt)
            ctx.bar()
            nc.vector.tensor_scalar(tmp, PS[:, 0:TT], Bp[0:128, co:co + 1], None,
                                    ALU.add)
            nc.vector.tensor_mul(tmp, tmp, gt)
            nc.vector.tensor_add(sb.x[co][:, tsl], sb.x[co][:, tsl], tmp)
            ctx.bar()
    sb.release()


def _l3_edge_fix(ctx, sb, qsp, ksp, vsp, ztk, em, is_first, identb, eb):
    nc = ctx.nc
    PS = ctx.PS
    PSb = ctx.PS.ap().bitcast(BF16)
    sp_p, sp_pT, sp_v, sp_vT = eb["sp_p"], eb["sp_pT"], eb["sp_v"], eb["sp_vT"]
    sp_z, spzf, red, rcp, edgep = eb["sp_z"], eb["spzf"], eb["red"], eb["rcp"], eb["edgep"]
    qc = slice(0, 3) if is_first else slice(10, 13)
    sco = 2 * TT
    if is_first:
        nc.tensor.matmul(PS[0:3, sco:sco + 6], qsp[0:64, qc], ksp[0:64, 0:6],
                         start=True, stop=True)
        nc.tensor.matmul(PS[0:3, sco + 6:sco + 7], qsp[0:64, qc], ksp[0:64, 12:13],
                         start=True, stop=True)
        patt = edgep[0:3, 0:7]
        vcols = list(range(0, 6)) + [12]
    else:
        nc.tensor.matmul(PS[0:3, sco:sco + 1], qsp[0:64, qc], ksp[0:64, 0:1],
                         start=True, stop=True)
        nc.tensor.matmul(PS[0:3, sco + 1:sco + 7], qsp[0:64, qc], ksp[0:64, 7:13],
                         start=True, stop=True)
        patt = edgep[0:3, 8:15]
        vcols = [0] + list(range(7, 13))
    ctx.edge("tensor", "vector")
    nc.vector.tensor_copy(sp_p[0:3, 0:7], PS[0:3, sco:sco + 7])
    ctx.bar()
    nc.vector.tensor_tensor(sp_p[0:3, 0:7], sp_p[0:3, 0:7], patt, ALU.add)
    ctx.bar()
    nc.vector.reduce_max(red[0:3, :], sp_p[0:3, 0:7], axis=AX)
    ctx.bar()
    nc.vector.tensor_scalar(red[0:3, :], red[0:3, :], -1.0, None, ALU.mult)
    ctx.edge("vector", "scalar")
    nc.scalar.activation(sp_p[0:3, 0:7], sp_p[0:3, 0:7], AF.Exp, bias=red[0:3, :],
                         scale=1.0)
    ctx.edge("scalar", "vector")
    nc.vector.tensor_reduce(red[0:3, :], sp_p[0:3, 0:7], AX, ALU.add)
    ctx.bar()
    nc.vector.reciprocal(rcp[0:3, :], red[0:3, :])
    for j, vc in enumerate(vcols):
        nc.vector.tensor_copy(sp_v[0:64, j:j + 1], vsp[0:64, vc:vc + 1])
    nc.vector.memset(sp_p[0:3, 7:8], 0.0)
    nc.vector.memset(sp_v[0:64, 7:8], 0.0)
    ctx.edge("vector", "tensor")
    nc.tensor.transpose(PSb[0:8, 4096:4096 + 3], sp_p[0:3, 0:8], identb[0:3, 0:3])
    nc.tensor.transpose(PSb[0:8, 4352:4352 + 64], sp_v[0:64, 0:8],
                        identb[0:64, 0:64])
    ctx.edge("tensor", "vector")
    nc.vector.tensor_copy(sp_pT[0:8, 0:3], PSb[0:8, 4096:4096 + 3])
    nc.vector.tensor_copy(sp_vT[0:8, 0:64], PSb[0:8, 4352:4352 + 64])
    ctx.edge("vector", "tensor")
    nc.tensor.matmul(PS[0:3, sco:sco + 64], sp_pT[0:8, 0:3], sp_vT[0:8, 0:64],
                     start=True, stop=True)
    ctx.edge("tensor", "vector")
    nc.vector.tensor_scalar(sp_z[0:3, 0:64], PS[0:3, sco:sco + 64], rcp[0:3, :],
                            None, ALU.mult)
    ctx.bar()
    nc.vector.memset(spzf[:, 0:64], 0.0)
    ctx.edge("vector", "sync")
    r0 = 0 if is_first else 125
    ctx.dma(spzf[r0:r0 + 3, 0:64], sp_z[0:3, 0:64])
    ctx.dwait()
    nc.vector.tensor_sub(spzf[:, 0:64], spzf[:, 0:64], ztk[:, 0:64])
    nc.vector.tensor_scalar(spzf[:, 0:64], spzf[:, 0:64], em, None, ALU.mult)
    nc.vector.tensor_add(ztk[:, 0:64], ztk[:, 0:64], spzf[:, 0:64])
    ctx.bar()


def emit_l3_ffn(ctx, sb, stats_dram, g_dram, tag="l3f"):
    nc = ctx.nc
    PS = ctx.PS
    cw1 = ctx.inp("L3f_cw1", [1024, 1024], F32)
    cb1 = ctx.inp("L3f_cb1", [128, 8], F32)
    cw2 = ctx.inp("L3f_cw2", [1024, 512], F32)
    cb2 = ctx.inp("L3f_cb2", [128, 4], F32)
    cw3 = ctx.inp("L3f_cw3", [512, 256], F32)
    cb3 = ctx.inp("L3f_cb3", [128, 2], F32)
    cw4 = ctx.inp("L3f_cw4", [128, 2], F32)
    cb4 = ctx.inp("L3f_cb4", [1, 1], F32)
    gw1 = ctx.inp("L3f_gw1", [1024, 512], F32)
    gw2 = ctx.inp("L3f_gw2", [512, 256], F32)
    gw3 = ctx.inp("L3f_gw3", [128, 2], F32)
    gb1 = ctx.inp("L3f_gb1", [128, 4], F32)
    gb2 = ctx.inp("L3f_gb2", [128, 2], F32)
    gb3 = ctx.inp("L3f_gb3", [1, 1], F32)
    arng = ctx.inp("L3f_ar", [1, 1], F32)
    outA = nc.dram_tensor(f"outA_{tag}", [D, TPC], F32, kind="Internal")
    outB = nc.dram_tensor(f"outB_{tag}", [D, TPC], F32, kind="Internal")
    gcin = nc.dram_tensor(f"gci_{tag}", [128, 8], F32, kind="Internal")
    gcout = nc.dram_tensor(f"gco_{tag}", [128, 8], F32, kind="Internal")

    sb.mark()
    W1 = sb.abf(NCH * 1024)
    ctx.dmag(W1, cw1.rearrange("(c p) o -> p c o", p=128))
    W2 = sb.abf(NCH * 512)
    ctx.dmag(W2, cw2.rearrange("(c p) o -> p c o", p=128))
    W3 = sb.abf(4 * 256)
    ctx.dmag(W3, cw3.rearrange("(c p) o -> p c o", p=128))
    W4 = sb.abf(2)
    ctx.dmag(W4, cw4)
    G1 = sb.abf(NCH * 512)
    ctx.dmag(G1, gw1.rearrange("(c p) o -> p c o", p=128))
    G2 = sb.abf(4 * 256)
    ctx.dmag(G2, gw2.rearrange("(c p) o -> p c o", p=128))
    G3 = sb.abf(2)
    ctx.dmag(G3, gw3)
    B1 = sb.af(8)
    ctx.dma(B1, cb1)
    B2 = sb.af(4)
    ctx.dma(B2, cb2)
    B3 = sb.af(2)
    ctx.dma(B3, cb3)
    B4 = sb.af(8)[0:1, 0:1]
    ctx.dma(B4, cb4)
    GB1 = sb.af(4)
    ctx.dma(GB1, gb1)
    GB2 = sb.af(2)
    ctx.dma(GB2, gb2)
    GB3 = sb.af(8)[0:1, 1:2]
    ctx.dma(GB3, gb3)
    AR = sb.af(8)[0:1, 2:3]
    ctx.dma(AR, arng)
    onesb2 = sb.abf(128)[0:1, :]
    strow = sb.af(2 * TT)[0:1, :]
    strowb = sb.abf(2 * TT)[0:1, :]
    ctx.dwait()
    nc.vector.memset(onesb2, 1.0)
    ctx.bar()

    ot = [sb.af(TT) for _ in range(NCH)]
    otb = [sb.abf(TT) for _ in range(NCH)]
    h1t = [sb.abf(TT) for _ in range(NCH)]
    h2t = [sb.abf(TT) for _ in range(4)]
    h3t = [sb.abf(TT) for _ in range(2)]
    twf = sb.af(TT)
    twg = sb.af(TT)
    psum = sb.af(8)
    tmp1 = sb.af(8)[:, 0:1]
    gcb = sb.af(8)
    gcbb = sb.abf(8)
    g1t = [sb.abf(8)[:, i:i + 1] for i in range(4)]
    g2t = [sb.abf(8)[:, 4 + i:5 + i] for i in range(2)]
    trow = sb.af(TT)[0:1, :]
    trow2 = sb.af(TT)[0:1, :]
    trowb = sb.abf(TT)[0:1, :]
    nc.vector.memset(psum, 0.0)
    ctx.bar()
    for t in range(NTT):
        tsl = slice(t * TT, (t + 1) * TT)
        ctx.dma(strow[0:1, 0:TT], stats_dram.ap()[0:1, tsl])
        ctx.dma(strow[0:1, TT:2 * TT], stats_dram.ap()[1:2, tsl])
        ctx.dwait()
        nc.vector.tensor_copy(strowb, strow)
        ctx.edge("vector", "tensor")
        nc.tensor.matmul(PS[:, 2 * TT:3 * TT], onesb2, strowb[0:1, 0:TT],
                         start=True, stop=True)
        nc.tensor.matmul(PS[:, 3 * TT:4 * TT], onesb2, strowb[0:1, TT:2 * TT],
                         start=True, stop=True)
        ctx.edge("tensor", "vector")
        for c in range(NCH):
            nc.vector.tensor_tensor(ot[c], sb.x[c][:, tsl], PS[:, 2 * TT:3 * TT],
                                    ALU.mult)
            nc.vector.tensor_add(ot[c], ot[c], PS[:, 3 * TT:4 * TT])
            nc.vector.reduce_sum(tmp1, ot[c], axis=AX)
            nc.vector.tensor_add(psum[:, c:c + 1], psum[:, c:c + 1], tmp1)
            ctx.edge("vector", "sync")
            ctx.dma(outA.ap()[c * 128:(c + 1) * 128, tsl], ot[c])
        nc.sync.wait_ge(ctx.dsem, ctx.dcnt)
        nc.vector.wait_ge(ctx.dsem, ctx.dcnt)
    ctx.dma(gcin.ap(), psum)
    ctx.dwait()
    ctx.coll("AllReduce", ALU.add, GROUPS_BATCH, gcin.ap(), gcout.ap())
    ctx.cwait()
    ctx.dma(gcb, gcout.ap())
    ctx.dwait()
    nc.vector.tensor_scalar(gcb, gcb, 1.0 / float(S), None, ALU.mult)
    ctx.bar()
    nc.vector.tensor_copy(gcbb, gcb)
    ctx.bar()
    for oc in range(4):
        for c in range(NCH):
            nc.tensor.matmul(PS[:, 3 * TT + oc:3 * TT + oc + 1],
                             G1[:, c * 512 + oc * 128: c * 512 + (oc + 1) * 128],
                             gcbb[:, c:c + 1], start=(c == 0), stop=(c == NCH - 1))
    ctx.edge("tensor", "scalar")
    for oc in range(4):
        nc.scalar.activation(g1t[oc], PS[:, 3 * TT + oc:3 * TT + oc + 1], AF.Gelu,
                             bias=GB1[:, oc:oc + 1], scale=1.0)
    ctx.edge("scalar", "tensor", "vector")
    for oc in range(2):
        for c in range(4):
            nc.tensor.matmul(PS[:, 3 * TT + oc:3 * TT + oc + 1],
                             G2[:, c * 256 + oc * 128: c * 256 + (oc + 1) * 128],
                             g1t[c], start=(c == 0), stop=(c == 3))
    ctx.edge("tensor", "scalar")
    for oc in range(2):
        nc.scalar.activation(g2t[oc], PS[:, 3 * TT + oc:3 * TT + oc + 1], AF.Gelu,
                             bias=GB2[:, oc:oc + 1], scale=1.0)
    ctx.edge("scalar", "tensor", "vector")
    for c in range(2):
        nc.tensor.matmul(PS[0:1, 3 * TT + 4:3 * TT + 5],
                         G3[:, c:c + 1], g2t[c], start=(c == 0), stop=(c == 1))
    ctx.edge("tensor", "scalar")
    gf7 = sb.af(8)[0:1, 3:4]
    ar2 = sb.af(8)[0:1, 4:5]
    nc.scalar.activation(gf7, PS[0:1, 3 * TT + 4:3 * TT + 5], AF.Sigmoid,
                         bias=GB3, scale=1.0)
    ctx.edge("scalar", "vector")
    nc.vector.tensor_scalar(gf7, gf7, 0.7, None, ALU.mult)
    nc.vector.tensor_scalar(ar2, AR, 2.0, None, ALU.mult)
    ctx.bar()

    SIGNS = [1.0, 1.0, -1.0, -1.0, -1.0, -1.0, 1.0, 1.0]
    SIGMA = [4, 5, 6, 7, 0, 1, 2, 3]
    bufs = [outA, outB]
    for cyc in range(3):
        src, dst = bufs[cyc % 2], bufs[(cyc + 1) % 2]
        for t in range(NTT):
            tsl = slice(t * TT, (t + 1) * TT)
            ctx.edge("tensor", "sync")
            ctx.edge("vector", "sync")
            for c in range(NCH):
                ctx.dma(ot[c], src.ap()[c * 128:(c + 1) * 128, tsl])
            ctx.dwait()
            for c in range(NCH):
                nc.vector.tensor_copy(otb[c], ot[c])
            ctx.edge("vector", "tensor")
            for oc in range(NCH):
                for c in range(NCH):
                    nc.tensor.matmul(PS[:, 0:TT],
                                     W1[:, c * 1024 + oc * 128: c * 1024 + (oc + 1) * 128],
                                     otb[c], start=(c == 0), stop=(c == NCH - 1))
                ctx.edge("tensor", "scalar")
                nc.scalar.activation(h1t[oc], PS[:, 0:TT], AF.Gelu,
                                     bias=B1[:, oc:oc + 1], scale=1.0)
                ctx.edge("scalar", "tensor")
            ctx.edge("scalar", "vector")
            for oc in range(4):
                for c in range(NCH):
                    nc.tensor.matmul(PS[:, 0:TT],
                                     W2[:, c * 512 + oc * 128: c * 512 + (oc + 1) * 128],
                                     h1t[c], start=(c == 0), stop=(c == NCH - 1))
                ctx.edge("tensor", "scalar")
                nc.scalar.activation(h2t[oc], PS[:, 0:TT], AF.Gelu,
                                     bias=B2[:, oc:oc + 1], scale=1.0)
                ctx.edge("scalar", "tensor")
            for oc in range(2):
                for c in range(4):
                    nc.tensor.matmul(PS[:, 0:TT],
                                     W3[:, c * 256 + oc * 128: c * 256 + (oc + 1) * 128],
                                     h2t[c], start=(c == 0), stop=(c == 3))
                ctx.edge("tensor", "scalar")
                nc.scalar.activation(h3t[oc], PS[:, 0:TT], AF.Gelu,
                                     bias=B3[:, oc:oc + 1], scale=1.0)
                ctx.edge("scalar", "tensor")
            for c in range(2):
                nc.tensor.matmul(PS[0:1, 2 * TT:3 * TT], W4[:, c:c + 1], h3t[c],
                                 start=(c == 0), stop=(c == 1))
            ctx.edge("tensor", "scalar")
            nc.scalar.activation(trow, PS[0:1, 2 * TT:3 * TT], AF.Sigmoid,
                                 bias=B4, scale=1.0)
            ctx.edge("scalar", "vector")
            nc.vector.tensor_scalar(trow2, trow, 0.3, gf7, ALU.mult, ALU.add)
            nc.vector.tensor_scalar(trow2, trow2, -0.5, None, ALU.add)
            nc.vector.tensor_scalar(trow2, trow2, ar2, 0.1, ALU.mult, ALU.add)
            nc.vector.tensor_copy(trowb, trow2)
            ctx.edge("vector", "tensor")
            nc.tensor.matmul(PS[:, 2 * TT:3 * TT], onesb2, trowb, start=True,
                             stop=True)
            ctx.edge("tensor", "vector")
            last = (cyc == 2)
            for c in range(NCH):
                nc.vector.tensor_tensor(twf, ot[SIGMA[c]], PS[:, 2 * TT:3 * TT],
                                        ALU.mult)
                nc.vector.scalar_tensor_tensor(twg, twf, SIGNS[c], ot[c],
                                               ALU.mult, ALU.add)
                ctx.edge("vector", "sync")
                ctx.dma(dst.ap()[c * 128:(c + 1) * 128, tsl], twg)
                if last:
                    ctx.edge("vector", "scalar")
                    nc.scalar.activation(twf, twg, AF.Silu)
                    ctx.edge("scalar", "sync")
                    ctx.dma(g_dram.ap()[c * 128:(c + 1) * 128, tsl], twf)
                    nc.scalar.wait_ge(ctx.dsem, ctx.dcnt)
                nc.sync.wait_ge(ctx.dsem, ctx.dcnt)
                nc.vector.wait_ge(ctx.dsem, ctx.dcnt)
    sb.release()


def mk_xn_ag(ctx, tag):
    nc = ctx.nc
    xn_in = nc.dram_tensor(f"xnin_{tag}", [D, TPC], BF16, kind="Internal")
    xn_out = nc.dram_tensor(f"xnout_{tag}", [4 * D, TPC], BF16, kind="Internal")
    return xn_in, xn_out


def build_test_l2():
    nc = bass.Bass("TRN2", target_bir_lowering=False, detect_race_conditions=False)
    ctx = Ctx(nc)
    sb = Sbuf(nc)
    x_in = ctx.inp("x_in", [TPC, D], F32)
    g_in = ctx.inp("g_in", [D, TPC], F32)
    x_out = nc.dram_tensor("x_out", [TPC, D], F32, kind="ExternalOutput").ap()

    emit_load_x(ctx, sb, x_in)
    xn_in, xn_out = mk_xn_ag(ctx, "l2")
    emit_layernorm(ctx, sb, xn_in.ap())
    ctx.coll("AllGather", ALU.bypass, GROUPS_BATCH, xn_in.ap(), xn_out.ap())
    ctx.cwait()
    emit_layer2_attn(ctx, sb, xn_out.ap(), g_in)
    emit_store_x(ctx, sb, x_out)
    return nc, ctx



def build_test_layer(li):
    nc = bass.Bass("TRN2", target_bir_lowering=False, detect_race_conditions=False)
    ctx = Ctx(nc)
    sb = Sbuf(nc)
    x_in = ctx.inp("x_in", [TPC, D], F32)
    g_in = ctx.inp("g_in", [D, TPC], F32) if li < 3 else None
    x_out = nc.dram_tensor("x_out", [TPC, D], F32, kind="ExternalOutput").ap()

    emit_load_x(ctx, sb, x_in)
    xn_in, xn_out = mk_xn_ag(ctx, f"l{li}")
    stats3 = nc.dram_tensor("stats3", [2, TPC], F32, kind="Internal") if li == 3 else None
    emit_layernorm(ctx, sb, xn_in.ap(),
                   stats_dram=(stats3.ap() if li == 3 else None))
    ctx.coll("AllGather", ALU.bypass, GROUPS_BATCH, xn_in.ap(), xn_out.ap())
    ctx.cwait()
    if li == 0:
        emit_std_attn_layer(ctx, sb, xn_out.ap(), g_in, 0, False, False, "l0")
    elif li == 1:
        emit_layer1_attn(ctx, sb, xn_out.ap(), xn_in.ap(), g_in)
    elif li == 2:
        emit_std_attn_layer(ctx, sb, xn_out.ap(), g_in, 2, True, True, "l2")
    elif li == 3:
        g_dram = nc.dram_tensor("g3", [D, TPC], F32, kind="Internal")
        emit_l3_ffn(ctx, sb, stats3, g_dram)
        if DEBUG_ATTN:
            gdump = nc.dram_tensor("g_dump", [D, TPC], F32, kind="ExternalOutput").ap()
            sb.mark()
            st_ = sb.af(TPC)
            for c_ in range(NCH):
                ctx.dma(st_, g_dram.ap()[c_ * 128:(c_ + 1) * 128, :])
                ctx.dwait()
                ctx.dma(gdump[c_ * 128:(c_ + 1) * 128, :], st_)
                ctx.dwait()
            sb.release()
        emit_l3_attn(ctx, sb, xn_out.ap(), xn_in.ap(), g_dram)
    emit_store_x(ctx, sb, x_out)
    return nc, ctx



def build_test_ffn(li):
    nc = bass.Bass("TRN2", target_bir_lowering=False, detect_race_conditions=False)
    ctx = Ctx(nc)
    sb = Sbuf(nc)
    x_in = ctx.inp("x_in", [TPC, D], F32)
    g_out = nc.dram_tensor("g_out", [D, TPC], F32, kind="ExternalOutput").ap()
    g_dram = nc.dram_tensor("g_dram", [D, TPC], F32, kind="Internal")
    stats_dram = nc.dram_tensor("stats", [2, TPC], F32, kind="Internal")
    xn_in, xn_out = mk_xn_ag(ctx, f"f{li}")

    emit_load_x(ctx, sb, x_in)
    emit_layernorm(ctx, sb, xn_in.ap(), stats_dram=stats_dram.ap(),
                   track_absmax=(li == 1))
    if li == 0:
        ctx.coll("AllGather", ALU.bypass, GROUPS_BATCH, xn_in.ap(), xn_out.ap())
        ctx.cwait()
        emit_l0_ffn(ctx, sb, xn_out.ap(), xn_in.ap(), g_dram.ap())
    elif li == 2:
        emit_l2_ffn(ctx, sb, stats_dram, g_dram)
    elif li == 1:
        emit_l1_ffn(ctx, sb, stats_dram, g_dram, sb.R[0:1, 2047:2048])
    sb.mark()
    st = sb.af(TPC)
    for c in range(NCH):
        ctx.dma(st, g_dram.ap()[c * 128:(c + 1) * 128, :])
        ctx.dwait()
        ctx.dma(g_out[c * 128:(c + 1) * 128, :], st)
        ctx.dwait()
    sb.release()
    return nc, ctx


def build_test_ln():
    nc = bass.Bass("TRN2", target_bir_lowering=False, detect_race_conditions=False)
    ctx = Ctx(nc)
    sb = Sbuf(nc)
    x_in = ctx.inp("x_in", [TPC, D], F32)
    xn_out = nc.dram_tensor("xn_out", [D, TPC], BF16, kind="ExternalOutput").ap()
    amax_out = nc.dram_tensor("amax_out", [1, 1], F32, kind="ExternalOutput").ap()
    xn_i = nc.dram_tensor("xn_i", [D, TPC], BF16, kind="Internal")

    emit_load_x(ctx, sb, x_in)
    emit_layernorm(ctx, sb, xn_i.ap(), track_absmax=True)
    sb.mark()
    stage = sb.abf(1024)
    for c in range(NCH):
        for t in range(NTT):
            ctx.dma(stage[:, 0:TT], xn_i.ap()[c * 128:(c + 1) * 128, t * TT:(t + 1) * TT])
            ctx.dwait()
            ctx.dma(xn_out[c * 128:(c + 1) * 128, t * TT:(t + 1) * TT], stage[:, 0:TT])
            ctx.dwait()
    ctx.dma(amax_out, sb.R[0:1, 2047:2048])
    ctx.dwait()
    sb.release()
    return nc, ctx



def _f32(a):
    return np.ascontiguousarray(np.asarray(a, np.float32))


def l0a_in_map(inp, c):
    h0 = 4 * (c % 4)
    WK, WQ, WV = inp["L0_attn_W_K"], inp["L0_attn_W_Q"], inp["L0_attn_W_V"]
    cols = []
    for W in (WQ, WK, WV):
        W = _f32(W)
        cols.append(np.concatenate([W[h0 + h].T for h in range(4)], axis=1))
    return {"L0w_qkv_m": np.ascontiguousarray(np.concatenate(cols, axis=1)),
            "L0w_o": np.ascontiguousarray(_f32(inp["L0_attn_W_O"]).T)}


def l1a_in_map(inp, c):
    h0 = 4 * (c % 4)
    wq, wka, wkb = _f32(inp["L1_attn_wq"]), _f32(inp["L1_attn_wk_a"]), \
        _f32(inp["L1_attn_wk_b"])
    wva, wvb, wo = _f32(inp["L1_attn_wv_a"]), _f32(inp["L1_attn_wv_b"]), \
        _f32(inp["L1_attn_wo"])
    im = {}
    im["L1w_q_m"] = np.ascontiguousarray(
        np.concatenate([wq[(h0 + h) * 64:(h0 + h + 1) * 64, :]
                        for h in range(4)], axis=1))
    im["L1w_kb_m"] = np.ascontiguousarray(
        np.concatenate([wkb[(h0 + h) * 64:(h0 + h + 1) * 64, :]
                        for h in range(4)], axis=1))
    im["L1w_vbT_m"] = np.ascontiguousarray(
        np.concatenate([wvb[(h0 + h) * 64:(h0 + h + 1) * 64, :].T
                        for h in range(4)], axis=1))
    im["L1w_kaT"] = np.ascontiguousarray(wka.T)
    im["L1w_vaT"] = np.ascontiguousarray(wva.T)
    im["L1w_oT"] = np.ascontiguousarray(wo.T)
    RTs = []
    for h in range(4):
        hh = h0 + h
        R = np.eye(64, dtype=np.float64)
        for j in range(8):
            a = hh * (1.0 / 10000.0 ** (2.0 * j / 64.0))
            ca, sa = np.cos(a), np.sin(a)
            R[2 * j, 2 * j] = ca
            R[2 * j, 2 * j + 1] = -sa
            R[2 * j + 1, 2 * j] = sa
            R[2 * j + 1, 2 * j + 1] = ca
        RTs.append(R.T.astype(np.float32))
    im["L1_RT_m"] = np.ascontiguousarray(np.concatenate(RTs, axis=1))
    return im


def l2a_in_map(inp, c):
    h0 = 4 * (c % 4)
    qkv_w, qkv_b = _f32(inp["L2_attn_qkv_w"]), _f32(inp["L2_attn_qkv_b"])
    wcols = np.concatenate([qkv_w[:, p * 1024 + h0 * 64: p * 1024 + (h0 + 4) * 64]
                            for p in range(3)], axis=1)
    bcols = np.stack([qkv_b[p * 1024 + (h0 + h) * 64: p * 1024 + (h0 + h + 1) * 64]
                      for p in range(3) for h in range(4)], axis=1)
    return {"L2w_qkv_m": np.ascontiguousarray(wcols),
            "L2b_qkv_m": np.ascontiguousarray(bcols),
            "L2w_o": _f32(inp["L2_attn_out_w"]),
            "L2b_o": np.ascontiguousarray(_f32(inp["L2_attn_out_b"]).reshape(8, 128).T)}


def l1f_in_map(inp, c):
    wg, wu, wd = _f32(inp["L1_ffn_w_gate"]), _f32(inp["L1_ffn_w_up"]), \
        _f32(inp["L1_ffn_w_down"])
    return {"L1f_wg_m": np.ascontiguousarray(wg[c * 384:(c + 1) * 384]),
            "L1f_wu_m": np.ascontiguousarray(wu[c * 384:(c + 1) * 384]),
            "L1f_wd_m": np.ascontiguousarray(wd[c * 128:(c + 1) * 128])}


def l2f_in_map(inp, c):
    wg, wu, wd = _f32(inp["L2_ffn_w_gate"]), _f32(inp["L2_ffn_w_up"]), \
        _f32(inp["L2_ffn_w_down"])
    wgp = np.zeros((3072, 1024), np.float32)
    wgp[0:2816] = wg
    wup = np.zeros((3072, 1024), np.float32)
    wup[0:2816] = wu
    return {"L2f_wg_m": np.ascontiguousarray(wgp[c * 384:(c + 1) * 384]),
            "L2f_wu_m": np.ascontiguousarray(wup[c * 384:(c + 1) * 384]),
            "L2f_wd_m": np.ascontiguousarray(wd[c * 128:(c + 1) * 128])}


def l3_in_map(inp, c):
    q = c % 4
    Wqkv, bq = _f32(inp["L3_attn_W_qkv"]), _f32(inp["L3_attn_b_qkv"])
    Wlat, blat = _f32(inp["L3_attn_W_lat"]), _f32(inp["L3_attn_b_lat"])
    Wproj, bproj = _f32(inp["L3_attn_W_proj"]), _f32(inp["L3_attn_b_proj"])
    im = {}
    bb = np.zeros((128, 24), np.float32)
    for sec in range(3):
        for hc in range(8):
            bb[:, sec * 8 + hc] = bq[sec * 1024 + hc * 128: sec * 1024 + (hc + 1) * 128]
    im["L3w_qkvT"] = np.ascontiguousarray(Wqkv.T)
    im["L3b_qkv"] = bb
    BD = np.zeros((128, 128), np.float32)
    BD[0:64, 0:64] = Wlat.T
    BD[64:128, 64:128] = Wlat.T
    im["L3w_latBD"] = BD
    im["L3b_lat2"] = np.ascontiguousarray(np.concatenate([blat, blat]).reshape(128, 1))
    im["L3w_projT"] = np.ascontiguousarray(Wproj.T)
    im["L3b_proj"] = np.ascontiguousarray(bproj.reshape(8, 128).T)
    emf = np.zeros((128, 1), np.float32)
    eml = np.zeros((128, 1), np.float32)
    if q == 0:
        emf[0:3] = 1.0
    if q == 3:
        eml[125:128] = 1.0
    im["L3_em_first"] = emf
    im["L3_em_last"] = eml
    im["L3f_cw1"] = _f32(inp["L3_ffn_cn_w1"])
    im["L3f_cb1"] = np.ascontiguousarray(_f32(inp["L3_ffn_cn_b1"]).reshape(8, 128).T)
    im["L3f_cw2"] = _f32(inp["L3_ffn_cn_w2"])
    im["L3f_cb2"] = np.ascontiguousarray(_f32(inp["L3_ffn_cn_b2"]).reshape(4, 128).T)
    im["L3f_cw3"] = _f32(inp["L3_ffn_cn_w3"])
    im["L3f_cb3"] = np.ascontiguousarray(_f32(inp["L3_ffn_cn_b3"]).reshape(2, 128).T)
    im["L3f_cw4"] = np.ascontiguousarray(_f32(inp["L3_ffn_cn_w4"]).reshape(2, 128).T)
    im["L3f_cb4"] = np.ascontiguousarray(_f32(inp["L3_ffn_cn_b4"]).reshape(1, 1))
    im["L3f_gw1"] = _f32(inp["L3_ffn_gc_w1"])
    im["L3f_gb1"] = np.ascontiguousarray(_f32(inp["L3_ffn_gc_b1"]).reshape(4, 128).T)
    im["L3f_gw2"] = _f32(inp["L3_ffn_gc_w2"])
    im["L3f_gb2"] = np.ascontiguousarray(_f32(inp["L3_ffn_gc_b2"]).reshape(2, 128).T)
    im["L3f_gw3"] = np.ascontiguousarray(_f32(inp["L3_ffn_gc_w3"]).reshape(2, 128).T)
    im["L3f_gb3"] = np.ascontiguousarray(_f32(inp["L3_ffn_gc_b3"]).reshape(1, 1))
    im["L3f_ar"] = np.ascontiguousarray(_f32(inp["L3_ffn_adaptive_range"]).reshape(1, 1))
    return im



DEBUG_FUSED = False


def build_fused():
    nc = bass.Bass("TRN2", target_bir_lowering=False, detect_race_conditions=False)
    ctx = Ctx(nc)
    sb = Sbuf(nc)
    x_in = ctx.inp("x_in", [TPC, D], F32)
    x_out = nc.dram_tensor("x_out", [TPC, D], F32, kind="ExternalOutput").ap()

    emit_load_x(ctx, sb, x_in)
    for li in range(4):
        if DEBUG_FUSED and li > 0:
            dx = nc.dram_tensor(f"dbg_x{li}", [TPC, D], F32,
                                kind="ExternalOutput").ap()
            emit_store_x(ctx, sb, dx)
        xn_in, xn_out = mk_xn_ag(ctx, f"L{li}")
        stats = nc.dram_tensor(f"stats_L{li}", [2, TPC], F32, kind="Internal") \
            if li in (1, 2, 3) else None
        g_dram = nc.dram_tensor(f"g_L{li}", [D, TPC], F32, kind="Internal")
        emit_layernorm(ctx, sb, xn_in.ap(),
                       stats_dram=(stats.ap() if stats is not None else None),
                       track_absmax=(li == 1))
        ctx.coll("AllGather", ALU.bypass, GROUPS_BATCH, xn_in.ap(), xn_out.ap())
        ctx.cwait()
        if li == 0:
            emit_l0_ffn(ctx, sb, xn_out.ap(), xn_in.ap(), g_dram.ap())
            emit_std_attn_layer(ctx, sb, xn_out.ap(), g_dram.ap(), 0,
                                False, False, "l0")
        elif li == 1:
            emit_l1_ffn(ctx, sb, stats, g_dram, sb.R[0:1, 2047:2048])
            emit_layer1_attn(ctx, sb, xn_out.ap(), xn_in.ap(), g_dram.ap())
        elif li == 2:
            emit_l2_ffn(ctx, sb, stats, g_dram)
            emit_layer2_attn(ctx, sb, xn_out.ap(), g_dram.ap())
        else:
            emit_l3_ffn(ctx, sb, stats, g_dram)
            if DEBUG_FUSED:
                dg = nc.dram_tensor("dbg_g3", [D, TPC], F32,
                                    kind="ExternalOutput").ap()
                sb.mark()
                st_ = sb.af(TPC)
                for c_ in range(NCH):
                    ctx.dma(st_, g_dram.ap()[c_ * 128:(c_ + 1) * 128, :])
                    ctx.dwait()
                    ctx.dma(dg[c_ * 128:(c_ + 1) * 128, :], st_)
                    ctx.dwait()
                sb.release()
            emit_l3_attn(ctx, sb, xn_out.ap(), xn_in.ap(), g_dram)
    emit_store_x(ctx, sb, x_out)
    return nc, ctx


def fused_in_map(inputs, c):
    im = {}
    im.update(l0f_in_map(inputs, c))
    im.update(l0a_in_map(inputs, c))
    im.update(l1f_in_map(inputs, c))
    im.update(l1a_in_map(inputs, c))
    im.update(l2f_in_map(inputs, c))
    im.update(l2a_in_map(inputs, c))
    im.update(l3_in_map(inputs, c))
    return im



FUSED = True
_PROGS = {}


def _prog(name, builder):
    if name not in _PROGS:
        _PROGS[name] = builder()
    return _PROGS[name]


def kernel(**inputs):
    x = np.ascontiguousarray(np.asarray(inputs["L0_attn_x"], np.float32))

    def xsl(c):
        return np.ascontiguousarray(x[c // 4, (c % 4) * TPC:(c % 4 + 1) * TPC, :])

    def launch(prog, maps):
        nc, ctx = prog
        return run_bass_kernel_spmd(nc, maps, core_ids=list(range(8))).results

    if FUSED:
        res = launch(_prog("fused", build_fused),
                     [{"x_in": xsl(c), **fused_in_map(inputs, c)}
                      for c in range(NCORE)])
        xo = np.empty_like(x)
        for c in range(NCORE):
            xo[c // 4, (c % 4) * TPC:(c % 4 + 1) * TPC, :] = res[c]["x_out"]
        return xo

    for li in range(4):
        if li == 3:
            res = launch(_prog("l3", lambda: build_test_layer(3)),
                         [{"x_in": xsl(c), **l3_in_map(inputs, c)}
                          for c in range(NCORE)])
        else:
            fmap = (l0f_in_map, l1f_in_map, l2f_in_map)[li]
            amap = (l0a_in_map, l1a_in_map, l2a_in_map)[li]
            resf = launch(_prog(f"f{li}", lambda li=li: build_test_ffn(li)),
                          [{"x_in": xsl(c), **fmap(inputs, c)}
                           for c in range(NCORE)])
            res = launch(_prog(f"a{li}", lambda li=li: build_test_layer(li)),
                         [{"x_in": xsl(c), "g_in": resf[c]["g_out"],
                           **amap(inputs, c)} for c in range(NCORE)])
        xo = np.empty_like(x)
        for c in range(NCORE):
            xo[c // 4, (c % 4) * TPC:(c % 4 + 1) * TPC, :] = res[c]["x_out"]
        x = xo
    return x


if __name__ == "__main__":
    pass
